# version 04
# baseline (speedup 1.0000x reference)
import numpy as np
from contextlib import ExitStack

import concourse.bass as bass
import concourse.mybir as mybir
from concourse.alu_op_type import AluOpType as OP
from concourse.bass_utils import run_bass_kernel_spmd

F32 = mybir.dt.float32
BF16 = mybir.dt.bfloat16
AF = mybir.ActivationFunctionType
AX = mybir.AxisListType

T = 4096
D = 2048
NT = T // 512
ENGINES = ("tensor", "vector", "scalar", "sync", "gpsimd")


def host_constants():
    c = {}
    c["c_ident"] = np.eye(128, dtype=np.float32)
    c["c_ident_bf"] = np.eye(128, dtype=np.float32)
    ROPE = 64
    inv_freq = 1.0 / (10000.0 ** (np.arange(0, ROPE, 2, dtype=np.float64) / ROPE))
    t = np.arange(T, dtype=np.float64)
    freqs = np.outer(t, inv_freq)
    emb = np.concatenate([freqs, freqs], axis=-1)
    cosT = np.cos(emb).T.astype(np.float32)
    sinT = np.sin(emb).T.astype(np.float32)
    c["c_cos2"] = np.concatenate([cosT, cosT], axis=0)
    c["c_sin2"] = np.concatenate([sinT, sinT], axis=0)
    R64 = np.zeros((64, 64), dtype=np.float32)
    for p in range(32):
        R64[p, p + 32] = -1.0
        R64[p + 32, p] = 1.0
    R128 = np.zeros((128, 128), dtype=np.float32)
    R128[:64, :64] = R64
    R128[64:, 64:] = R64
    c["c_rrot_lhsT"] = R128.T.copy()
    c["c_uincl"] = np.triu(np.ones((128, 128), dtype=np.float32))
    m = np.zeros((128, 128), dtype=np.float32)
    m[np.tril_indices(128, -1)] = -1e30
    c["c_maskneg"] = m
    H = np.array([[1.0]])
    while H.shape[0] < 128:
        H = np.block([[H, H], [H, -H]])
    c["c_h128s"] = (H * (8192.0 ** -0.5)).astype(np.float32)
    c["c_ones_col"] = np.ones((128, 1), dtype=np.float32)
    c["c_ones_row"] = np.ones((1, 128), dtype=np.float32)
    return c


class Sem:
    __slots__ = ("h", "n", "floor")

    def __init__(self, h):
        self.h = h
        self.n = 0
        self.floor = 0


class Ctx:
    def __init__(self, nc, es):
        self.nc = nc
        self.es = es
        self.waited = {}
        self.fences = {}
        self.stage_idx = 0
        self.s_bar = self.sem("s_bar")
        self.sems = {}

    def sem(self, name):
        return Sem(self.es.enter_context(self.nc.semaphore(name)))

    def inc(self, engname, instr, s, k=1):
        instr.then_inc(s.h, k)
        s.n += k
        self.fences.setdefault(engname, {})[id(s)] = (s, s.n)
        return s.n

    def wait(self, eng, engname, s, val=None):
        if val is None:
            val = s.n
        if val <= s.floor:
            return
        w = self.waited.setdefault(engname, {})
        if w.get(id(s), -1) < val:
            eng.wait_ge(s.h, val)
            w[id(s)] = val


class Rec:

    def __init__(self, ctx):
        self.ctx = ctx
        self.ops = {e: [] for e in ENGINES}
        self.waited = {e: {} for e in ENGINES}
        self.fences = {e: {} for e in ENGINES}

    def wait(self, engname, s, val=None):
        if val is None:
            val = s.n
        assert val <= s.n, "wait on a not-yet-recorded increment (deadlock)"
        if val <= s.floor:
            return
        w = self.waited[engname]
        if w.get(id(s), -1) >= val:
            return
        w[id(s)] = val
        self.ops[engname].append(lambda eng, h=s.h, v=val: eng.wait_ge(h, v))

    def add(self, engname, fn, sem=None, k=1):
        if sem is not None:
            sem.n += k
            self.fences[engname][id(sem)] = (sem, sem.n)
            def thunk(eng, fn=fn, h=sem.h, kk=k):
                fn(eng).then_inc(h, kk)
            self.ops[engname].append(thunk)
            return sem.n
        self.ops[engname].append(lambda eng, fn=fn: fn(eng))
        return None

    def mm_group(self, psum_ap, pairs, sem):
        n = len(pairs)
        def fn(eng, pairs=tuple(pairs), psum_ap=psum_ap, n=n):
            for i, (l, r) in enumerate(pairs[:-1]):
                eng.matmul(psum_ap, l, r, start=(i == 0), stop=False)
            l, r = pairs[-1]
            return eng.matmul(psum_ap, l, r, start=(n == 1), stop=True)
        return self.add("tensor", fn, sem, 1)

    def as_emitters(self):
        def mk(nm):
            def f(eng):
                for th in self.ops[nm]:
                    th(eng)
            return f
        return {nm: mk(nm) for nm in ENGINES if self.ops[nm]}


def run_rec_stage(ctx, name, rec, track_sems):
    idx = ctx.stage_idx
    base = 5 * idx
    with ctx.nc.Block() as block:
        for engname in ENGINES:
            def make(nm):
                def f(eng):
                    if idx > 0:
                        eng.wait_ge(ctx.s_bar.h, base)
                    for th in rec.ops[nm]:
                        th(eng)
                    for s, cnt in rec.fences[nm].values():
                        w = rec.waited[nm]
                        if w.get(id(s), -1) < cnt:
                            eng.wait_ge(s.h, cnt)
                            w[id(s)] = cnt
                    eng.nop().then_inc(ctx.s_bar.h, 1)
                return f
            getattr(block, engname)(make(engname))
    ctx.s_bar.n += 5
    ctx.stage_idx += 1
    for s in track_sems:
        s.floor = s.n


def run_stage(ctx, name, emitters, track_sems):
    idx = ctx.stage_idx
    base = 5 * idx
    with ctx.nc.Block() as block:
        for engname in ENGINES:
            def make(nm):
                def f(eng):
                    if idx > 0:
                        eng.wait_ge(ctx.s_bar.h, base)
                    ctx.waited[nm] = {}
                    ctx.fences[nm] = {}
                    em = emitters.get(nm)
                    if em:
                        em(eng)
                    for s, cnt in ctx.fences[nm].values():
                        ctx.wait(eng, nm, s, cnt)
                    eng.nop().then_inc(ctx.s_bar.h, 1)
                return f
            getattr(block, engname)(make(engname))
    ctx.s_bar.n += 5
    ctx.stage_idx += 1
    for s in track_sems:
        s.floor = s.n


def emit_import(ctx, sems, x_in, XT, sb, psb):
    nc = ctx.nc
    s_ld, s_t, s_sc, s_st = sems["ld"], sems["t"], sems["sc"], sems["st"]
    xa = sb("s0_xa", [128, 4 * 2048], F32)
    ob = sb("s0_ob", [128, 16 * 512], F32)
    ident = sb("s0_id", [128, 128], F32)
    pss = [psb(f"s0_ps{i}", [128, 512], F32) for i in range(4)]

    n_chunk = NT
    b_ld, b_t, b_sc, b_st = s_ld.n, s_t.n, s_sc.n, s_st.n

    def em_sync(s):
        ctx.inc("sync", s.dma_start(ident[:, :], ctx.cident), s_ld, 16)
        for c in range(n_chunk):
            if c >= 1:
                ctx.wait(s, "sync", s_t, b_t + c * 16)
            for k in range(4):
                t0 = c * 512 + k * 128
                ctx.inc("sync", s.dma_start(xa[:, k * 2048:(k + 1) * 2048],
                                            x_in[t0:t0 + 128, :]), s_ld, 16)

    def em_tensor(t):
        ctx.wait(t, "tensor", s_ld, b_ld + 16)
        for c in range(n_chunk):
            ctx.wait(t, "tensor", s_ld, b_ld + 16 + (c + 1) * 64)
            for d in range(16):
                bank = pss[d % 4]
                if c * 16 + d >= 4:
                    ctx.wait(t, "tensor", s_sc, b_sc + c * 16 + d - 3)
                for k in range(4):
                    ins = t.transpose(bank[:, k * 128:(k + 1) * 128],
                                      xa[:, k * 2048 + d * 128: k * 2048 + (d + 1) * 128],
                                      ident[:, :])
                ctx.inc("tensor", ins, s_t, 1)

    def em_scalar(sc):
        for c in range(n_chunk):
            if c >= 1:
                ctx.wait(sc, "scalar", s_st, b_st + c * 16 * 16)
            for d in range(16):
                bank = pss[d % 4]
                ctx.wait(sc, "scalar", s_t, b_t + c * 16 + d + 1)
                ctx.inc("scalar", sc.copy(ob[:, d * 512:(d + 1) * 512], bank[:, :]), s_sc, 1)
                ctx.wait(sc, "scalar", s_sc, s_sc.n)
                ctx.inc("scalar", sc.dma_start(XT[d, :, c * 512:(c + 1) * 512],
                                               ob[:, d * 512:(d + 1) * 512]), s_st, 16)

    return {"sync": em_sync, "tensor": em_tensor, "scalar": em_scalar}


def emit_export(ctx, sems, XT, y_out, sb, psb):
    s_ld, s_t, s_sc, s_st = sems["ld"], sems["t"], sems["sc"], sems["st"]
    xa = sb("s7_xa", [128, 16 * 512], F32)
    ob = sb("s7_ob", [128, 2048], F32)
    ident = sb("s7_id", [128, 128], F32)
    pss = [psb(f"s7_ps{i}", [128, 512], F32) for i in range(4)]
    b_ld, b_t, b_sc, b_st = s_ld.n, s_t.n, s_sc.n, s_st.n

    def em_sync(s):
        ctx.inc("sync", s.dma_start(ident[:, :], ctx.cident), s_ld, 16)
        for c in range(NT):
            if c >= 1:
                ctx.wait(s, "sync", s_t, b_t + c * 4)
            for d in range(16):
                ctx.inc("sync", s.dma_start(xa[:, d * 512:(d + 1) * 512],
                                            XT[d, :, c * 512:(c + 1) * 512]), s_ld, 16)

    def em_tensor(t):
        ctx.wait(t, "tensor", s_ld, b_ld + 16)
        for c in range(NT):
            ctx.wait(t, "tensor", s_ld, b_ld + 16 + (c + 1) * 256)
            for k in range(4):
                for grp in range(4):
                    bank = pss[grp]
                    if c * 4 + k >= 1:
                        ctx.wait(t, "tensor", s_sc, b_sc + (c * 4 + k - 1) * 4 + grp + 1)
                    for dd in range(4):
                        d = grp * 4 + dd
                        ins = t.transpose(bank[:, dd * 128:(dd + 1) * 128],
                                          xa[:, d * 512 + k * 128: d * 512 + k * 128 + 128],
                                          ident[:, :])
                ctx.inc("tensor", ins, s_t, 1)

    def em_scalar(sc):
        for c in range(NT):
            for k in range(4):
                i = c * 4 + k
                ctx.wait(sc, "scalar", s_t, b_t + i + 1)
                if i >= 1:
                    ctx.wait(sc, "scalar", s_st, b_st + i * 16)
                for grp in range(4):
                    bank = pss[grp]
                    ctx.inc("scalar", sc.copy(ob[:, grp * 512:(grp + 1) * 512], bank[:, :]), s_sc, 1)
                ctx.wait(sc, "scalar", s_sc, s_sc.n)
                t0 = c * 512 + k * 128
                ctx.inc("scalar", sc.dma_start(y_out[t0:t0 + 128, :], ob[:, :]), s_st, 16)

    return {"sync": em_sync, "tensor": em_tensor, "scalar": em_scalar}



def rec_prep(ctx, sems, P):
    rec = Rec(ctx)
    s_ld, s_ldg, s_g = sems["ld"], sems["ldg"], sems["g"]
    C = ctx.consts
    rec.add("sync", lambda e: e.dma_start(P["uincl_f"][:, :], C["c_uincl"]), s_ld, 16)
    rec.add("sync", lambda e: e.dma_start(P["maskneg"][:, :], C["c_maskneg"]), s_ld, 16)
    rec.add("sync", lambda e: e.dma_start(P["ones_row_f"][:, :], C["c_ones_row"]), s_ld, 16)
    rec.add("sync", lambda e: e.dma_start(P["ident_f"][:, :], C["c_ident"]), s_ld, 16)
    rec.add("sync", lambda e: e.dma_start(P["ones_col_f"][:, :], C["c_ones_col"]), s_ld, 16)
    rec.add("gpsimd", lambda e: e.dma_start(P["ident_b"][:, :], C["c_ident"]), s_ldg, 16)
    rec.add("gpsimd", lambda e: e.dma_start(P["uincl_b"][:, :], C["c_uincl"]), s_ldg, 16)
    rec.add("gpsimd", lambda e: e.dma_start(P["h128s_b"][:, :], C["c_h128s"]), s_ldg, 16)
    rec.add("gpsimd", lambda e: e.dma_start(P["rrot_b"][:, :], C["c_rrot_lhsT"]), s_ldg, 16)
    rec.add("gpsimd", lambda e: e.dma_start(P["ones_col_b"][:, :], C["c_ones_col"]), s_ldg, 16)
    W = ctx.wdram
    for src, dst in (("w_qa", "QAB"), ("w_qb", "QBB"), ("w_kva", "KVAB"),
                     ("w_kvb", "KVBB"), ("w_ow", "OWB")):
        rec.add("gpsimd", lambda e, a=ctx.wins[src], b=W[dst]: e.dma_start(b, a), s_ldg, 16)
    rec.add("gpsimd", lambda e: e.memset(P["ones_row_b"][:, :], 1.0), s_g, 1)
    rec.add("gpsimd", lambda e: e.memset(P["ones_1b"][:, :], 1.0), s_g, 1)
    rec.add("gpsimd", lambda e: e.memset(P["eps"][:, 0:1], 1e-5), s_g, 1)
    rec.add("gpsimd", lambda e: e.memset(P["eps"][:, 1:2], 1e-6), s_g, 1)
    rec.add("gpsimd", lambda e: e.memset(P["eps"][:, 2:3], 1e-8), s_g, 1)
    rec.add("gpsimd", lambda e: e.memset(P["eps"][:, 3:4], 1e-30), s_g, 1)
    ones_row_T = P["rowT_b"]
    rec.add("gpsimd", lambda e: e.memset(ones_row_T[:, :], -1.0), s_g, 1)
    rec.wait("gpsimd", s_g)
    for cc in range(NT):
        rec.add("gpsimd", lambda e, cc=cc: e.dma_start(
            ctx.scr["KPEA"][64:65, cc * 512:(cc + 1) * 512], ones_row_T[:, :]), s_ldg, 16)
    return rec


def rec_ln_chunk(rec, sems, P, bufs, c, n_feat_tiles, inv_n, eps_col):
    ctx = rec.ctx
    s_t, s_v, s_sc, s_g = sems["t"], sems["v"], sems["sc"], sems["g"]
    xt, xbf, sqbf = bufs["xt"], bufs["xbf"], bufs["sqbf"]
    xln_at = bufs.get("xln_at") or (lambda d: bufs["xln"][:, d, :])
    tmpf, rows, ab = bufs["tmpf"], bufs["rows"], bufs["ab"]
    ps_row0, ps_row1 = bufs["ps_row0"], bufs["ps_row1"]
    ps_bc0, ps_bc1 = bufs["ps_bc0"], bufs["ps_bc1"]
    nf = n_feat_tiles
    v_cast = []
    sc_sq = []
    for d in range(nf):
        if xbf is not None:
            rec.add("vector", lambda e, d=d: e.tensor_copy(xbf[:, d, :], xt[:, d, :]), s_v, 1)
            v_cast.append(s_v.n)
        rec.add("scalar", lambda e, d=d: e.activation(sqbf[:, d, :], xt[:, d, :], AF.Square), s_sc, 1)
        sc_sq.append(s_sc.n)
    xt_ready = bufs.get("xt_ready")
    if xt_ready:
        rec.wait("tensor", sems[xt_ready[0]], xt_ready[1])
    if xbf is not None:
        rec.wait("tensor", s_v, v_cast[-1])
        rec.mm_group(ps_row0[0:1, :], [(P["ones_col_b"][:, :], xbf[:, d, :]) for d in range(nf)], s_t)
    else:
        rec.mm_group(ps_row0[0:1, :], [(P["ones_col_f"][:, :], xt[:, d, :]) for d in range(nf)], s_t)
    t_sum = s_t.n
    rec.wait("tensor", s_sc, sc_sq[-1])
    rec.mm_group(ps_row1[0:1, :], [(P["ones_col_b"][:, :], sqbf[:, d, :]) for d in range(nf)], s_t)
    t_ssq = s_t.n
    mu = rows[:, 0:512]; msq = rows[:, 512:1024]; var = rows[:, 1024:1536]
    sd = rows[:, 1536:2048]; rstd = rows[:, 512:1024]; Br = rows[:, 1024:1536]
    rec.wait("vector", s_t, t_sum)
    rec.add("vector", lambda e: e.tensor_scalar(mu, ps_row0[0:1, :], inv_n, None, OP.mult), s_v, 1)
    rec.wait("vector", s_t, t_ssq)
    rec.add("vector", lambda e: e.tensor_scalar(msq, ps_row1[0:1, :], inv_n, None, OP.mult), s_v, 1)
    rec.wait("vector", s_v)
    rec.add("vector", lambda e: e.tensor_tensor(var, mu, mu, OP.mult), s_v, 1)
    rec.wait("vector", s_v)
    rec.add("vector", lambda e: e.tensor_tensor(var, msq, var, OP.subtract), s_v, 1)
    rec.wait("scalar", s_v)
    rec.add("scalar", lambda e: e.activation(sd, var, AF.Sqrt, bias=P["eps"][0:1, eps_col:eps_col + 1]), s_sc, 1)
    rec.wait("vector", s_sc)
    rec.add("vector", lambda e: e.reciprocal(rstd, sd), s_v, 1)
    rec.wait("vector", s_v)
    rec.add("vector", lambda e: e.scalar_tensor_tensor(Br, mu, -1.0, rstd, OP.mult, OP.mult), s_v, 1)
    v_stats = s_v.n
    rec.wait("tensor", s_v, v_stats)
    rec.add("tensor", lambda e: e.matmul(ps_bc0[:, :], P["ones_row_f"][:, :], rstd, start=True, stop=True), s_t, 1)
    rec.add("tensor", lambda e: e.matmul(ps_bc1[:, :], P["ones_row_f"][:, :], Br, start=True, stop=True), s_t, 1)
    t_bc = s_t.n
    A_b = ab[:, 0:512]; B_b = ab[:, 512:1024]
    rec.wait("scalar", s_t, t_bc)
    ab_guard = bufs.get("ab_guard")
    if ab_guard:
        rec.wait("scalar", s_v, ab_guard[0])
        rec.wait("scalar", s_g, ab_guard[1])
    rec.add("scalar", lambda e: e.copy(A_b, ps_bc0[:, :]), s_sc, 1)
    rec.add("scalar", lambda e: e.copy(B_b, ps_bc1[:, :]), s_sc, 1)
    sc_ab = s_sc.n
    gdone = []
    for d in range(nf):
        rec.wait("vector", s_sc, sc_ab)
        if d >= 2:
            rec.wait("vector", s_g, gdone[d - 2])
        rec.add("vector", lambda e, d=d: e.tensor_tensor(tmpf[:, d % 2, :], xt[:, d, :], A_b, OP.mult), s_v, 1)
        rec.wait("gpsimd", s_v)
        rec.wait("gpsimd", s_sc, sc_ab)
        rec.wait("gpsimd", s_t, t_sum)
        rec.add("gpsimd", lambda e, d=d: e.tensor_tensor(xln_at(d), tmpf[:, d % 2, :], B_b, OP.add), s_g, 1)
        gdone.append(s_g.n)
    return {
        "xt_free": (("v", s_v.n), ("sc", sc_sq[-1])) + ((("t", t_sum),) if xbf is None else ()),
        "xbf_free_t": t_sum, "sqbf_free_t": t_ssq,
        "psrow_free_v": s_v.n, "psbc_free_sc": sc_ab,
        "ab_free": (s_v.n, gdone[-1]),
        "xln_done_g": gdone[-1], "xln_done_list": gdone,
    }



def rec_s1a(ctx, sems, P):
    rec = Rec(ctx)
    nc = ctx.nc
    s_ld, s_t, s_v, s_sc, s_g, s_st, s_stg = (sems[k] for k in ("ld", "t", "v", "sc", "g", "st", "stg"))
    sb, psb = ctx.sb, ctx.psb
    scr, W = ctx.scr, ctx.wdram

    qa_wT = sb("qa_wT", [128, 16, 1536], BF16)
    kva_wT = sb("kva_wT", [128, 16, 576], BF16)
    wq_cols = sb("wq_cols", [128, 12], F32)
    wkv_cols = sb("wkv_cols", [128, 4], F32)
    xt = sb("xt", [128, 16, 512], F32)
    xbf = sb("xbf", [128, 16, 512], BF16)
    sqbf = sb("sqbf", [128, 16, 512], BF16)
    xln = xbf
    tmpf = sb("tmpf", [128, 2, 512], F32)
    rows = sb("rows", [1, 12 * 512], F32)
    ab = sb("ab", [128, 2 * 512], F32)
    qa_bf = sb("qa_bf", [128, 12, 512], BF16)
    qsq_bf = sqbf[:, 0:12, :]
    qar_bf = sb("qar_bf", [128, 2, 512], BF16)
    rq_b = sb("rq_b", [128, 512], F32)
    ckv_bf = sb("ckv_bf", [128, 4, 512], BF16)
    ckvsq_bf = sqbf[:, 12:16, :]
    ckvr_bf = sb("ckvr_bf", [128, 2, 512], BF16)
    rkv_b = sb("rkv_b", [128, 512], F32)
    cosk = sb("cosk", [64, 512], F32)
    sink = sb("sink", [64, 512], F32)
    kraw_bf = sb("kraw_bf", [64, 512], BF16)
    krope_t = sb("krope_t", [64, 2 * 512], F32)
    kpe_bf = sb("kpe_bf", [64, 512], BF16)

    ps_q = [psb("ps_q0", [128, 512], F32), psb("ps_q1", [128, 512], F32)]
    ps_row0 = psb("ps_row0", [128, 512], F32)
    ps_row1 = psb("ps_row1", [128, 512], F32)
    ps_bc0 = psb("ps_bc0", [128, 512], F32)
    ps_bc1 = psb("ps_bc1", [128, 512], F32)
    ps_rot = psb("ps_rot", [128, 512], F32)

    for d in range(16):
        for rch in range(3):
            rec.add("sync", lambda e, d=d, r=rch: e.dma_start_transpose(
                qa_wT[:, d, r * 512:(r + 1) * 512],
                W["QAB"][r * 512:(r + 1) * 512, d * 128:(d + 1) * 128]), s_ld, 16)
        rec.add("sync", lambda e, d=d: e.dma_start_transpose(
            kva_wT[:, d, 0:512], W["KVAB"][0:512, d * 128:(d + 1) * 128]), s_ld, 16)
        rec.add("sync", lambda e, d=d: e.dma_start_transpose(
            kva_wT[:, d, 448:576], W["KVAB"][448:576, d * 128:(d + 1) * 128]), s_ld, 16)
    wqln = ctx.wins["w_qa_ln"].rearrange("(j p b) -> j p b", p=128, b=1)
    for j in range(12):
        rec.add("sync", lambda e, j=j: e.dma_start(wq_cols[:, j:j + 1], wqln[j]), s_ld, 16)
    wkln = ctx.wins["w_kva_ln"].rearrange("(j p b) -> j p b", p=128, b=1)
    for j in range(4):
        rec.add("sync", lambda e, j=j: e.dma_start(wkv_cols[:, j:j + 1], wkln[j]), s_ld, 16)
    w_ready = s_ld.n

    for src, dst in (("w_W1", "W1B"), ("w_W3", "W3B"), ("w1_gate", "GATB"),
                     ("w1_up", "UPB"), ("w1_shg", "SHGB"), ("w1_shu", "SHUB"),
                     ("w1_shd", "SHDB"), ("w1_down", "DWNB"), ("w1_router", "RTRB")):
        rec.add("gpsimd", lambda e, a=ctx.wins[src], b=W[dst]: e.dma_start(b, a), sems["ldg"], 16)

    prev = {}
    for c in range(NT):
        c0 = c * 512
        if prev:
            for nm, val in prev["xt_free"]:
                rec.wait("sync", sems[nm], val)
        ld_tiles = []
        for d in range(16):
            rec.add("sync", lambda e, d=d, c0=c0: e.dma_start(
                xt[:, d, :], scr["XT"][d, :, c0:c0 + 512]), s_ld, 16)
            ld_tiles.append(s_ld.n)
        if prev:
            rec.wait("sync", s_v, prev["cos_free_v"])
        rec.add("sync", lambda e, c0=c0: e.dma_start(cosk[:, :], ctx.consts["c_cos2"][0:64, c0:c0 + 512]), s_ld, 16)
        rec.add("sync", lambda e, c0=c0: e.dma_start(sink[:, :], ctx.consts["c_sin2"][0:64, c0:c0 + 512]), s_ld, 16)
        ld_cos = s_ld.n

        for e_ in ("vector", "scalar"):
            rec.wait(e_, s_ld, ld_tiles[-1])
        if prev:
            rec.wait("vector", s_t, prev["xln_free_t"])
            rec.wait("scalar", s_t, prev["xln_free_t"])
            rec.wait("tensor", s_v, prev["psrow_free_v"])
            rec.wait("tensor", s_sc, prev["psbc_free_sc"])
            rec.wait("gpsimd", s_t, prev["xln_free_t"])
        lnc = rec_ln_chunk(rec, sems, P, {
            "xt": xt, "xbf": xbf, "sqbf": sqbf, "xln": xln, "tmpf": tmpf,
            "rows": rows, "ab": ab, "ab_guard": prev.get("ab_free"),
            "ps_row0": ps_row0, "ps_row1": ps_row1,
            "ps_bc0": ps_bc0, "ps_bc1": ps_bc1}, c, 16, 1.0 / 2048.0, 0)

        rec.wait("tensor", s_ld, w_ready)
        rec.wait("tensor", s_g, lnc["xln_done_g"])
        evac_sc = {}
        t_mm = {}
        for rc in range(12):
            bank = ps_q[rc % 2]
            if rc >= 2:
                rec.wait("tensor", s_sc, evac_sc[rc - 2])
            elif prev:
                rec.wait("tensor", s_sc, prev["psq_free_sc"][rc % 2])
            rec.mm_group(bank[:, :], [(qa_wT[:, d, rc * 128:(rc + 1) * 128], xln[:, d, :])
                                      for d in range(16)], s_t)
            t_mm[rc] = s_t.n
            rec.wait("scalar", s_t, t_mm[rc])
            if prev and rc == 0:
                rec.wait("scalar", s_v, prev["qabf_free_v"])
                rec.wait("scalar", s_t, prev["qsq_free_t"])
            rec.wait("scalar", s_t, lnc["sqbf_free_t"])
            rec.add("scalar", lambda e, rc=rc, b=bank: e.copy(qa_bf[:, rc, :], b[:, :]), s_sc, 1)
            rec.add("scalar", lambda e, rc=rc, b=bank: e.activation(qsq_bf[:, rc, :], b[:, :], AF.Square), s_sc, 1)
            evac_sc[rc] = s_sc.n
        rec.wait("tensor", s_sc, evac_sc[11])
        rec.wait("tensor", s_v, lnc["psrow_free_v"])
        rec.mm_group(ps_row1[0:1, :], [(P["ones_col_b"][:, :], qsq_bf[:, rc, :]) for rc in range(12)], s_t)
        t_qssq = s_t.n
        qsq_free_t = s_t.n
        mq = rows[:, 3072:3584]; sdq = rows[:, 3584:4096]; rq = rows[:, 4096:4608]
        rec.wait("vector", s_t, t_qssq)
        rec.add("vector", lambda e: e.tensor_scalar(mq, ps_row1[0:1, :], 1.0 / 1536.0, None, OP.mult), s_v, 1)
        rec.wait("scalar", s_v)
        rec.add("scalar", lambda e: e.activation(sdq, mq, AF.Sqrt, bias=P["eps"][0:1, 1:2]), s_sc, 1)
        rec.wait("vector", s_sc)
        rec.add("vector", lambda e: e.reciprocal(rq, sdq), s_v, 1)
        v_rq = s_v.n
        rec.wait("tensor", s_v, v_rq)
        rec.wait("tensor", s_sc, lnc["psbc_free_sc"])
        rec.add("tensor", lambda e: e.matmul(ps_bc0[:, :], P["ones_row_f"][:, :], rq, start=True, stop=True), s_t, 1)
        t_bcq = s_t.n
        rec.wait("scalar", s_t, t_bcq)
        if prev:
            rec.wait("scalar", s_v, prev["rqb_free_v"])
        rec.add("scalar", lambda e: e.copy(rq_b[:, :], ps_bc0[:, :]), s_sc, 1)
        sc_rqb = s_sc.n
        v_qar = {}
        st_qar = {}
        for rc in range(12):
            rec.wait("vector", s_sc, sc_rqb)
            rec.wait("vector", s_sc, evac_sc[rc])
            rec.wait("vector", s_ld, w_ready)
            if rc >= 2:
                rec.wait("vector", s_stg, st_qar[rc - 2])
            elif prev:
                rec.wait("vector", s_stg, prev["qar_store"])
            rec.add("vector", lambda e, rc=rc: e.scalar_tensor_tensor(
                qar_bf[:, rc % 2, :], qa_bf[:, rc, :], wq_cols[:, rc:rc + 1], rq_b[:, :],
                OP.mult, OP.mult), s_v, 1)
            v_qar[rc] = s_v.n
            rec.wait("gpsimd", s_v, v_qar[rc])
            rec.add("gpsimd", lambda e, rc=rc, c0=c0: e.dma_start(
                scr["QAR"][rc, :, c0:c0 + 512], qar_bf[:, rc % 2, :]), s_stg, 16)
            st_qar[rc] = s_stg.n
        qar_store = s_stg.n

        kv_mm = {}
        kv_evac = {}
        for rc in range(4):
            bank = ps_q[rc % 2]
            rec.wait("tensor", s_sc, evac_sc[10 + rc] if rc < 2 else kv_evac[rc - 2])
            rec.mm_group(bank[:, :], [(kva_wT[:, d, rc * 128:(rc + 1) * 128], xln[:, d, :])
                                      for d in range(16)], s_t)
            kv_mm[rc] = s_t.n
            rec.wait("scalar", s_t, kv_mm[rc])
            if prev and rc == 0:
                rec.wait("scalar", s_v, prev["ckv_free_v"])
                rec.wait("scalar", s_t, prev["ckvsq_free_t"])
            rec.add("scalar", lambda e, rc=rc, b=bank: e.copy(ckv_bf[:, rc, :], b[:, :]), s_sc, 1)
            rec.add("scalar", lambda e, rc=rc, b=bank: e.activation(ckvsq_bf[:, rc, :], b[:, :], AF.Square), s_sc, 1)
            kv_evac[rc] = s_sc.n
        rec.wait("tensor", s_sc, kv_evac[3])
        rec.wait("vector", s_t, t_qssq)
        rec.wait("tensor", s_v, v_rq)
        rec.mm_group(ps_row1[0:1, :], [(P["ones_col_b"][:, :], ckvsq_bf[:, rc, :]) for rc in range(4)], s_t)
        t_kssq = s_t.n
        ckvsq_free_t = s_t.n
        mkv = rows[:, 4608:5120]; sdkv = rows[:, 5120:5632]; rkv = rows[:, 5632:6144]
        rec.wait("vector", s_t, t_kssq)
        rec.add("vector", lambda e: e.tensor_scalar(mkv, ps_row1[0:1, :], 1.0 / 512.0, None, OP.mult), s_v, 1)
        rec.wait("scalar", s_v)
        rec.add("scalar", lambda e: e.activation(sdkv, mkv, AF.Sqrt, bias=P["eps"][0:1, 1:2]), s_sc, 1)
        rec.wait("vector", s_sc)
        rec.add("vector", lambda e: e.reciprocal(rkv, sdkv), s_v, 1)
        v_rkv = s_v.n
        rec.wait("tensor", s_v, v_rkv)
        rec.wait("tensor", s_sc, sc_rqb)
        rec.add("tensor", lambda e: e.matmul(ps_bc0[:, :], P["ones_row_f"][:, :], rkv, start=True, stop=True), s_t, 1)
        t_bckv = s_t.n
        rec.wait("scalar", s_t, t_bckv)
        if prev:
            rec.wait("scalar", s_v, prev["rkvb_free_v"])
        rec.add("scalar", lambda e: e.copy(rkv_b[:, :], ps_bc0[:, :]), s_sc, 1)
        sc_rkvb = s_sc.n
        v_ckvr = {}
        st_ckvr = {}
        for rc in range(4):
            rec.wait("vector", s_sc, sc_rkvb)
            rec.wait("vector", s_sc, kv_evac[rc])
            if rc >= 2:
                rec.wait("vector", s_stg, st_ckvr[rc - 2])
            elif prev:
                rec.wait("vector", s_stg, prev["ckvr_store"])
            rec.add("vector", lambda e, rc=rc: e.scalar_tensor_tensor(
                ckvr_bf[:, rc % 2, :], ckv_bf[:, rc, :], wkv_cols[:, rc:rc + 1], rkv_b[:, :],
                OP.mult, OP.mult), s_v, 1)
            v_ckvr[rc] = s_v.n
            rec.wait("gpsimd", s_v, v_ckvr[rc])
            rec.add("gpsimd", lambda e, rc=rc, c0=c0: e.dma_start(
                scr["CKVR"][rc, :, c0:c0 + 512], ckvr_bf[:, rc % 2, :]), s_stg, 16)
            st_ckvr[rc] = s_stg.n
        ckvr_store = s_stg.n

        bank = ps_q[0]
        rec.wait("tensor", s_sc, kv_evac[2])
        rec.mm_group(bank[0:64, :], [(kva_wT[:, d, 512:576], xln[:, d, :]) for d in range(16)], s_t)
        t_kpe = s_t.n
        rec.wait("scalar", s_t, t_kpe)
        if prev:
            rec.wait("scalar", s_t, prev["kraw_free_t"])
            rec.wait("scalar", s_v, prev["kraw_free_v"])
        rec.add("scalar", lambda e, b=bank: e.copy(kraw_bf[:, :], b[0:64, :]), s_sc, 1)
        sc_kraw = s_sc.n
        rec.wait("tensor", s_sc, sc_kraw)
        if prev:
            rec.wait("tensor", s_v, prev["psrot_free_v"])
        rec.add("tensor", lambda e, b=bank: e.matmul(ps_rot[0:64, :], P["rrot_b"][0:64, 0:64],
                                                     kraw_bf[:, :], start=True, stop=True), s_t, 1)
        t_rot = s_t.n
        kraw_free_t = s_t.n
        tmpc = krope_t[:, 0:512]; tmps = krope_t[:, 512:1024]
        rec.wait("vector", s_sc, sc_kraw)
        rec.wait("vector", s_ld, ld_cos)
        rec.add("vector", lambda e: e.tensor_tensor(tmpc, kraw_bf[:, :], cosk[:, :], OP.mult), s_v, 1)
        kraw_free_v = s_v.n
        rec.wait("vector", s_t, t_rot)
        rec.add("vector", lambda e: e.tensor_tensor(tmps, ps_rot[0:64, :], sink[:, :], OP.mult), s_v, 1)
        psrot_free_v = s_v.n
        cos_free_v = s_v.n
        rec.wait("vector", s_v)
        if prev:
            rec.wait("vector", s_stg, prev["kpe_store"])
        rec.add("vector", lambda e: e.tensor_tensor(kpe_bf[:, :], tmpc, tmps, OP.add), s_v, 1)
        v_kpe = s_v.n
        rec.wait("gpsimd", s_v, v_kpe)
        rec.add("gpsimd", lambda e, c0=c0: e.dma_start(scr["KPEA"][0:64, c0:c0 + 512], kpe_bf[:, :]), s_stg, 16)
        kpe_store = s_stg.n

        prev = {
            "xt_free": lnc["xt_free"],
            "ab_free": lnc["ab_free"],
            "xbf_sq_free_t": max(lnc["xbf_free_t"], lnc["sqbf_free_t"]),
            "psrow_free_v": s_v.n,
            "psbc_free_sc": sc_rkvb,
            "xln_free_g": lnc["xln_done_g"],
            "xln_free_t": t_kpe,
            "psq_free_sc": {0: sc_kraw, 1: kv_evac[3]},
            "qabf_free_v": v_qar[11],
            "qsq_free_t": qsq_free_t,
            "rqb_free_v": v_qar[11],
            "qar_store": qar_store,
            "ckv_free_v": v_ckvr[3],
            "ckvsq_free_t": ckvsq_free_t,
            "rkvb_free_v": v_ckvr[3],
            "ckvr_store": ckvr_store,
            "kraw_free_t": kraw_free_t,
            "kraw_free_v": kraw_free_v,
            "psrot_free_v": psrot_free_v,
            "cos_free_v": cos_free_v,
            "kpe_store": kpe_store,
        }
    return rec



def rec_s1b(ctx, sems, P):
    rec = Rec(ctx)
    s_ld, s_t, s_v, s_sc, s_g, s_st, s_stg = (sems[k] for k in ("ld", "t", "v", "sc", "g", "st", "stg"))
    sb, psb = ctx.sb, ctx.psb
    scr, W = ctx.scr, ctx.wdram
    SCALE = float(192.0 ** -0.5)

    qb_wT = sb("qb_wT", [128, 12, 3072], BF16)
    kvb_wT = sb("kvb_wT", [128, 4, 4096], BF16)
    qar = sb("qar", [128, 12, 512], BF16)
    ckvr = sb("ckvr", [128, 4, 512], BF16)
    cosq = sb("cosq", [128, 512], F32)
    sinq = sb("sinq", [128, 512], F32)
    ob = sb("ob", [128, 2, 512], BF16)
    qpraw = sb("qpraw", [128, 512], BF16)
    ropet = sb("ropet", [128, 2 * 512], F32)
    qpe_bf = sb("qpe_bf", [128, 512], BF16)
    vb = sb("vb", [128, 2, 512], BF16)

    banks = [psb("s1b_ps0", [128, 512], F32), psb("s1b_ps1", [128, 512], F32)]
    ps_rot = psb("s1b_rot", [128, 512], F32)

    for rc in range(12):
        for og in range(6):
            rec.add("sync", lambda e, rc=rc, og=og: e.dma_start_transpose(
                qb_wT[:, rc, og * 512:(og + 1) * 512],
                W["QBB"][og * 512:(og + 1) * 512, rc * 128:(rc + 1) * 128]), s_ld, 16)
    for rr in range(4):
        for og in range(8):
            rec.add("sync", lambda e, rr=rr, og=og: e.dma_start_transpose(
                kvb_wT[:, rr, og * 512:(og + 1) * 512],
                W["KVBB"][og * 512:(og + 1) * 512, rr * 128:(rr + 1) * 128]), s_ld, 16)
    w_ready = s_ld.n
    for src, dst in (("w1_q", "Q1B"), ("w1_k", "K1B"), ("w1_v", "V1B"),
                     ("w1_g", "G1B"), ("w1_s", "S1B"), ("w1_o", "O1B")):
        rec.add("gpsimd", lambda e, a=ctx.wins[src], b=W[dst]: e.dma_start(b, a), sems["ldg"], 16)

    prev = {}
    for c in range(NT):
        c0 = c * 512
        if prev:
            rec.wait("sync", s_t, prev["qar_free_t"])
        for rc in range(12):
            rec.add("sync", lambda e, rc=rc, c0=c0: e.dma_start(
                qar[:, rc, :], scr["QAR"][rc, :, c0:c0 + 512]), s_ld, 16)
        for rr in range(4):
            rec.add("sync", lambda e, rr=rr, c0=c0: e.dma_start(
                ckvr[:, rr, :], scr["CKVR"][rr, :, c0:c0 + 512]), s_ld, 16)
        ld_act = s_ld.n
        if prev:
            rec.wait("sync", s_v, prev["cos_free_v"])
        rec.add("sync", lambda e, c0=c0: e.dma_start(cosq[:, :], ctx.consts["c_cos2"][:, c0:c0 + 512]), s_ld, 16)
        rec.add("sync", lambda e, c0=c0: e.dma_start(sinq[:, :], ctx.consts["c_sin2"][:, c0:c0 + 512]), s_ld, 16)
        ld_cos = s_ld.n

        rec.wait("tensor", s_ld, w_ready)
        rec.wait("tensor", s_ld, ld_act)

        evac = {}
        store = {}
        slot_writer = {}

        def qk_block(kind, oc, pairs, scale):
            i = len(evac)
            bank = banks[i % 2]
            if i >= 2:
                rec.wait("tensor", s_sc, evac[i - 2])
            elif prev:
                nm, cnt = prev["bank_free"][i % 2]
                rec.wait("tensor", sems[nm], cnt)
            rec.mm_group(bank[:, :], pairs, s_t)
            tmm = s_t.n
            rec.wait("scalar", s_t, tmm)
            if i >= 2:
                rec.wait("scalar", s_stg, store[i - 2])
            elif prev:
                rec.wait("scalar", s_stg, prev["ob_store"])
            sl = i % 2
            if scale is None:
                rec.add("scalar", lambda e, sl=sl, b=bank: e.copy(ob[:, sl, :], b[:, :]), s_sc, 1)
            else:
                rec.add("scalar", lambda e, sl=sl, b=bank: e.mul(ob[:, sl, :], b[:, :], scale), s_sc, 1)
            evac[i] = s_sc.n
            rec.wait("gpsimd", s_sc, evac[i])
            dst = kind
            rec.add("gpsimd", lambda e, dst=dst, oc=oc, sl=sl, c0=c0: e.dma_start(
                scr[dst][oc, :, c0:c0 + 512], ob[:, sl, :]), s_stg, 16)
            store[i] = s_stg.n
            return tmm

        for oc in range(16):
            qk_block("QNT", oc, [(qb_wT[:, rc, oc * 128:(oc + 1) * 128], qar[:, rc, :])
                                 for rc in range(12)], SCALE)
        t_qar_last = None
        for oc in range(16):
            qk_block("KNT", oc, [(kvb_wT[:, rr, oc * 128:(oc + 1) * 128], ckvr[:, rr, :])
                                 for rr in range(4)], None)
        t_qar_free = 0
        rope_store = prev.get("qpe_store", 0)
        for oc in range(8):
            i = len(evac)
            bank = banks[i % 2]
            rec.wait("tensor", s_sc, evac[i - 2])
            rec.mm_group(bank[:, :], [(qb_wT[:, rc, 2048 + oc * 128: 2048 + (oc + 1) * 128], qar[:, rc, :])
                                      for rc in range(12)], s_t)
            t_qar_free = s_t.n
            tmm = s_t.n
            evac[i] = evac[i - 1]
            rec.wait("scalar", s_t, tmm)
            if oc >= 1:
                rec.wait("scalar", s_t, rot_done)
                rec.wait("scalar", s_v, v_mixc)
            elif prev:
                rec.wait("scalar", s_t, prev["qpraw_free_t"])
                rec.wait("scalar", s_v, prev["qpraw_free_v"])
            rec.add("scalar", lambda e, b=bank: e.mul(qpraw[:, :], b[:, :], SCALE), s_sc, 1)
            sc_qp = s_sc.n
            evac[i] = sc_qp
            rec.wait("tensor", s_sc, sc_qp)
            if oc >= 1:
                rec.wait("tensor", s_v, v_mixs)
            elif prev:
                rec.wait("tensor", s_v, prev["psrot_free_v"])
            rec.add("tensor", lambda e: e.matmul(ps_rot[:, :], P["rrot_b"][:, :], qpraw[:, :],
                                                 start=True, stop=True), s_t, 1)
            rot_done = s_t.n
            tmpc = ropet[:, 0:512]; tmps = ropet[:, 512:1024]
            rec.wait("vector", s_sc, sc_qp)
            rec.wait("vector", s_ld, ld_cos)
            if oc >= 1:
                rec.wait("vector", s_v, v_qpe)
            rec.add("vector", lambda e: e.tensor_tensor(tmpc, qpraw[:, :], cosq[:, :], OP.mult), s_v, 1)
            v_mixc = s_v.n
            rec.wait("vector", s_t, rot_done)
            rec.add("vector", lambda e: e.tensor_tensor(tmps, ps_rot[:, :], sinq[:, :], OP.mult), s_v, 1)
            v_mixs = s_v.n
            rec.wait("vector", s_v)
            if oc >= 1:
                rec.wait("vector", s_stg, qpe_st)
            elif prev:
                rec.wait("vector", s_stg, prev["qpe_store"])
            rec.add("vector", lambda e: e.tensor_tensor(qpe_bf[:, :], tmpc, tmps, OP.add), s_v, 1)
            v_qpe = s_v.n
            rec.wait("gpsimd", s_v, v_qpe)
            rec.add("gpsimd", lambda e, oc=oc, c0=c0: e.dma_start(
                scr["QPE"][oc, :, c0:c0 + 512], qpe_bf[:, :]), s_stg, 16)
            qpe_st = s_stg.n

        n_v = 0
        v_evac = {}
        v_store = {}
        for sub in range(4):
            for vc in range(4):
                i = n_v; n_v += 1
                bank = banks[i % 2]
                if i >= 2:
                    rec.wait("tensor", s_v, v_evac[i - 2])
                else:
                    rec.wait("tensor", s_sc, evac[len(evac) - 2 + i])
                rec.mm_group(bank[:, :], [(ckvr[:, rr, sub * 128:(sub + 1) * 128],
                                           kvb_wT[:, rr, 2048 + vc * 512: 2048 + (vc + 1) * 512])
                                          for rr in range(4)], s_t)
                tmm = s_t.n
                t_v_last = s_t.n
                rec.wait("vector", s_t, tmm)
                if i >= 2:
                    rec.wait("vector", s_stg, v_store[i - 2])
                elif prev:
                    rec.wait("vector", s_stg, prev["v_store"])
                sl = i % 2
                rec.add("vector", lambda e, sl=sl, b=bank: e.tensor_copy(vb[:, sl, :], b[:, :]), s_v, 1)
                v_evac[i] = s_v.n
                rec.wait("gpsimd", s_v, v_evac[i])
                rec.add("gpsimd", lambda e, sub=sub, vc=vc, sl=sl, c0=c0: e.dma_start(
                    scr["VNT"][c0 + sub * 128: c0 + (sub + 1) * 128, vc * 512:(vc + 1) * 512],
                    vb[:, sl, :]), s_stg, 16)
                v_store[i] = s_stg.n

        prev = {
            "qar_free_t": t_v_last,
            "cos_free_v": v_mixs,
            "bank_free": {0: ("v", v_evac[14]), 1: ("v", v_evac[15])},
            "ob_store": store[len(store) - 1] if store else 0,
            "qpraw_free_t": rot_done,
            "qpraw_free_v": v_mixc,
            "psrot_free_v": v_mixs,
            "qpe_store": qpe_st,
            "v_store": v_store[15],
        }
    return rec



def rec_s2(ctx, sems, P):
    rec = Rec(ctx)
    s_ld, s_t, s_v, s_sc, s_g, s_st, s_stg = (sems[k] for k in ("ld", "t", "v", "sc", "g", "st", "stg"))
    sb, psb = ctx.sb, ctx.psb
    scr = ctx.scr

    qn = sb("s2_qn", [128, T], BF16)
    qpa = sb("s2_qpa", [65, T], BF16)
    kn = sb("s2_kn", [128, T], BF16)
    kpa = sb("s2_kpa", [65, T], BF16)
    vn = sb("s2_vn", [128, T], BF16)
    sq = sb("s2_sq", [128, T], BF16)
    sqp = sb("s2_sqp", [64, T], BF16)
    pt = sb("s2_pt", [128, 3, 512], BF16)
    rb = sb("s2_rb", [128, 512], F32)
    ao = sb("s2_ao", [128, 512], BF16)
    kmx = sb("s2_kmx", [1, 4], F32)
    rr_f = sb("s2_rr", [1, 512], F32)

    ps_st = [psb("s2_st0", [128, 512], F32), psb("s2_st1", [128, 512], F32)]
    ps_attn = psb("s2_attn", [128, 512], F32)
    ps_r = psb("s2_r", [128, 512], F32)
    ps_rb = psb("s2_psrb", [128, 512], F32)

    free = {}

    def guard(engname, res):
        if res in free:
            nm, cnt = free[res]
            rec.wait(engname, sems[nm], cnt)

    def setfree(res, nm, cnt):
        free[res] = (nm, cnt)

    rec.add("sync", lambda e: e.dma_start(kpa[:, :], scr["KPEA"][0:65, :]), s_ld, 16)

    for h in range(16):
        guard("sync", "qk")
        rec.add("sync", lambda e, h=h: e.dma_start(qn[:, :], scr["QNT"][h]), s_ld, 16)
        rec.add("sync", lambda e, h=h: e.dma_start(kn[:, :], scr["KNT"][h]), s_ld, 16)
        rec.add("sync", lambda e, h=h: e.dma_start(
            qpa[0:64, :], scr["QPE"][h // 2, 64 * (h % 2):64 * (h % 2) + 64, :]), s_ld, 16)
        vsrc = scr["VNT"].rearrange("(b p) (g v) -> p b g v", p=128, v=128)
        rec.add("sync", lambda e, h=h, vs=vsrc: e.dma_start(
            vn.rearrange("p (b v) -> p b v", v=128), vs[:, :, h, :]), s_ld, 16)
        ld_head = s_ld.n

        rec.wait("scalar", s_ld, ld_head)
        guard("scalar", "sq")
        rec.add("scalar", lambda e: e.activation(sq[:, :], kn[:, :], AF.Square), s_sc, 1)
        rec.add("scalar", lambda e: e.activation(sqp[:, :], kpa[0:64, :], AF.Square), s_sc, 1)
        sc_sqk = s_sc.n
        rec.wait("tensor", s_sc, sc_sqk)
        for j in range(NT):
            bank = ps_st[j % 2]
            guard("tensor", f"st{j % 2}")
            rec.mm_group(bank[0:1, :],
                         [(P["ones_col_b"][:, :], sq[:, j * 512:(j + 1) * 512]),
                          (P["ones_col_b"][0:64, :], sqp[:, j * 512:(j + 1) * 512])], s_t)
            t_kn = s_t.n
            rec.wait("vector", s_t, t_kn)
            if j == 0:
                guard("vector", "kmx")
                rec.add("vector", lambda e, b=bank: e.reduce_max(
                    kmx[0:1, 0:1], b[0:1, :], axis=AX.X), s_v, 1)
            else:
                rec.add("vector", lambda e, b=bank: e.reduce_max(
                    kmx[0:1, 1:2], b[0:1, :], axis=AX.X), s_v, 1)
                rec.wait("vector", s_v)
                rec.add("vector", lambda e: e.tensor_tensor(
                    kmx[0:1, 0:1], kmx[0:1, 0:1], kmx[0:1, 1:2], OP.max), s_v, 1)
            setfree(f"st{j % 2}", "v", s_v.n)
        v_kmx = s_v.n
        t_knorm = s_t.n

        rec.wait("scalar", s_t, t_knorm)
        rec.add("scalar", lambda e: e.activation(sq[:, :], qn[:, :], AF.Square), s_sc, 1)
        rec.add("scalar", lambda e: e.activation(sqp[:, :], qpa[0:64, :], AF.Square), s_sc, 1)
        sc_sqq = s_sc.n
        rec.wait("tensor", s_sc, sc_sqq)
        for j in range(NT):
            bank = ps_st[j % 2]
            guard("tensor", f"st{j % 2}")
            rec.mm_group(bank[0:1, :],
                         [(P["ones_col_b"][:, :], sq[:, j * 512:(j + 1) * 512]),
                          (P["ones_col_b"][0:64, :], sqp[:, j * 512:(j + 1) * 512])], s_t)
            t_qn = s_t.n
            rec.wait("scalar", s_t, t_qn)
            if j == 0:
                rec.wait("scalar", s_v, v_kmx)
                guard("scalar", "qpa_row")
            rec.add("scalar", lambda e, b=bank, j=j: e.activation(
                qpa[64:65, j * 512:(j + 1) * 512], b[0:1, :], AF.Sqrt,
                scale=kmx[0:1, 0:1]), s_sc, 1)
            setfree(f"st{j % 2}", "sc", s_sc.n)
        sc_qrow = s_sc.n
        setfree("sq", "t", s_t.n)
        setfree("kmx", "sc", sc_qrow)

        rec.wait("tensor", s_ld, ld_head)
        rec.wait("tensor", s_sc, sc_qrow)
        for i in range(NT):
            i0 = i * 512
            njt = 4 * i + 4
            pend = None

            def emit_rpv(slot, sc_exp, j_sl, st, sp):
                rec.wait("tensor", s_sc, sc_exp)
                if st:
                    guard("tensor", "r")
                    guard("tensor", "attn")
                rec.add("tensor", lambda e, slot=slot, st=st, sp=sp: e.matmul(
                    ps_r[0:1, :], P["ones_col_b"][:, :], pt[:, slot, :],
                    start=st, stop=sp), s_t, 1)
                rec.add("tensor", lambda e, slot=slot, j_sl=j_sl, st=st, sp=sp: e.matmul(
                    ps_attn[:, :], vn[:, j_sl], pt[:, slot, :],
                    start=st, stop=sp), s_t, 1)
                setfree(f"pt{slot}", "t", s_t.n)

            for j128 in range(njt):
                bank = ps_st[j128 % 2]
                guard("tensor", f"st{j128 % 2}")
                j_sl = slice(j128 * 128, (j128 + 1) * 128)
                rec.mm_group(bank[:, :],
                             [(kn[:, j_sl], qn[:, i0:i0 + 512]),
                              (kpa[0:65, j_sl], qpa[0:65, i0:i0 + 512])], s_t)
                t_st = s_t.n
                rel = j128 - 4 * i
                slot = j128 % 3
                guard("vector", f"pt{slot}")
                guard("scalar", f"pt{slot}")
                if rel >= 0:
                    rec.wait("vector", s_t, t_st)
                    rec.add("vector", lambda e, b=bank, rel=rel: e.tensor_tensor(
                        b[:, rel * 128:(rel + 1) * 128], b[:, rel * 128:(rel + 1) * 128],
                        P["maskneg"][:, :], OP.add), s_v, 1)
                    if rel > 0:
                        rec.add("vector", lambda e, slot=slot, rel=rel: e.memset(
                            pt[:, slot, 0:rel * 128], 0.0), s_v, 1)
                    v_zero = s_v.n
                    rec.wait("scalar", s_v, v_zero)
                    rec.wait("scalar", s_t, t_st)
                    rec.add("scalar", lambda e, b=bank, slot=slot, rel=rel: e.activation(
                        pt[:, slot, rel * 128:512], b[:, rel * 128:512], AF.Exp), s_sc, 1)
                else:
                    rec.wait("scalar", s_t, t_st)
                    rec.add("scalar", lambda e, b=bank, slot=slot: e.activation(
                        pt[:, slot, :], b[:, :], AF.Exp), s_sc, 1)
                sc_exp = s_sc.n
                setfree(f"st{j128 % 2}", "sc", sc_exp)
                if pend is not None:
                    emit_rpv(*pend)
                pend = (slot, sc_exp, j_sl, j128 == 0, j128 == njt - 1)
            emit_rpv(*pend)
            t_pv = s_t.n
            rec.wait("vector", s_t, t_pv)
            guard("vector", "rr")
            rec.add("vector", lambda e: e.reciprocal(rr_f[:, :], ps_r[0:1, :]), s_v, 1)
            v_rr = s_v.n
            setfree("r", "v", v_rr)
            rec.wait("tensor", s_v, v_rr)
            guard("tensor", "rb")
            rec.add("tensor", lambda e: e.matmul(ps_rb[:, :], P["ones_row_f"][:, :], rr_f[:, :],
                                                 start=True, stop=True), s_t, 1)
            t_rb = s_t.n
            setfree("rr", "t", t_rb)
            rec.wait("scalar", s_t, t_rb)
            guard("scalar", "rb_sb")
            rec.add("scalar", lambda e: e.copy(rb[:, :], ps_rb[:, :]), s_sc, 1)
            sc_rb = s_sc.n
            setfree("rb", "sc", sc_rb)
            rec.wait("vector", s_sc, sc_rb)
            rec.wait("vector", s_t, t_pv)
            guard("vector", "ao")
            rec.add("vector", lambda e: e.tensor_tensor(ao[:, :], ps_attn[:, :], rb[:, :],
                                                        OP.mult), s_v, 1)
            v_ao = s_v.n
            setfree("attn", "v", v_ao)
            setfree("rb_sb", "v", v_ao)
            rec.wait("gpsimd", s_v, v_ao)
            rec.add("gpsimd", lambda e, h=h, i0=i0: e.dma_start(
                scr["AOT"][h, :, i0:i0 + 512], ao[:, :]), s_stg, 16)
            setfree("ao", "stg", s_stg.n)
        setfree("qk", "t", t_pv)
        setfree("qpa_row", "t", t_pv)
    return rec



def rec_s3(ctx, sems, P):
    rec = Rec(ctx)
    s_ld, s_t, s_v, s_sc, s_g, s_st, s_stg = (sems[k] for k in ("ld", "t", "v", "sc", "g", "st", "stg"))
    sb, psb = ctx.sb, ctx.psb
    scr, W = ctx.scr, ctx.wdram

    owT = sb("s3_owT", [128, 16, 2048], BF16)
    at = sb("s3_at", [128, 16, 512], BF16)
    xt = sb("s3_xt", [128, 16, 512], F32)
    xn = sb("s3_xn", [128, 2, 512], F32)
    banks = [psb("s3_ps0", [128, 512], F32), psb("s3_ps1", [128, 512], F32)]

    for hvt in range(16):
        for dg in range(4):
            rec.add("sync", lambda e, hvt=hvt, dg=dg: e.dma_start_transpose(
                owT[:, hvt, dg * 512:(dg + 1) * 512],
                W["OWB"][dg * 512:(dg + 1) * 512, hvt * 128:(hvt + 1) * 128]), s_ld, 16)
    w_ready = s_ld.n

    free = {}
    def guard(engname, res):
        if res in free:
            nm, cnt = free[res]
            rec.wait(engname, sems[nm], cnt)
    def setfree(res, nm, cnt):
        free[res] = (nm, cnt)

    for c in range(NT):
        c0 = c * 512
        guard("sync", "at")
        for hvt in range(16):
            rec.add("sync", lambda e, hvt=hvt, c0=c0: e.dma_start(
                at[:, hvt, :], scr["AOT"][hvt, :, c0:c0 + 512]), s_ld, 16)
        guard("sync", "xt")
        for d in range(16):
            rec.add("sync", lambda e, d=d, c0=c0: e.dma_start(
                xt[:, d, :], scr["XT"][d, :, c0:c0 + 512]), s_ld, 16)
        ld_c = s_ld.n
        rec.wait("tensor", s_ld, w_ready)
        rec.wait("tensor", s_ld, ld_c)
        last_t = 0
        v_done = {}
        st_done = {}
        for oc in range(16):
            bank = banks[oc % 2]
            guard("tensor", f"b{oc % 2}")
            rec.mm_group(bank[:, :], [(owT[:, hvt, oc * 128:(oc + 1) * 128], at[:, hvt, :])
                                      for hvt in range(16)], s_t)
            last_t = s_t.n
            sl = oc % 2
            rec.wait("vector", s_t, last_t)
            rec.wait("vector", s_ld, ld_c)
            guard("vector", f"xn{sl}")
            rec.add("vector", lambda e, sl=sl, b=bank, oc=oc: e.tensor_tensor(
                xn[:, sl, :], b[:, :], xt[:, oc, :], OP.add), s_v, 1)
            setfree(f"b{sl}", "v", s_v.n)
            v_done[oc] = s_v.n
            rec.wait("gpsimd", s_v, v_done[oc])
            rec.add("gpsimd", lambda e, oc=oc, sl=sl, c0=c0: e.dma_start(
                scr["XT"][oc, :, c0:c0 + 512], xn[:, sl, :]), s_stg, 16)
            st_done[oc] = s_stg.n
            setfree(f"xn{sl}", "stg", st_done[oc])
        setfree("at", "t", last_t)
        setfree("xt", "v", v_done[15])
    return rec



def rec_s4ln(ctx, sems, P, XL2T):
    rec = Rec(ctx)
    s_ld, s_t, s_v, s_sc, s_g, s_st, s_stg = (sems[k] for k in ("ld", "t", "v", "sc", "g", "st", "stg"))
    sb, psb = ctx.sb, ctx.psb
    scr = ctx.scr
    xt = sb("s4l_xt", [128, 16, 512], F32)
    xbf = None
    sqbf = sb("s4l_sqbf", [128, 16, 512], BF16)
    tmpf = sb("s4l_tmpf", [128, 2, 512], F32)
    rows = sb("s4l_rows", [1, 4 * 512], F32)
    ab = sb("s4l_ab", [128, 2 * 512], F32)
    ps_row0 = psb("s4l_r0", [128, 512], F32)
    ps_row1 = psb("s4l_r1", [128, 512], F32)
    ps_bc0 = psb("s4l_b0", [128, 512], F32)
    ps_bc1 = psb("s4l_b1", [128, 512], F32)
    prev = {}
    for c in range(NT):
        c0 = c * 512
        if prev:
            for nm, val in prev["xt_free"]:
                rec.wait("sync", sems[nm], val)
        for d in range(16):
            rec.add("sync", lambda e, d=d, c0=c0: e.dma_start(
                xt[:, d, :], scr["XT"][d, :, c0:c0 + 512]), s_ld, 16)
        ld_tiles = s_ld.n
        for e_ in ("vector", "scalar"):
            rec.wait(e_, s_ld, ld_tiles)
        if prev:
            rec.wait("scalar", s_t, prev["xbf_free_t"])
            rec.wait("tensor", s_v, prev["psrow_free_v"])
            rec.wait("tensor", s_sc, prev["psbc_free_sc"])
        lnc = rec_ln_chunk(rec, sems, P, {
            "xt": xt, "xbf": xbf, "sqbf": sqbf,
            "xln_at": (lambda d, c=c: XL2T[:, d, c, :]),
            "xt_ready": ("ld", ld_tiles),
            "tmpf": tmpf, "rows": rows, "ab": ab,
            "ab_guard": prev.get("ab_free"),
            "ps_row0": ps_row0, "ps_row1": ps_row1,
            "ps_bc0": ps_bc0, "ps_bc1": ps_bc1}, c, 16, 1.0 / 2048.0, 0)
        prev = {
            "xt_free": lnc["xt_free"],
            "ab_free": lnc["ab_free"],
            "xbf_free_t": max(lnc["xbf_free_t"], lnc["sqbf_free_t"]),
            "psrow_free_v": lnc["psrow_free_v"],
            "psbc_free_sc": lnc["psbc_free_sc"],
            "ab_free": lnc["ab_free"],
        }
    return rec


def rec_s4w2prep(ctx, sems, P, rec=None):
    if rec is None:
        rec = Rec(ctx)
    s_ld, s_t, s_v, s_sc, s_g, s_st, s_stg = (sems[k] for k in ("ld", "t", "v", "sc", "g", "st", "stg"))
    sb = ctx.sb
    W = ctx.wdram
    wA = sb("s4p_wA", [128, 8192], F32)
    wB = sb("s4p_wB", [128, 8192], F32)
    wbf = sb("s4p_wbf", [128, 8192], BF16)
    tb = sb("s4p_tb", [128, 2, 2048], BF16)
    prev = {}
    for rc in range(16):
        if prev:
            rec.wait("sync", s_v, prev["wA_free_v"])
            rec.wait("sync", s_g, prev["wA_free_g"])
        rec.add("sync", lambda e, rc=rc: e.dma_start(
            wA[:, :], ctx.wins["w_W2"][rc * 128:(rc + 1) * 128, :]), s_ld, 16)
        rec.wait("vector", s_ld, s_ld.n)
        rec.wait("gpsimd", s_ld, s_ld.n)
        src, dst = wA, wB
        for lvl in range(6):
            sseg = 1 << lvl
            nb = 8192 // (2 * sseg * 128)
            wl = sseg * 128
            def mk(out, a, b, op):
                def f(e, out=out, a=a, b=b, op=op):
                    return e.tensor_tensor(out, a, b, op)
                return f
            sv = src.rearrange("p (g two w) -> p g two w", two=2, w=wl)
            dv = dst.rearrange("p (g two w) -> p g two w", two=2, w=wl)
            if lvl > 0:
                rec.wait("vector", s_g)
                rec.wait("gpsimd", s_v)
            rec.add("vector", mk(dv[:, :, 0, :], sv[:, :, 0, :], sv[:, :, 1, :], OP.add), s_v, 1)
            rec.add("gpsimd", mk(dv[:, :, 1, :], sv[:, :, 0, :], sv[:, :, 1, :], OP.subtract), s_g, 1)
            src, dst = dst, src
        rec.wait("vector", s_g)
        rec.wait("vector", s_v)
        if prev:
            rec.wait("vector", s_stg, prev["wbf_store"])
        rec.add("vector", lambda e: e.tensor_copy(wbf[:, :], wA[:, :]), s_v, 1)
        wA_free_v = s_v.n
        rec.wait("gpsimd", s_v, wA_free_v)
        rec.add("gpsimd", lambda e, rc=rc: e.dma_start(
            W["W2BF"][rc * 128:(rc + 1) * 128, :], wbf[:, :]), s_stg, 16)
        prev = {"wA_free_v": wA_free_v, "wA_free_g": s_g.n, "wbf_store": s_stg.n}
    all_stored = s_stg.n
    rec.wait("scalar", s_stg, all_stored)
    pp = {}
    for ht in range(64):
        sl = ht % 2
        if ht >= 2:
            rec.wait("scalar", s_stg, pp[ht - 2])
        rec.add("scalar", lambda e, ht=ht, sl=sl: e.dma_start_transpose(
            tb[:, sl, :], W["W2BF"][:, ht * 128:(ht + 1) * 128]), s_ld, 16)
        rec.wait("gpsimd", s_ld, s_ld.n)
        rec.add("gpsimd", lambda e, ht=ht, sl=sl: e.dma_start(
            W["W2T"][ht], tb[:, sl, :]), s_stg, 16)
        pp[ht] = s_stg.n
    return rec


def rec_s4uv(ctx, sems, P, XL2T):
    rec = Rec(ctx)
    s_ld, s_ld2, s_t, s_v, s_sc, s_g, s_st, s_stg = (sems[k] for k in ("ld", "ld2", "t", "v", "sc", "g", "st", "stg"))
    sb, psb = ctx.sb, ctx.psb
    scr, W = ctx.scr, ctx.wdram
    w1t = sb("s4u_w1t", [128, 2, 16, 512], BF16)
    w3t = sb("s4u_w3t", [128, 2, 16, 512], BF16)
    sU = sb("s4u_sU", [128, 2, 512], BF16)
    hda = sb("s4u_hda", [128, 2, 512], BF16)
    hdo = sb("s4u_hdo", [128, 2, 512], BF16)
    psU = [psb("s4u_pU0", [128, 512], F32), psb("s4u_pU1", [128, 512], F32)]
    psV = [psb("s4u_pV0", [128, 512], F32), psb("s4u_pV1", [128, 512], F32)]
    psH = [psb("s4u_pH0", [128, 512], F32), psb("s4u_pH1", [128, 512], F32)]

    free = {}
    def guard(engname, res):
        if res in free:
            nm, cnt = free[res]
            rec.wait(engname, sems[nm], cnt)
    def setfree(res, nm, cnt):
        free[res] = (nm, cnt)

    def load_og(og):
        slw = og % 2
        guard("sync", f"w1t{slw}")
        for d in range(16):
            rec.add("sync", lambda e, og=og, d=d, slw=slw: e.dma_start_transpose(
                w1t[:, slw, d, :], W["W1B"][og * 512:(og + 1) * 512, d * 128:(d + 1) * 128]),
                s_ld, 16)
            rec.add("sync", lambda e, og=og, d=d, slw=slw: e.dma_start_transpose(
                w3t[:, slw, d, :], W["W3B"][og * 512:(og + 1) * 512, d * 128:(d + 1) * 128]),
                s_ld, 16)
        return s_ld.n

    wready = {0: load_og(0)}
    for og in range(16):
        if og + 1 < 16:
            wready[og + 1] = load_og(og + 1)
        slw = og % 2
        last_t = 0
        for oi in range(4):
            oc = og * 4 + oi
            for c in range(NT):
                ii = (oc * NT + c) % 2
                rec.wait("tensor", s_ld, wready[og])
                guard("tensor", f"pU{ii}")
                rec.mm_group(psU[ii][:, :],
                             [(w1t[:, slw, d, oi * 128:(oi + 1) * 128], XL2T[:, d, c, :])
                              for d in range(16)], s_t)
                tU = s_t.n
                guard("tensor", f"pV{ii}")
                rec.mm_group(psV[ii][:, :],
                             [(w3t[:, slw, d, oi * 128:(oi + 1) * 128], XL2T[:, d, c, :])
                              for d in range(16)], s_t)
                tV = s_t.n
                last_t = tV
                rec.wait("scalar", s_t, tU)
                guard("scalar", f"sU{ii}")
                rec.add("scalar", lambda e, ii=ii: e.activation(
                    sU[:, ii, :], psU[ii][:, :], AF.Silu), s_sc, 1)
                sc_silu = s_sc.n
                setfree(f"pU{ii}", "sc", sc_silu)
                rec.wait("vector", s_sc, sc_silu)
                rec.wait("vector", s_t, tV)
                guard("vector", f"hda{ii}")
                rec.add("vector", lambda e, ii=ii: e.tensor_tensor(
                    hda[:, ii, :], psV[ii][:, :], sU[:, ii, :], OP.mult), s_v, 1)
                v_hda = s_v.n
                setfree(f"pV{ii}", "v", v_hda)
                setfree(f"sU{ii}", "v", v_hda)
                rec.wait("tensor", s_v, v_hda)
                guard("tensor", f"pH{ii}")
                rec.add("tensor", lambda e, ii=ii: e.matmul(
                    psH[ii][:, :], P["h128s_b"][:, :], hda[:, ii, :],
                    start=True, stop=True), s_t, 1)
                tH = s_t.n
                setfree(f"hda{ii}", "t", tH)
                rec.wait("scalar", s_t, tH)
                guard("scalar", f"hdo{ii}")
                rec.add("scalar", lambda e, ii=ii: e.copy(hdo[:, ii, :], psH[ii][:, :]), s_sc, 1)
                sc_hdo = s_sc.n
                setfree(f"pH{ii}", "sc", sc_hdo)
                rec.wait("gpsimd", s_sc, sc_hdo)
                rec.add("gpsimd", lambda e, oc=oc, c=c, ii=ii: e.dma_start(
                    scr["HD"][oc, :, c * 512:(c + 1) * 512], hdo[:, ii, :]), s_stg, 16)
                setfree(f"hdo{ii}", "stg", s_stg.n)
        setfree(f"w1t{slw}", "t", last_t)
    return rec


def rec_s4w2(ctx, sems, P):
    rec = Rec(ctx)
    s_ld, s_ld2, s_t, s_v, s_sc, s_g, s_st, s_stg = (sems[k] for k in ("ld", "ld2", "t", "v", "sc", "g", "st", "stg"))
    sb, psb = ctx.sb, ctx.psb
    scr, W = ctx.scr, ctx.wdram
    hd = sb("s4w_hd", [128, 64, 512], BF16)
    xt = sb("s4w_xt", [128, 16, 512], F32)
    w2b = sb("s4w_w2b", [128, 2, 16, 1024], BF16)
    xn = sb("s4w_xn", [128, 2, 512], F32)
    banks = [psb(f"s4w_p{i}", [128, 512], F32) for i in range(8)]

    free = {}
    def guard(engname, res):
        if res in free:
            nm, cnt = free[res]
            rec.wait(engname, sems[nm], cnt)
    def setfree(res, nm, cnt):
        free[res] = (nm, cnt)

    for c in range(NT):
        c0 = c * 512
        guard("sync", "hd")
        for ht in range(64):
            rec.add("sync", lambda e, ht=ht, c0=c0: e.dma_start(
                hd[:, ht, :], scr["HD"][ht, :, c0:c0 + 512]), s_ld, 16)
        guard("sync", "xt")
        for d in range(16):
            rec.add("sync", lambda e, d=d, c0=c0: e.dma_start(
                xt[:, d, :], scr["XT"][d, :, c0:c0 + 512]), s_ld, 16)
        ld_c = s_ld.n
        rec.wait("tensor", s_ld, ld_c)
        last_t = 0
        v_res = 0
        for dh in range(2):
            for q in range(4):
                slq = q % 2
                guard("scalar", f"w2b{slq}")
                for ht16 in range(16):
                    rec.add("scalar", lambda e, q=q, ht16=ht16, dh=dh, slq=slq: e.dma_start(
                        w2b[:, slq, ht16, :],
                        W["W2T"][q * 16 + ht16, :, dh * 1024:(dh + 1) * 1024]), s_ld2, 16)
                ldw = s_ld2.n
                rec.wait("tensor", s_ld2, ldw)
                for d8 in range(8):
                    bank = banks[d8]
                    if q == 0:
                        guard("tensor", f"bk{d8}")
                    def grp(e, q=q, d8=d8, slq=slq, bank=bank, first=(q == 0), last=(q == 3)):
                        for ht16 in range(15):
                            e.matmul(bank[:, :], w2b[:, slq, ht16, d8 * 128:(d8 + 1) * 128],
                                     hd[:, q * 16 + ht16, :], start=(first and ht16 == 0), stop=False)
                        return e.matmul(bank[:, :], w2b[:, slq, 15, d8 * 128:(d8 + 1) * 128],
                                        hd[:, q * 16 + 15, :], start=False, stop=last)
                    rec.add("tensor", grp, s_t, 1)
                    last_t = s_t.n
                setfree(f"w2b{slq}", "t", last_t)
            for d8 in range(8):
                doc = dh * 8 + d8
                sl = doc % 2
                rec.wait("vector", s_t, last_t)
                guard("vector", f"xn{sl}")
                rec.add("vector", lambda e, d8=d8, doc=doc, sl=sl: e.tensor_tensor(
                    xn[:, sl, :], banks[d8][:, :], xt[:, doc, :], OP.add), s_v, 1)
                setfree(f"bk{d8}", "v", s_v.n)
                v_res = s_v.n
                rec.wait("gpsimd", s_v, v_res)
                rec.add("gpsimd", lambda e, doc=doc, sl=sl, c0=c0: e.dma_start(
                    scr["XT"][doc, :, c0:c0 + 512], xn[:, sl, :]), s_stg, 16)
                setfree(f"xn{sl}", "stg", s_stg.n)
        setfree("hd", "t", last_t)
        setfree("xt", "v", v_res)
    return rec



def rec_dumpxl(ctx, sems, XL2T):
    rec = Rec(ctx)
    s_stg = sems["stg"]
    for d in range(16):
        for c in range(NT):
            rec.add("gpsimd", lambda e, d=d, c=c: e.dma_start(
                ctx.scr["XLD"][d, :, c * 512:(c + 1) * 512], XL2T[:, d, c, :]), s_stg, 16)
    return rec



def rec_s5(ctx, sems, P):
    rec = Rec(ctx)
    s_ld, s_ldg, s_t, s_v, s_sc, s_g, s_st, s_stg = (
        sems[k] for k in ("ld", "ldg", "t", "v", "sc", "g", "st", "stg"))
    sb, psb = ctx.sb, ctx.psb
    scr = ctx.scr
    wins = ctx.wins
    SCALE = 0.125

    xt = sb("s5_xt", [128, 16, 512], F32)
    sqbf = sb("s5_sqbf", [128, 16, 512], BF16)
    xlt = sb("s5_xlt", [128, 16, 512], BF16)
    tmpf = sb("s5_tmpf", [128, 2, 512], F32)
    rows = sb("s5_rows", [1, 4 * 512], F32)
    ab = sb("s5_ab", [128, 2 * 512], F32)
    wmv = sb("s5_wmv", [128, 16, 512], BF16)
    qt = sb("s5_qt", [128, 8, 512], BF16)
    kt = sb("s5_kt", [128, 8, 512], BF16)
    vnb = sb("s5_vnb", [128, 4, 4, 512], BF16)
    gnb = sb("s5_gnb", [128, 4, 4, 512], BF16)
    knb = sb("s5_knb", [128, 4, 2, 512], BF16)
    enb = sb("s5_enb", [128, 4, 2, 512], BF16)
    rnb = sb("s5_rnb", [128, 4, 2, 512], BF16)
    scl = sb("s5_scl", [128, 2, 512], F32)
    carry = sb("s5_carry", [1, 1024], BF16)
    HkS = sb("s5_HkS", [64, 1024], F32)
    HkB = sb("s5_HkB", [128, 1024], BF16)
    HvS = sb("s5_HvS", [64, 2048], F32)
    HvB = sb("s5_HvB", [64, 2048], BF16)
    db = sb("s5_db", [128, 2, 128], BF16)
    gb = sb("s5_gb", [128, 2, 128], BF16)
    okn = sb("s5_okn", [128, 2, 64], F32)
    mxc = sb("s5_mxc", [128, 4], F32)
    pvb = sb("s5_pvb", [128, 2, 64], BF16)
    ptt = sb("s5_ptt", [64, 2, 128], BF16)
    ett = sb("s5_ett", [64, 2, 128], BF16)
    sqd = sb("s5_sqd", [128, 128], BF16)
    colf = sb("s5_colf", [128, 8], F32)
    sgb = sb("s5_sgb", [128, 128], BF16)
    ggf = sb("s5_ggf", [128, 2, 128], F32)
    o1f = sb("s5_o1f", [128, 2, 128], F32)
    awt = sqbf
    aob_at = lambda h: xlt[:, h, 0:128]
    gnw_row = sb("s5_gnwrow", [1, 128], F32)
    gnwB = sb("s5_gnwB", [128, 128], F32)
    xn = tmpf

    B = [psb(f"s5_b{i}", [128, 512], F32) for i in range(7)]
    b7 = psb("s5_b7", [128, 1024], BF16)

    free = {}
    def guard(engname, res):
        if res in free:
            nm, cnt = free[res]
            rec.wait(engname, sems[nm], cnt)
    def setfree(res, nm, cnt):
        free[res] = (nm, cnt)

    rec.add("gpsimd", lambda e: e.memset(HkS[:, :], 0.0), s_g, 1)
    rec.add("gpsimd", lambda e: e.memset(HkB[:, :], 0.0), s_g, 1)
    rec.add("gpsimd", lambda e: e.memset(HvS[:, :], 0.0), s_g, 1)
    rec.add("gpsimd", lambda e: e.memset(HvB[:, :], 0.0), s_g, 1)
    rec.add("gpsimd", lambda e: e.memset(carry[:, :], 0.0), s_g, 1)
    g_init = s_g.n
    for h in range(16):
        setfree(f"hkb{h}", "g", g_init)
        setfree(f"hvb{h}", "g", g_init)
    setfree("carry_w", "g", g_init)
    gn_row = wins["w1_gn"].rearrange("(a b) -> a b", a=1)
    rec.add("sync", lambda e: e.dma_start(gnw_row[:, :], gn_row), s_ld, 16)
    rec.wait("tensor", s_ld, s_ld.n)
    rec.add("tensor", lambda e: e.matmul(B[0][:, 0:128], P["ones_row_f"][:, :], gnw_row[:, :],
                                         start=True, stop=True), s_t, 1)
    rec.wait("scalar", s_t, s_t.n)
    rec.add("scalar", lambda e: e.copy(gnwB[:, :], B[0][:, 0:128]), s_sc, 1)
    setfree("B0", "sc", s_sc.n)

    WB = ctx.wdram
    L1MAP = {"w1_q": "Q1B", "w1_k": "K1B", "w1_v": "V1B", "w1_g": "G1B",
             "w1_s": "S1B", "w1_o": "O1B"}

    QKO = {"q": "Q1B", "k": "K1B", "o": "O1B"}

    def load_qk(which, oc, slot):
        src = WB[QKO[which]].rearrange("(dt p) o -> p dt o", p=128)[:, :, oc * 128:(oc + 1) * 128]
        guard("sync", f"mvq{slot}")
        guard("sync", "mvfull")
        rec.add("sync", lambda e, src=src, slot=slot: e.dma_start(
            wmv[:, :, slot * 128:(slot + 1) * 128], src), s_ld, 16)
        return s_ld.n

    def load_mv(name, ocols):
        src = WB[L1MAP[name]].rearrange("(dt p) o -> p dt o", p=128)[:, :, ocols]
        guard("sync", "mvq0")
        guard("sync", "mvq1")
        guard("sync", "mvfull")
        rec.add("sync", lambda e, src=src: e.dma_start(wmv[:, :, :], src), s_ld, 16)
        return s_ld.n

    def load_wo(oc, slot):
        return load_qk("o", oc, slot)

    prev_ab = None
    for c in range(NT):
        c0 = c * 512
        guard("sync", "xt")
        for d in range(16):
            rec.add("sync", lambda e, d=d, c0=c0: e.dma_start(
                xt[:, d, :], scr["XT"][d, :, c0:c0 + 512]), s_ld, 16)
        ld_tiles = s_ld.n
        for e_ in ("vector", "scalar"):
            rec.wait(e_, s_ld, ld_tiles)
        guard("scalar", "sqbf")
        guard("vector", "xn0")
        guard("vector", "xn1")
        guard("tensor", "B2")
        guard("tensor", "B3")
        guard("tensor", "B4")
        guard("tensor", "B5")
        guard("gpsimd", "xlt_w")
        lnc = rec_ln_chunk(rec, sems, P, {
            "xt": xt, "xbf": None, "sqbf": sqbf,
            "xln_at": (lambda d: xlt[:, d, :]),
            "xt_ready": ("ld", ld_tiles),
            "tmpf": tmpf, "rows": rows, "ab": ab, "ab_guard": prev_ab,
            "ps_row0": B[2], "ps_row1": B[3],
            "ps_bc0": B[4], "ps_bc1": B[5]}, c, 16, 1.0 / 2048.0, 0)
        prev_ab = lnc["ab_free"]
        sq_free_t = lnc["sqbf_free_t"]
        setfree("B2", "v", lnc["psrow_free_v"])
        setfree("B3", "v", lnc["psrow_free_v"])
        setfree("B4", "sc", lnc["psbc_free_sc"])
        setfree("B5", "sc", lnc["psbc_free_sc"])
        xln_g = lnc["xln_done_g"]
        rec.wait("tensor", s_g, xln_g)

        nmm = 0
        pe = {}
        for which, buf in (("q", qt), ("k", kt)):
            for oc in range(8):
                slot = oc % 2
                lw = load_qk(which, oc, slot)
                par = nmm % 2
                bank = B[par]
                guard("tensor", f"B{par}")
                rec.wait("tensor", s_ld, lw)
                rec.mm_group(bank[:, :], [(wmv[:, d, slot * 128:(slot + 1) * 128], xlt[:, d, :])
                                          for d in range(16)], s_t)
                tmm = s_t.n
                setfree(f"mvq{slot}", "t", tmm)
                rec.wait("scalar", s_t, tmm)
                guard("scalar", "qt" if which == "q" else "kt")
                if which == "q":
                    rec.add("scalar", lambda e, oc=oc, b=bank: e.mul(qt[:, oc, :], b[:, :], SCALE), s_sc, 1)
                else:
                    rec.add("scalar", lambda e, oc=oc, b=bank: e.copy(kt[:, oc, :], b[:, :]), s_sc, 1)
                setfree(f"B{par}", "sc", s_sc.n)
                nmm += 1
        if "QKD" in ctx.dbg:
            rec.wait("gpsimd", s_sc, s_sc.n)
            for oc in range(8):
                rec.add("gpsimd", lambda e, c0=c0, oc=oc: e.dma_start(
                    ctx.scr["QKD"][0, oc * 128:(oc + 1) * 128, c0:c0 + 512], qt[:, oc, :]), s_stg, 16)
                rec.add("gpsimd", lambda e, c0=c0, oc=oc: e.dma_start(
                    ctx.scr["QKD"][1, oc * 128:(oc + 1) * 128, c0:c0 + 512], kt[:, oc, :]), s_stg, 16)
        ldi = 0
        for name, buf, nch, kind in (("w1_v", vnb, 4, "v"), ("w1_g", gnb, 4, "g"),
                                     ("w1_k", knb, 2, "kn"), ("w1_s", None, 2, "s")):
            for j in range(nch):
                ldi += 1
                lw = load_mv(name, slice(j * 512, (j + 1) * 512))
                for sub in range(4):
                    t_sl = slice(sub * 128, (sub + 1) * 128)
                    par = nmm % 2
                    bank = B[par]
                    guard("tensor", f"B{par}")
                    rec.wait("tensor", s_ld, lw)
                    rec.mm_group(bank[:, :], [(xlt[:, d, t_sl], wmv[:, d, :])
                                              for d in range(16)], s_t)
                    tmm = s_t.n
                    if sub == 3:
                        setfree("mvfull", "t", tmm)
                    if kind == "s":
                        cpar = nmm % 2
                        rec.wait("vector", s_t, tmm)
                        guard("vector", f"scl{cpar}")
                        rec.add("vector", lambda e, b=bank, cpar=cpar: e.tensor_scalar(
                            scl[:, cpar, :], b[:, :], -32.0, 32.0, OP.max, OP.min), s_v, 1)
                        v_cl = s_v.n
                        setfree(f"B{par}", "v", v_cl)
                        rec.wait("scalar", s_v, v_cl)
                        guard("scalar", "mvbufs_t")
                        rec.add("scalar", lambda e, sub=sub, j=j, cpar=cpar: e.activation(
                            enb[:, sub, j, :], scl[:, cpar, :], AF.Exp), s_sc, 1)
                        setfree(f"scl{cpar}", "sc", s_sc.n)
                        pe[(sub, j)] = s_sc.n
                    else:
                        rec.wait("vector", s_t, tmm)
                        guard("vector", "mvbufs_t")
                        if kind == "g":
                            guard("vector", "gnb_sc")
                        rec.add("vector", lambda e, b=bank, buf=buf, sub=sub, j=j: e.tensor_copy(
                            buf[:, sub, j, :], b[:, :]), s_v, 1)
                        setfree(f"B{par}", "v", s_v.n)
                    nmm += 1
        xlt_free_t = s_t.n
        rec.wait("vector", s_t, xlt_free_t)
        for sub in range(4):
            for j in range(2):
                guard("tensor", "B2")
                rec.wait("tensor", s_sc, pe[(sub, j)])
                guard("tensor", "carry_w")
                def cmm(e, sub=sub, j=j):
                    e.matmul(B[2][:, :], P["uincl_b"][:, :], enb[:, sub, j, :], start=True, stop=False)
                    return e.matmul(B[2][:, :], P["ones_row_b"][:, :], carry[:, j * 512:(j + 1) * 512],
                                    start=False, stop=True)
                rec.add("tensor", cmm, s_t, 1)
                t_c = s_t.n
                guard("tensor", "B3")
                def csum(e, sub=sub, j=j):
                    e.matmul(B[3][0:1, :], P["ones_col_b"][:, :], enb[:, sub, j, :], start=True, stop=False)
                    return e.matmul(B[3][0:1, :], P["ones_1b"][:, :], carry[:, j * 512:(j + 1) * 512],
                                    start=False, stop=True)
                rec.add("tensor", csum, s_t, 1)
                t_cs = s_t.n
                rec.wait("vector", s_t, t_c)
                def reciprnb(e, sub=sub, j=j):
                    with ctx.nc.allow_low_precision(reason="R=1/C used as bf16 matmul operand"):
                        return e.reciprocal(rnb[:, sub, j, :], B[2][:, :])
                rec.add("vector", reciprnb, s_v, 1)
                setfree("B2", "v", s_v.n)
                rec.wait("scalar", s_t, t_cs)
                rec.wait("scalar", s_t, t_c)
                rec.add("scalar", lambda e, j=j: e.copy(carry[:, j * 512:(j + 1) * 512], B[3][0:1, :]), s_sc, 1)
                setfree("B3", "sc", s_sc.n)
                setfree("carry_w", "sc", s_sc.n)

        for sub in range(4):
            for h in range(16):
                par = h % 2
                qsl = qt[64 * par:64 * par + 64, h // 2, sub * 128:(sub + 1) * 128]
                ksl = kt[64 * par:64 * par + 64, h // 2, sub * 128:(sub + 1) * 128]
                esl = enb[:, sub, h // 8, (h % 8) * 64:(h % 8) * 64 + 64]
                rsl = rnb[:, sub, h // 8, (h % 8) * 64:(h % 8) * 64 + 64]
                knsl = knb[:, sub, h // 8, (h % 8) * 64:(h % 8) * 64 + 64]
                vsl = vnb[:, sub, h // 4, (h % 4) * 128:(h % 4) * 128 + 128]
                gsl = gnb[:, sub, h // 4, (h % 4) * 128:(h % 4) * 128 + 128]
                cellb = B[4 + par]
                ovb = B[par]
                guard("tensor", f"B{4 + par}")
                rec.wait("tensor", s_sc, pe[(sub, h // 8)])
                rec.add("tensor", lambda e, b=cellb, ksl=ksl, qsl=qsl: e.matmul(
                    b[:, 0:128], ksl, qsl, start=True, stop=True), s_t, 1)
                t_a = s_t.n
                rec.wait("vector", s_t, t_a)
                guard("vector", f"db{par}")
                rec.add("vector", lambda e, b=cellb, par=par: e.tensor_tensor(
                    db[:, par, :], b[:, 0:128], P["uincl_f"][:, :], OP.mult), s_v, 1)
                v_d = s_v.n
                guard("tensor", "B6")
                rec.wait("tensor", s_v, v_d)
                guard("tensor", f"hkb{h}")
                def okmm(e, par=par, esl=esl, qsl=qsl, h=h):
                    e.matmul(B[6][:, 0:64], db[:, par, :], esl, start=True, stop=False)
                    return e.matmul(B[6][:, 0:64], qsl,
                                    HkB[64 * par:64 * par + 64, h * 64:(h + 1) * 64],
                                    start=False, stop=True)
                rec.add("tensor", okmm, s_t, 1)
                t_ok = s_t.n
                setfree(f"db{par}", "t", t_ok)
                rec.wait("vector", s_t, t_ok)
                guard("vector", f"okn{par}")
                rec.add("vector", lambda e, par=par, rsl=rsl: e.tensor_tensor(
                    okn[:, par, :], B[6][:, 0:64], rsl, OP.mult), s_v, 1)
                v_okn = s_v.n
                rec.wait("vector", s_v)
                rec.add("vector", lambda e, par=par: e.reduce_max(
                    mxc[:, par:par + 1], okn[:, par, :], axis=AX.X), s_v, 1)
                rec.wait("vector", s_v)
                rec.add("vector", lambda e, par=par: e.tensor_scalar(
                    mxc[:, 2 + par:3 + par], mxc[:, par:par + 1], -1.0, None, OP.mult), s_v, 1)
                v_nm = s_v.n
                rec.wait("scalar", s_v, v_nm)
                guard("scalar", f"pvb{par}")
                guard("scalar", "colf01")
                rec.add("scalar", lambda e, par=par: e.activation(
                    pvb[:, par, :], okn[:, par, :], AF.Exp,
                    bias=mxc[:, 2 + par:3 + par], accum_out=colf[:, par:par + 1]), s_sc, 1)
                sc_pv = s_sc.n
                setfree(f"okn{par}", "sc", sc_pv)
                rec.wait("vector", s_sc, sc_pv)
                rec.add("vector", lambda e, par=par: e.reciprocal(
                    colf[:, 2 + par:3 + par], colf[:, par:par + 1]), s_v, 1)
                setfree("colf01", "v", s_v.n)
                rec.wait("vector", s_v)
                rec.add("vector", lambda e, par=par, rsl=rsl: e.scalar_tensor_tensor(
                    pvb[:, par, :], pvb[:, par, :], colf[:, 2 + par:3 + par], rsl,
                    OP.mult, OP.mult), s_v, 1)
                v_pt = s_v.n
                if "PVD" in ctx.dbg:
                    rec.wait("gpsimd", s_v, v_pt)
                    rec.add("gpsimd", lambda e, c0=c0, sub=sub, h=h, par=par: e.dma_start(
                        ctx.scr["PVD"][c0 + sub * 128:c0 + (sub + 1) * 128, h * 64:(h + 1) * 64],
                        pvb[:, par, :]), s_stg, 16)
                guard("tensor", "B7")
                rec.wait("tensor", s_v, v_pt)
                rec.add("tensor", lambda e, par=par: e.transpose(
                    b7[0:64, 0:128], pvb[:, par, :], P["ident_b"][:, :]), s_t, 1)
                setfree(f"pvb{par}", "t", s_t.n)
                rec.add("tensor", lambda e, esl=esl: e.transpose(
                    b7[0:64, 128:256], esl, P["ident_b"][:, :]), s_t, 1)
                t_tr = s_t.n
                rec.wait("scalar", s_t, t_tr)
                guard("scalar", f"ptt{par}")
                rec.add("scalar", lambda e, par=par: e.copy(ptt[:, par, :], b7[0:64, 0:128]), s_sc, 1)
                rec.add("scalar", lambda e, par=par: e.copy(ett[:, par, :], b7[0:64, 128:256]), s_sc, 1)
                sc_tr = s_sc.n
                setfree("B7", "sc", sc_tr)
                rec.wait("tensor", s_sc, sc_tr)
                rec.add("tensor", lambda e, b=cellb, par=par: e.matmul(
                    b[:, 128:256], ett[:, par, :], ptt[:, par, :], start=True, stop=True), s_t, 1)
                t_g = s_t.n
                setfree(f"B{4 + par}", "t", t_g)
                rec.wait("vector", s_t, t_g)
                guard("vector", f"gb{par}")
                rec.add("vector", lambda e, b=cellb, par=par: e.tensor_tensor(
                    gb[:, par, :], b[:, 128:256], P["uincl_f"][:, :], OP.mult), s_v, 1)
                v_gb = s_v.n
                if "GBD" in ctx.dbg and c == 0 and sub == 0 and h == 0:
                    rec.wait("gpsimd", s_v, v_gb)
                    rec.add("gpsimd", lambda e, par=par: e.dma_start(
                        ctx.scr["GBD"][:, :], gb[:, par, :]), s_stg, 16)
                    rec.wait("gpsimd", s_sc, sc_tr)
                    rec.add("gpsimd", lambda e, par=par: e.dma_start(
                        ctx.scr["TTD"][:, 0:128], ptt[:, par, :]), s_stg, 16)
                    rec.add("gpsimd", lambda e, par=par: e.dma_start(
                        ctx.scr["TTD"][:, 128:256], ett[:, par, :]), s_stg, 16)
                guard("tensor", f"B{par}")
                rec.wait("tensor", s_v, v_gb)
                guard("tensor", f"hvb{h}")
                def ovmm(e, par=par, vsl=vsl, h=h):
                    e.matmul(B[par][:, 0:128], ptt[:, par, :], HvB[:, h * 128:(h + 1) * 128],
                             start=True, stop=False)
                    return e.matmul(B[par][:, 0:128], gb[:, par, :], vsl,
                                    start=False, stop=True)
                rec.add("tensor", ovmm, s_t, 1)
                t_ov = s_t.n
                setfree(f"gb{par}", "t", t_ov)
                setfree(f"ptt{par}", "t", t_ov)
                if "OVD" in ctx.dbg and c == 0 and sub == 0 and h == 0:
                    rec.wait("scalar", s_t, t_ov)
                    rec.add("scalar", lambda e, par=par: e.copy(
                        ctx.P["ovd_sb"][:, 0:128], B[par][:, 0:128]), s_sc, 1)
                    rec.wait("gpsimd", s_sc, s_sc.n)
                    rec.add("gpsimd", lambda e: e.dma_start(
                        ctx.scr["OVD"][:, 0:128], ctx.P["ovd_sb"][:, 0:128]), s_stg, 16)
                rec.wait("tensor", s_v, v_okn)
                def hkmm(e, knsl=knsl, esl=esl):
                    return e.matmul(B[6][0:64, 64:128], knsl, esl, start=True, stop=True)
                rec.add("tensor", hkmm, s_t, 1)
                t_hk = s_t.n
                def hvmm(e, esl=esl, vsl=vsl):
                    return e.matmul(B[6][0:64, 128:256], esl, vsl, start=True, stop=True)
                rec.add("tensor", hvmm, s_t, 1)
                t_hv = s_t.n
                rec.wait("vector", s_t, t_hk)
                rec.add("vector", lambda e, h=h: e.tensor_tensor(
                    HkS[:, h * 64:(h + 1) * 64], HkS[:, h * 64:(h + 1) * 64],
                    B[6][0:64, 64:128], OP.add), s_v, 1)
                rec.wait("vector", s_v)
                rec.add("vector", lambda e, h=h: e.tensor_copy(
                    HkB[0:64, h * 64:(h + 1) * 64], HkS[:, h * 64:(h + 1) * 64]), s_v, 1)
                rec.add("vector", lambda e, h=h: e.tensor_copy(
                    HkB[64:128, h * 64:(h + 1) * 64], HkS[:, h * 64:(h + 1) * 64]), s_v, 1)
                setfree(f"hkb{h}", "v", s_v.n)
                rec.wait("vector", s_t, t_hv)
                rec.add("vector", lambda e, h=h: e.tensor_tensor(
                    HvS[:, h * 128:(h + 1) * 128], HvS[:, h * 128:(h + 1) * 128],
                    B[6][0:64, 128:256], OP.add), s_v, 1)
                rec.wait("vector", s_v)
                rec.add("vector", lambda e, h=h: e.tensor_copy(
                    HvB[:, h * 128:(h + 1) * 128], HvS[:, h * 128:(h + 1) * 128]), s_v, 1)
                v_hv = s_v.n
                setfree(f"hvb{h}", "v", v_hv)
                setfree("B6", "v", v_hv)
                rec.wait("scalar", s_t, t_ov)
                guard("scalar", "sqd")
                guard("scalar", "colf45")
                rec.add("scalar", lambda e, par=par: e.activation(
                    sqd[:, :], B[par][:, 0:128], AF.Square,
                    accum_out=colf[:, 4:5]), s_sc, 1)
                sc_sq = s_sc.n
                setfree("sqd", "sc", sc_sq)
                rec.wait("vector", s_sc, sc_sq)
                rec.add("vector", lambda e: e.tensor_scalar(
                    colf[:, 5:6], colf[:, 4:5], 1.0 / 128.0, 1e-5, OP.mult, OP.add), s_v, 1)
                v_va = s_v.n
                setfree("colf45", "v", v_va)
                rec.wait("scalar", s_v, v_va)
                guard("scalar", "colf6")
                rec.add("scalar", lambda e: e.activation(colf[:, 6:7], colf[:, 5:6], AF.Sqrt), s_sc, 1)
                sc_sd = s_sc.n
                rec.wait("vector", s_sc, sc_sd)
                rec.add("vector", lambda e: e.reciprocal(colf[:, 7:8], colf[:, 6:7]), s_v, 1)
                v_rstd = s_v.n
                setfree("colf6", "v", v_rstd)
                guard("scalar", "sgb")
                rec.add("scalar", lambda e, gsl=gsl: e.activation(sgb[:, :], gsl, AF.Silu), s_sc, 1)
                sc_sg = s_sc.n
                rec.wait("vector", s_sc, sc_sg)
                guard("vector", f"ggf{par}")
                rec.add("vector", lambda e, par=par: e.tensor_copy(
                    ggf[:, par, :], sgb[:, :]), s_v, 1)
                v_gg = s_v.n
                setfree("sgb", "v", v_gg)
                rec.wait("vector", s_v, v_rstd)
                guard("vector", f"o1f{par}")
                rec.add("vector", lambda e, par=par: e.scalar_tensor_tensor(
                    o1f[:, par, :], B[par][:, 0:128], colf[:, 7:8], gnwB[:, :],
                    OP.mult, OP.mult), s_v, 1)
                v_o1 = s_v.n
                setfree(f"B{par}", "v", v_o1)
                rec.wait("vector", s_v)
                guard("vector", f"aob{h}")
                rec.add("vector", lambda e, par=par, h=h: e.tensor_tensor(
                    aob_at(h), o1f[:, par, :], ggf[:, par, :], OP.mult), s_v, 1)
                if "OVD" in ctx.dbg and c == 0 and sub == 0 and h == 0:
                    rec.wait("gpsimd", s_v, s_v.n)
                    rec.add("gpsimd", lambda e, par=par: e.dma_start(
                        ctx.scr["OVD"][:, 128:256], o1f[:, par, :]), s_stg, 16)
                    rec.add("gpsimd", lambda e, par=par: e.dma_start(
                        ctx.scr["OVD"][:, 256:384], ggf[:, par, :]), s_stg, 16)
                v_ao = s_v.n
                setfree(f"ggf{par}", "v", v_ao)
                setfree(f"o1f{par}", "v", v_ao)
                setfree(f"aob_w{h}", "v", v_ao)
            for ht in range(16):
                guard("tensor", "B7")
                nm_, cnt_ = free[f"aob_w{ht}"]
                rec.wait("tensor", sems[nm_], cnt_)
                rec.add("tensor", lambda e, ht=ht: e.transpose(
                    b7[:, 256:384], aob_at(ht), P["ident_b"][:, :]), s_t, 1)
                t_tr2 = s_t.n
                setfree(f"aob{ht}", "t", t_tr2)
                rec.wait("scalar", s_t, t_tr2)
                rec.wait("scalar", s_t, sq_free_t)
                guard("scalar", f"awt{ht}")
                rec.add("scalar", lambda e, ht=ht, sub=sub: e.copy(
                    awt[:, ht, sub * 128:(sub + 1) * 128], b7[:, 256:384]), s_sc, 1)
                setfree("B7", "sc", s_sc.n)
        awt_sc = s_sc.n
        setfree("mvbufs_t", "t", s_t.n)
        setfree("gnb_sc", "sc", s_sc.n)
        last_t = 0
        end_v = 0
        for oc in range(16):
            slot = oc % 2
            lw = load_wo(oc, slot)
            bank = B[2 + slot]
            guard("tensor", f"B{2 + slot}")
            rec.wait("tensor", s_ld, lw)
            rec.wait("tensor", s_sc, awt_sc)
            rec.mm_group(bank[:, :], [(wmv[:, hv, slot * 128:(slot + 1) * 128], awt[:, hv, :])
                                      for hv in range(16)], s_t)
            last_t = s_t.n
            setfree(f"mvq{slot}", "t", last_t)
            rec.wait("vector", s_t, last_t)
            rec.wait("vector", s_ld, ld_tiles)
            rec.wait("vector", s_g, xln_g)
            guard("vector", f"xn{slot}")
            rec.add("vector", lambda e, oc=oc, b=bank, slot=slot: e.tensor_tensor(
                xn[:, slot, :], b[:, :], xt[:, oc, :], OP.add), s_v, 1)
            end_v = s_v.n
            setfree(f"B{2 + slot}", "v", end_v)
            rec.wait("gpsimd", s_v, end_v)
            rec.add("gpsimd", lambda e, oc=oc, slot=slot, c0=c0: e.dma_start(
                scr["XT"][oc, :, c0:c0 + 512], xn[:, slot, :]), s_stg, 16)
            setfree(f"xn{slot}", "stg", s_stg.n)
        setfree("xt", "v", end_v)
        setfree("xlt_w", "t", last_t)
        setfree("qt", "t", last_t)
        setfree("kt", "t", last_t)
        setfree("sqbf", "t", last_t)
        for ht in range(16):
            setfree(f"awt{ht}", "t", last_t)
    return rec

def rec_s6(ctx, sems, P):
    rec = Rec(ctx)
    s_ld, s_ldg, s_t, s_v, s_sc, s_g, s_st, s_stg = (
        sems[k] for k in ("ld", "ldg", "t", "v", "sc", "g", "st", "stg"))
    sb, psb = ctx.sb, ctx.psb
    scr = ctx.scr
    wins = ctx.wins

    xt = sb("s6_xt", [128, 16, 512], F32)
    sqbf = sb("s6_sqbf", [128, 16, 512], BF16)
    xlt = sb("s6_xlt", [128, 16, 512], BF16)
    tmpf = sb("s6_tmpf", [128, 2, 512], F32)
    rows = sb("s6_rows", [1, 4 * 512], F32)
    ab = sb("s6_ab", [128, 2 * 512], F32)
    wrt = sb("s6_wrt", [128, 16, 128], BF16)
    amp = sb("s6_amp", [128, 16], F32)
    esc = sb("s6_esc", [128, 1], F32)
    mxk = sb("s6_mxk", [128, 16], F32)
    prb = sb("s6_prb", [128, 3, 128], F32)
    mk = sb("s6_mk", [128, 128], F32)
    mskd = sb("s6_mskd", [128, 128], BF16)
    mskdT = sb("s6_mskdT", [128, 128], BF16)
    Mx = sb("s6_Mx", [128, 8], BF16)
    cbuf = sb("s6_cbuf", [128, 4, 8], F32)
    wgt = sb("s6_wgt", [128, 16, 512], BF16)
    wut = sb("s6_wut", [128, 16, 512], BF16)
    sg = sb("s6_sg", [128, 512], BF16)
    wpart = sb("s6_wpart", [128, 512], F32)
    wsum = sb("s6_wsum", [128, 4, 1024], F32)
    shb = sb("s6_shb", [128, 4, 4, 512], BF16)
    wsT = sb("s6_wsT", [128, 8, 4, 128], BF16)
    shT = sb("s6_shT", [128, 16, 4, 128], BF16)
    wdnt = sb("s6_wdnt", [128, 8, 128], BF16)
    shdt = sb("s6_shdt", [128, 16, 128], BF16)
    xn = sb("s6_xn", [128, 2, 512], F32)

    B = [psb(f"s6_b{i}", [128, 512], F32) for i in range(7)]
    b7 = psb("s6_b7", [128, 1024], BF16)

    free = {}
    def guard(engname, res):
        if res in free:
            nm, cnt = free[res]
            rec.wait(engname, sems[nm], cnt)
    def setfree(res, nm, cnt):
        free[res] = (nm, cnt)

    wr_src = ctx.wdram["RTRB"].rearrange("(dt p) o -> p dt o", p=128)
    rec.add("sync", lambda e: e.dma_start(wrt[:, :, :], wr_src), s_ld, 16)
    rec.add("gpsimd", lambda e: e.dma_start(
        amp[:, :], wins["w1_amps"].rearrange("e k two -> e (k two)")), s_ldg, 16)
    rec.add("gpsimd", lambda e: e.dma_start(
        esc[:, :], wins["w1_escale"].rearrange("(e b) -> e b", b=1)), s_ldg, 16)
    ld_const = s_ldg.n
    rec.wait("vector", s_ldg, ld_const)
    amp3 = amp.rearrange("p (k two) -> p k two", two=2)
    rec.add("vector", lambda e: e.tensor_tensor(mxk[:, 0:8], amp3[:, :, 0], amp3[:, :, 0], OP.mult), s_v, 1)
    rec.wait("vector", s_v)
    rec.add("vector", lambda e: e.tensor_tensor(prb[:, 0, 0:8], amp3[:, :, 1], amp3[:, :, 1], OP.mult), s_v, 1)
    rec.wait("vector", s_v)
    rec.add("vector", lambda e: e.tensor_tensor(mxk[:, 0:8], mxk[:, 0:8], prb[:, 0, 0:8], OP.add), s_v, 1)
    rec.wait("vector", s_v)
    rec.add("vector", lambda e: e.reduce_sum(mxk[:, 8:9], mxk[:, 0:8], axis=AX.X), s_v, 1)
    rec.wait("vector", s_v)
    rec.add("vector", lambda e: e.tensor_scalar(mxk[:, 9:10], mxk[:, 8:9], 1.0, 1e-8, OP.mult, OP.add), s_v, 1)
    rec.wait("vector", s_v)
    rec.add("vector", lambda e: e.reciprocal(mxk[:, 10:11], mxk[:, 9:10]), s_v, 1)
    rec.wait("vector", s_v)
    rec.add("vector", lambda e: e.tensor_scalar(mxk[:, 0:8], mxk[:, 0:8], mxk[:, 10:11], None, OP.mult), s_v, 1)
    rec.wait("vector", s_v)
    def mkmx(e):
        with ctx.nc.allow_low_precision(reason="expert mix as bf16 matmul operand"):
            return e.tensor_scalar(Mx[:, :], mxk[:, 0:8], esc[:, :], None, OP.mult)
    rec.add("vector", mkmx, s_v, 1)
    v_mx = s_v.n

    prev_ab = None
    for c in range(NT):
        c0 = c * 512
        guard("sync", "xt")
        for d in range(16):
            rec.add("sync", lambda e, d=d, c0=c0: e.dma_start(
                xt[:, d, :], scr["XT"][d, :, c0:c0 + 512]), s_ld, 16)
        ld_tiles = s_ld.n
        for e_ in ("vector", "scalar"):
            rec.wait(e_, s_ld, ld_tiles)
        guard("scalar", "sqbf")
        guard("tensor", "B2")
        guard("tensor", "B3")
        guard("tensor", "B4")
        guard("tensor", "B5")
        guard("gpsimd", "xlt_w")
        lnc = rec_ln_chunk(rec, sems, P, {
            "xt": xt, "xbf": None, "sqbf": sqbf,
            "xln_at": (lambda d: xlt[:, d, :]),
            "xt_ready": ("ld", ld_tiles),
            "tmpf": tmpf, "rows": rows, "ab": ab, "ab_guard": prev_ab,
            "ps_row0": B[2], "ps_row1": B[3],
            "ps_bc0": B[4], "ps_bc1": B[5]}, c, 16, 1.0 / 2048.0, 0)
        prev_ab = lnc["ab_free"]
        setfree("sqbf", "t", lnc["sqbf_free_t"])
        setfree("B2", "v", lnc["psrow_free_v"])
        setfree("B3", "v", lnc["psrow_free_v"])
        setfree("B4", "sc", lnc["psbc_free_sc"])
        setfree("B5", "sc", lnc["psbc_free_sc"])
        rec.wait("tensor", s_g, lnc["xln_done_g"])

        for sub in range(4):
            t_sl = slice(sub * 128, (sub + 1) * 128)
            guard("tensor", "B2")
            rec.wait("tensor", s_ld, s_ld.n)
            rec.wait("tensor", s_ldg, ld_const)
            rec.mm_group(B[2][:, 0:128], [(xlt[:, d, t_sl], wrt[:, d, :]) for d in range(16)], s_t)
            t_rt = s_t.n
            rec.wait("vector", s_t, t_rt)
            guard("vector", "rtcols")
            rec.add("vector", lambda e: e.reduce_max(mxk[:, 11:12], B[2][:, 0:128], axis=AX.X), s_v, 1)
            rec.wait("vector", s_v)
            rec.add("vector", lambda e: e.tensor_scalar(mxk[:, 12:13], mxk[:, 11:12], -1.0, None, OP.mult), s_v, 1)
            v_neg = s_v.n
            rec.wait("scalar", s_v, v_neg)
            guard("scalar", "expb")
            rec.add("scalar", lambda e: e.activation(
                prb[:, 0, :], B[2][:, 0:128], AF.Exp, bias=mxk[:, 12:13],
                accum_out=mxk[:, 13:14]), s_sc, 1)
            sc_exp = s_sc.n
            setfree("B2", "sc", sc_exp)
            rec.wait("vector", s_sc, sc_exp)
            rec.add("vector", lambda e: e.reciprocal(mxk[:, 14:15], mxk[:, 13:14]), s_v, 1)
            rec.wait("vector", s_v)
            rec.add("vector", lambda e: e.tensor_scalar(
                prb[:, 0, :], prb[:, 0, :], mxk[:, 14:15], None, OP.mult), s_v, 1)
            v_probs = s_v.n
            cur = 0
            for it in range(3):
                rec.wait("vector", s_v)
                rec.add("vector", lambda e, cur=cur: e.reduce_max(
                    mxk[:, 11:12], prb[:, cur, :], axis=AX.X), s_v, 1)
                rec.wait("vector", s_v)
                rec.add("vector", lambda e, cur=cur: e.tensor_scalar(
                    mk[:, :], prb[:, cur, :], mxk[:, 11:12], None, OP.is_ge), s_v, 1)
                rec.wait("vector", s_v)
                nxt = 1 + (it % 2)
                rec.add("vector", lambda e, cur=cur, nxt=nxt: e.scalar_tensor_tensor(
                    prb[:, nxt, :], mk[:, :], -2.0, prb[:, cur, :], OP.mult, OP.add), s_v, 1)
                cur = nxt
            rec.wait("vector", s_v)
            rec.add("vector", lambda e, cur=cur: e.reduce_max(
                mxk[:, 11:12], prb[:, cur, :], axis=AX.X), s_v, 1)
            rec.wait("vector", s_v)
            guard("vector", "mskd")
            rec.add("vector", lambda e: e.scalar_tensor_tensor(
                mskd[:, :], prb[:, 0, :], mxk[:, 11:12], prb[:, 0, :],
                OP.is_ge, OP.mult, accum_out=mxk[:, 15:16]), s_v, 1)
            v_mskd = s_v.n
            rec.wait("vector", s_v)
            rec.add("vector", lambda e: e.tensor_scalar(
                mxk[:, 13:14], mxk[:, 15:16], 1.0, 1e-8, OP.mult, OP.add), s_v, 1)
            rec.wait("vector", s_v)
            rec.add("vector", lambda e: e.reciprocal(mxk[:, 14:15], mxk[:, 13:14]), s_v, 1)
            v_rcp = s_v.n
            guard("tensor", "B7")
            rec.wait("tensor", s_v, v_mskd)
            rec.add("tensor", lambda e: e.transpose(
                b7[:, 0:128], mskd[:, :], P["ident_b"][:, :]), s_t, 1)
            t_tm = s_t.n
            setfree("mskd", "t", t_tm)
            rec.wait("scalar", s_t, t_tm)
            guard("scalar", "mskdT")
            rec.add("scalar", lambda e: e.copy(mskdT[:, :], b7[:, 0:128]), s_sc, 1)
            sc_tm = s_sc.n
            setfree("B7", "sc", sc_tm)
            guard("tensor", "B3")
            rec.wait("tensor", s_sc, sc_tm)
            rec.wait("tensor", s_v, v_mx)
            rec.add("tensor", lambda e: e.matmul(B[3][:, 0:8], mskdT[:, :], Mx[:, :],
                                                 start=True, stop=True), s_t, 1)
            t_c = s_t.n
            setfree("mskdT", "t", t_c)
            rec.wait("vector", s_t, t_c)
            rec.wait("vector", s_v, v_rcp)
            guard("vector", f"cbuf{sub}")
            rec.add("vector", lambda e, sub=sub: e.tensor_scalar(
                cbuf[:, sub, :], B[3][:, 0:8], mxk[:, 14:15], None, OP.mult), s_v, 1)
            v_c = s_v.n
            setfree("B3", "v", v_c)
            setfree("rtcols", "v", v_c)
            setfree("expb", "v", v_c)

        if "CBD" in ctx.dbg:
            for sub in range(4):
                rec.wait("gpsimd", s_v, s_v.n)
                rec.add("gpsimd", lambda e, c0=c0, sub=sub: e.dma_start(
                    ctx.scr["CBD"][c0 + sub * 128:c0 + (sub + 1) * 128, :], cbuf[:, sub, :]), s_stg, 16)
        for oc in range(16):
            kk = oc // 2
            half = oc % 2
            guard("sync", "wgt")
            wg_src = ctx.wdram["GATB"].rearrange("(dt p) o -> p dt o", p=128)[:, :, oc * 512:(oc + 1) * 512]
            wu_src = ctx.wdram["UPB"].rearrange("(dt p) o -> p dt o", p=128)[:, :, oc * 512:(oc + 1) * 512]
            rec.add("sync", lambda e, src=wg_src: e.dma_start(wgt[:, :, :], src), s_ld, 16)
            rec.add("sync", lambda e, src=wu_src: e.dma_start(wut[:, :, :], src), s_ld, 16)
            ldw = s_ld.n
            for sub in range(4):
                t_sl = slice(sub * 128, (sub + 1) * 128)
                guard("tensor", "B0")
                rec.wait("tensor", s_ld, ldw)
                rec.mm_group(B[0][:, :], [(xlt[:, d, t_sl], wgt[:, d, :]) for d in range(16)], s_t)
                t_gt = s_t.n
                guard("tensor", "B1")
                rec.mm_group(B[1][:, :], [(xlt[:, d, t_sl], wut[:, d, :]) for d in range(16)], s_t)
                t_up = s_t.n
                rec.wait("scalar", s_t, t_gt)
                guard("scalar", "sg")
                rec.add("scalar", lambda e: e.activation(sg[:, :], B[0][:, :], AF.Silu), s_sc, 1)
                sc_sg = s_sc.n
                setfree("B0", "sc", sc_sg)
                rec.wait("vector", s_sc, sc_sg)
                rec.wait("vector", s_t, t_up)
                rec.wait("vector", s_v, v_c if sub == 3 else s_v.n)
                if kk == 0:
                    guard("vector", f"wsumw{sub}{half}")
                    rec.add("vector", lambda e, sub=sub, half=half, kk=kk: e.scalar_tensor_tensor(
                        wsum[:, sub, half * 512:(half + 1) * 512], B[1][:, :],
                        cbuf[:, sub, kk:kk + 1], sg[:, :], OP.mult, OP.mult), s_v, 1)
                else:
                    rec.add("vector", lambda e, sub=sub, kk=kk: e.scalar_tensor_tensor(
                        wpart[:, :], B[1][:, :], cbuf[:, sub, kk:kk + 1], sg[:, :],
                        OP.mult, OP.mult), s_v, 1)
                    rec.wait("vector", s_v)
                    rec.add("vector", lambda e, sub=sub, half=half: e.tensor_tensor(
                        wsum[:, sub, half * 512:(half + 1) * 512],
                        wsum[:, sub, half * 512:(half + 1) * 512], wpart[:, :], OP.add), s_v, 1)
                v_ws = s_v.n
                setfree("B1", "v", v_ws)
                setfree("sg", "v", v_ws)
            setfree("wgt", "t", s_t.n)
        v_wsum_done = s_v.n

        for sc_ch in range(4):
            shg_src = ctx.wdram["SHGB"].rearrange("(dt p) o -> p dt o", p=128)[:, :, sc_ch * 512:(sc_ch + 1) * 512]
            shu_src = ctx.wdram["SHUB"].rearrange("(dt p) o -> p dt o", p=128)[:, :, sc_ch * 512:(sc_ch + 1) * 512]
            guard("sync", "wgt")
            rec.add("sync", lambda e, src=shg_src: e.dma_start(wgt[:, :, :], src), s_ld, 16)
            rec.add("sync", lambda e, src=shu_src: e.dma_start(wut[:, :, :], src), s_ld, 16)
            ldw = s_ld.n
            for sub in range(4):
                t_sl = slice(sub * 128, (sub + 1) * 128)
                guard("tensor", "B4")
                rec.wait("tensor", s_ld, ldw)
                rec.mm_group(B[4][:, :], [(xlt[:, d, t_sl], wgt[:, d, :]) for d in range(16)], s_t)
                t_gt = s_t.n
                guard("tensor", "B5")
                rec.mm_group(B[5][:, :], [(xlt[:, d, t_sl], wut[:, d, :]) for d in range(16)], s_t)
                t_up = s_t.n
                rec.wait("scalar", s_t, t_gt)
                guard("scalar", "sg")
                rec.add("scalar", lambda e: e.activation(sg[:, :], B[4][:, :], AF.Silu), s_sc, 1)
                sc_sg2 = s_sc.n
                setfree("B4", "sc", sc_sg2)
                rec.wait("vector", s_sc, sc_sg2)
                rec.wait("vector", s_t, t_up)
                guard("vector", f"shb{sub}{sc_ch}")
                rec.add("vector", lambda e, sub=sub, sc_ch=sc_ch: e.tensor_tensor(
                    shb[:, sub, sc_ch, :], B[5][:, :], sg[:, :], OP.mult), s_v, 1)
                v_sh = s_v.n
                setfree("B5", "v", v_sh)
                setfree("sg", "v", v_sh)
            setfree("wgt", "t", s_t.n)
        v_sh_done = s_v.n

        if "WSD" in ctx.dbg:
            for sub in range(4):
                rec.wait("gpsimd", s_v, v_wsum_done)
                rec.add("gpsimd", lambda e, c0=c0, sub=sub: e.dma_start(
                    ctx.scr["WSD"][c0 + sub * 128:c0 + (sub + 1) * 128, :], wsum[:, sub, :]), s_stg, 16)
        if "SHBD" in ctx.dbg:
            for sub in range(4):
                rec.wait("gpsimd", s_v, v_sh_done)
                for j in range(4):
                    rec.add("gpsimd", lambda e, c0=c0, sub=sub, j=j: e.dma_start(
                        ctx.scr["SHBD"][c0 + sub * 128:c0 + (sub + 1) * 128, j * 512:(j + 1) * 512],
                        shb[:, sub, j, :]), s_stg, 16)
        for sub in range(4):
            for i8 in range(8):
                guard("tensor", "B6")
                rec.wait("vector", s_t, 0)
                rec.wait("tensor", s_v, v_wsum_done)
                rec.add("tensor", lambda e, sub=sub, i8=i8: e.transpose(
                    B[6][:, 0:128], wsum[:, sub, i8 * 128:(i8 + 1) * 128],
                    P["ident_f"][:, :]), s_t, 1)
                t_tr = s_t.n
                rec.wait("scalar", s_t, t_tr)
                guard("scalar", f"wsT{sub}{i8}")
                rec.add("scalar", lambda e, sub=sub, i8=i8: e.copy(
                    wsT[:, i8, sub, :], B[6][:, 0:128]), s_sc, 1)
                setfree("B6", "sc", s_sc.n)
            for i16 in range(16):
                guard("tensor", "B7")
                rec.wait("tensor", s_v, v_sh_done)
                rec.add("tensor", lambda e, sub=sub, i16=i16: e.transpose(
                    b7[:, 0:128], shb[:, sub, i16 // 4, (i16 % 4) * 128:(i16 % 4) * 128 + 128],
                    P["ident_b"][:, :]), s_t, 1)
                t_tr = s_t.n
                rec.wait("scalar", s_t, t_tr)
                guard("scalar", f"shT{sub}{i16}")
                rec.add("scalar", lambda e, sub=sub, i16=i16: e.copy(
                    shT[:, i16, sub, :], b7[:, 0:128]), s_sc, 1)
                setfree("B7", "sc", s_sc.n)
        sc_tr_done = s_sc.n
        setfree("wsum_w", "t", s_t.n)

        for doc in range(16):
            wd_src = ctx.wdram["DWNB"].rearrange("(dt p) o -> p dt o", p=128)[:, :, doc * 128:(doc + 1) * 128]
            sd_src = ctx.wdram["SHDB"].rearrange("(dt p) o -> p dt o", p=128)[:, :, doc * 128:(doc + 1) * 128]
            guard("sync", "wdnt")
            rec.add("sync", lambda e, src=wd_src: e.dma_start(wdnt[:, :, :], src), s_ld, 16)
            rec.add("sync", lambda e, src=sd_src: e.dma_start(shdt[:, :, :], src), s_ld, 16)
            ldw = s_ld.n
            par = doc % 2
            bank = B[par]
            guard("tensor", f"B{par}")
            rec.wait("tensor", s_ld, ldw)
            rec.wait("tensor", s_sc, sc_tr_done)
            def outmm2(e, doc=doc, bank=bank):
                for i8 in range(8):
                    e.matmul(bank[:, :], wdnt[:, i8, :],
                             wsT[:, i8, :, :].rearrange("p a b -> p (a b)"),
                             start=(i8 == 0), stop=False)
                last = None
                for i16 in range(16):
                    last = e.matmul(bank[:, :], shdt[:, i16, :],
                                    shT[:, i16, :, :].rearrange("p a b -> p (a b)"),
                                    start=False, stop=(i16 == 15))
                return last
            rec.add("tensor", outmm2, s_t, 1)
            t_out = s_t.n
            setfree("wdnt", "t", t_out)
            rec.wait("vector", s_t, t_out)
            rec.wait("vector", s_ld, ld_tiles)
            guard("vector", f"xn{par}")
            rec.add("vector", lambda e, doc=doc, bank=bank, par=par: e.tensor_tensor(
                xn[:, par, :], bank[:, :], xt[:, doc, :], OP.add), s_v, 1)
            v_out = s_v.n
            setfree(f"B{par}", "v", v_out)
            rec.wait("gpsimd", s_v, v_out)
            rec.add("gpsimd", lambda e, doc=doc, par=par, c0=c0: e.dma_start(
                scr["XT"][doc, :, c0:c0 + 512], xn[:, par, :]), s_stg, 16)
            setfree(f"xn{par}", "stg", s_stg.n)
        end_t = s_t.n
        end_v = s_v.n
        setfree("xt", "v", end_v)
        setfree("xlt_w", "t", end_t)
        for sub in range(4):
            setfree(f"cbuf{sub}", "v", end_v)
            for i8 in range(8):
                setfree(f"wsT{sub}{i8}", "t", end_t)
            for i16 in range(16):
                setfree(f"shT{sub}{i16}", "t", end_t)
            for h_ in range(2):
                setfree(f"wsumw{sub}{h_}", "t", end_t)
            for scch in range(4):
                setfree(f"shb{sub}{scch}", "t", end_t)
    return rec


SCRATCH = {
    "XT":   ([16, 128, T], F32),
    "QAR":  ([12, 128, T], BF16),
    "CKVR": ([4, 128, T], BF16),
    "KPEA": ([65, T], BF16),
    "QNT":  ([16, 128, T], BF16),
    "QPE":  ([8, 128, T], BF16),
    "KNT":  ([16, 128, T], BF16),
    "VNT":  ([T, 2048], BF16),
    "AOT":  ([16, 128, T], BF16),
    "HD":   ([64, 128, T], BF16),
    "XLD":  ([16, 128, T], BF16),
    "AOD":  ([T, 2048], BF16),
    "ED":   ([T, 1024], BF16),
    "RD":   ([T, 1024], BF16),
    "QKD":  ([2, 1024, T], BF16),
    "PVD":  ([T, 1024], BF16),
    "GBD":  ([128, 128], BF16),
    "VGD":  ([2, 128, 2048], BF16),
    "OVD":  ([128, 384], F32),
    "CBD":  ([T, 8], F32),
    "WSD":  ([T, 1024], F32),
    "SHBD": ([T, 2048], BF16),
    "TTD":  ([64, 256], BF16),
}
WSCRATCH = {
    "QAB": [1536, 2048], "QBB": [3072, 1536], "KVAB": [576, 2048],
    "KVBB": [4096, 512], "OWB": [2048, 2048],
    "W1B": [8192, 2048], "W3B": [8192, 2048],
    "W2BF": [2048, 8192], "W2T": [64, 128, 2048],
    "Q1B": [2048, 1024], "K1B": [2048, 1024], "V1B": [2048, 2048],
    "G1B": [2048, 2048], "S1B": [2048, 1024], "O1B": [2048, 2048],
    "RTRB": [2048, 128], "GATB": [2048, 8192], "UPB": [2048, 8192],
    "DWNB": [1024, 2048], "SHGB": [2048, 2048], "SHUB": [2048, 2048],
    "SHDB": [2048, 2048],
}
WEIGHT_INPUTS = {
    "w_qa": [1536, 2048], "w_qa_ln": [1536], "w_qb": [3072, 1536],
    "w_kva": [576, 2048], "w_kva_ln": [512], "w_kvb": [4096, 512],
    "w_ow": [2048, 2048], "w_W1": [8192, 2048], "w_W2": [2048, 8192],
    "w_W3": [8192, 2048],
    "w1_q": [2048, 1024], "w1_k": [2048, 1024], "w1_v": [2048, 2048],
    "w1_g": [2048, 2048], "w1_s": [2048, 1024], "w1_o": [2048, 2048],
    "w1_gn": [128], "w1_router": [2048, 128], "w1_gate": [2048, 8192],
    "w1_up": [2048, 8192], "w1_down": [1024, 2048], "w1_amps": [128, 8, 2],
    "w1_escale": [128], "w1_shg": [2048, 2048], "w1_shu": [2048, 2048],
    "w1_shd": [2048, 2048],
}


def build_program(upto=99, dbg=(), skip=(), **bass_kw):
    nc = bass.Bass("TRN2", target_bir_lowering=False, **bass_kw)
    es = ExitStack()
    nc._es_keepalive = es

    x_in = nc.dram_tensor("x_in", [T, D], F32, kind="ExternalInput").ap()
    y_out = nc.dram_tensor("y_out", [T, D], F32, kind="ExternalOutput").ap()
    consts = {}
    for name, arr in host_constants().items():
        consts[name] = nc.dram_tensor(name, list(arr.shape), F32, kind="ExternalInput").ap()
    wins = {n: nc.dram_tensor(n, shp, F32, kind="ExternalInput").ap()
            for n, shp in WEIGHT_INPUTS.items()}
    scr = {}
    for n, (shp, dt) in SCRATCH.items():
        kind = "ExternalOutput" if n in dbg else "Internal"
        scr[n] = nc.dram_tensor(n, shp, dt, kind=kind).ap()
    wdram = {n: nc.dram_tensor("ws_" + n, shp, BF16,
                               kind=("ExternalOutput" if n in dbg else "Internal")).ap()
             for n, shp in WSCRATCH.items()}

    ctx = Ctx(nc, es)
    ctx.cident = consts["c_ident"]
    ctx.consts = consts
    ctx.wins = wins
    ctx.scr = scr
    ctx.wdram = wdram
    ctx.dbg = dbg

    sems = {k: ctx.sem("s_" + k) for k in ("ld", "ld2", "ldg", "t", "v", "sc", "g", "st", "stg")}
    track = list(sems.values()) + [ctx.s_bar]

    P = {}
    P["ident_f"] = es.enter_context(nc.sbuf_tensor("P_ident_f", [128, 128], F32))
    P["ident_b"] = es.enter_context(nc.sbuf_tensor("P_ident_b", [128, 128], BF16))
    P["uincl_f"] = es.enter_context(nc.sbuf_tensor("P_uincl_f", [128, 128], F32))
    P["uincl_b"] = es.enter_context(nc.sbuf_tensor("P_uincl_b", [128, 128], BF16))
    P["maskneg"] = es.enter_context(nc.sbuf_tensor("P_maskneg", [128, 128], F32))
    P["h128s_b"] = es.enter_context(nc.sbuf_tensor("P_h128s_b", [128, 128], BF16))
    P["rrot_b"] = es.enter_context(nc.sbuf_tensor("P_rrot_b", [128, 128], BF16))
    P["ones_col_b"] = es.enter_context(nc.sbuf_tensor("P_ones_col_b", [128, 1], BF16))
    P["ones_row_f"] = es.enter_context(nc.sbuf_tensor("P_ones_row_f", [1, 128], F32))
    P["rowT_b"] = es.enter_context(nc.sbuf_tensor("P_rowT_b", [1, 512], BF16))
    P["eps"] = es.enter_context(nc.sbuf_tensor("P_eps", [128, 4], F32))
    P["ovd_sb"] = es.enter_context(nc.sbuf_tensor("P_ovd_sb", [128, 128], F32))
    P["ones_col_f"] = es.enter_context(nc.sbuf_tensor("P_ones_col_f", [128, 1], F32))
    P["ones_row_b"] = es.enter_context(nc.sbuf_tensor("P_ones_row_b", [1, 128], BF16))
    P["ones_1b"] = es.enter_context(nc.sbuf_tensor("P_ones_1b", [1, 1], BF16))
    ctx.P = P

    stage_es = [None]

    def open_stage():
        stage_es[0] = ExitStack()

    def close_stage():
        stage_es[0].close()

    ctx.sb = lambda n, shp, dt: stage_es[0].enter_context(nc.sbuf_tensor(n, shp, dt))
    ctx.psb = lambda n, shp, dt=F32: stage_es[0].enter_context(nc.psum_tensor(n, shp, dt))

    open_stage()
    run_rec_stage(ctx, "prep", rec_prep(ctx, sems, P), track)
    close_stage()

    open_stage()
    st_sb = lambda n, shp, dt: stage_es[0].enter_context(nc.sbuf_tensor(n, shp, dt))
    st_ps = lambda n, shp, dt: stage_es[0].enter_context(nc.psum_tensor(n, shp, dt))
    run_stage(ctx, "import", emit_import(ctx, sems, x_in, scr["XT"], st_sb, st_ps), track)
    close_stage()

    if upto >= 1:
        if "s1a" not in skip:
            open_stage()
            run_rec_stage(ctx, "s1a", rec_s1a(ctx, sems, P), track)
            close_stage()

    if upto >= 2:
        if "s1b" not in skip:
            open_stage()
            run_rec_stage(ctx, "s1b", rec_s1b(ctx, sems, P), track)
            close_stage()

    if upto >= 3:
        if "s2" not in skip:
            open_stage()
            rec2 = rec_s2(ctx, sems, P)
            if upto >= 5 and "s4w2prep" not in skip:
                rec_s4w2prep(ctx, sems, P, rec=rec2)
            run_rec_stage(ctx, "s2", rec2, track)
            close_stage()

    if upto >= 4:
        if "s3" not in skip:
            open_stage()
            run_rec_stage(ctx, "s3", rec_s3(ctx, sems, P), track)
            close_stage()

    if upto >= 5:
        if "s4w2prep" not in skip and ("s2" in skip or upto < 3):
            open_stage()
            run_rec_stage(ctx, "s4w2prep", rec_s4w2prep(ctx, sems, P), track)
            close_stage()
        grp_es = ExitStack()
        XL2T = grp_es.enter_context(nc.sbuf_tensor("XL2T", [128, 16, NT, 512], BF16))
        if "s4ln" not in skip:
            open_stage()
            run_rec_stage(ctx, "s4ln", rec_s4ln(ctx, sems, P, XL2T), track)
            close_stage()
        if "XLD" in dbg:
            open_stage()
            run_rec_stage(ctx, "dumpxl", rec_dumpxl(ctx, sems, XL2T), track)
            close_stage()
        if "s4uv" not in skip:
            open_stage()
            run_rec_stage(ctx, "s4uv", rec_s4uv(ctx, sems, P, XL2T), track)
            close_stage()
        grp_es.close()
        if "s4w2" not in skip:
            open_stage()
            run_rec_stage(ctx, "s4w2", rec_s4w2(ctx, sems, P), track)
            close_stage()

    if upto >= 6 and "s5" not in skip:
        open_stage()
        run_rec_stage(ctx, "s5", rec_s5(ctx, sems, P), track)
        close_stage()

    if upto >= 7 and "s6" not in skip:
        open_stage()
        run_rec_stage(ctx, "s6", rec_s6(ctx, sems, P), track)
        close_stage()


    open_stage()
    st_sb = lambda n, shp, dt: stage_es[0].enter_context(nc.sbuf_tensor(n, shp, dt))
    st_ps = lambda n, shp, dt: stage_es[0].enter_context(nc.psum_tensor(n, shp, dt))
    run_stage(ctx, "export", emit_export(ctx, sems, scr["XT"], y_out, st_sb, st_ps), track)
    close_stage()

    with nc.Block() as block:
        @block.sync
        def _(s):
            s.wait_ge(ctx.s_bar.h, ctx.s_bar.n)
            s.wait_ge(sems["st"].h, sems["st"].n)
            s.wait_ge(sems["stg"].h, sems["stg"].n)
    return nc


def host_weight_maps(inputs):
    H, NOPE, ROPE, VD = 16, 128, 64, 128
    qb = np.ascontiguousarray(inputs["L0_attn_q_b_w"])
    idx_nope = np.concatenate([np.arange(h * 192, h * 192 + 128) for h in range(H)])
    idx_pe = np.concatenate([np.arange(h * 192 + 128, h * 192 + 192) for h in range(H)])
    qb_r = np.ascontiguousarray(qb[np.concatenate([idx_nope, idx_pe])])
    kvb = np.ascontiguousarray(inputs["L0_attn_kv_b_w"])
    idx_k = np.concatenate([np.arange(h * 256, h * 256 + 128) for h in range(H)])
    idx_v = np.concatenate([np.arange(h * 256 + 128, h * 256 + 256) for h in range(H)])
    kvb_r = np.ascontiguousarray(kvb[np.concatenate([idx_k, idx_v])])
    return {
        "w_qa": inputs["L0_attn_q_a_w"],
        "w_qa_ln": inputs["L0_attn_q_a_ln_w"],
        "w_qb": qb_r,
        "w_kva": inputs["L0_attn_kv_a_w"],
        "w_kva_ln": inputs["L0_attn_kv_a_ln_w"],
        "w_kvb": kvb_r,
        "w_ow": inputs["L0_attn_o_w"],
        "w_W1": inputs["L0_ffn_W1"],
        "w_W2": inputs["L0_ffn_W2"],
        "w_W3": inputs["L0_ffn_W3"],
        "w1_q": inputs["L1_attn_Wq"],
        "w1_k": inputs["L1_attn_Wk"],
        "w1_v": inputs["L1_attn_Wv"],
        "w1_g": inputs["L1_attn_Wg"],
        "w1_s": inputs["L1_attn_Ws"],
        "w1_o": inputs["L1_attn_Wo"],
        "w1_gn": inputs["L1_attn_g_norm_weight"],
        "w1_router": inputs["L1_ffn_w_router"],
        "w1_gate": inputs["L1_ffn_w_gate"],
        "w1_up": inputs["L1_ffn_w_up"],
        "w1_down": inputs["L1_ffn_w_down"],
        "w1_amps": inputs["L1_ffn_amplitudes"],
        "w1_escale": inputs["L1_ffn_expert_scale"],
        "w1_shg": inputs["L1_ffn_shared_w_gate"],
        "w1_shu": inputs["L1_ffn_shared_w_up"],
        "w1_shd": inputs["L1_ffn_shared_w_down"],
    }


def kernel(**inputs):
    x = np.ascontiguousarray(inputs["L0_attn_hidden_states"])
    nc = build_program()
    consts = host_constants()
    wmap = {k: np.ascontiguousarray(v, dtype=np.float32) for k, v in host_weight_maps(inputs).items()}
    in_maps = []
    for c in range(8):
        m = {"x_in": np.ascontiguousarray(x[c])}
        m.update(consts)
        m.update(wmap)
        in_maps.append(m)
    res = run_bass_kernel_spmd(nc, in_maps, core_ids=list(range(8)))
    out = np.stack([res.results[c]["y_out"] for c in range(8)], axis=0)
    return out


# version 05
# speedup vs baseline: 1.1572x; 1.1572x over previous
import numpy as np
from contextlib import ExitStack

import concourse.bass as bass
import concourse.mybir as mybir
from concourse.alu_op_type import AluOpType as OP
from concourse.bass_utils import run_bass_kernel_spmd

F32 = mybir.dt.float32
BF16 = mybir.dt.bfloat16
AF = mybir.ActivationFunctionType
AX = mybir.AxisListType

T = 4096
D = 2048
NT = T // 512
ENGINES = ("tensor", "vector", "scalar", "sync", "gpsimd")


def host_constants():
    c = {}
    c["c_ident"] = np.eye(128, dtype=np.float32)
    c["c_ident_bf"] = np.eye(128, dtype=np.float32)
    ROPE = 64
    inv_freq = 1.0 / (10000.0 ** (np.arange(0, ROPE, 2, dtype=np.float64) / ROPE))
    t = np.arange(T, dtype=np.float64)
    freqs = np.outer(t, inv_freq)
    emb = np.concatenate([freqs, freqs], axis=-1)
    cosT = np.cos(emb).T.astype(np.float32)
    sinT = np.sin(emb).T.astype(np.float32)
    c["c_cos2"] = np.concatenate([cosT, cosT], axis=0)
    c["c_sin2"] = np.concatenate([sinT, sinT], axis=0)
    R64 = np.zeros((64, 64), dtype=np.float32)
    for p in range(32):
        R64[p, p + 32] = -1.0
        R64[p + 32, p] = 1.0
    R128 = np.zeros((128, 128), dtype=np.float32)
    R128[:64, :64] = R64
    R128[64:, 64:] = R64
    c["c_rrot_lhsT"] = R128.T.copy()
    c["c_uincl"] = np.triu(np.ones((128, 128), dtype=np.float32))
    m = np.zeros((128, 128), dtype=np.float32)
    m[np.tril_indices(128, -1)] = -1e30
    c["c_maskneg"] = m
    H = np.array([[1.0]])
    while H.shape[0] < 128:
        H = np.block([[H, H], [H, -H]])
    c["c_h128s"] = (H * (8192.0 ** -0.5)).astype(np.float32)
    c["c_ones_col"] = np.ones((128, 1), dtype=np.float32)
    c["c_ones_row"] = np.ones((1, 128), dtype=np.float32)
    return c


class Sem:
    __slots__ = ("h", "n", "floor")

    def __init__(self, h):
        self.h = h
        self.n = 0
        self.floor = 0


class Ctx:
    def __init__(self, nc, es):
        self.nc = nc
        self.es = es
        self.waited = {}
        self.fences = {}
        self.stage_idx = 0
        self.s_bar = self.sem("s_bar")
        self.sems = {}

    def sem(self, name):
        return Sem(self.es.enter_context(self.nc.semaphore(name)))

    def inc(self, engname, instr, s, k=1):
        instr.then_inc(s.h, k)
        s.n += k
        self.fences.setdefault(engname, {})[id(s)] = (s, s.n)
        return s.n

    def wait(self, eng, engname, s, val=None):
        if val is None:
            val = s.n
        if val <= s.floor:
            return
        w = self.waited.setdefault(engname, {})
        if w.get(id(s), -1) < val:
            eng.wait_ge(s.h, val)
            w[id(s)] = val


class Rec:

    def __init__(self, ctx):
        self.ctx = ctx
        self.ops = {e: [] for e in ENGINES}
        self.waited = {e: {} for e in ENGINES}
        self.fences = {e: {} for e in ENGINES}

    def wait(self, engname, s, val=None):
        if val is None:
            val = s.n
        assert val <= s.n, "wait on a not-yet-recorded increment (deadlock)"
        if val <= s.floor:
            return
        w = self.waited[engname]
        if w.get(id(s), -1) >= val:
            return
        w[id(s)] = val
        self.ops[engname].append(lambda eng, h=s.h, v=val: eng.wait_ge(h, v))

    def add(self, engname, fn, sem=None, k=1):
        if sem is not None:
            sem.n += k
            self.fences[engname][id(sem)] = (sem, sem.n)
            def thunk(eng, fn=fn, h=sem.h, kk=k):
                fn(eng).then_inc(h, kk)
            self.ops[engname].append(thunk)
            return sem.n
        self.ops[engname].append(lambda eng, fn=fn: fn(eng))
        return None

    def mm_group(self, psum_ap, pairs, sem):
        n = len(pairs)
        def fn(eng, pairs=tuple(pairs), psum_ap=psum_ap, n=n):
            for i, (l, r) in enumerate(pairs[:-1]):
                eng.matmul(psum_ap, l, r, start=(i == 0), stop=False)
            l, r = pairs[-1]
            return eng.matmul(psum_ap, l, r, start=(n == 1), stop=True)
        return self.add("tensor", fn, sem, 1)

    def as_emitters(self):
        def mk(nm):
            def f(eng):
                for th in self.ops[nm]:
                    th(eng)
            return f
        return {nm: mk(nm) for nm in ENGINES if self.ops[nm]}


def run_rec_stage(ctx, name, rec, track_sems):
    idx = ctx.stage_idx
    base = 5 * idx
    with ctx.nc.Block() as block:
        for engname in ENGINES:
            def make(nm):
                def f(eng):
                    if idx > 0:
                        eng.wait_ge(ctx.s_bar.h, base)
                    for th in rec.ops[nm]:
                        th(eng)
                    for s, cnt in rec.fences[nm].values():
                        w = rec.waited[nm]
                        if w.get(id(s), -1) < cnt:
                            eng.wait_ge(s.h, cnt)
                            w[id(s)] = cnt
                    eng.nop().then_inc(ctx.s_bar.h, 1)
                return f
            getattr(block, engname)(make(engname))
    ctx.s_bar.n += 5
    ctx.stage_idx += 1
    for s in track_sems:
        s.floor = s.n


def run_stage(ctx, name, emitters, track_sems):
    idx = ctx.stage_idx
    base = 5 * idx
    with ctx.nc.Block() as block:
        for engname in ENGINES:
            def make(nm):
                def f(eng):
                    if idx > 0:
                        eng.wait_ge(ctx.s_bar.h, base)
                    ctx.waited[nm] = {}
                    ctx.fences[nm] = {}
                    em = emitters.get(nm)
                    if em:
                        em(eng)
                    for s, cnt in ctx.fences[nm].values():
                        ctx.wait(eng, nm, s, cnt)
                    eng.nop().then_inc(ctx.s_bar.h, 1)
                return f
            getattr(block, engname)(make(engname))
    ctx.s_bar.n += 5
    ctx.stage_idx += 1
    for s in track_sems:
        s.floor = s.n


def emit_import(ctx, sems, x_in, XT, sb, psb):
    nc = ctx.nc
    s_ld, s_t, s_sc, s_st = sems["ld"], sems["t"], sems["sc"], sems["st"]
    xa = sb("s0_xa", [128, 4 * 2048], F32)
    ob = sb("s0_ob", [128, 16 * 512], F32)
    ident = sb("s0_id", [128, 128], F32)
    pss = [psb(f"s0_ps{i}", [128, 512], F32) for i in range(4)]

    n_chunk = NT
    b_ld, b_t, b_sc, b_st = s_ld.n, s_t.n, s_sc.n, s_st.n

    def em_sync(s):
        ctx.inc("sync", s.dma_start(ident[:, :], ctx.cident), s_ld, 16)
        for c in range(n_chunk):
            if c >= 1:
                ctx.wait(s, "sync", s_t, b_t + c * 16)
            for k in range(4):
                t0 = c * 512 + k * 128
                ctx.inc("sync", s.dma_start(xa[:, k * 2048:(k + 1) * 2048],
                                            x_in[t0:t0 + 128, :]), s_ld, 16)

    def em_tensor(t):
        ctx.wait(t, "tensor", s_ld, b_ld + 16)
        for c in range(n_chunk):
            ctx.wait(t, "tensor", s_ld, b_ld + 16 + (c + 1) * 64)
            for d in range(16):
                bank = pss[d % 4]
                if c * 16 + d >= 4:
                    ctx.wait(t, "tensor", s_sc, b_sc + c * 16 + d - 3)
                for k in range(4):
                    ins = t.transpose(bank[:, k * 128:(k + 1) * 128],
                                      xa[:, k * 2048 + d * 128: k * 2048 + (d + 1) * 128],
                                      ident[:, :])
                ctx.inc("tensor", ins, s_t, 1)

    def em_scalar(sc):
        for c in range(n_chunk):
            if c >= 1:
                ctx.wait(sc, "scalar", s_st, b_st + c * 16 * 16)
            for d in range(16):
                bank = pss[d % 4]
                ctx.wait(sc, "scalar", s_t, b_t + c * 16 + d + 1)
                ctx.inc("scalar", sc.copy(ob[:, d * 512:(d + 1) * 512], bank[:, :]), s_sc, 1)
                ctx.wait(sc, "scalar", s_sc, s_sc.n)
                ctx.inc("scalar", sc.dma_start(XT[d, :, c * 512:(c + 1) * 512],
                                               ob[:, d * 512:(d + 1) * 512]), s_st, 16)

    return {"sync": em_sync, "tensor": em_tensor, "scalar": em_scalar}


def emit_export(ctx, sems, XT, y_out, sb, psb):
    s_ld, s_t, s_sc, s_st = sems["ld"], sems["t"], sems["sc"], sems["st"]
    xa = sb("s7_xa", [128, 16 * 512], F32)
    ob = sb("s7_ob", [128, 2048], F32)
    ident = sb("s7_id", [128, 128], F32)
    pss = [psb(f"s7_ps{i}", [128, 512], F32) for i in range(4)]
    b_ld, b_t, b_sc, b_st = s_ld.n, s_t.n, s_sc.n, s_st.n

    def em_sync(s):
        ctx.inc("sync", s.dma_start(ident[:, :], ctx.cident), s_ld, 16)
        for c in range(NT):
            if c >= 1:
                ctx.wait(s, "sync", s_t, b_t + c * 4)
            for d in range(16):
                ctx.inc("sync", s.dma_start(xa[:, d * 512:(d + 1) * 512],
                                            XT[d, :, c * 512:(c + 1) * 512]), s_ld, 16)

    def em_tensor(t):
        ctx.wait(t, "tensor", s_ld, b_ld + 16)
        for c in range(NT):
            ctx.wait(t, "tensor", s_ld, b_ld + 16 + (c + 1) * 256)
            for k in range(4):
                for grp in range(4):
                    bank = pss[grp]
                    if c * 4 + k >= 1:
                        ctx.wait(t, "tensor", s_sc, b_sc + (c * 4 + k - 1) * 4 + grp + 1)
                    for dd in range(4):
                        d = grp * 4 + dd
                        ins = t.transpose(bank[:, dd * 128:(dd + 1) * 128],
                                          xa[:, d * 512 + k * 128: d * 512 + k * 128 + 128],
                                          ident[:, :])
                ctx.inc("tensor", ins, s_t, 1)

    def em_scalar(sc):
        for c in range(NT):
            for k in range(4):
                i = c * 4 + k
                ctx.wait(sc, "scalar", s_t, b_t + i + 1)
                if i >= 1:
                    ctx.wait(sc, "scalar", s_st, b_st + i * 16)
                for grp in range(4):
                    bank = pss[grp]
                    ctx.inc("scalar", sc.copy(ob[:, grp * 512:(grp + 1) * 512], bank[:, :]), s_sc, 1)
                ctx.wait(sc, "scalar", s_sc, s_sc.n)
                t0 = c * 512 + k * 128
                ctx.inc("scalar", sc.dma_start(y_out[t0:t0 + 128, :], ob[:, :]), s_st, 16)

    return {"sync": em_sync, "tensor": em_tensor, "scalar": em_scalar}



def rec_prep(ctx, sems, P):
    rec = Rec(ctx)
    s_ld, s_ldg, s_g = sems["ld"], sems["ldg"], sems["g"]
    C = ctx.consts
    rec.add("sync", lambda e: e.dma_start(P["uincl_f"][:, :], C["c_uincl"]), s_ld, 16)
    rec.add("sync", lambda e: e.dma_start(P["maskneg"][:, :], C["c_maskneg"]), s_ld, 16)
    rec.add("sync", lambda e: e.dma_start(P["ones_row_f"][:, :], C["c_ones_row"]), s_ld, 16)
    rec.add("sync", lambda e: e.dma_start(P["ident_f"][:, :], C["c_ident"]), s_ld, 16)
    rec.add("sync", lambda e: e.dma_start(P["ones_col_f"][:, :], C["c_ones_col"]), s_ld, 16)
    rec.add("gpsimd", lambda e: e.dma_start(P["ident_b"][:, :], C["c_ident"]), s_ldg, 16)
    rec.add("gpsimd", lambda e: e.dma_start(P["uincl_b"][:, :], C["c_uincl"]), s_ldg, 16)
    rec.add("gpsimd", lambda e: e.dma_start(P["h128s_b"][:, :], C["c_h128s"]), s_ldg, 16)
    rec.add("gpsimd", lambda e: e.dma_start(P["rrot_b"][:, :], C["c_rrot_lhsT"]), s_ldg, 16)
    rec.add("gpsimd", lambda e: e.dma_start(P["ones_col_b"][:, :], C["c_ones_col"]), s_ldg, 16)
    W = ctx.wdram
    for src, dst in (("w_qa", "QAB"), ("w_qb", "QBB"), ("w_kva", "KVAB"),
                     ("w_kvb", "KVBB"), ("w_ow", "OWB")):
        rec.add("gpsimd", lambda e, a=ctx.wins[src], b=W[dst]: e.dma_start(b, a), s_ldg, 16)
    rec.add("gpsimd", lambda e: e.memset(P["ones_row_b"][:, :], 1.0), s_g, 1)
    rec.add("gpsimd", lambda e: e.memset(P["ones_1b"][:, :], 1.0), s_g, 1)
    rec.add("gpsimd", lambda e: e.memset(P["eps"][:, 0:1], 1e-5), s_g, 1)
    rec.add("gpsimd", lambda e: e.memset(P["eps"][:, 1:2], 1e-6), s_g, 1)
    rec.add("gpsimd", lambda e: e.memset(P["eps"][:, 2:3], 1e-8), s_g, 1)
    rec.add("gpsimd", lambda e: e.memset(P["eps"][:, 3:4], 1e-30), s_g, 1)
    ones_row_T = P["rowT_b"]
    rec.add("gpsimd", lambda e: e.memset(ones_row_T[:, :], -1.0), s_g, 1)
    rec.wait("gpsimd", s_g)
    for cc in range(NT):
        rec.add("gpsimd", lambda e, cc=cc: e.dma_start(
            ctx.scr["KPEA"][64:65, cc * 512:(cc + 1) * 512], ones_row_T[:, :]), s_ldg, 16)
    return rec


def rec_ln_chunk(rec, sems, P, bufs, c, n_feat_tiles, inv_n, eps_col):
    ctx = rec.ctx
    s_t, s_v, s_sc, s_g = sems["t"], sems["v"], sems["sc"], sems["g"]
    xt, xbf, sqbf = bufs["xt"], bufs["xbf"], bufs["sqbf"]
    xln_at = bufs.get("xln_at") or (lambda d: bufs["xln"][:, d, :])
    tmpf, rows, ab = bufs["tmpf"], bufs["rows"], bufs["ab"]
    ps_row0, ps_row1 = bufs["ps_row0"], bufs["ps_row1"]
    ps_bc0, ps_bc1 = bufs["ps_bc0"], bufs["ps_bc1"]
    nf = n_feat_tiles
    v_cast = []
    sc_sq = []
    for d in range(nf):
        if xbf is not None:
            rec.add("vector", lambda e, d=d: e.tensor_copy(xbf[:, d, :], xt[:, d, :]), s_v, 1)
            v_cast.append(s_v.n)
        rec.add("scalar", lambda e, d=d: e.activation(sqbf[:, d, :], xt[:, d, :], AF.Square), s_sc, 1)
        sc_sq.append(s_sc.n)
    xt_ready = bufs.get("xt_ready")
    if xt_ready:
        rec.wait("tensor", sems[xt_ready[0]], xt_ready[1])
    if xbf is not None:
        rec.wait("tensor", s_v, v_cast[-1])
        rec.mm_group(ps_row0[0:1, :], [(P["ones_col_b"][:, :], xbf[:, d, :]) for d in range(nf)], s_t)
    else:
        rec.mm_group(ps_row0[0:1, :], [(P["ones_col_f"][:, :], xt[:, d, :]) for d in range(nf)], s_t)
    t_sum = s_t.n
    rec.wait("tensor", s_sc, sc_sq[-1])
    rec.mm_group(ps_row1[0:1, :], [(P["ones_col_b"][:, :], sqbf[:, d, :]) for d in range(nf)], s_t)
    t_ssq = s_t.n
    mu = rows[:, 0:512]; msq = rows[:, 512:1024]; var = rows[:, 1024:1536]
    sd = rows[:, 1536:2048]; rstd = rows[:, 512:1024]; Br = rows[:, 1024:1536]
    rec.wait("vector", s_t, t_sum)
    rec.add("vector", lambda e: e.tensor_scalar(mu, ps_row0[0:1, :], inv_n, None, OP.mult), s_v, 1)
    rec.wait("vector", s_t, t_ssq)
    rec.add("vector", lambda e: e.tensor_scalar(msq, ps_row1[0:1, :], inv_n, None, OP.mult), s_v, 1)
    rec.wait("vector", s_v)
    rec.add("vector", lambda e: e.tensor_tensor(var, mu, mu, OP.mult), s_v, 1)
    rec.wait("vector", s_v)
    rec.add("vector", lambda e: e.tensor_tensor(var, msq, var, OP.subtract), s_v, 1)
    rec.wait("scalar", s_v)
    rec.add("scalar", lambda e: e.activation(sd, var, AF.Sqrt, bias=P["eps"][0:1, eps_col:eps_col + 1]), s_sc, 1)
    rec.wait("vector", s_sc)
    rec.add("vector", lambda e: e.reciprocal(rstd, sd), s_v, 1)
    rec.wait("vector", s_v)
    rec.add("vector", lambda e: e.scalar_tensor_tensor(Br, mu, -1.0, rstd, OP.mult, OP.mult), s_v, 1)
    v_stats = s_v.n
    rec.wait("tensor", s_v, v_stats)
    rec.add("tensor", lambda e: e.matmul(ps_bc0[:, :], P["ones_row_f"][:, :], rstd, start=True, stop=True), s_t, 1)
    rec.add("tensor", lambda e: e.matmul(ps_bc1[:, :], P["ones_row_f"][:, :], Br, start=True, stop=True), s_t, 1)
    t_bc = s_t.n
    A_b = ab[:, 0:512]; B_b = ab[:, 512:1024]
    rec.wait("scalar", s_t, t_bc)
    ab_guard = bufs.get("ab_guard")
    if ab_guard:
        rec.wait("scalar", s_v, ab_guard[0])
        rec.wait("scalar", s_g, ab_guard[1])
    rec.add("scalar", lambda e: e.copy(A_b, ps_bc0[:, :]), s_sc, 1)
    rec.add("scalar", lambda e: e.copy(B_b, ps_bc1[:, :]), s_sc, 1)
    sc_ab = s_sc.n
    gdone = []
    for d in range(nf):
        rec.wait("vector", s_sc, sc_ab)
        if d >= 2:
            rec.wait("vector", s_g, gdone[d - 2])
        rec.add("vector", lambda e, d=d: e.tensor_tensor(tmpf[:, d % 2, :], xt[:, d, :], A_b, OP.mult), s_v, 1)
        rec.wait("gpsimd", s_v)
        rec.wait("gpsimd", s_sc, sc_ab)
        rec.wait("gpsimd", s_t, t_sum)
        rec.add("gpsimd", lambda e, d=d: e.tensor_tensor(xln_at(d), tmpf[:, d % 2, :], B_b, OP.add), s_g, 1)
        gdone.append(s_g.n)
    return {
        "xt_free": (("v", s_v.n), ("sc", sc_sq[-1])) + ((("t", t_sum),) if xbf is None else ()),
        "xbf_free_t": t_sum, "sqbf_free_t": t_ssq,
        "psrow_free_v": s_v.n, "psbc_free_sc": sc_ab,
        "ab_free": (s_v.n, gdone[-1]),
        "xln_done_g": gdone[-1], "xln_done_list": gdone,
    }



def rec_s1a(ctx, sems, P):
    rec = Rec(ctx)
    nc = ctx.nc
    s_ld, s_t, s_v, s_sc, s_g, s_st, s_stg = (sems[k] for k in ("ld", "t", "v", "sc", "g", "st", "stg"))
    sb, psb = ctx.sb, ctx.psb
    scr, W = ctx.scr, ctx.wdram

    qa_wT = sb("qa_wT", [128, 16, 1536], BF16)
    kva_wT = sb("kva_wT", [128, 16, 576], BF16)
    wq_cols = sb("wq_cols", [128, 12], F32)
    wkv_cols = sb("wkv_cols", [128, 4], F32)
    xt = sb("xt", [128, 16, 512], F32)
    xbf = sb("xbf", [128, 16, 512], BF16)
    sqbf = sb("sqbf", [128, 16, 512], BF16)
    xln = xbf
    tmpf = sb("tmpf", [128, 2, 512], F32)
    rows = sb("rows", [1, 12 * 512], F32)
    ab = sb("ab", [128, 2 * 512], F32)
    qa_bf = sb("qa_bf", [128, 12, 512], BF16)
    qsq_bf = sqbf[:, 0:12, :]
    qar_bf = sb("qar_bf", [128, 2, 512], BF16)
    rq_b = sb("rq_b", [128, 512], F32)
    ckv_bf = sb("ckv_bf", [128, 4, 512], BF16)
    ckvsq_bf = sqbf[:, 12:16, :]
    ckvr_bf = sb("ckvr_bf", [128, 2, 512], BF16)
    rkv_b = sb("rkv_b", [128, 512], F32)
    cosk = sb("cosk", [64, 512], F32)
    sink = sb("sink", [64, 512], F32)
    kraw_bf = sb("kraw_bf", [64, 512], BF16)
    krope_t = sb("krope_t", [64, 2 * 512], F32)
    kpe_bf = sb("kpe_bf", [64, 512], BF16)

    ps_q = [psb("ps_q0", [128, 512], F32), psb("ps_q1", [128, 512], F32)]
    ps_row0 = psb("ps_row0", [128, 512], F32)
    ps_row1 = psb("ps_row1", [128, 512], F32)
    ps_bc0 = psb("ps_bc0", [128, 512], F32)
    ps_bc1 = psb("ps_bc1", [128, 512], F32)
    ps_rot = psb("ps_rot", [128, 512], F32)

    for d in range(16):
        for rch in range(3):
            rec.add("sync", lambda e, d=d, r=rch: e.dma_start_transpose(
                qa_wT[:, d, r * 512:(r + 1) * 512],
                W["QAB"][r * 512:(r + 1) * 512, d * 128:(d + 1) * 128]), s_ld, 16)
        rec.add("sync", lambda e, d=d: e.dma_start_transpose(
            kva_wT[:, d, 0:512], W["KVAB"][0:512, d * 128:(d + 1) * 128]), s_ld, 16)
        rec.add("sync", lambda e, d=d: e.dma_start_transpose(
            kva_wT[:, d, 448:576], W["KVAB"][448:576, d * 128:(d + 1) * 128]), s_ld, 16)
    wqln = ctx.wins["w_qa_ln"].rearrange("(j p b) -> j p b", p=128, b=1)
    for j in range(12):
        rec.add("sync", lambda e, j=j: e.dma_start(wq_cols[:, j:j + 1], wqln[j]), s_ld, 16)
    wkln = ctx.wins["w_kva_ln"].rearrange("(j p b) -> j p b", p=128, b=1)
    for j in range(4):
        rec.add("sync", lambda e, j=j: e.dma_start(wkv_cols[:, j:j + 1], wkln[j]), s_ld, 16)
    w_ready = s_ld.n

    for src, dst in (("w_W1", "W1B"), ("w_W3", "W3B"), ("w1_gate", "GATB"),
                     ("w1_up", "UPB"), ("w1_shg", "SHGB"), ("w1_shu", "SHUB"),
                     ("w1_shd", "SHDB"), ("w1_down", "DWNB"), ("w1_router", "RTRB")):
        rec.add("gpsimd", lambda e, a=ctx.wins[src], b=W[dst]: e.dma_start(b, a), sems["ldg"], 16)

    prev = {}
    for c in range(NT):
        c0 = c * 512
        if prev:
            for nm, val in prev["xt_free"]:
                rec.wait("sync", sems[nm], val)
        ld_tiles = []
        for d in range(16):
            rec.add("sync", lambda e, d=d, c0=c0: e.dma_start(
                xt[:, d, :], scr["XT"][d, :, c0:c0 + 512]), s_ld, 16)
            ld_tiles.append(s_ld.n)
        if prev:
            rec.wait("sync", s_v, prev["cos_free_v"])
        rec.add("sync", lambda e, c0=c0: e.dma_start(cosk[:, :], ctx.consts["c_cos2"][0:64, c0:c0 + 512]), s_ld, 16)
        rec.add("sync", lambda e, c0=c0: e.dma_start(sink[:, :], ctx.consts["c_sin2"][0:64, c0:c0 + 512]), s_ld, 16)
        ld_cos = s_ld.n

        for e_ in ("vector", "scalar"):
            rec.wait(e_, s_ld, ld_tiles[-1])
        if prev:
            rec.wait("vector", s_t, prev["xln_free_t"])
            rec.wait("scalar", s_t, prev["xln_free_t"])
            rec.wait("tensor", s_v, prev["psrow_free_v"])
            rec.wait("tensor", s_sc, prev["psbc_free_sc"])
            rec.wait("gpsimd", s_t, prev["xln_free_t"])
        lnc = rec_ln_chunk(rec, sems, P, {
            "xt": xt, "xbf": xbf, "sqbf": sqbf, "xln": xln, "tmpf": tmpf,
            "rows": rows, "ab": ab, "ab_guard": prev.get("ab_free"),
            "ps_row0": ps_row0, "ps_row1": ps_row1,
            "ps_bc0": ps_bc0, "ps_bc1": ps_bc1}, c, 16, 1.0 / 2048.0, 0)

        rec.wait("tensor", s_ld, w_ready)
        rec.wait("tensor", s_g, lnc["xln_done_g"])
        evac_sc = {}
        t_mm = {}
        for rc in range(12):
            bank = ps_q[rc % 2]
            if rc >= 2:
                rec.wait("tensor", s_sc, evac_sc[rc - 2])
            elif prev:
                rec.wait("tensor", s_sc, prev["psq_free_sc"][rc % 2])
            rec.mm_group(bank[:, :], [(qa_wT[:, d, rc * 128:(rc + 1) * 128], xln[:, d, :])
                                      for d in range(16)], s_t)
            t_mm[rc] = s_t.n
            rec.wait("scalar", s_t, t_mm[rc])
            if prev and rc == 0:
                rec.wait("scalar", s_v, prev["qabf_free_v"])
                rec.wait("scalar", s_t, prev["qsq_free_t"])
            rec.wait("scalar", s_t, lnc["sqbf_free_t"])
            rec.add("scalar", lambda e, rc=rc, b=bank: e.copy(qa_bf[:, rc, :], b[:, :]), s_sc, 1)
            rec.add("scalar", lambda e, rc=rc, b=bank: e.activation(qsq_bf[:, rc, :], b[:, :], AF.Square), s_sc, 1)
            evac_sc[rc] = s_sc.n
        rec.wait("tensor", s_sc, evac_sc[11])
        rec.wait("tensor", s_v, lnc["psrow_free_v"])
        rec.mm_group(ps_row1[0:1, :], [(P["ones_col_b"][:, :], qsq_bf[:, rc, :]) for rc in range(12)], s_t)
        t_qssq = s_t.n
        qsq_free_t = s_t.n
        mq = rows[:, 3072:3584]; sdq = rows[:, 3584:4096]; rq = rows[:, 4096:4608]
        rec.wait("vector", s_t, t_qssq)
        rec.add("vector", lambda e: e.tensor_scalar(mq, ps_row1[0:1, :], 1.0 / 1536.0, None, OP.mult), s_v, 1)
        rec.wait("scalar", s_v)
        rec.add("scalar", lambda e: e.activation(sdq, mq, AF.Sqrt, bias=P["eps"][0:1, 1:2]), s_sc, 1)
        rec.wait("vector", s_sc)
        rec.add("vector", lambda e: e.reciprocal(rq, sdq), s_v, 1)
        v_rq = s_v.n
        rec.wait("tensor", s_v, v_rq)
        rec.wait("tensor", s_sc, lnc["psbc_free_sc"])
        rec.add("tensor", lambda e: e.matmul(ps_bc0[:, :], P["ones_row_f"][:, :], rq, start=True, stop=True), s_t, 1)
        t_bcq = s_t.n
        rec.wait("scalar", s_t, t_bcq)
        if prev:
            rec.wait("scalar", s_v, prev["rqb_free_v"])
        rec.add("scalar", lambda e: e.copy(rq_b[:, :], ps_bc0[:, :]), s_sc, 1)
        sc_rqb = s_sc.n
        v_qar = {}
        st_qar = {}
        for rc in range(12):
            rec.wait("vector", s_sc, sc_rqb)
            rec.wait("vector", s_sc, evac_sc[rc])
            rec.wait("vector", s_ld, w_ready)
            if rc >= 2:
                rec.wait("vector", s_stg, st_qar[rc - 2])
            elif prev:
                rec.wait("vector", s_stg, prev["qar_store"])
            rec.add("vector", lambda e, rc=rc: e.scalar_tensor_tensor(
                qar_bf[:, rc % 2, :], qa_bf[:, rc, :], wq_cols[:, rc:rc + 1], rq_b[:, :],
                OP.mult, OP.mult), s_v, 1)
            v_qar[rc] = s_v.n
            rec.wait("gpsimd", s_v, v_qar[rc])
            rec.add("gpsimd", lambda e, rc=rc, c0=c0: e.dma_start(
                scr["QAR"][rc, :, c0:c0 + 512], qar_bf[:, rc % 2, :]), s_stg, 16)
            st_qar[rc] = s_stg.n
        qar_store = s_stg.n

        kv_mm = {}
        kv_evac = {}
        for rc in range(4):
            bank = ps_q[rc % 2]
            rec.wait("tensor", s_sc, evac_sc[10 + rc] if rc < 2 else kv_evac[rc - 2])
            rec.mm_group(bank[:, :], [(kva_wT[:, d, rc * 128:(rc + 1) * 128], xln[:, d, :])
                                      for d in range(16)], s_t)
            kv_mm[rc] = s_t.n
            rec.wait("scalar", s_t, kv_mm[rc])
            if prev and rc == 0:
                rec.wait("scalar", s_v, prev["ckv_free_v"])
                rec.wait("scalar", s_t, prev["ckvsq_free_t"])
            rec.add("scalar", lambda e, rc=rc, b=bank: e.copy(ckv_bf[:, rc, :], b[:, :]), s_sc, 1)
            rec.add("scalar", lambda e, rc=rc, b=bank: e.activation(ckvsq_bf[:, rc, :], b[:, :], AF.Square), s_sc, 1)
            kv_evac[rc] = s_sc.n
        rec.wait("tensor", s_sc, kv_evac[3])
        rec.wait("vector", s_t, t_qssq)
        rec.wait("tensor", s_v, v_rq)
        rec.mm_group(ps_row1[0:1, :], [(P["ones_col_b"][:, :], ckvsq_bf[:, rc, :]) for rc in range(4)], s_t)
        t_kssq = s_t.n
        ckvsq_free_t = s_t.n
        mkv = rows[:, 4608:5120]; sdkv = rows[:, 5120:5632]; rkv = rows[:, 5632:6144]
        rec.wait("vector", s_t, t_kssq)
        rec.add("vector", lambda e: e.tensor_scalar(mkv, ps_row1[0:1, :], 1.0 / 512.0, None, OP.mult), s_v, 1)
        rec.wait("scalar", s_v)
        rec.add("scalar", lambda e: e.activation(sdkv, mkv, AF.Sqrt, bias=P["eps"][0:1, 1:2]), s_sc, 1)
        rec.wait("vector", s_sc)
        rec.add("vector", lambda e: e.reciprocal(rkv, sdkv), s_v, 1)
        v_rkv = s_v.n
        rec.wait("tensor", s_v, v_rkv)
        rec.wait("tensor", s_sc, sc_rqb)
        rec.add("tensor", lambda e: e.matmul(ps_bc0[:, :], P["ones_row_f"][:, :], rkv, start=True, stop=True), s_t, 1)
        t_bckv = s_t.n
        rec.wait("scalar", s_t, t_bckv)
        if prev:
            rec.wait("scalar", s_v, prev["rkvb_free_v"])
        rec.add("scalar", lambda e: e.copy(rkv_b[:, :], ps_bc0[:, :]), s_sc, 1)
        sc_rkvb = s_sc.n
        v_ckvr = {}
        st_ckvr = {}
        for rc in range(4):
            rec.wait("vector", s_sc, sc_rkvb)
            rec.wait("vector", s_sc, kv_evac[rc])
            if rc >= 2:
                rec.wait("vector", s_stg, st_ckvr[rc - 2])
            elif prev:
                rec.wait("vector", s_stg, prev["ckvr_store"])
            rec.add("vector", lambda e, rc=rc: e.scalar_tensor_tensor(
                ckvr_bf[:, rc % 2, :], ckv_bf[:, rc, :], wkv_cols[:, rc:rc + 1], rkv_b[:, :],
                OP.mult, OP.mult), s_v, 1)
            v_ckvr[rc] = s_v.n
            rec.wait("gpsimd", s_v, v_ckvr[rc])
            rec.add("gpsimd", lambda e, rc=rc, c0=c0: e.dma_start(
                scr["CKVR"][rc, :, c0:c0 + 512], ckvr_bf[:, rc % 2, :]), s_stg, 16)
            st_ckvr[rc] = s_stg.n
        ckvr_store = s_stg.n

        bank = ps_q[0]
        rec.wait("tensor", s_sc, kv_evac[2])
        rec.mm_group(bank[0:64, :], [(kva_wT[:, d, 512:576], xln[:, d, :]) for d in range(16)], s_t)
        t_kpe = s_t.n
        rec.wait("scalar", s_t, t_kpe)
        if prev:
            rec.wait("scalar", s_t, prev["kraw_free_t"])
            rec.wait("scalar", s_v, prev["kraw_free_v"])
        rec.add("scalar", lambda e, b=bank: e.copy(kraw_bf[:, :], b[0:64, :]), s_sc, 1)
        sc_kraw = s_sc.n
        rec.wait("tensor", s_sc, sc_kraw)
        if prev:
            rec.wait("tensor", s_v, prev["psrot_free_v"])
        rec.add("tensor", lambda e, b=bank: e.matmul(ps_rot[0:64, :], P["rrot_b"][0:64, 0:64],
                                                     kraw_bf[:, :], start=True, stop=True), s_t, 1)
        t_rot = s_t.n
        kraw_free_t = s_t.n
        tmpc = krope_t[:, 0:512]; tmps = krope_t[:, 512:1024]
        rec.wait("vector", s_sc, sc_kraw)
        rec.wait("vector", s_ld, ld_cos)
        rec.add("vector", lambda e: e.tensor_tensor(tmpc, kraw_bf[:, :], cosk[:, :], OP.mult), s_v, 1)
        kraw_free_v = s_v.n
        rec.wait("vector", s_t, t_rot)
        rec.add("vector", lambda e: e.tensor_tensor(tmps, ps_rot[0:64, :], sink[:, :], OP.mult), s_v, 1)
        psrot_free_v = s_v.n
        cos_free_v = s_v.n
        rec.wait("vector", s_v)
        if prev:
            rec.wait("vector", s_stg, prev["kpe_store"])
        rec.add("vector", lambda e: e.tensor_tensor(kpe_bf[:, :], tmpc, tmps, OP.add), s_v, 1)
        v_kpe = s_v.n
        rec.wait("gpsimd", s_v, v_kpe)
        rec.add("gpsimd", lambda e, c0=c0: e.dma_start(scr["KPEA"][0:64, c0:c0 + 512], kpe_bf[:, :]), s_stg, 16)
        kpe_store = s_stg.n

        prev = {
            "xt_free": lnc["xt_free"],
            "ab_free": lnc["ab_free"],
            "xbf_sq_free_t": max(lnc["xbf_free_t"], lnc["sqbf_free_t"]),
            "psrow_free_v": s_v.n,
            "psbc_free_sc": sc_rkvb,
            "xln_free_g": lnc["xln_done_g"],
            "xln_free_t": t_kpe,
            "psq_free_sc": {0: sc_kraw, 1: kv_evac[3]},
            "qabf_free_v": v_qar[11],
            "qsq_free_t": qsq_free_t,
            "rqb_free_v": v_qar[11],
            "qar_store": qar_store,
            "ckv_free_v": v_ckvr[3],
            "ckvsq_free_t": ckvsq_free_t,
            "rkvb_free_v": v_ckvr[3],
            "ckvr_store": ckvr_store,
            "kraw_free_t": kraw_free_t,
            "kraw_free_v": kraw_free_v,
            "psrot_free_v": psrot_free_v,
            "cos_free_v": cos_free_v,
            "kpe_store": kpe_store,
        }
    return rec



def rec_s1b(ctx, sems, P):
    rec = Rec(ctx)
    s_ld, s_t, s_v, s_sc, s_g, s_st, s_stg = (sems[k] for k in ("ld", "t", "v", "sc", "g", "st", "stg"))
    sb, psb = ctx.sb, ctx.psb
    scr, W = ctx.scr, ctx.wdram
    SCALE = float(192.0 ** -0.5)

    qb_wT = sb("qb_wT", [128, 12, 3072], BF16)
    kvb_wT = sb("kvb_wT", [128, 4, 4096], BF16)
    qar = sb("qar", [128, 12, 512], BF16)
    ckvr = sb("ckvr", [128, 4, 512], BF16)
    cosq = sb("cosq", [128, 512], F32)
    sinq = sb("sinq", [128, 512], F32)
    ob = sb("ob", [128, 2, 512], BF16)
    qpraw = sb("qpraw", [128, 512], BF16)
    ropet = sb("ropet", [128, 2 * 512], F32)
    qpe_bf = sb("qpe_bf", [128, 512], BF16)
    vb = sb("vb", [128, 2, 512], BF16)

    banks = [psb("s1b_ps0", [128, 512], F32), psb("s1b_ps1", [128, 512], F32)]
    ps_rot = psb("s1b_rot", [128, 512], F32)

    for rc in range(12):
        for og in range(6):
            rec.add("sync", lambda e, rc=rc, og=og: e.dma_start_transpose(
                qb_wT[:, rc, og * 512:(og + 1) * 512],
                W["QBB"][og * 512:(og + 1) * 512, rc * 128:(rc + 1) * 128]), s_ld, 16)
    for rr in range(4):
        for og in range(8):
            rec.add("sync", lambda e, rr=rr, og=og: e.dma_start_transpose(
                kvb_wT[:, rr, og * 512:(og + 1) * 512],
                W["KVBB"][og * 512:(og + 1) * 512, rr * 128:(rr + 1) * 128]), s_ld, 16)
    w_ready = s_ld.n
    for src, dst in (("w1_q", "Q1B"), ("w1_k", "K1B"), ("w1_v", "V1B"),
                     ("w1_g", "G1B"), ("w1_s", "S1B"), ("w1_o", "O1B")):
        rec.add("gpsimd", lambda e, a=ctx.wins[src], b=W[dst]: e.dma_start(b, a), sems["ldg"], 16)

    prev = {}
    for c in range(NT):
        c0 = c * 512
        if prev:
            rec.wait("sync", s_t, prev["qar_free_t"])
        for rc in range(12):
            rec.add("sync", lambda e, rc=rc, c0=c0: e.dma_start(
                qar[:, rc, :], scr["QAR"][rc, :, c0:c0 + 512]), s_ld, 16)
        for rr in range(4):
            rec.add("sync", lambda e, rr=rr, c0=c0: e.dma_start(
                ckvr[:, rr, :], scr["CKVR"][rr, :, c0:c0 + 512]), s_ld, 16)
        ld_act = s_ld.n
        if prev:
            rec.wait("sync", s_v, prev["cos_free_v"])
        rec.add("sync", lambda e, c0=c0: e.dma_start(cosq[:, :], ctx.consts["c_cos2"][:, c0:c0 + 512]), s_ld, 16)
        rec.add("sync", lambda e, c0=c0: e.dma_start(sinq[:, :], ctx.consts["c_sin2"][:, c0:c0 + 512]), s_ld, 16)
        ld_cos = s_ld.n

        rec.wait("tensor", s_ld, w_ready)
        rec.wait("tensor", s_ld, ld_act)

        evac = {}
        store = {}
        slot_writer = {}

        def qk_block(kind, oc, pairs, scale):
            i = len(evac)
            bank = banks[i % 2]
            if i >= 2:
                rec.wait("tensor", s_sc, evac[i - 2])
            elif prev:
                nm, cnt = prev["bank_free"][i % 2]
                rec.wait("tensor", sems[nm], cnt)
            rec.mm_group(bank[:, :], pairs, s_t)
            tmm = s_t.n
            rec.wait("scalar", s_t, tmm)
            if i >= 2:
                rec.wait("scalar", s_stg, store[i - 2])
            elif prev:
                rec.wait("scalar", s_stg, prev["ob_store"])
            sl = i % 2
            if scale is None:
                rec.add("scalar", lambda e, sl=sl, b=bank: e.copy(ob[:, sl, :], b[:, :]), s_sc, 1)
            else:
                rec.add("scalar", lambda e, sl=sl, b=bank: e.mul(ob[:, sl, :], b[:, :], scale), s_sc, 1)
            evac[i] = s_sc.n
            rec.wait("gpsimd", s_sc, evac[i])
            dst = kind
            rec.add("gpsimd", lambda e, dst=dst, oc=oc, sl=sl, c0=c0: e.dma_start(
                scr[dst][oc, :, c0:c0 + 512], ob[:, sl, :]), s_stg, 16)
            store[i] = s_stg.n
            return tmm

        for oc in range(16):
            qk_block("QNT", oc, [(qb_wT[:, rc, oc * 128:(oc + 1) * 128], qar[:, rc, :])
                                 for rc in range(12)], SCALE)
        t_qar_last = None
        for oc in range(16):
            qk_block("KNT", oc, [(kvb_wT[:, rr, oc * 128:(oc + 1) * 128], ckvr[:, rr, :])
                                 for rr in range(4)], None)
        t_qar_free = 0
        rope_store = prev.get("qpe_store", 0)
        for oc in range(8):
            i = len(evac)
            bank = banks[i % 2]
            rec.wait("tensor", s_sc, evac[i - 2])
            rec.mm_group(bank[:, :], [(qb_wT[:, rc, 2048 + oc * 128: 2048 + (oc + 1) * 128], qar[:, rc, :])
                                      for rc in range(12)], s_t)
            t_qar_free = s_t.n
            tmm = s_t.n
            evac[i] = evac[i - 1]
            rec.wait("scalar", s_t, tmm)
            if oc >= 1:
                rec.wait("scalar", s_t, rot_done)
                rec.wait("scalar", s_v, v_mixc)
            elif prev:
                rec.wait("scalar", s_t, prev["qpraw_free_t"])
                rec.wait("scalar", s_v, prev["qpraw_free_v"])
            rec.add("scalar", lambda e, b=bank: e.mul(qpraw[:, :], b[:, :], SCALE), s_sc, 1)
            sc_qp = s_sc.n
            evac[i] = sc_qp
            rec.wait("tensor", s_sc, sc_qp)
            if oc >= 1:
                rec.wait("tensor", s_v, v_mixs)
            elif prev:
                rec.wait("tensor", s_v, prev["psrot_free_v"])
            rec.add("tensor", lambda e: e.matmul(ps_rot[:, :], P["rrot_b"][:, :], qpraw[:, :],
                                                 start=True, stop=True), s_t, 1)
            rot_done = s_t.n
            tmpc = ropet[:, 0:512]; tmps = ropet[:, 512:1024]
            rec.wait("vector", s_sc, sc_qp)
            rec.wait("vector", s_ld, ld_cos)
            if oc >= 1:
                rec.wait("vector", s_v, v_qpe)
            rec.add("vector", lambda e: e.tensor_tensor(tmpc, qpraw[:, :], cosq[:, :], OP.mult), s_v, 1)
            v_mixc = s_v.n
            rec.wait("vector", s_t, rot_done)
            rec.add("vector", lambda e: e.tensor_tensor(tmps, ps_rot[:, :], sinq[:, :], OP.mult), s_v, 1)
            v_mixs = s_v.n
            rec.wait("vector", s_v)
            if oc >= 1:
                rec.wait("vector", s_stg, qpe_st)
            elif prev:
                rec.wait("vector", s_stg, prev["qpe_store"])
            rec.add("vector", lambda e: e.tensor_tensor(qpe_bf[:, :], tmpc, tmps, OP.add), s_v, 1)
            v_qpe = s_v.n
            rec.wait("gpsimd", s_v, v_qpe)
            rec.add("gpsimd", lambda e, oc=oc, c0=c0: e.dma_start(
                scr["QPE"][oc, :, c0:c0 + 512], qpe_bf[:, :]), s_stg, 16)
            qpe_st = s_stg.n

        n_v = 0
        v_evac = {}
        v_store = {}
        for sub in range(4):
            for vc in range(4):
                i = n_v; n_v += 1
                bank = banks[i % 2]
                if i >= 2:
                    rec.wait("tensor", s_v, v_evac[i - 2])
                else:
                    rec.wait("tensor", s_sc, evac[len(evac) - 2 + i])
                rec.mm_group(bank[:, :], [(ckvr[:, rr, sub * 128:(sub + 1) * 128],
                                           kvb_wT[:, rr, 2048 + vc * 512: 2048 + (vc + 1) * 512])
                                          for rr in range(4)], s_t)
                tmm = s_t.n
                t_v_last = s_t.n
                rec.wait("vector", s_t, tmm)
                if i >= 2:
                    rec.wait("vector", s_stg, v_store[i - 2])
                elif prev:
                    rec.wait("vector", s_stg, prev["v_store"])
                sl = i % 2
                rec.add("vector", lambda e, sl=sl, b=bank: e.tensor_copy(vb[:, sl, :], b[:, :]), s_v, 1)
                v_evac[i] = s_v.n
                rec.wait("gpsimd", s_v, v_evac[i])
                rec.add("gpsimd", lambda e, sub=sub, vc=vc, sl=sl, c0=c0: e.dma_start(
                    scr["VNT"][c0 + sub * 128: c0 + (sub + 1) * 128, vc * 512:(vc + 1) * 512],
                    vb[:, sl, :]), s_stg, 16)
                v_store[i] = s_stg.n

        prev = {
            "qar_free_t": t_v_last,
            "cos_free_v": v_mixs,
            "bank_free": {0: ("v", v_evac[14]), 1: ("v", v_evac[15])},
            "ob_store": store[len(store) - 1] if store else 0,
            "qpraw_free_t": rot_done,
            "qpraw_free_v": v_mixc,
            "psrot_free_v": v_mixs,
            "qpe_store": qpe_st,
            "v_store": v_store[15],
        }
    return rec



def rec_s2(ctx, sems, P):
    rec = Rec(ctx)
    s_ld, s_t, s_v, s_sc, s_g, s_st, s_stg = (sems[k] for k in ("ld", "t", "v", "sc", "g", "st", "stg"))
    sb, psb = ctx.sb, ctx.psb
    scr = ctx.scr

    qn = sb("s2_qn", [128, T], BF16)
    qpa = sb("s2_qpa", [65, T], BF16)
    kn = sb("s2_kn", [128, T], BF16)
    kpa = sb("s2_kpa", [65, T], BF16)
    vn = sb("s2_vn", [128, T], BF16)
    sq = sb("s2_sq", [128, T], BF16)
    sqp = sb("s2_sqp", [64, T], BF16)
    pt = sb("s2_pt", [128, 3, 512], BF16)
    rb = sb("s2_rb", [128, 512], F32)
    ao = sb("s2_ao", [128, 512], BF16)
    kmx = sb("s2_kmx", [1, 4], F32)
    rr_f = sb("s2_rr", [1, 512], F32)

    ps_st = [psb("s2_st0", [128, 512], F32), psb("s2_st1", [128, 512], F32)]
    ps_attn = psb("s2_attn", [128, 512], F32)
    ps_r = psb("s2_r", [128, 512], F32)
    ps_rb = psb("s2_psrb", [128, 512], F32)

    free = {}

    def guard(engname, res):
        if res in free:
            nm, cnt = free[res]
            rec.wait(engname, sems[nm], cnt)

    def setfree(res, nm, cnt):
        free[res] = (nm, cnt)

    rec.add("sync", lambda e: e.dma_start(kpa[:, :], scr["KPEA"][0:65, :]), s_ld, 16)

    for h in range(16):
        guard("sync", "qk")
        rec.add("sync", lambda e, h=h: e.dma_start(qn[:, :], scr["QNT"][h]), s_ld, 16)
        rec.add("sync", lambda e, h=h: e.dma_start(kn[:, :], scr["KNT"][h]), s_ld, 16)
        rec.add("sync", lambda e, h=h: e.dma_start(
            qpa[0:64, :], scr["QPE"][h // 2, 64 * (h % 2):64 * (h % 2) + 64, :]), s_ld, 16)
        vsrc = scr["VNT"].rearrange("(b p) (g v) -> p b g v", p=128, v=128)
        rec.add("sync", lambda e, h=h, vs=vsrc: e.dma_start(
            vn.rearrange("p (b v) -> p b v", v=128), vs[:, :, h, :]), s_ld, 16)
        ld_head = s_ld.n

        rec.wait("scalar", s_ld, ld_head)
        guard("scalar", "sq")
        rec.add("scalar", lambda e: e.activation(sq[:, :], kn[:, :], AF.Square), s_sc, 1)
        rec.add("scalar", lambda e: e.activation(sqp[:, :], kpa[0:64, :], AF.Square), s_sc, 1)
        sc_sqk = s_sc.n
        rec.wait("tensor", s_sc, sc_sqk)
        for j in range(NT):
            bank = ps_st[j % 2]
            guard("tensor", f"st{j % 2}")
            rec.mm_group(bank[0:1, :],
                         [(P["ones_col_b"][:, :], sq[:, j * 512:(j + 1) * 512]),
                          (P["ones_col_b"][0:64, :], sqp[:, j * 512:(j + 1) * 512])], s_t)
            t_kn = s_t.n
            rec.wait("vector", s_t, t_kn)
            if j == 0:
                guard("vector", "kmx")
                rec.add("vector", lambda e, b=bank: e.reduce_max(
                    kmx[0:1, 0:1], b[0:1, :], axis=AX.X), s_v, 1)
            else:
                rec.add("vector", lambda e, b=bank: e.reduce_max(
                    kmx[0:1, 1:2], b[0:1, :], axis=AX.X), s_v, 1)
                rec.wait("vector", s_v)
                rec.add("vector", lambda e: e.tensor_tensor(
                    kmx[0:1, 0:1], kmx[0:1, 0:1], kmx[0:1, 1:2], OP.max), s_v, 1)
            setfree(f"st{j % 2}", "v", s_v.n)
        v_kmx = s_v.n
        t_knorm = s_t.n

        rec.wait("scalar", s_t, t_knorm)
        rec.add("scalar", lambda e: e.activation(sq[:, :], qn[:, :], AF.Square), s_sc, 1)
        rec.add("scalar", lambda e: e.activation(sqp[:, :], qpa[0:64, :], AF.Square), s_sc, 1)
        sc_sqq = s_sc.n
        rec.wait("tensor", s_sc, sc_sqq)
        for j in range(NT):
            bank = ps_st[j % 2]
            guard("tensor", f"st{j % 2}")
            rec.mm_group(bank[0:1, :],
                         [(P["ones_col_b"][:, :], sq[:, j * 512:(j + 1) * 512]),
                          (P["ones_col_b"][0:64, :], sqp[:, j * 512:(j + 1) * 512])], s_t)
            t_qn = s_t.n
            rec.wait("scalar", s_t, t_qn)
            if j == 0:
                rec.wait("scalar", s_v, v_kmx)
                guard("scalar", "qpa_row")
            rec.add("scalar", lambda e, b=bank, j=j: e.activation(
                qpa[64:65, j * 512:(j + 1) * 512], b[0:1, :], AF.Sqrt,
                scale=kmx[0:1, 0:1]), s_sc, 1)
            setfree(f"st{j % 2}", "sc", s_sc.n)
        sc_qrow = s_sc.n
        setfree("sq", "t", s_t.n)
        setfree("kmx", "sc", sc_qrow)

        rec.wait("tensor", s_ld, ld_head)
        rec.wait("tensor", s_sc, sc_qrow)
        for i in range(NT):
            i0 = i * 512
            njt = 4 * i + 4
            pend = None

            def emit_rpv(slot, sc_exp, j_sl, st, sp):
                rec.wait("tensor", s_sc, sc_exp)
                if st:
                    guard("tensor", "r")
                    guard("tensor", "attn")
                rec.add("tensor", lambda e, slot=slot, st=st, sp=sp: e.matmul(
                    ps_r[0:1, :], P["ones_col_b"][:, :], pt[:, slot, :],
                    start=st, stop=sp), s_t, 1)
                rec.add("tensor", lambda e, slot=slot, j_sl=j_sl, st=st, sp=sp: e.matmul(
                    ps_attn[:, :], vn[:, j_sl], pt[:, slot, :],
                    start=st, stop=sp), s_t, 1)
                setfree(f"pt{slot}", "t", s_t.n)

            for j128 in range(njt):
                bank = ps_st[j128 % 2]
                guard("tensor", f"st{j128 % 2}")
                j_sl = slice(j128 * 128, (j128 + 1) * 128)
                rec.mm_group(bank[:, :],
                             [(kn[:, j_sl], qn[:, i0:i0 + 512]),
                              (kpa[0:65, j_sl], qpa[0:65, i0:i0 + 512])], s_t)
                t_st = s_t.n
                rel = j128 - 4 * i
                slot = j128 % 3
                guard("vector", f"pt{slot}")
                guard("scalar", f"pt{slot}")
                if rel >= 0:
                    rec.wait("vector", s_t, t_st)
                    rec.add("vector", lambda e, b=bank, rel=rel: e.tensor_tensor(
                        b[:, rel * 128:(rel + 1) * 128], b[:, rel * 128:(rel + 1) * 128],
                        P["maskneg"][:, :], OP.add), s_v, 1)
                    if rel > 0:
                        rec.add("vector", lambda e, slot=slot, rel=rel: e.memset(
                            pt[:, slot, 0:rel * 128], 0.0), s_v, 1)
                    v_zero = s_v.n
                    rec.wait("scalar", s_v, v_zero)
                    rec.wait("scalar", s_t, t_st)
                    rec.add("scalar", lambda e, b=bank, slot=slot, rel=rel: e.activation(
                        pt[:, slot, rel * 128:512], b[:, rel * 128:512], AF.Exp), s_sc, 1)
                else:
                    rec.wait("scalar", s_t, t_st)
                    rec.add("scalar", lambda e, b=bank, slot=slot: e.activation(
                        pt[:, slot, :], b[:, :], AF.Exp), s_sc, 1)
                sc_exp = s_sc.n
                setfree(f"st{j128 % 2}", "sc", sc_exp)
                if pend is not None:
                    emit_rpv(*pend)
                pend = (slot, sc_exp, j_sl, j128 == 0, j128 == njt - 1)
            emit_rpv(*pend)
            t_pv = s_t.n
            rec.wait("vector", s_t, t_pv)
            guard("vector", "rr")
            rec.add("vector", lambda e: e.reciprocal(rr_f[:, :], ps_r[0:1, :]), s_v, 1)
            v_rr = s_v.n
            setfree("r", "v", v_rr)
            rec.wait("tensor", s_v, v_rr)
            guard("tensor", "rb")
            rec.add("tensor", lambda e: e.matmul(ps_rb[:, :], P["ones_row_f"][:, :], rr_f[:, :],
                                                 start=True, stop=True), s_t, 1)
            t_rb = s_t.n
            setfree("rr", "t", t_rb)
            rec.wait("scalar", s_t, t_rb)
            guard("scalar", "rb_sb")
            rec.add("scalar", lambda e: e.copy(rb[:, :], ps_rb[:, :]), s_sc, 1)
            sc_rb = s_sc.n
            setfree("rb", "sc", sc_rb)
            rec.wait("vector", s_sc, sc_rb)
            rec.wait("vector", s_t, t_pv)
            guard("vector", "ao")
            rec.add("vector", lambda e: e.tensor_tensor(ao[:, :], ps_attn[:, :], rb[:, :],
                                                        OP.mult), s_v, 1)
            v_ao = s_v.n
            setfree("attn", "v", v_ao)
            setfree("rb_sb", "v", v_ao)
            rec.wait("gpsimd", s_v, v_ao)
            rec.add("gpsimd", lambda e, h=h, i0=i0: e.dma_start(
                scr["AOT"][h, :, i0:i0 + 512], ao[:, :]), s_stg, 16)
            setfree("ao", "stg", s_stg.n)
        setfree("qk", "t", t_pv)
        setfree("qpa_row", "t", t_pv)
    return rec



def rec_s3(ctx, sems, P):
    rec = Rec(ctx)
    s_ld, s_t, s_v, s_sc, s_g, s_st, s_stg = (sems[k] for k in ("ld", "t", "v", "sc", "g", "st", "stg"))
    sb, psb = ctx.sb, ctx.psb
    scr, W = ctx.scr, ctx.wdram

    owT = sb("s3_owT", [128, 16, 2048], BF16)
    at = sb("s3_at", [128, 16, 512], BF16)
    xt = sb("s3_xt", [128, 16, 512], F32)
    xn = sb("s3_xn", [128, 2, 512], F32)
    banks = [psb("s3_ps0", [128, 512], F32), psb("s3_ps1", [128, 512], F32)]

    for hvt in range(16):
        for dg in range(4):
            rec.add("sync", lambda e, hvt=hvt, dg=dg: e.dma_start_transpose(
                owT[:, hvt, dg * 512:(dg + 1) * 512],
                W["OWB"][dg * 512:(dg + 1) * 512, hvt * 128:(hvt + 1) * 128]), s_ld, 16)
    w_ready = s_ld.n

    free = {}
    def guard(engname, res):
        if res in free:
            nm, cnt = free[res]
            rec.wait(engname, sems[nm], cnt)
    def setfree(res, nm, cnt):
        free[res] = (nm, cnt)

    for c in range(NT):
        c0 = c * 512
        guard("sync", "at")
        for hvt in range(16):
            rec.add("sync", lambda e, hvt=hvt, c0=c0: e.dma_start(
                at[:, hvt, :], scr["AOT"][hvt, :, c0:c0 + 512]), s_ld, 16)
        guard("sync", "xt")
        for d in range(16):
            rec.add("sync", lambda e, d=d, c0=c0: e.dma_start(
                xt[:, d, :], scr["XT"][d, :, c0:c0 + 512]), s_ld, 16)
        ld_c = s_ld.n
        rec.wait("tensor", s_ld, w_ready)
        rec.wait("tensor", s_ld, ld_c)
        last_t = 0
        v_done = {}
        st_done = {}
        for oc in range(16):
            bank = banks[oc % 2]
            guard("tensor", f"b{oc % 2}")
            rec.mm_group(bank[:, :], [(owT[:, hvt, oc * 128:(oc + 1) * 128], at[:, hvt, :])
                                      for hvt in range(16)], s_t)
            last_t = s_t.n
            sl = oc % 2
            rec.wait("vector", s_t, last_t)
            rec.wait("vector", s_ld, ld_c)
            guard("vector", f"xn{sl}")
            rec.add("vector", lambda e, sl=sl, b=bank, oc=oc: e.tensor_tensor(
                xn[:, sl, :], b[:, :], xt[:, oc, :], OP.add), s_v, 1)
            setfree(f"b{sl}", "v", s_v.n)
            v_done[oc] = s_v.n
            rec.wait("gpsimd", s_v, v_done[oc])
            rec.add("gpsimd", lambda e, oc=oc, sl=sl, c0=c0: e.dma_start(
                scr["XT"][oc, :, c0:c0 + 512], xn[:, sl, :]), s_stg, 16)
            st_done[oc] = s_stg.n
            setfree(f"xn{sl}", "stg", st_done[oc])
        setfree("at", "t", last_t)
        setfree("xt", "v", v_done[15])
    return rec



def rec_s4ln(ctx, sems, P, XL2T):
    rec = Rec(ctx)
    s_ld, s_t, s_v, s_sc, s_g, s_st, s_stg = (sems[k] for k in ("ld", "t", "v", "sc", "g", "st", "stg"))
    sb, psb = ctx.sb, ctx.psb
    scr = ctx.scr
    xt = sb("s4l_xt", [128, 16, 512], F32)
    xbf = None
    sqbf = sb("s4l_sqbf", [128, 16, 512], BF16)
    tmpf = sb("s4l_tmpf", [128, 2, 512], F32)
    rows = sb("s4l_rows", [1, 4 * 512], F32)
    ab = sb("s4l_ab", [128, 2 * 512], F32)
    ps_row0 = psb("s4l_r0", [128, 512], F32)
    ps_row1 = psb("s4l_r1", [128, 512], F32)
    ps_bc0 = psb("s4l_b0", [128, 512], F32)
    ps_bc1 = psb("s4l_b1", [128, 512], F32)
    prev = {}
    for c in range(NT):
        c0 = c * 512
        if prev:
            for nm, val in prev["xt_free"]:
                rec.wait("sync", sems[nm], val)
        for d in range(16):
            rec.add("sync", lambda e, d=d, c0=c0: e.dma_start(
                xt[:, d, :], scr["XT"][d, :, c0:c0 + 512]), s_ld, 16)
        ld_tiles = s_ld.n
        for e_ in ("vector", "scalar"):
            rec.wait(e_, s_ld, ld_tiles)
        if prev:
            rec.wait("scalar", s_t, prev["xbf_free_t"])
            rec.wait("tensor", s_v, prev["psrow_free_v"])
            rec.wait("tensor", s_sc, prev["psbc_free_sc"])
        lnc = rec_ln_chunk(rec, sems, P, {
            "xt": xt, "xbf": xbf, "sqbf": sqbf,
            "xln_at": (lambda d, c=c: XL2T[:, d, c, :]),
            "xt_ready": ("ld", ld_tiles),
            "tmpf": tmpf, "rows": rows, "ab": ab,
            "ab_guard": prev.get("ab_free"),
            "ps_row0": ps_row0, "ps_row1": ps_row1,
            "ps_bc0": ps_bc0, "ps_bc1": ps_bc1}, c, 16, 1.0 / 2048.0, 0)
        prev = {
            "xt_free": lnc["xt_free"],
            "ab_free": lnc["ab_free"],
            "xbf_free_t": max(lnc["xbf_free_t"], lnc["sqbf_free_t"]),
            "psrow_free_v": lnc["psrow_free_v"],
            "psbc_free_sc": lnc["psbc_free_sc"],
            "ab_free": lnc["ab_free"],
        }
    return rec


def rec_s4w2prep(ctx, sems, P, rec=None):
    if rec is None:
        rec = Rec(ctx)
    s_ld, s_t, s_v, s_sc, s_g, s_st, s_stg = (sems[k] for k in ("ld", "t", "v", "sc", "g", "st", "stg"))
    sb = ctx.sb
    W = ctx.wdram
    wA = sb("s4p_wA", [128, 8192], F32)
    wB = sb("s4p_wB", [128, 8192], F32)
    wbf = sb("s4p_wbf", [128, 8192], BF16)
    tb = sb("s4p_tb", [128, 2, 2048], BF16)
    prev = {}
    for rc in range(16):
        if prev:
            rec.wait("sync", s_v, prev["wA_free_v"])
            rec.wait("sync", s_g, prev["wA_free_g"])
        rec.add("sync", lambda e, rc=rc: e.dma_start(
            wA[:, :], ctx.wins["w_W2"][rc * 128:(rc + 1) * 128, :]), s_ld, 16)
        rec.wait("vector", s_ld, s_ld.n)
        rec.wait("gpsimd", s_ld, s_ld.n)
        src, dst = wA, wB
        for lvl in range(6):
            sseg = 1 << lvl
            nb = 8192 // (2 * sseg * 128)
            wl = sseg * 128
            def mk(out, a, b, op):
                def f(e, out=out, a=a, b=b, op=op):
                    return e.tensor_tensor(out, a, b, op)
                return f
            sv = src.rearrange("p (g two w) -> p g two w", two=2, w=wl)
            dv = dst.rearrange("p (g two w) -> p g two w", two=2, w=wl)
            if lvl > 0:
                rec.wait("vector", s_g)
                rec.wait("gpsimd", s_v)
            rec.add("vector", mk(dv[:, :, 0, :], sv[:, :, 0, :], sv[:, :, 1, :], OP.add), s_v, 1)
            rec.add("gpsimd", mk(dv[:, :, 1, :], sv[:, :, 0, :], sv[:, :, 1, :], OP.subtract), s_g, 1)
            src, dst = dst, src
        rec.wait("vector", s_g)
        rec.wait("vector", s_v)
        if prev:
            rec.wait("vector", s_stg, prev["wbf_store"])
        rec.add("vector", lambda e: e.tensor_copy(wbf[:, :], wA[:, :]), s_v, 1)
        wA_free_v = s_v.n
        rec.wait("gpsimd", s_v, wA_free_v)
        rec.add("gpsimd", lambda e, rc=rc: e.dma_start(
            W["W2BF"][rc * 128:(rc + 1) * 128, :], wbf[:, :]), s_stg, 16)
        prev = {"wA_free_v": wA_free_v, "wA_free_g": s_g.n, "wbf_store": s_stg.n}
    all_stored = s_stg.n
    rec.wait("scalar", s_stg, all_stored)
    pp = {}
    for ht in range(64):
        sl = ht % 2
        if ht >= 2:
            rec.wait("scalar", s_stg, pp[ht - 2])
        rec.add("scalar", lambda e, ht=ht, sl=sl: e.dma_start_transpose(
            tb[:, sl, :], W["W2BF"][:, ht * 128:(ht + 1) * 128]), s_ld, 16)
        rec.wait("gpsimd", s_ld, s_ld.n)
        rec.add("gpsimd", lambda e, ht=ht, sl=sl: e.dma_start(
            W["W2T"][ht], tb[:, sl, :]), s_stg, 16)
        pp[ht] = s_stg.n
    return rec


def rec_s4uv(ctx, sems, P, XL2T):
    rec = Rec(ctx)
    s_ld, s_ld2, s_t, s_v, s_sc, s_g, s_st, s_stg = (sems[k] for k in ("ld", "ld2", "t", "v", "sc", "g", "st", "stg"))
    sb, psb = ctx.sb, ctx.psb
    scr, W = ctx.scr, ctx.wdram
    w1t = sb("s4u_w1t", [128, 2, 16, 512], BF16)
    w3t = sb("s4u_w3t", [128, 2, 16, 512], BF16)
    sU = sb("s4u_sU", [128, 2, 512], BF16)
    hda = sb("s4u_hda", [128, 2, 512], BF16)
    hdo = sb("s4u_hdo", [128, 2, 512], BF16)
    psU = [psb("s4u_pU0", [128, 512], F32), psb("s4u_pU1", [128, 512], F32)]
    psV = [psb("s4u_pV0", [128, 512], F32), psb("s4u_pV1", [128, 512], F32)]
    psH = [psb("s4u_pH0", [128, 512], F32), psb("s4u_pH1", [128, 512], F32)]

    free = {}
    def guard(engname, res):
        if res in free:
            nm, cnt = free[res]
            rec.wait(engname, sems[nm], cnt)
    def setfree(res, nm, cnt):
        free[res] = (nm, cnt)

    def load_og(og):
        slw = og % 2
        guard("sync", f"w1t{slw}")
        for d in range(16):
            rec.add("sync", lambda e, og=og, d=d, slw=slw: e.dma_start_transpose(
                w1t[:, slw, d, :], W["W1B"][og * 512:(og + 1) * 512, d * 128:(d + 1) * 128]),
                s_ld, 16)
            rec.add("sync", lambda e, og=og, d=d, slw=slw: e.dma_start_transpose(
                w3t[:, slw, d, :], W["W3B"][og * 512:(og + 1) * 512, d * 128:(d + 1) * 128]),
                s_ld, 16)
        return s_ld.n

    wready = {0: load_og(0)}
    for og in range(16):
        if og + 1 < 16:
            wready[og + 1] = load_og(og + 1)
        slw = og % 2
        last_t = 0
        for oi in range(4):
            oc = og * 4 + oi
            for c in range(NT):
                ii = (oc * NT + c) % 2
                rec.wait("tensor", s_ld, wready[og])
                guard("tensor", f"pU{ii}")
                rec.mm_group(psU[ii][:, :],
                             [(w1t[:, slw, d, oi * 128:(oi + 1) * 128], XL2T[:, d, c, :])
                              for d in range(16)], s_t)
                tU = s_t.n
                guard("tensor", f"pV{ii}")
                rec.mm_group(psV[ii][:, :],
                             [(w3t[:, slw, d, oi * 128:(oi + 1) * 128], XL2T[:, d, c, :])
                              for d in range(16)], s_t)
                tV = s_t.n
                last_t = tV
                rec.wait("scalar", s_t, tU)
                guard("scalar", f"sU{ii}")
                rec.add("scalar", lambda e, ii=ii: e.activation(
                    sU[:, ii, :], psU[ii][:, :], AF.Silu), s_sc, 1)
                sc_silu = s_sc.n
                setfree(f"pU{ii}", "sc", sc_silu)
                rec.wait("vector", s_sc, sc_silu)
                rec.wait("vector", s_t, tV)
                guard("vector", f"hda{ii}")
                rec.add("vector", lambda e, ii=ii: e.tensor_tensor(
                    hda[:, ii, :], psV[ii][:, :], sU[:, ii, :], OP.mult), s_v, 1)
                v_hda = s_v.n
                setfree(f"pV{ii}", "v", v_hda)
                setfree(f"sU{ii}", "v", v_hda)
                rec.wait("tensor", s_v, v_hda)
                guard("tensor", f"pH{ii}")
                rec.add("tensor", lambda e, ii=ii: e.matmul(
                    psH[ii][:, :], P["h128s_b"][:, :], hda[:, ii, :],
                    start=True, stop=True), s_t, 1)
                tH = s_t.n
                setfree(f"hda{ii}", "t", tH)
                rec.wait("scalar", s_t, tH)
                guard("scalar", f"hdo{ii}")
                rec.add("scalar", lambda e, ii=ii: e.copy(hdo[:, ii, :], psH[ii][:, :]), s_sc, 1)
                sc_hdo = s_sc.n
                setfree(f"pH{ii}", "sc", sc_hdo)
                rec.wait("gpsimd", s_sc, sc_hdo)
                rec.add("gpsimd", lambda e, oc=oc, c=c, ii=ii: e.dma_start(
                    scr["HD"][oc, :, c * 512:(c + 1) * 512], hdo[:, ii, :]), s_stg, 16)
                setfree(f"hdo{ii}", "stg", s_stg.n)
        setfree(f"w1t{slw}", "t", last_t)
    return rec


def rec_s4w2(ctx, sems, P):
    rec = Rec(ctx)
    s_ld, s_ld2, s_t, s_v, s_sc, s_g, s_st, s_stg = (sems[k] for k in ("ld", "ld2", "t", "v", "sc", "g", "st", "stg"))
    sb, psb = ctx.sb, ctx.psb
    scr, W = ctx.scr, ctx.wdram
    hd = sb("s4w_hd", [128, 64, 512], BF16)
    xt = sb("s4w_xt", [128, 16, 512], F32)
    w2b = sb("s4w_w2b", [128, 2, 16, 1024], BF16)
    xn = sb("s4w_xn", [128, 2, 512], F32)
    banks = [psb(f"s4w_p{i}", [128, 512], F32) for i in range(8)]

    free = {}
    def guard(engname, res):
        if res in free:
            nm, cnt = free[res]
            rec.wait(engname, sems[nm], cnt)
    def setfree(res, nm, cnt):
        free[res] = (nm, cnt)

    for c in range(NT):
        c0 = c * 512
        guard("sync", "hd")
        for ht in range(64):
            rec.add("sync", lambda e, ht=ht, c0=c0: e.dma_start(
                hd[:, ht, :], scr["HD"][ht, :, c0:c0 + 512]), s_ld, 16)
        guard("sync", "xt")
        for d in range(16):
            rec.add("sync", lambda e, d=d, c0=c0: e.dma_start(
                xt[:, d, :], scr["XT"][d, :, c0:c0 + 512]), s_ld, 16)
        ld_c = s_ld.n
        rec.wait("tensor", s_ld, ld_c)
        last_t = 0
        v_res = 0
        for dh in range(2):
            for q in range(4):
                slq = q % 2
                guard("scalar", f"w2b{slq}")
                for ht16 in range(16):
                    rec.add("scalar", lambda e, q=q, ht16=ht16, dh=dh, slq=slq: e.dma_start(
                        w2b[:, slq, ht16, :],
                        W["W2T"][q * 16 + ht16, :, dh * 1024:(dh + 1) * 1024]), s_ld2, 16)
                ldw = s_ld2.n
                rec.wait("tensor", s_ld2, ldw)
                for d8 in range(8):
                    bank = banks[d8]
                    if q == 0:
                        guard("tensor", f"bk{d8}")
                    def grp(e, q=q, d8=d8, slq=slq, bank=bank, first=(q == 0), last=(q == 3)):
                        for ht16 in range(15):
                            e.matmul(bank[:, :], w2b[:, slq, ht16, d8 * 128:(d8 + 1) * 128],
                                     hd[:, q * 16 + ht16, :], start=(first and ht16 == 0), stop=False)
                        return e.matmul(bank[:, :], w2b[:, slq, 15, d8 * 128:(d8 + 1) * 128],
                                        hd[:, q * 16 + 15, :], start=False, stop=last)
                    rec.add("tensor", grp, s_t, 1)
                    last_t = s_t.n
                setfree(f"w2b{slq}", "t", last_t)
            for d8 in range(8):
                doc = dh * 8 + d8
                sl = doc % 2
                rec.wait("vector", s_t, last_t)
                guard("vector", f"xn{sl}")
                rec.add("vector", lambda e, d8=d8, doc=doc, sl=sl: e.tensor_tensor(
                    xn[:, sl, :], banks[d8][:, :], xt[:, doc, :], OP.add), s_v, 1)
                setfree(f"bk{d8}", "v", s_v.n)
                v_res = s_v.n
                rec.wait("gpsimd", s_v, v_res)
                rec.add("gpsimd", lambda e, doc=doc, sl=sl, c0=c0: e.dma_start(
                    scr["XT"][doc, :, c0:c0 + 512], xn[:, sl, :]), s_stg, 16)
                setfree(f"xn{sl}", "stg", s_stg.n)
        setfree("hd", "t", last_t)
        setfree("xt", "v", v_res)
    return rec



def rec_dumpxl(ctx, sems, XL2T):
    rec = Rec(ctx)
    s_stg = sems["stg"]
    for d in range(16):
        for c in range(NT):
            rec.add("gpsimd", lambda e, d=d, c=c: e.dma_start(
                ctx.scr["XLD"][d, :, c * 512:(c + 1) * 512], XL2T[:, d, c, :]), s_stg, 16)
    return rec



def rec_s5(ctx, sems, P):
    rec = Rec(ctx)
    s_ld, s_ldg, s_t, s_v, s_sc, s_g, s_st, s_stg = (
        sems[k] for k in ("ld", "ldg", "t", "v", "sc", "g", "st", "stg"))
    sb, psb = ctx.sb, ctx.psb
    scr = ctx.scr
    wins = ctx.wins
    SCALE = 0.125

    xt = sb("s5_xt", [128, 16, 512], F32)
    sqbf = sb("s5_sqbf", [128, 16, 512], BF16)
    xlt = sb("s5_xlt", [128, 16, 512], BF16)
    tmpf = sb("s5_tmpf", [128, 2, 512], F32)
    rows = sb("s5_rows", [1, 4 * 512], F32)
    ab = sb("s5_ab", [128, 2 * 512], F32)
    wmv = sb("s5_wmv", [128, 16, 512], BF16)
    qt = sb("s5_qt", [128, 8, 512], BF16)
    kt = sb("s5_kt", [128, 8, 512], BF16)
    vnb = sb("s5_vnb", [128, 4, 4, 512], BF16)
    gnb = sb("s5_gnb", [128, 4, 4, 512], BF16)
    knb = sb("s5_knb", [128, 4, 2, 512], BF16)
    enb = sb("s5_enb", [128, 4, 2, 512], BF16)
    rnb = sb("s5_rnb", [128, 4, 2, 512], BF16)
    scl = sb("s5_scl", [128, 2, 512], F32)
    carry = sb("s5_carry", [1, 1024], BF16)
    HkS = sb("s5_HkS", [64, 1024], F32)
    HkB = sb("s5_HkB", [128, 1024], BF16)
    HvS = sb("s5_HvS", [64, 2048], F32)
    HvB = sb("s5_HvB", [64, 2048], BF16)
    db = sb("s5_db", [128, 2, 128], BF16)
    gb = sb("s5_gb", [128, 2, 128], BF16)
    okn = sb("s5_okn", [128, 2, 64], F32)
    mxc = sb("s5_mxc", [128, 4], F32)
    pvb = sb("s5_pvb", [128, 2, 64], BF16)
    ptt = sb("s5_ptt", [64, 2, 128], BF16)
    ett = sb("s5_ett", [64, 2, 128], BF16)
    sqd = sb("s5_sqd", [128, 128], BF16)
    colf = sb("s5_colf", [128, 8], F32)
    sgb = sb("s5_sgb", [128, 128], BF16)
    ggf = sb("s5_ggf", [128, 2, 128], F32)
    o1f = sb("s5_o1f", [128, 2, 128], F32)
    awt = sqbf
    aob_at = lambda h: xlt[:, h, 0:128]
    gnw_row = sb("s5_gnwrow", [1, 128], F32)
    gnwB = sb("s5_gnwB", [128, 128], F32)
    xn = tmpf

    B = [psb(f"s5_b{i}", [128, 512], F32) for i in range(7)]
    b7 = psb("s5_b7", [128, 1024], BF16)

    free = {}
    def guard(engname, res):
        if res in free:
            nm, cnt = free[res]
            rec.wait(engname, sems[nm], cnt)
    def setfree(res, nm, cnt):
        free[res] = (nm, cnt)

    rec.add("gpsimd", lambda e: e.memset(HkS[:, :], 0.0), s_g, 1)
    rec.add("gpsimd", lambda e: e.memset(HkB[:, :], 0.0), s_g, 1)
    rec.add("gpsimd", lambda e: e.memset(HvS[:, :], 0.0), s_g, 1)
    rec.add("gpsimd", lambda e: e.memset(HvB[:, :], 0.0), s_g, 1)
    rec.add("gpsimd", lambda e: e.memset(carry[:, :], 0.0), s_g, 1)
    g_init = s_g.n
    for h in range(16):
        setfree(f"hkb{h}", "g", g_init)
        setfree(f"hvb{h}", "g", g_init)
    setfree("carry_w", "g", g_init)
    gn_row = wins["w1_gn"].rearrange("(a b) -> a b", a=1)
    rec.add("sync", lambda e: e.dma_start(gnw_row[:, :], gn_row), s_ld, 16)
    rec.wait("tensor", s_ld, s_ld.n)
    rec.add("tensor", lambda e: e.matmul(B[0][:, 0:128], P["ones_row_f"][:, :], gnw_row[:, :],
                                         start=True, stop=True), s_t, 1)
    rec.wait("scalar", s_t, s_t.n)
    rec.add("scalar", lambda e: e.copy(gnwB[:, :], B[0][:, 0:128]), s_sc, 1)
    setfree("B0", "sc", s_sc.n)

    WB = ctx.wdram
    L1MAP = {"w1_q": "Q1B", "w1_k": "K1B", "w1_v": "V1B", "w1_g": "G1B",
             "w1_s": "S1B", "w1_o": "O1B"}

    QKO = {"q": "Q1B", "k": "K1B", "o": "O1B"}

    def load_qk(which, oc, slot):
        src = WB[QKO[which]].rearrange("(dt p) o -> p dt o", p=128)[:, :, oc * 128:(oc + 1) * 128]
        guard("sync", f"mvq{slot}")
        guard("sync", "mvfull")
        rec.add("sync", lambda e, src=src, slot=slot: e.dma_start(
            wmv[:, :, slot * 128:(slot + 1) * 128], src), s_ld, 16)
        return s_ld.n

    def load_mv(name, ocols):
        src = WB[L1MAP[name]].rearrange("(dt p) o -> p dt o", p=128)[:, :, ocols]
        guard("sync", "mvq0")
        guard("sync", "mvq1")
        guard("sync", "mvfull")
        rec.add("sync", lambda e, src=src: e.dma_start(wmv[:, :, :], src), s_ld, 16)
        return s_ld.n

    def load_wo(oc, slot):
        return load_qk("o", oc, slot)

    prev_ab = None
    hk_gdone = {}
    hv_gdone = {}
    for c in range(NT):
        c0 = c * 512
        guard("sync", "xt")
        for d in range(16):
            rec.add("sync", lambda e, d=d, c0=c0: e.dma_start(
                xt[:, d, :], scr["XT"][d, :, c0:c0 + 512]), s_ld, 16)
        ld_tiles = s_ld.n
        for e_ in ("vector", "scalar"):
            rec.wait(e_, s_ld, ld_tiles)
        guard("scalar", "sqbf")
        guard("vector", "xn0")
        guard("vector", "xn1")
        guard("tensor", "B2")
        guard("tensor", "B3")
        guard("tensor", "B4")
        guard("tensor", "B5")
        guard("gpsimd", "xlt_w")
        lnc = rec_ln_chunk(rec, sems, P, {
            "xt": xt, "xbf": None, "sqbf": sqbf,
            "xln_at": (lambda d: xlt[:, d, :]),
            "xt_ready": ("ld", ld_tiles),
            "tmpf": tmpf, "rows": rows, "ab": ab, "ab_guard": prev_ab,
            "ps_row0": B[2], "ps_row1": B[3],
            "ps_bc0": B[4], "ps_bc1": B[5]}, c, 16, 1.0 / 2048.0, 0)
        prev_ab = lnc["ab_free"]
        sq_free_t = lnc["sqbf_free_t"]
        setfree("B2", "v", lnc["psrow_free_v"])
        setfree("B3", "v", lnc["psrow_free_v"])
        setfree("B4", "sc", lnc["psbc_free_sc"])
        setfree("B5", "sc", lnc["psbc_free_sc"])
        xln_g = lnc["xln_done_g"]
        rec.wait("tensor", s_g, xln_g)

        nmm = 0
        pe = {}
        for which, buf in (("q", qt), ("k", kt)):
            for oc in range(8):
                slot = oc % 2
                lw = load_qk(which, oc, slot)
                par = nmm % 2
                bank = B[par]
                guard("tensor", f"B{par}")
                rec.wait("tensor", s_ld, lw)
                rec.mm_group(bank[:, :], [(wmv[:, d, slot * 128:(slot + 1) * 128], xlt[:, d, :])
                                          for d in range(16)], s_t)
                tmm = s_t.n
                setfree(f"mvq{slot}", "t", tmm)
                rec.wait("scalar", s_t, tmm)
                guard("scalar", "qt" if which == "q" else "kt")
                if which == "q":
                    rec.add("scalar", lambda e, oc=oc, b=bank: e.mul(qt[:, oc, :], b[:, :], SCALE), s_sc, 1)
                else:
                    rec.add("scalar", lambda e, oc=oc, b=bank: e.copy(kt[:, oc, :], b[:, :]), s_sc, 1)
                setfree(f"B{par}", "sc", s_sc.n)
                nmm += 1
        if "QKD" in ctx.dbg:
            rec.wait("gpsimd", s_sc, s_sc.n)
            for oc in range(8):
                rec.add("gpsimd", lambda e, c0=c0, oc=oc: e.dma_start(
                    ctx.scr["QKD"][0, oc * 128:(oc + 1) * 128, c0:c0 + 512], qt[:, oc, :]), s_stg, 16)
                rec.add("gpsimd", lambda e, c0=c0, oc=oc: e.dma_start(
                    ctx.scr["QKD"][1, oc * 128:(oc + 1) * 128, c0:c0 + 512], kt[:, oc, :]), s_stg, 16)
        ldi = 0
        for name, buf, nch, kind in (("w1_v", vnb, 4, "v"), ("w1_g", gnb, 4, "g"),
                                     ("w1_k", knb, 2, "kn"), ("w1_s", None, 2, "s")):
            for j in range(nch):
                ldi += 1
                lw = load_mv(name, slice(j * 512, (j + 1) * 512))
                for sub in range(4):
                    t_sl = slice(sub * 128, (sub + 1) * 128)
                    par = nmm % 2
                    bank = B[par]
                    guard("tensor", f"B{par}")
                    rec.wait("tensor", s_ld, lw)
                    rec.mm_group(bank[:, :], [(xlt[:, d, t_sl], wmv[:, d, :])
                                              for d in range(16)], s_t)
                    tmm = s_t.n
                    if sub == 3:
                        setfree("mvfull", "t", tmm)
                    if kind == "s":
                        cpar = nmm % 2
                        rec.wait("vector", s_t, tmm)
                        guard("vector", f"scl{cpar}")
                        rec.add("vector", lambda e, b=bank, cpar=cpar: e.tensor_scalar(
                            scl[:, cpar, :], b[:, :], -32.0, 32.0, OP.max, OP.min), s_v, 1)
                        v_cl = s_v.n
                        setfree(f"B{par}", "v", v_cl)
                        rec.wait("scalar", s_v, v_cl)
                        guard("scalar", "mvbufs_t")
                        rec.add("scalar", lambda e, sub=sub, j=j, cpar=cpar: e.activation(
                            enb[:, sub, j, :], scl[:, cpar, :], AF.Exp), s_sc, 1)
                        setfree(f"scl{cpar}", "sc", s_sc.n)
                        pe[(sub, j)] = s_sc.n
                    else:
                        rec.wait("vector", s_t, tmm)
                        guard("vector", "mvbufs_t")
                        if kind == "g":
                            guard("vector", "gnb_sc")
                        rec.add("vector", lambda e, b=bank, buf=buf, sub=sub, j=j: e.tensor_copy(
                            buf[:, sub, j, :], b[:, :]), s_v, 1)
                        setfree(f"B{par}", "v", s_v.n)
                    nmm += 1
        xlt_free_t = s_t.n
        rec.wait("vector", s_t, xlt_free_t)
        for sub in range(4):
            for j in range(2):
                guard("tensor", "B2")
                rec.wait("tensor", s_sc, pe[(sub, j)])
                guard("tensor", "carry_w")
                def cmm(e, sub=sub, j=j):
                    e.matmul(B[2][:, :], P["uincl_b"][:, :], enb[:, sub, j, :], start=True, stop=False)
                    return e.matmul(B[2][:, :], P["ones_row_b"][:, :], carry[:, j * 512:(j + 1) * 512],
                                    start=False, stop=True)
                rec.add("tensor", cmm, s_t, 1)
                t_c = s_t.n
                guard("tensor", "B3")
                def csum(e, sub=sub, j=j):
                    e.matmul(B[3][0:1, :], P["ones_col_b"][:, :], enb[:, sub, j, :], start=True, stop=False)
                    return e.matmul(B[3][0:1, :], P["ones_1b"][:, :], carry[:, j * 512:(j + 1) * 512],
                                    start=False, stop=True)
                rec.add("tensor", csum, s_t, 1)
                t_cs = s_t.n
                rec.wait("vector", s_t, t_c)
                def reciprnb(e, sub=sub, j=j):
                    with ctx.nc.allow_low_precision(reason="R=1/C used as bf16 matmul operand"):
                        return e.reciprocal(rnb[:, sub, j, :], B[2][:, :])
                rec.add("vector", reciprnb, s_v, 1)
                setfree("B2", "v", s_v.n)
                rec.wait("scalar", s_t, t_cs)
                rec.wait("scalar", s_t, t_c)
                rec.add("scalar", lambda e, j=j: e.copy(carry[:, j * 512:(j + 1) * 512], B[3][0:1, :]), s_sc, 1)
                setfree("B3", "sc", s_sc.n)
                setfree("carry_w", "sc", s_sc.n)

        for sub in range(4):
            for h in range(16):
                par = h % 2
                qsl = qt[64 * par:64 * par + 64, h // 2, sub * 128:(sub + 1) * 128]
                ksl = kt[64 * par:64 * par + 64, h // 2, sub * 128:(sub + 1) * 128]
                esl = enb[:, sub, h // 8, (h % 8) * 64:(h % 8) * 64 + 64]
                rsl = rnb[:, sub, h // 8, (h % 8) * 64:(h % 8) * 64 + 64]
                knsl = knb[:, sub, h // 8, (h % 8) * 64:(h % 8) * 64 + 64]
                vsl = vnb[:, sub, h // 4, (h % 4) * 128:(h % 4) * 128 + 128]
                gsl = gnb[:, sub, h // 4, (h % 4) * 128:(h % 4) * 128 + 128]
                cellb = B[4 + par]
                ovb = B[par]
                guard("tensor", f"B{4 + par}")
                rec.wait("tensor", s_sc, pe[(sub, h // 8)])
                rec.add("tensor", lambda e, b=cellb, ksl=ksl, qsl=qsl: e.matmul(
                    b[:, 0:128], ksl, qsl, start=True, stop=True), s_t, 1)
                t_a = s_t.n
                rec.wait("vector", s_t, t_a)
                guard("vector", f"db{par}")
                rec.add("vector", lambda e, b=cellb, par=par: e.tensor_tensor(
                    db[:, par, :], b[:, 0:128], P["uincl_f"][:, :], OP.mult), s_v, 1)
                v_d = s_v.n
                guard("tensor", "B6")
                rec.wait("tensor", s_v, v_d)
                guard("tensor", f"hkb{h}")
                def okmm(e, par=par, esl=esl, qsl=qsl, h=h):
                    e.matmul(B[6][:, 0:64], db[:, par, :], esl, start=True, stop=False)
                    return e.matmul(B[6][:, 0:64], qsl,
                                    HkB[64 * par:64 * par + 64, h * 64:(h + 1) * 64],
                                    start=False, stop=True)
                rec.add("tensor", okmm, s_t, 1)
                t_ok = s_t.n
                setfree(f"db{par}", "t", t_ok)
                rec.wait("vector", s_t, t_ok)
                guard("vector", f"okn{par}")
                rec.add("vector", lambda e, par=par, rsl=rsl: e.tensor_tensor(
                    okn[:, par, :], B[6][:, 0:64], rsl, OP.mult), s_v, 1)
                v_okn = s_v.n
                rec.wait("vector", s_v)
                rec.add("vector", lambda e, par=par: e.reduce_max(
                    mxc[:, par:par + 1], okn[:, par, :], axis=AX.X), s_v, 1)
                rec.wait("vector", s_v)
                rec.add("vector", lambda e, par=par: e.tensor_scalar(
                    mxc[:, 2 + par:3 + par], mxc[:, par:par + 1], -1.0, None, OP.mult), s_v, 1)
                v_nm = s_v.n
                rec.wait("scalar", s_v, v_nm)
                guard("scalar", f"pvb{par}")
                guard("scalar", "colf01")
                rec.add("scalar", lambda e, par=par: e.activation(
                    pvb[:, par, :], okn[:, par, :], AF.Exp,
                    bias=mxc[:, 2 + par:3 + par], accum_out=colf[:, par:par + 1]), s_sc, 1)
                sc_pv = s_sc.n
                setfree(f"okn{par}", "sc", sc_pv)
                rec.wait("vector", s_sc, sc_pv)
                rec.add("vector", lambda e, par=par: e.reciprocal(
                    colf[:, 2 + par:3 + par], colf[:, par:par + 1]), s_v, 1)
                setfree("colf01", "v", s_v.n)
                rec.wait("vector", s_v)
                rec.add("vector", lambda e, par=par, rsl=rsl: e.scalar_tensor_tensor(
                    pvb[:, par, :], pvb[:, par, :], colf[:, 2 + par:3 + par], rsl,
                    OP.mult, OP.mult), s_v, 1)
                v_pt = s_v.n
                if "PVD" in ctx.dbg:
                    rec.wait("gpsimd", s_v, v_pt)
                    rec.add("gpsimd", lambda e, c0=c0, sub=sub, h=h, par=par: e.dma_start(
                        ctx.scr["PVD"][c0 + sub * 128:c0 + (sub + 1) * 128, h * 64:(h + 1) * 64],
                        pvb[:, par, :]), s_stg, 16)
                guard("tensor", "B7")
                rec.wait("tensor", s_v, v_pt)
                rec.add("tensor", lambda e, par=par: e.transpose(
                    b7[0:64, 0:128], pvb[:, par, :], P["ident_b"][:, :]), s_t, 1)
                setfree(f"pvb{par}", "t", s_t.n)
                rec.add("tensor", lambda e, esl=esl: e.transpose(
                    b7[0:64, 128:256], esl, P["ident_b"][:, :]), s_t, 1)
                t_tr = s_t.n
                rec.wait("scalar", s_t, t_tr)
                guard("scalar", f"ptt{par}")
                rec.add("scalar", lambda e, par=par: e.copy(ptt[:, par, :], b7[0:64, 0:128]), s_sc, 1)
                rec.add("scalar", lambda e, par=par: e.copy(ett[:, par, :], b7[0:64, 128:256]), s_sc, 1)
                sc_tr = s_sc.n
                setfree("B7", "sc", sc_tr)
                rec.wait("tensor", s_sc, sc_tr)
                rec.add("tensor", lambda e, b=cellb, par=par: e.matmul(
                    b[:, 128:256], ett[:, par, :], ptt[:, par, :], start=True, stop=True), s_t, 1)
                t_g = s_t.n
                setfree(f"B{4 + par}", "t", t_g)
                rec.wait("vector", s_t, t_g)
                guard("vector", f"gb{par}")
                rec.add("vector", lambda e, b=cellb, par=par: e.tensor_tensor(
                    gb[:, par, :], b[:, 128:256], P["uincl_f"][:, :], OP.mult), s_v, 1)
                v_gb = s_v.n
                if "GBD" in ctx.dbg and c == 0 and sub == 0 and h == 0:
                    rec.wait("gpsimd", s_v, v_gb)
                    rec.add("gpsimd", lambda e, par=par: e.dma_start(
                        ctx.scr["GBD"][:, :], gb[:, par, :]), s_stg, 16)
                    rec.wait("gpsimd", s_sc, sc_tr)
                    rec.add("gpsimd", lambda e, par=par: e.dma_start(
                        ctx.scr["TTD"][:, 0:128], ptt[:, par, :]), s_stg, 16)
                    rec.add("gpsimd", lambda e, par=par: e.dma_start(
                        ctx.scr["TTD"][:, 128:256], ett[:, par, :]), s_stg, 16)
                guard("tensor", f"B{par}")
                rec.wait("tensor", s_v, v_gb)
                guard("tensor", f"hvb{h}")
                def ovmm(e, par=par, vsl=vsl, h=h):
                    e.matmul(B[par][:, 0:128], ptt[:, par, :], HvB[:, h * 128:(h + 1) * 128],
                             start=True, stop=False)
                    return e.matmul(B[par][:, 0:128], gb[:, par, :], vsl,
                                    start=False, stop=True)
                rec.add("tensor", ovmm, s_t, 1)
                t_ov = s_t.n
                setfree(f"gb{par}", "t", t_ov)
                setfree(f"ptt{par}", "t", t_ov)
                if "OVD" in ctx.dbg and c == 0 and sub == 0 and h == 0:
                    rec.wait("scalar", s_t, t_ov)
                    rec.add("scalar", lambda e, par=par: e.copy(
                        ctx.P["ovd_sb"][:, 0:128], B[par][:, 0:128]), s_sc, 1)
                    rec.wait("gpsimd", s_sc, s_sc.n)
                    rec.add("gpsimd", lambda e: e.dma_start(
                        ctx.scr["OVD"][:, 0:128], ctx.P["ovd_sb"][:, 0:128]), s_stg, 16)
                rec.wait("tensor", s_v, v_okn)
                def hkmm(e, knsl=knsl, esl=esl):
                    return e.matmul(B[6][0:64, 64:128], knsl, esl, start=True, stop=True)
                rec.add("tensor", hkmm, s_t, 1)
                t_hk = s_t.n
                def hvmm(e, esl=esl, vsl=vsl):
                    return e.matmul(B[6][0:64, 128:256], esl, vsl, start=True, stop=True)
                rec.add("tensor", hvmm, s_t, 1)
                t_hv = s_t.n
                rec.wait("vector", s_t, t_hk)
                if h in hk_gdone:
                    rec.wait("vector", s_g, hk_gdone[h])
                rec.add("vector", lambda e, h=h: e.tensor_tensor(
                    HkS[:, h * 64:(h + 1) * 64], HkS[:, h * 64:(h + 1) * 64],
                    B[6][0:64, 64:128], OP.add), s_v, 1)
                v_hk_add = s_v.n
                rec.wait("gpsimd", s_v, v_hk_add)
                rec.add("gpsimd", lambda e, h=h: e.tensor_copy(
                    HkB[0:64, h * 64:(h + 1) * 64], HkS[:, h * 64:(h + 1) * 64]), s_g, 1)
                rec.add("gpsimd", lambda e, h=h: e.tensor_copy(
                    HkB[64:128, h * 64:(h + 1) * 64], HkS[:, h * 64:(h + 1) * 64]), s_g, 1)
                hk_gdone[h] = s_g.n
                setfree(f"hkb{h}", "g", s_g.n)
                rec.wait("vector", s_t, t_hv)
                if h in hv_gdone:
                    rec.wait("vector", s_g, hv_gdone[h])
                rec.add("vector", lambda e, h=h: e.tensor_tensor(
                    HvS[:, h * 128:(h + 1) * 128], HvS[:, h * 128:(h + 1) * 128],
                    B[6][0:64, 128:256], OP.add), s_v, 1)
                v_hv = s_v.n
                rec.wait("gpsimd", s_v, v_hv)
                rec.add("gpsimd", lambda e, h=h: e.tensor_copy(
                    HvB[:, h * 128:(h + 1) * 128], HvS[:, h * 128:(h + 1) * 128]), s_g, 1)
                hv_gdone[h] = s_g.n
                setfree(f"hvb{h}", "g", s_g.n)
                setfree("B6", "v", v_hv)
                rec.wait("scalar", s_t, t_ov)
                guard("scalar", "sqd")
                guard("scalar", "colf45")
                rec.add("scalar", lambda e, par=par: e.activation(
                    sqd[:, :], B[par][:, 0:128], AF.Square,
                    accum_out=colf[:, 4:5]), s_sc, 1)
                sc_sq = s_sc.n
                setfree("sqd", "sc", sc_sq)
                guard("scalar", "colf6")
                rec.add("scalar", lambda e: e.activation(
                    colf[:, 6:7], colf[:, 4:5], AF.Sqrt,
                    bias=P["eps"][:, 0:1], scale=1.0 / 128.0), s_sc, 1)
                sc_sd = s_sc.n
                setfree("colf45", "sc", sc_sd)
                guard("scalar", f"ggf{par}")
                rec.add("scalar", lambda e, gsl=gsl, par=par: e.activation(
                    ggf[:, par, :], gsl, AF.Silu), s_sc, 1)
                sc_sg = s_sc.n
                rec.wait("vector", s_sc, sc_sd)
                guard("vector", "colf7")
                rec.add("vector", lambda e: e.reciprocal(colf[:, 7:8], colf[:, 6:7]), s_v, 1)
                setfree("colf6", "v", s_v.n)
                guard("vector", f"o1f{par}")
                rec.add("vector", lambda e, par=par: e.scalar_tensor_tensor(
                    o1f[:, par, :], B[par][:, 0:128], colf[:, 7:8], gnwB[:, :],
                    OP.mult, OP.mult), s_v, 1)
                v_o1 = s_v.n
                setfree(f"B{par}", "v", v_o1)
                setfree("colf7", "v", v_o1)
                rec.wait("vector", s_sc, sc_sg)
                guard("vector", f"aob{h}")
                rec.add("vector", lambda e, par=par, h=h: e.tensor_tensor(
                    aob_at(h), o1f[:, par, :], ggf[:, par, :], OP.mult), s_v, 1)
                if "OVD" in ctx.dbg and c == 0 and sub == 0 and h == 0:
                    rec.wait("gpsimd", s_v, s_v.n)
                    rec.add("gpsimd", lambda e, par=par: e.dma_start(
                        ctx.scr["OVD"][:, 128:256], o1f[:, par, :]), s_stg, 16)
                    rec.add("gpsimd", lambda e, par=par: e.dma_start(
                        ctx.scr["OVD"][:, 256:384], ggf[:, par, :]), s_stg, 16)
                v_ao = s_v.n
                setfree(f"ggf{par}", "v", v_ao)
                setfree(f"o1f{par}", "v", v_ao)
                setfree(f"aob_w{h}", "v", v_ao)
            for ht in range(16):
                guard("tensor", "B7")
                nm_, cnt_ = free[f"aob_w{ht}"]
                rec.wait("tensor", sems[nm_], cnt_)
                rec.add("tensor", lambda e, ht=ht: e.transpose(
                    b7[:, 256:384], aob_at(ht), P["ident_b"][:, :]), s_t, 1)
                t_tr2 = s_t.n
                setfree(f"aob{ht}", "t", t_tr2)
                rec.wait("scalar", s_t, t_tr2)
                rec.wait("scalar", s_t, sq_free_t)
                guard("scalar", f"awt{ht}")
                rec.add("scalar", lambda e, ht=ht, sub=sub: e.copy(
                    awt[:, ht, sub * 128:(sub + 1) * 128], b7[:, 256:384]), s_sc, 1)
                setfree("B7", "sc", s_sc.n)
        awt_sc = s_sc.n
        setfree("mvbufs_t", "t", s_t.n)
        setfree("gnb_sc", "sc", s_sc.n)
        last_t = 0
        end_v = 0
        for oc in range(16):
            slot = oc % 2
            lw = load_wo(oc, slot)
            bank = B[2 + slot]
            guard("tensor", f"B{2 + slot}")
            rec.wait("tensor", s_ld, lw)
            rec.wait("tensor", s_sc, awt_sc)
            rec.mm_group(bank[:, :], [(wmv[:, hv, slot * 128:(slot + 1) * 128], awt[:, hv, :])
                                      for hv in range(16)], s_t)
            last_t = s_t.n
            setfree(f"mvq{slot}", "t", last_t)
            rec.wait("vector", s_t, last_t)
            rec.wait("vector", s_ld, ld_tiles)
            rec.wait("vector", s_g, xln_g)
            guard("vector", f"xn{slot}")
            rec.add("vector", lambda e, oc=oc, b=bank, slot=slot: e.tensor_tensor(
                xn[:, slot, :], b[:, :], xt[:, oc, :], OP.add), s_v, 1)
            end_v = s_v.n
            setfree(f"B{2 + slot}", "v", end_v)
            rec.wait("gpsimd", s_v, end_v)
            rec.add("gpsimd", lambda e, oc=oc, slot=slot, c0=c0: e.dma_start(
                scr["XT"][oc, :, c0:c0 + 512], xn[:, slot, :]), s_stg, 16)
            setfree(f"xn{slot}", "stg", s_stg.n)
        setfree("xt", "v", end_v)
        setfree("xlt_w", "t", last_t)
        setfree("qt", "t", last_t)
        setfree("kt", "t", last_t)
        setfree("sqbf", "t", last_t)
        for ht in range(16):
            setfree(f"awt{ht}", "t", last_t)
    return rec

def rec_s6(ctx, sems, P):
    rec = Rec(ctx)
    s_ld, s_ldg, s_t, s_v, s_sc, s_g, s_st, s_stg = (
        sems[k] for k in ("ld", "ldg", "t", "v", "sc", "g", "st", "stg"))
    sb, psb = ctx.sb, ctx.psb
    scr = ctx.scr
    wins = ctx.wins

    xt = sb("s6_xt", [128, 16, 512], F32)
    sqbf = sb("s6_sqbf", [128, 16, 512], BF16)
    xlt = sb("s6_xlt", [128, 16, 512], BF16)
    tmpf = sb("s6_tmpf", [128, 2, 512], F32)
    rows = sb("s6_rows", [1, 4 * 512], F32)
    ab = sb("s6_ab", [128, 2 * 512], F32)
    wrt = sb("s6_wrt", [128, 16, 128], BF16)
    amp = sb("s6_amp", [128, 16], F32)
    esc = sb("s6_esc", [128, 1], F32)
    mxk = sb("s6_mxk", [128, 16], F32)
    prb = sb("s6_prb", [128, 3, 128], F32)
    mk = sb("s6_mk", [128, 128], F32)
    mskd = sb("s6_mskd", [128, 128], BF16)
    mskdT = sb("s6_mskdT", [128, 128], BF16)
    Mx = sb("s6_Mx", [128, 8], BF16)
    cbuf = sb("s6_cbuf", [128, 4, 8], F32)
    wgt = sb("s6_wgt", [128, 16, 512], BF16)
    wut = sb("s6_wut", [128, 16, 512], BF16)
    sg = sb("s6_sg", [128, 512], BF16)
    wpart = sb("s6_wpart", [128, 512], F32)
    wsum = sb("s6_wsum", [128, 4, 1024], F32)
    shb = sb("s6_shb", [128, 4, 4, 512], BF16)
    wsT = sb("s6_wsT", [128, 8, 4, 128], BF16)
    shT = sb("s6_shT", [128, 16, 4, 128], BF16)
    wdnt = sb("s6_wdnt", [128, 8, 128], BF16)
    shdt = sb("s6_shdt", [128, 16, 128], BF16)
    xn = sb("s6_xn", [128, 2, 512], F32)

    B = [psb(f"s6_b{i}", [128, 512], F32) for i in range(7)]
    b7 = psb("s6_b7", [128, 1024], BF16)

    free = {}
    def guard(engname, res):
        if res in free:
            nm, cnt = free[res]
            rec.wait(engname, sems[nm], cnt)
    def setfree(res, nm, cnt):
        free[res] = (nm, cnt)

    wr_src = ctx.wdram["RTRB"].rearrange("(dt p) o -> p dt o", p=128)
    rec.add("sync", lambda e: e.dma_start(wrt[:, :, :], wr_src), s_ld, 16)
    rec.add("gpsimd", lambda e: e.dma_start(
        amp[:, :], wins["w1_amps"].rearrange("e k two -> e (k two)")), s_ldg, 16)
    rec.add("gpsimd", lambda e: e.dma_start(
        esc[:, :], wins["w1_escale"].rearrange("(e b) -> e b", b=1)), s_ldg, 16)
    ld_const = s_ldg.n
    rec.wait("vector", s_ldg, ld_const)
    amp3 = amp.rearrange("p (k two) -> p k two", two=2)
    rec.add("vector", lambda e: e.tensor_tensor(mxk[:, 0:8], amp3[:, :, 0], amp3[:, :, 0], OP.mult), s_v, 1)
    rec.wait("vector", s_v)
    rec.add("vector", lambda e: e.tensor_tensor(prb[:, 0, 0:8], amp3[:, :, 1], amp3[:, :, 1], OP.mult), s_v, 1)
    rec.wait("vector", s_v)
    rec.add("vector", lambda e: e.tensor_tensor(mxk[:, 0:8], mxk[:, 0:8], prb[:, 0, 0:8], OP.add), s_v, 1)
    rec.wait("vector", s_v)
    rec.add("vector", lambda e: e.reduce_sum(mxk[:, 8:9], mxk[:, 0:8], axis=AX.X), s_v, 1)
    rec.wait("vector", s_v)
    rec.add("vector", lambda e: e.tensor_scalar(mxk[:, 9:10], mxk[:, 8:9], 1.0, 1e-8, OP.mult, OP.add), s_v, 1)
    rec.wait("vector", s_v)
    rec.add("vector", lambda e: e.reciprocal(mxk[:, 10:11], mxk[:, 9:10]), s_v, 1)
    rec.wait("vector", s_v)
    rec.add("vector", lambda e: e.tensor_scalar(mxk[:, 0:8], mxk[:, 0:8], mxk[:, 10:11], None, OP.mult), s_v, 1)
    rec.wait("vector", s_v)
    def mkmx(e):
        with ctx.nc.allow_low_precision(reason="expert mix as bf16 matmul operand"):
            return e.tensor_scalar(Mx[:, :], mxk[:, 0:8], esc[:, :], None, OP.mult)
    rec.add("vector", mkmx, s_v, 1)
    v_mx = s_v.n

    prev_ab = None
    for c in range(NT):
        c0 = c * 512
        guard("sync", "xt")
        for d in range(16):
            rec.add("sync", lambda e, d=d, c0=c0: e.dma_start(
                xt[:, d, :], scr["XT"][d, :, c0:c0 + 512]), s_ld, 16)
        ld_tiles = s_ld.n
        for e_ in ("vector", "scalar"):
            rec.wait(e_, s_ld, ld_tiles)
        guard("scalar", "sqbf")
        guard("tensor", "B2")
        guard("tensor", "B3")
        guard("tensor", "B4")
        guard("tensor", "B5")
        guard("gpsimd", "xlt_w")
        lnc = rec_ln_chunk(rec, sems, P, {
            "xt": xt, "xbf": None, "sqbf": sqbf,
            "xln_at": (lambda d: xlt[:, d, :]),
            "xt_ready": ("ld", ld_tiles),
            "tmpf": tmpf, "rows": rows, "ab": ab, "ab_guard": prev_ab,
            "ps_row0": B[2], "ps_row1": B[3],
            "ps_bc0": B[4], "ps_bc1": B[5]}, c, 16, 1.0 / 2048.0, 0)
        prev_ab = lnc["ab_free"]
        setfree("sqbf", "t", lnc["sqbf_free_t"])
        setfree("B2", "v", lnc["psrow_free_v"])
        setfree("B3", "v", lnc["psrow_free_v"])
        setfree("B4", "sc", lnc["psbc_free_sc"])
        setfree("B5", "sc", lnc["psbc_free_sc"])
        rec.wait("tensor", s_g, lnc["xln_done_g"])

        for sub in range(4):
            t_sl = slice(sub * 128, (sub + 1) * 128)
            guard("tensor", "B2")
            rec.wait("tensor", s_ld, s_ld.n)
            rec.wait("tensor", s_ldg, ld_const)
            rec.mm_group(B[2][:, 0:128], [(xlt[:, d, t_sl], wrt[:, d, :]) for d in range(16)], s_t)
            t_rt = s_t.n
            rec.wait("vector", s_t, t_rt)
            guard("vector", "rtcols")
            rec.add("vector", lambda e: e.reduce_max(mxk[:, 11:12], B[2][:, 0:128], axis=AX.X), s_v, 1)
            rec.wait("vector", s_v)
            rec.add("vector", lambda e: e.tensor_scalar(mxk[:, 12:13], mxk[:, 11:12], -1.0, None, OP.mult), s_v, 1)
            v_neg = s_v.n
            rec.wait("scalar", s_v, v_neg)
            guard("scalar", "expb")
            rec.add("scalar", lambda e: e.activation(
                prb[:, 0, :], B[2][:, 0:128], AF.Exp, bias=mxk[:, 12:13],
                accum_out=mxk[:, 13:14]), s_sc, 1)
            sc_exp = s_sc.n
            setfree("B2", "sc", sc_exp)
            rec.wait("vector", s_sc, sc_exp)
            rec.add("vector", lambda e: e.reciprocal(mxk[:, 14:15], mxk[:, 13:14]), s_v, 1)
            rec.wait("vector", s_v)
            rec.add("vector", lambda e: e.tensor_scalar(
                prb[:, 0, :], prb[:, 0, :], mxk[:, 14:15], None, OP.mult), s_v, 1)
            v_probs = s_v.n
            cur = 0
            for it in range(3):
                rec.wait("vector", s_v)
                rec.add("vector", lambda e, cur=cur: e.reduce_max(
                    mxk[:, 11:12], prb[:, cur, :], axis=AX.X), s_v, 1)
                rec.wait("vector", s_v)
                rec.add("vector", lambda e, cur=cur: e.tensor_scalar(
                    mk[:, :], prb[:, cur, :], mxk[:, 11:12], None, OP.is_ge), s_v, 1)
                rec.wait("vector", s_v)
                nxt = 1 + (it % 2)
                rec.add("vector", lambda e, cur=cur, nxt=nxt: e.scalar_tensor_tensor(
                    prb[:, nxt, :], mk[:, :], -2.0, prb[:, cur, :], OP.mult, OP.add), s_v, 1)
                cur = nxt
            rec.wait("vector", s_v)
            rec.add("vector", lambda e, cur=cur: e.reduce_max(
                mxk[:, 11:12], prb[:, cur, :], axis=AX.X), s_v, 1)
            rec.wait("vector", s_v)
            guard("vector", "mskd")
            rec.add("vector", lambda e: e.scalar_tensor_tensor(
                mskd[:, :], prb[:, 0, :], mxk[:, 11:12], prb[:, 0, :],
                OP.is_ge, OP.mult, accum_out=mxk[:, 15:16]), s_v, 1)
            v_mskd = s_v.n
            rec.wait("vector", s_v)
            rec.add("vector", lambda e: e.tensor_scalar(
                mxk[:, 13:14], mxk[:, 15:16], 1.0, 1e-8, OP.mult, OP.add), s_v, 1)
            rec.wait("vector", s_v)
            rec.add("vector", lambda e: e.reciprocal(mxk[:, 14:15], mxk[:, 13:14]), s_v, 1)
            v_rcp = s_v.n
            guard("tensor", "B7")
            rec.wait("tensor", s_v, v_mskd)
            rec.add("tensor", lambda e: e.transpose(
                b7[:, 0:128], mskd[:, :], P["ident_b"][:, :]), s_t, 1)
            t_tm = s_t.n
            setfree("mskd", "t", t_tm)
            rec.wait("scalar", s_t, t_tm)
            guard("scalar", "mskdT")
            rec.add("scalar", lambda e: e.copy(mskdT[:, :], b7[:, 0:128]), s_sc, 1)
            sc_tm = s_sc.n
            setfree("B7", "sc", sc_tm)
            guard("tensor", "B3")
            rec.wait("tensor", s_sc, sc_tm)
            rec.wait("tensor", s_v, v_mx)
            rec.add("tensor", lambda e: e.matmul(B[3][:, 0:8], mskdT[:, :], Mx[:, :],
                                                 start=True, stop=True), s_t, 1)
            t_c = s_t.n
            setfree("mskdT", "t", t_c)
            rec.wait("vector", s_t, t_c)
            rec.wait("vector", s_v, v_rcp)
            guard("vector", f"cbuf{sub}")
            rec.add("vector", lambda e, sub=sub: e.tensor_scalar(
                cbuf[:, sub, :], B[3][:, 0:8], mxk[:, 14:15], None, OP.mult), s_v, 1)
            v_c = s_v.n
            setfree("B3", "v", v_c)
            setfree("rtcols", "v", v_c)
            setfree("expb", "v", v_c)

        if "CBD" in ctx.dbg:
            for sub in range(4):
                rec.wait("gpsimd", s_v, s_v.n)
                rec.add("gpsimd", lambda e, c0=c0, sub=sub: e.dma_start(
                    ctx.scr["CBD"][c0 + sub * 128:c0 + (sub + 1) * 128, :], cbuf[:, sub, :]), s_stg, 16)
        for oc in range(16):
            kk = oc // 2
            half = oc % 2
            guard("sync", "wgt")
            wg_src = ctx.wdram["GATB"].rearrange("(dt p) o -> p dt o", p=128)[:, :, oc * 512:(oc + 1) * 512]
            wu_src = ctx.wdram["UPB"].rearrange("(dt p) o -> p dt o", p=128)[:, :, oc * 512:(oc + 1) * 512]
            rec.add("sync", lambda e, src=wg_src: e.dma_start(wgt[:, :, :], src), s_ld, 16)
            rec.add("sync", lambda e, src=wu_src: e.dma_start(wut[:, :, :], src), s_ld, 16)
            ldw = s_ld.n
            for sub in range(4):
                t_sl = slice(sub * 128, (sub + 1) * 128)
                guard("tensor", "B0")
                rec.wait("tensor", s_ld, ldw)
                rec.mm_group(B[0][:, :], [(xlt[:, d, t_sl], wgt[:, d, :]) for d in range(16)], s_t)
                t_gt = s_t.n
                guard("tensor", "B1")
                rec.mm_group(B[1][:, :], [(xlt[:, d, t_sl], wut[:, d, :]) for d in range(16)], s_t)
                t_up = s_t.n
                rec.wait("scalar", s_t, t_gt)
                guard("scalar", "sg")
                rec.add("scalar", lambda e: e.activation(sg[:, :], B[0][:, :], AF.Silu), s_sc, 1)
                sc_sg = s_sc.n
                setfree("B0", "sc", sc_sg)
                rec.wait("vector", s_sc, sc_sg)
                rec.wait("vector", s_t, t_up)
                rec.wait("vector", s_v, v_c if sub == 3 else s_v.n)
                if kk == 0:
                    guard("vector", f"wsumw{sub}{half}")
                    rec.add("vector", lambda e, sub=sub, half=half, kk=kk: e.scalar_tensor_tensor(
                        wsum[:, sub, half * 512:(half + 1) * 512], B[1][:, :],
                        cbuf[:, sub, kk:kk + 1], sg[:, :], OP.mult, OP.mult), s_v, 1)
                else:
                    rec.add("vector", lambda e, sub=sub, kk=kk: e.scalar_tensor_tensor(
                        wpart[:, :], B[1][:, :], cbuf[:, sub, kk:kk + 1], sg[:, :],
                        OP.mult, OP.mult), s_v, 1)
                    rec.wait("vector", s_v)
                    rec.add("vector", lambda e, sub=sub, half=half: e.tensor_tensor(
                        wsum[:, sub, half * 512:(half + 1) * 512],
                        wsum[:, sub, half * 512:(half + 1) * 512], wpart[:, :], OP.add), s_v, 1)
                v_ws = s_v.n
                setfree("B1", "v", v_ws)
                setfree("sg", "v", v_ws)
            setfree("wgt", "t", s_t.n)
        v_wsum_done = s_v.n

        for sc_ch in range(4):
            shg_src = ctx.wdram["SHGB"].rearrange("(dt p) o -> p dt o", p=128)[:, :, sc_ch * 512:(sc_ch + 1) * 512]
            shu_src = ctx.wdram["SHUB"].rearrange("(dt p) o -> p dt o", p=128)[:, :, sc_ch * 512:(sc_ch + 1) * 512]
            guard("sync", "wgt")
            rec.add("sync", lambda e, src=shg_src: e.dma_start(wgt[:, :, :], src), s_ld, 16)
            rec.add("sync", lambda e, src=shu_src: e.dma_start(wut[:, :, :], src), s_ld, 16)
            ldw = s_ld.n
            for sub in range(4):
                t_sl = slice(sub * 128, (sub + 1) * 128)
                guard("tensor", "B4")
                rec.wait("tensor", s_ld, ldw)
                rec.mm_group(B[4][:, :], [(xlt[:, d, t_sl], wgt[:, d, :]) for d in range(16)], s_t)
                t_gt = s_t.n
                guard("tensor", "B5")
                rec.mm_group(B[5][:, :], [(xlt[:, d, t_sl], wut[:, d, :]) for d in range(16)], s_t)
                t_up = s_t.n
                rec.wait("scalar", s_t, t_gt)
                guard("scalar", "sg")
                rec.add("scalar", lambda e: e.activation(sg[:, :], B[4][:, :], AF.Silu), s_sc, 1)
                sc_sg2 = s_sc.n
                setfree("B4", "sc", sc_sg2)
                rec.wait("vector", s_sc, sc_sg2)
                rec.wait("vector", s_t, t_up)
                guard("vector", f"shb{sub}{sc_ch}")
                rec.add("vector", lambda e, sub=sub, sc_ch=sc_ch: e.tensor_tensor(
                    shb[:, sub, sc_ch, :], B[5][:, :], sg[:, :], OP.mult), s_v, 1)
                v_sh = s_v.n
                setfree("B5", "v", v_sh)
                setfree("sg", "v", v_sh)
            setfree("wgt", "t", s_t.n)
        v_sh_done = s_v.n

        if "WSD" in ctx.dbg:
            for sub in range(4):
                rec.wait("gpsimd", s_v, v_wsum_done)
                rec.add("gpsimd", lambda e, c0=c0, sub=sub: e.dma_start(
                    ctx.scr["WSD"][c0 + sub * 128:c0 + (sub + 1) * 128, :], wsum[:, sub, :]), s_stg, 16)
        if "SHBD" in ctx.dbg:
            for sub in range(4):
                rec.wait("gpsimd", s_v, v_sh_done)
                for j in range(4):
                    rec.add("gpsimd", lambda e, c0=c0, sub=sub, j=j: e.dma_start(
                        ctx.scr["SHBD"][c0 + sub * 128:c0 + (sub + 1) * 128, j * 512:(j + 1) * 512],
                        shb[:, sub, j, :]), s_stg, 16)
        for sub in range(4):
            for i8 in range(8):
                guard("tensor", "B6")
                rec.wait("vector", s_t, 0)
                rec.wait("tensor", s_v, v_wsum_done)
                rec.add("tensor", lambda e, sub=sub, i8=i8: e.transpose(
                    B[6][:, 0:128], wsum[:, sub, i8 * 128:(i8 + 1) * 128],
                    P["ident_f"][:, :]), s_t, 1)
                t_tr = s_t.n
                rec.wait("scalar", s_t, t_tr)
                guard("scalar", f"wsT{sub}{i8}")
                rec.add("scalar", lambda e, sub=sub, i8=i8: e.copy(
                    wsT[:, i8, sub, :], B[6][:, 0:128]), s_sc, 1)
                setfree("B6", "sc", s_sc.n)
            for i16 in range(16):
                guard("tensor", "B7")
                rec.wait("tensor", s_v, v_sh_done)
                rec.add("tensor", lambda e, sub=sub, i16=i16: e.transpose(
                    b7[:, 0:128], shb[:, sub, i16 // 4, (i16 % 4) * 128:(i16 % 4) * 128 + 128],
                    P["ident_b"][:, :]), s_t, 1)
                t_tr = s_t.n
                rec.wait("scalar", s_t, t_tr)
                guard("scalar", f"shT{sub}{i16}")
                rec.add("scalar", lambda e, sub=sub, i16=i16: e.copy(
                    shT[:, i16, sub, :], b7[:, 0:128]), s_sc, 1)
                setfree("B7", "sc", s_sc.n)
        sc_tr_done = s_sc.n
        setfree("wsum_w", "t", s_t.n)

        for doc in range(16):
            wd_src = ctx.wdram["DWNB"].rearrange("(dt p) o -> p dt o", p=128)[:, :, doc * 128:(doc + 1) * 128]
            sd_src = ctx.wdram["SHDB"].rearrange("(dt p) o -> p dt o", p=128)[:, :, doc * 128:(doc + 1) * 128]
            guard("sync", "wdnt")
            rec.add("sync", lambda e, src=wd_src: e.dma_start(wdnt[:, :, :], src), s_ld, 16)
            rec.add("sync", lambda e, src=sd_src: e.dma_start(shdt[:, :, :], src), s_ld, 16)
            ldw = s_ld.n
            par = doc % 2
            bank = B[par]
            guard("tensor", f"B{par}")
            rec.wait("tensor", s_ld, ldw)
            rec.wait("tensor", s_sc, sc_tr_done)
            def outmm2(e, doc=doc, bank=bank):
                for i8 in range(8):
                    e.matmul(bank[:, :], wdnt[:, i8, :],
                             wsT[:, i8, :, :].rearrange("p a b -> p (a b)"),
                             start=(i8 == 0), stop=False)
                last = None
                for i16 in range(16):
                    last = e.matmul(bank[:, :], shdt[:, i16, :],
                                    shT[:, i16, :, :].rearrange("p a b -> p (a b)"),
                                    start=False, stop=(i16 == 15))
                return last
            rec.add("tensor", outmm2, s_t, 1)
            t_out = s_t.n
            setfree("wdnt", "t", t_out)
            rec.wait("vector", s_t, t_out)
            rec.wait("vector", s_ld, ld_tiles)
            guard("vector", f"xn{par}")
            rec.add("vector", lambda e, doc=doc, bank=bank, par=par: e.tensor_tensor(
                xn[:, par, :], bank[:, :], xt[:, doc, :], OP.add), s_v, 1)
            v_out = s_v.n
            setfree(f"B{par}", "v", v_out)
            rec.wait("gpsimd", s_v, v_out)
            rec.add("gpsimd", lambda e, doc=doc, par=par, c0=c0: e.dma_start(
                scr["XT"][doc, :, c0:c0 + 512], xn[:, par, :]), s_stg, 16)
            setfree(f"xn{par}", "stg", s_stg.n)
        end_t = s_t.n
        end_v = s_v.n
        setfree("xt", "v", end_v)
        setfree("xlt_w", "t", end_t)
        for sub in range(4):
            setfree(f"cbuf{sub}", "v", end_v)
            for i8 in range(8):
                setfree(f"wsT{sub}{i8}", "t", end_t)
            for i16 in range(16):
                setfree(f"shT{sub}{i16}", "t", end_t)
            for h_ in range(2):
                setfree(f"wsumw{sub}{h_}", "t", end_t)
            for scch in range(4):
                setfree(f"shb{sub}{scch}", "t", end_t)
    return rec


SCRATCH = {
    "XT":   ([16, 128, T], F32),
    "QAR":  ([12, 128, T], BF16),
    "CKVR": ([4, 128, T], BF16),
    "KPEA": ([65, T], BF16),
    "QNT":  ([16, 128, T], BF16),
    "QPE":  ([8, 128, T], BF16),
    "KNT":  ([16, 128, T], BF16),
    "VNT":  ([T, 2048], BF16),
    "AOT":  ([16, 128, T], BF16),
    "HD":   ([64, 128, T], BF16),
    "XLD":  ([16, 128, T], BF16),
    "AOD":  ([T, 2048], BF16),
    "ED":   ([T, 1024], BF16),
    "RD":   ([T, 1024], BF16),
    "QKD":  ([2, 1024, T], BF16),
    "PVD":  ([T, 1024], BF16),
    "GBD":  ([128, 128], BF16),
    "VGD":  ([2, 128, 2048], BF16),
    "OVD":  ([128, 384], F32),
    "CBD":  ([T, 8], F32),
    "WSD":  ([T, 1024], F32),
    "SHBD": ([T, 2048], BF16),
    "TTD":  ([64, 256], BF16),
}
WSCRATCH = {
    "QAB": [1536, 2048], "QBB": [3072, 1536], "KVAB": [576, 2048],
    "KVBB": [4096, 512], "OWB": [2048, 2048],
    "W1B": [8192, 2048], "W3B": [8192, 2048],
    "W2BF": [2048, 8192], "W2T": [64, 128, 2048],
    "Q1B": [2048, 1024], "K1B": [2048, 1024], "V1B": [2048, 2048],
    "G1B": [2048, 2048], "S1B": [2048, 1024], "O1B": [2048, 2048],
    "RTRB": [2048, 128], "GATB": [2048, 8192], "UPB": [2048, 8192],
    "DWNB": [1024, 2048], "SHGB": [2048, 2048], "SHUB": [2048, 2048],
    "SHDB": [2048, 2048],
}
WEIGHT_INPUTS = {
    "w_qa": [1536, 2048], "w_qa_ln": [1536], "w_qb": [3072, 1536],
    "w_kva": [576, 2048], "w_kva_ln": [512], "w_kvb": [4096, 512],
    "w_ow": [2048, 2048], "w_W1": [8192, 2048], "w_W2": [2048, 8192],
    "w_W3": [8192, 2048],
    "w1_q": [2048, 1024], "w1_k": [2048, 1024], "w1_v": [2048, 2048],
    "w1_g": [2048, 2048], "w1_s": [2048, 1024], "w1_o": [2048, 2048],
    "w1_gn": [128], "w1_router": [2048, 128], "w1_gate": [2048, 8192],
    "w1_up": [2048, 8192], "w1_down": [1024, 2048], "w1_amps": [128, 8, 2],
    "w1_escale": [128], "w1_shg": [2048, 2048], "w1_shu": [2048, 2048],
    "w1_shd": [2048, 2048],
}


def build_program(upto=99, dbg=(), skip=(), **bass_kw):
    nc = bass.Bass("TRN2", target_bir_lowering=False, **bass_kw)
    es = ExitStack()
    nc._es_keepalive = es

    x_in = nc.dram_tensor("x_in", [T, D], F32, kind="ExternalInput").ap()
    y_out = nc.dram_tensor("y_out", [T, D], F32, kind="ExternalOutput").ap()
    consts = {}
    for name, arr in host_constants().items():
        consts[name] = nc.dram_tensor(name, list(arr.shape), F32, kind="ExternalInput").ap()
    wins = {n: nc.dram_tensor(n, shp, F32, kind="ExternalInput").ap()
            for n, shp in WEIGHT_INPUTS.items()}
    scr = {}
    for n, (shp, dt) in SCRATCH.items():
        kind = "ExternalOutput" if n in dbg else "Internal"
        scr[n] = nc.dram_tensor(n, shp, dt, kind=kind).ap()
    wdram = {n: nc.dram_tensor("ws_" + n, shp, BF16,
                               kind=("ExternalOutput" if n in dbg else "Internal")).ap()
             for n, shp in WSCRATCH.items()}

    ctx = Ctx(nc, es)
    ctx.cident = consts["c_ident"]
    ctx.consts = consts
    ctx.wins = wins
    ctx.scr = scr
    ctx.wdram = wdram
    ctx.dbg = dbg

    sems = {k: ctx.sem("s_" + k) for k in ("ld", "ld2", "ldg", "t", "v", "sc", "g", "st", "stg")}
    track = list(sems.values()) + [ctx.s_bar]

    P = {}
    P["ident_f"] = es.enter_context(nc.sbuf_tensor("P_ident_f", [128, 128], F32))
    P["ident_b"] = es.enter_context(nc.sbuf_tensor("P_ident_b", [128, 128], BF16))
    P["uincl_f"] = es.enter_context(nc.sbuf_tensor("P_uincl_f", [128, 128], F32))
    P["uincl_b"] = es.enter_context(nc.sbuf_tensor("P_uincl_b", [128, 128], BF16))
    P["maskneg"] = es.enter_context(nc.sbuf_tensor("P_maskneg", [128, 128], F32))
    P["h128s_b"] = es.enter_context(nc.sbuf_tensor("P_h128s_b", [128, 128], BF16))
    P["rrot_b"] = es.enter_context(nc.sbuf_tensor("P_rrot_b", [128, 128], BF16))
    P["ones_col_b"] = es.enter_context(nc.sbuf_tensor("P_ones_col_b", [128, 1], BF16))
    P["ones_row_f"] = es.enter_context(nc.sbuf_tensor("P_ones_row_f", [1, 128], F32))
    P["rowT_b"] = es.enter_context(nc.sbuf_tensor("P_rowT_b", [1, 512], BF16))
    P["eps"] = es.enter_context(nc.sbuf_tensor("P_eps", [128, 4], F32))
    P["ovd_sb"] = es.enter_context(nc.sbuf_tensor("P_ovd_sb", [128, 128], F32))
    P["ones_col_f"] = es.enter_context(nc.sbuf_tensor("P_ones_col_f", [128, 1], F32))
    P["ones_row_b"] = es.enter_context(nc.sbuf_tensor("P_ones_row_b", [1, 128], BF16))
    P["ones_1b"] = es.enter_context(nc.sbuf_tensor("P_ones_1b", [1, 1], BF16))
    ctx.P = P

    stage_es = [None]

    def open_stage():
        stage_es[0] = ExitStack()

    def close_stage():
        stage_es[0].close()

    ctx.sb = lambda n, shp, dt: stage_es[0].enter_context(nc.sbuf_tensor(n, shp, dt))
    ctx.psb = lambda n, shp, dt=F32: stage_es[0].enter_context(nc.psum_tensor(n, shp, dt))

    open_stage()
    run_rec_stage(ctx, "prep", rec_prep(ctx, sems, P), track)
    close_stage()

    open_stage()
    st_sb = lambda n, shp, dt: stage_es[0].enter_context(nc.sbuf_tensor(n, shp, dt))
    st_ps = lambda n, shp, dt: stage_es[0].enter_context(nc.psum_tensor(n, shp, dt))
    run_stage(ctx, "import", emit_import(ctx, sems, x_in, scr["XT"], st_sb, st_ps), track)
    close_stage()

    if upto >= 1:
        if "s1a" not in skip:
            open_stage()
            run_rec_stage(ctx, "s1a", rec_s1a(ctx, sems, P), track)
            close_stage()

    if upto >= 2:
        if "s1b" not in skip:
            open_stage()
            run_rec_stage(ctx, "s1b", rec_s1b(ctx, sems, P), track)
            close_stage()

    if upto >= 3:
        if "s2" not in skip:
            open_stage()
            rec2 = rec_s2(ctx, sems, P)
            if upto >= 5 and "s4w2prep" not in skip:
                rec_s4w2prep(ctx, sems, P, rec=rec2)
            run_rec_stage(ctx, "s2", rec2, track)
            close_stage()

    if upto >= 4:
        if "s3" not in skip:
            open_stage()
            run_rec_stage(ctx, "s3", rec_s3(ctx, sems, P), track)
            close_stage()

    if upto >= 5:
        if "s4w2prep" not in skip and ("s2" in skip or upto < 3):
            open_stage()
            run_rec_stage(ctx, "s4w2prep", rec_s4w2prep(ctx, sems, P), track)
            close_stage()
        grp_es = ExitStack()
        XL2T = grp_es.enter_context(nc.sbuf_tensor("XL2T", [128, 16, NT, 512], BF16))
        if "s4ln" not in skip:
            open_stage()
            run_rec_stage(ctx, "s4ln", rec_s4ln(ctx, sems, P, XL2T), track)
            close_stage()
        if "XLD" in dbg:
            open_stage()
            run_rec_stage(ctx, "dumpxl", rec_dumpxl(ctx, sems, XL2T), track)
            close_stage()
        if "s4uv" not in skip:
            open_stage()
            run_rec_stage(ctx, "s4uv", rec_s4uv(ctx, sems, P, XL2T), track)
            close_stage()
        grp_es.close()
        if "s4w2" not in skip:
            open_stage()
            run_rec_stage(ctx, "s4w2", rec_s4w2(ctx, sems, P), track)
            close_stage()

    if upto >= 6 and "s5" not in skip:
        open_stage()
        run_rec_stage(ctx, "s5", rec_s5(ctx, sems, P), track)
        close_stage()

    if upto >= 7 and "s6" not in skip:
        open_stage()
        run_rec_stage(ctx, "s6", rec_s6(ctx, sems, P), track)
        close_stage()


    open_stage()
    st_sb = lambda n, shp, dt: stage_es[0].enter_context(nc.sbuf_tensor(n, shp, dt))
    st_ps = lambda n, shp, dt: stage_es[0].enter_context(nc.psum_tensor(n, shp, dt))
    run_stage(ctx, "export", emit_export(ctx, sems, scr["XT"], y_out, st_sb, st_ps), track)
    close_stage()

    with nc.Block() as block:
        @block.sync
        def _(s):
            s.wait_ge(ctx.s_bar.h, ctx.s_bar.n)
            s.wait_ge(sems["st"].h, sems["st"].n)
            s.wait_ge(sems["stg"].h, sems["stg"].n)
    return nc


def host_weight_maps(inputs):
    H, NOPE, ROPE, VD = 16, 128, 64, 128
    qb = np.ascontiguousarray(inputs["L0_attn_q_b_w"])
    idx_nope = np.concatenate([np.arange(h * 192, h * 192 + 128) for h in range(H)])
    idx_pe = np.concatenate([np.arange(h * 192 + 128, h * 192 + 192) for h in range(H)])
    qb_r = np.ascontiguousarray(qb[np.concatenate([idx_nope, idx_pe])])
    kvb = np.ascontiguousarray(inputs["L0_attn_kv_b_w"])
    idx_k = np.concatenate([np.arange(h * 256, h * 256 + 128) for h in range(H)])
    idx_v = np.concatenate([np.arange(h * 256 + 128, h * 256 + 256) for h in range(H)])
    kvb_r = np.ascontiguousarray(kvb[np.concatenate([idx_k, idx_v])])
    return {
        "w_qa": inputs["L0_attn_q_a_w"],
        "w_qa_ln": inputs["L0_attn_q_a_ln_w"],
        "w_qb": qb_r,
        "w_kva": inputs["L0_attn_kv_a_w"],
        "w_kva_ln": inputs["L0_attn_kv_a_ln_w"],
        "w_kvb": kvb_r,
        "w_ow": inputs["L0_attn_o_w"],
        "w_W1": inputs["L0_ffn_W1"],
        "w_W2": inputs["L0_ffn_W2"],
        "w_W3": inputs["L0_ffn_W3"],
        "w1_q": inputs["L1_attn_Wq"],
        "w1_k": inputs["L1_attn_Wk"],
        "w1_v": inputs["L1_attn_Wv"],
        "w1_g": inputs["L1_attn_Wg"],
        "w1_s": inputs["L1_attn_Ws"],
        "w1_o": inputs["L1_attn_Wo"],
        "w1_gn": inputs["L1_attn_g_norm_weight"],
        "w1_router": inputs["L1_ffn_w_router"],
        "w1_gate": inputs["L1_ffn_w_gate"],
        "w1_up": inputs["L1_ffn_w_up"],
        "w1_down": inputs["L1_ffn_w_down"],
        "w1_amps": inputs["L1_ffn_amplitudes"],
        "w1_escale": inputs["L1_ffn_expert_scale"],
        "w1_shg": inputs["L1_ffn_shared_w_gate"],
        "w1_shu": inputs["L1_ffn_shared_w_up"],
        "w1_shd": inputs["L1_ffn_shared_w_down"],
    }


def kernel(**inputs):
    x = np.ascontiguousarray(inputs["L0_attn_hidden_states"])
    nc = build_program()
    consts = host_constants()
    wmap = {k: np.ascontiguousarray(v, dtype=np.float32) for k, v in host_weight_maps(inputs).items()}
    in_maps = []
    for c in range(8):
        m = {"x_in": np.ascontiguousarray(x[c])}
        m.update(consts)
        m.update(wmap)
        in_maps.append(m)
    res = run_bass_kernel_spmd(nc, in_maps, core_ids=list(range(8)))
    out = np.stack([res.results[c]["y_out"] for c in range(8)], axis=0)
    return out


# version 06 (final)
# speedup vs baseline: 1.1651x; 1.0069x over previous
import numpy as np
from contextlib import ExitStack

import concourse.bass as bass
import concourse.mybir as mybir
from concourse.alu_op_type import AluOpType as OP
from concourse.bass_utils import run_bass_kernel_spmd

F32 = mybir.dt.float32
BF16 = mybir.dt.bfloat16
AF = mybir.ActivationFunctionType
AX = mybir.AxisListType

T = 4096
D = 2048
NT = T // 512
ENGINES = ("tensor", "vector", "scalar", "sync", "gpsimd")


def host_constants():
    c = {}
    c["c_ident"] = np.eye(128, dtype=np.float32)
    c["c_ident_bf"] = np.eye(128, dtype=np.float32)
    ROPE = 64
    inv_freq = 1.0 / (10000.0 ** (np.arange(0, ROPE, 2, dtype=np.float64) / ROPE))
    t = np.arange(T, dtype=np.float64)
    freqs = np.outer(t, inv_freq)
    emb = np.concatenate([freqs, freqs], axis=-1)
    cosT = np.cos(emb).T.astype(np.float32)
    sinT = np.sin(emb).T.astype(np.float32)
    c["c_cos2"] = np.concatenate([cosT, cosT], axis=0)
    c["c_sin2"] = np.concatenate([sinT, sinT], axis=0)
    R64 = np.zeros((64, 64), dtype=np.float32)
    for p in range(32):
        R64[p, p + 32] = -1.0
        R64[p + 32, p] = 1.0
    R128 = np.zeros((128, 128), dtype=np.float32)
    R128[:64, :64] = R64
    R128[64:, 64:] = R64
    c["c_rrot_lhsT"] = R128.T.copy()
    c["c_uincl"] = np.triu(np.ones((128, 128), dtype=np.float32))
    m = np.zeros((128, 128), dtype=np.float32)
    m[np.tril_indices(128, -1)] = -1e30
    c["c_maskneg"] = m
    H = np.array([[1.0]])
    while H.shape[0] < 128:
        H = np.block([[H, H], [H, -H]])
    c["c_h128s"] = (H * (8192.0 ** -0.5)).astype(np.float32)
    c["c_ones_col"] = np.ones((128, 1), dtype=np.float32)
    c["c_ones_row"] = np.ones((1, 128), dtype=np.float32)
    return c


class Sem:
    __slots__ = ("h", "n", "floor")

    def __init__(self, h):
        self.h = h
        self.n = 0
        self.floor = 0


class Ctx:
    def __init__(self, nc, es):
        self.nc = nc
        self.es = es
        self.waited = {}
        self.fences = {}
        self.stage_idx = 0
        self.s_bar = self.sem("s_bar")
        self.sems = {}

    def sem(self, name):
        return Sem(self.es.enter_context(self.nc.semaphore(name)))

    def inc(self, engname, instr, s, k=1):
        instr.then_inc(s.h, k)
        s.n += k
        self.fences.setdefault(engname, {})[id(s)] = (s, s.n)
        return s.n

    def wait(self, eng, engname, s, val=None):
        if val is None:
            val = s.n
        if val <= s.floor:
            return
        w = self.waited.setdefault(engname, {})
        if w.get(id(s), -1) < val:
            eng.wait_ge(s.h, val)
            w[id(s)] = val


class Rec:

    def __init__(self, ctx):
        self.ctx = ctx
        self.ops = {e: [] for e in ENGINES}
        self.waited = {e: {} for e in ENGINES}
        self.fences = {e: {} for e in ENGINES}

    def wait(self, engname, s, val=None):
        if val is None:
            val = s.n
        assert val <= s.n, "wait on a not-yet-recorded increment (deadlock)"
        if val <= s.floor:
            return
        w = self.waited[engname]
        if w.get(id(s), -1) >= val:
            return
        w[id(s)] = val
        self.ops[engname].append(lambda eng, h=s.h, v=val: eng.wait_ge(h, v))

    def add(self, engname, fn, sem=None, k=1):
        if sem is not None:
            sem.n += k
            self.fences[engname][id(sem)] = (sem, sem.n)
            def thunk(eng, fn=fn, h=sem.h, kk=k):
                fn(eng).then_inc(h, kk)
            self.ops[engname].append(thunk)
            return sem.n
        self.ops[engname].append(lambda eng, fn=fn: fn(eng))
        return None

    def mm_group(self, psum_ap, pairs, sem):
        n = len(pairs)
        def fn(eng, pairs=tuple(pairs), psum_ap=psum_ap, n=n):
            for i, (l, r) in enumerate(pairs[:-1]):
                eng.matmul(psum_ap, l, r, start=(i == 0), stop=False)
            l, r = pairs[-1]
            return eng.matmul(psum_ap, l, r, start=(n == 1), stop=True)
        return self.add("tensor", fn, sem, 1)

    def as_emitters(self):
        def mk(nm):
            def f(eng):
                for th in self.ops[nm]:
                    th(eng)
            return f
        return {nm: mk(nm) for nm in ENGINES if self.ops[nm]}


def run_rec_stage(ctx, name, rec, track_sems):
    idx = ctx.stage_idx
    base = 5 * idx
    with ctx.nc.Block() as block:
        for engname in ENGINES:
            def make(nm):
                def f(eng):
                    if idx > 0:
                        eng.wait_ge(ctx.s_bar.h, base)
                    for th in rec.ops[nm]:
                        th(eng)
                    for s, cnt in rec.fences[nm].values():
                        w = rec.waited[nm]
                        if w.get(id(s), -1) < cnt:
                            eng.wait_ge(s.h, cnt)
                            w[id(s)] = cnt
                    eng.nop().then_inc(ctx.s_bar.h, 1)
                return f
            getattr(block, engname)(make(engname))
    ctx.s_bar.n += 5
    ctx.stage_idx += 1
    for s in track_sems:
        s.floor = s.n


def run_stage(ctx, name, emitters, track_sems):
    idx = ctx.stage_idx
    base = 5 * idx
    with ctx.nc.Block() as block:
        for engname in ENGINES:
            def make(nm):
                def f(eng):
                    if idx > 0:
                        eng.wait_ge(ctx.s_bar.h, base)
                    ctx.waited[nm] = {}
                    ctx.fences[nm] = {}
                    em = emitters.get(nm)
                    if em:
                        em(eng)
                    for s, cnt in ctx.fences[nm].values():
                        ctx.wait(eng, nm, s, cnt)
                    eng.nop().then_inc(ctx.s_bar.h, 1)
                return f
            getattr(block, engname)(make(engname))
    ctx.s_bar.n += 5
    ctx.stage_idx += 1
    for s in track_sems:
        s.floor = s.n


def emit_import(ctx, sems, x_in, XT, sb, psb):
    nc = ctx.nc
    s_ld, s_t, s_sc, s_st = sems["ld"], sems["t"], sems["sc"], sems["st"]
    xa = sb("s0_xa", [128, 4 * 2048], F32)
    ob = sb("s0_ob", [128, 16 * 512], F32)
    ident = sb("s0_id", [128, 128], F32)
    pss = [psb(f"s0_ps{i}", [128, 512], F32) for i in range(4)]

    n_chunk = NT
    b_ld, b_t, b_sc, b_st = s_ld.n, s_t.n, s_sc.n, s_st.n

    def em_sync(s):
        ctx.inc("sync", s.dma_start(ident[:, :], ctx.cident), s_ld, 16)
        for c in range(n_chunk):
            if c >= 1:
                ctx.wait(s, "sync", s_t, b_t + c * 16)
            for k in range(4):
                t0 = c * 512 + k * 128
                ctx.inc("sync", s.dma_start(xa[:, k * 2048:(k + 1) * 2048],
                                            x_in[t0:t0 + 128, :]), s_ld, 16)

    def em_tensor(t):
        ctx.wait(t, "tensor", s_ld, b_ld + 16)
        for c in range(n_chunk):
            ctx.wait(t, "tensor", s_ld, b_ld + 16 + (c + 1) * 64)
            for d in range(16):
                bank = pss[d % 4]
                if c * 16 + d >= 4:
                    ctx.wait(t, "tensor", s_sc, b_sc + c * 16 + d - 3)
                for k in range(4):
                    ins = t.transpose(bank[:, k * 128:(k + 1) * 128],
                                      xa[:, k * 2048 + d * 128: k * 2048 + (d + 1) * 128],
                                      ident[:, :])
                ctx.inc("tensor", ins, s_t, 1)

    def em_scalar(sc):
        for c in range(n_chunk):
            if c >= 1:
                ctx.wait(sc, "scalar", s_st, b_st + c * 16 * 16)
            for d in range(16):
                bank = pss[d % 4]
                ctx.wait(sc, "scalar", s_t, b_t + c * 16 + d + 1)
                ctx.inc("scalar", sc.copy(ob[:, d * 512:(d + 1) * 512], bank[:, :]), s_sc, 1)
                ctx.wait(sc, "scalar", s_sc, s_sc.n)
                ctx.inc("scalar", sc.dma_start(XT[d, :, c * 512:(c + 1) * 512],
                                               ob[:, d * 512:(d + 1) * 512]), s_st, 16)

    return {"sync": em_sync, "tensor": em_tensor, "scalar": em_scalar}


def emit_export(ctx, sems, XT, y_out, sb, psb):
    s_ld, s_t, s_sc, s_st = sems["ld"], sems["t"], sems["sc"], sems["st"]
    xa = sb("s7_xa", [128, 16 * 512], F32)
    ob = sb("s7_ob", [128, 2048], F32)
    ident = sb("s7_id", [128, 128], F32)
    pss = [psb(f"s7_ps{i}", [128, 512], F32) for i in range(4)]
    b_ld, b_t, b_sc, b_st = s_ld.n, s_t.n, s_sc.n, s_st.n

    def em_sync(s):
        ctx.inc("sync", s.dma_start(ident[:, :], ctx.cident), s_ld, 16)
        for c in range(NT):
            if c >= 1:
                ctx.wait(s, "sync", s_t, b_t + c * 4)
            for d in range(16):
                ctx.inc("sync", s.dma_start(xa[:, d * 512:(d + 1) * 512],
                                            XT[d, :, c * 512:(c + 1) * 512]), s_ld, 16)

    def em_tensor(t):
        ctx.wait(t, "tensor", s_ld, b_ld + 16)
        for c in range(NT):
            ctx.wait(t, "tensor", s_ld, b_ld + 16 + (c + 1) * 256)
            for k in range(4):
                for grp in range(4):
                    bank = pss[grp]
                    if c * 4 + k >= 1:
                        ctx.wait(t, "tensor", s_sc, b_sc + (c * 4 + k - 1) * 4 + grp + 1)
                    for dd in range(4):
                        d = grp * 4 + dd
                        ins = t.transpose(bank[:, dd * 128:(dd + 1) * 128],
                                          xa[:, d * 512 + k * 128: d * 512 + k * 128 + 128],
                                          ident[:, :])
                ctx.inc("tensor", ins, s_t, 1)

    def em_scalar(sc):
        for c in range(NT):
            for k in range(4):
                i = c * 4 + k
                ctx.wait(sc, "scalar", s_t, b_t + i + 1)
                if i >= 1:
                    ctx.wait(sc, "scalar", s_st, b_st + i * 16)
                for grp in range(4):
                    bank = pss[grp]
                    ctx.inc("scalar", sc.copy(ob[:, grp * 512:(grp + 1) * 512], bank[:, :]), s_sc, 1)
                ctx.wait(sc, "scalar", s_sc, s_sc.n)
                t0 = c * 512 + k * 128
                ctx.inc("scalar", sc.dma_start(y_out[t0:t0 + 128, :], ob[:, :]), s_st, 16)

    return {"sync": em_sync, "tensor": em_tensor, "scalar": em_scalar}



def rec_prep(ctx, sems, P):
    rec = Rec(ctx)
    s_ld, s_ldg, s_g = sems["ld"], sems["ldg"], sems["g"]
    C = ctx.consts
    rec.add("sync", lambda e: e.dma_start(P["uincl_f"][:, :], C["c_uincl"]), s_ld, 16)
    rec.add("sync", lambda e: e.dma_start(P["maskneg"][:, :], C["c_maskneg"]), s_ld, 16)
    rec.add("sync", lambda e: e.dma_start(P["ones_row_f"][:, :], C["c_ones_row"]), s_ld, 16)
    rec.add("sync", lambda e: e.dma_start(P["ident_f"][:, :], C["c_ident"]), s_ld, 16)
    rec.add("sync", lambda e: e.dma_start(P["ones_col_f"][:, :], C["c_ones_col"]), s_ld, 16)
    rec.add("gpsimd", lambda e: e.dma_start(P["ident_b"][:, :], C["c_ident"]), s_ldg, 16)
    rec.add("gpsimd", lambda e: e.dma_start(P["uincl_b"][:, :], C["c_uincl"]), s_ldg, 16)
    rec.add("gpsimd", lambda e: e.dma_start(P["h128s_b"][:, :], C["c_h128s"]), s_ldg, 16)
    rec.add("gpsimd", lambda e: e.dma_start(P["rrot_b"][:, :], C["c_rrot_lhsT"]), s_ldg, 16)
    rec.add("gpsimd", lambda e: e.dma_start(P["ones_col_b"][:, :], C["c_ones_col"]), s_ldg, 16)
    W = ctx.wdram
    for src, dst in (("w_qa", "QAB"), ("w_qb", "QBB"), ("w_kva", "KVAB"),
                     ("w_kvb", "KVBB"), ("w_ow", "OWB")):
        rec.add("gpsimd", lambda e, a=ctx.wins[src], b=W[dst]: e.dma_start(b, a), s_ldg, 16)
    rec.add("gpsimd", lambda e: e.memset(P["ones_row_b"][:, :], 1.0), s_g, 1)
    rec.add("gpsimd", lambda e: e.memset(P["ones_1b"][:, :], 1.0), s_g, 1)
    rec.add("gpsimd", lambda e: e.memset(P["eps"][:, 0:1], 1e-5), s_g, 1)
    rec.add("gpsimd", lambda e: e.memset(P["eps"][:, 1:2], 1e-6), s_g, 1)
    rec.add("gpsimd", lambda e: e.memset(P["eps"][:, 2:3], 1e-8), s_g, 1)
    rec.add("gpsimd", lambda e: e.memset(P["eps"][:, 3:4], 1e-30), s_g, 1)
    ones_row_T = P["rowT_b"]
    rec.add("gpsimd", lambda e: e.memset(ones_row_T[:, :], -1.0), s_g, 1)
    rec.wait("gpsimd", s_g)
    for cc in range(NT):
        rec.add("gpsimd", lambda e, cc=cc: e.dma_start(
            ctx.scr["KPEA"][64:65, cc * 512:(cc + 1) * 512], ones_row_T[:, :]), s_ldg, 16)
    return rec


def rec_ln_chunk(rec, sems, P, bufs, c, n_feat_tiles, inv_n, eps_col):
    ctx = rec.ctx
    s_t, s_v, s_sc, s_g = sems["t"], sems["v"], sems["sc"], sems["g"]
    xt, xbf, sqbf = bufs["xt"], bufs["xbf"], bufs["sqbf"]
    xln_at = bufs.get("xln_at") or (lambda d: bufs["xln"][:, d, :])
    tmpf, rows, ab = bufs["tmpf"], bufs["rows"], bufs["ab"]
    ps_row0, ps_row1 = bufs["ps_row0"], bufs["ps_row1"]
    ps_bc0, ps_bc1 = bufs["ps_bc0"], bufs["ps_bc1"]
    nf = n_feat_tiles
    v_cast = []
    sc_sq = []
    for d in range(nf):
        if xbf is not None:
            rec.add("vector", lambda e, d=d: e.tensor_copy(xbf[:, d, :], xt[:, d, :]), s_v, 1)
            v_cast.append(s_v.n)
        rec.add("scalar", lambda e, d=d: e.activation(sqbf[:, d, :], xt[:, d, :], AF.Square), s_sc, 1)
        sc_sq.append(s_sc.n)
    xt_ready = bufs.get("xt_ready")
    if xt_ready:
        rec.wait("tensor", sems[xt_ready[0]], xt_ready[1])
    if xbf is not None:
        rec.wait("tensor", s_v, v_cast[-1])
        rec.mm_group(ps_row0[0:1, :], [(P["ones_col_b"][:, :], xbf[:, d, :]) for d in range(nf)], s_t)
    else:
        rec.mm_group(ps_row0[0:1, :], [(P["ones_col_f"][:, :], xt[:, d, :]) for d in range(nf)], s_t)
    t_sum = s_t.n
    rec.wait("tensor", s_sc, sc_sq[-1])
    rec.mm_group(ps_row1[0:1, :], [(P["ones_col_b"][:, :], sqbf[:, d, :]) for d in range(nf)], s_t)
    t_ssq = s_t.n
    mu = rows[:, 0:512]; msq = rows[:, 512:1024]; var = rows[:, 1024:1536]
    sd = rows[:, 1536:2048]; rstd = rows[:, 512:1024]; Br = rows[:, 1024:1536]
    rec.wait("vector", s_t, t_sum)
    rec.add("vector", lambda e: e.tensor_scalar(mu, ps_row0[0:1, :], inv_n, None, OP.mult), s_v, 1)
    rec.wait("vector", s_t, t_ssq)
    rec.add("vector", lambda e: e.tensor_scalar(msq, ps_row1[0:1, :], inv_n, None, OP.mult), s_v, 1)
    rec.wait("vector", s_v)
    rec.add("vector", lambda e: e.tensor_tensor(var, mu, mu, OP.mult), s_v, 1)
    rec.wait("vector", s_v)
    rec.add("vector", lambda e: e.tensor_tensor(var, msq, var, OP.subtract), s_v, 1)
    rec.wait("scalar", s_v)
    rec.add("scalar", lambda e: e.activation(sd, var, AF.Sqrt, bias=P["eps"][0:1, eps_col:eps_col + 1]), s_sc, 1)
    rec.wait("vector", s_sc)
    rec.add("vector", lambda e: e.reciprocal(rstd, sd), s_v, 1)
    rec.wait("vector", s_v)
    rec.add("vector", lambda e: e.scalar_tensor_tensor(Br, mu, -1.0, rstd, OP.mult, OP.mult), s_v, 1)
    v_stats = s_v.n
    rec.wait("tensor", s_v, v_stats)
    rec.add("tensor", lambda e: e.matmul(ps_bc0[:, :], P["ones_row_f"][:, :], rstd, start=True, stop=True), s_t, 1)
    rec.add("tensor", lambda e: e.matmul(ps_bc1[:, :], P["ones_row_f"][:, :], Br, start=True, stop=True), s_t, 1)
    t_bc = s_t.n
    A_b = ab[:, 0:512]; B_b = ab[:, 512:1024]
    rec.wait("scalar", s_t, t_bc)
    ab_guard = bufs.get("ab_guard")
    if ab_guard:
        rec.wait("scalar", s_v, ab_guard[0])
        rec.wait("scalar", s_g, ab_guard[1])
    rec.add("scalar", lambda e: e.copy(A_b, ps_bc0[:, :]), s_sc, 1)
    rec.add("scalar", lambda e: e.copy(B_b, ps_bc1[:, :]), s_sc, 1)
    sc_ab = s_sc.n
    gdone = []
    for d in range(nf):
        rec.wait("vector", s_sc, sc_ab)
        if d >= 2:
            rec.wait("vector", s_g, gdone[d - 2])
        rec.add("vector", lambda e, d=d: e.tensor_tensor(tmpf[:, d % 2, :], xt[:, d, :], A_b, OP.mult), s_v, 1)
        rec.wait("gpsimd", s_v)
        rec.wait("gpsimd", s_sc, sc_ab)
        rec.wait("gpsimd", s_t, t_sum)
        rec.add("gpsimd", lambda e, d=d: e.tensor_tensor(xln_at(d), tmpf[:, d % 2, :], B_b, OP.add), s_g, 1)
        gdone.append(s_g.n)
    return {
        "xt_free": (("v", s_v.n), ("sc", sc_sq[-1])) + ((("t", t_sum),) if xbf is None else ()),
        "xbf_free_t": t_sum, "sqbf_free_t": t_ssq,
        "psrow_free_v": s_v.n, "psbc_free_sc": sc_ab,
        "ab_free": (s_v.n, gdone[-1]),
        "xln_done_g": gdone[-1], "xln_done_list": gdone,
    }



def rec_s1a(ctx, sems, P):
    rec = Rec(ctx)
    nc = ctx.nc
    s_ld, s_t, s_v, s_sc, s_g, s_st, s_stg = (sems[k] for k in ("ld", "t", "v", "sc", "g", "st", "stg"))
    sb, psb = ctx.sb, ctx.psb
    scr, W = ctx.scr, ctx.wdram

    qa_wT = sb("qa_wT", [128, 16, 1536], BF16)
    kva_wT = sb("kva_wT", [128, 16, 576], BF16)
    wq_cols = sb("wq_cols", [128, 12], F32)
    wkv_cols = sb("wkv_cols", [128, 4], F32)
    xt = sb("xt", [128, 16, 512], F32)
    xbf = sb("xbf", [128, 16, 512], BF16)
    sqbf = sb("sqbf", [128, 16, 512], BF16)
    xln = xbf
    tmpf = sb("tmpf", [128, 2, 512], F32)
    rows = sb("rows", [1, 12 * 512], F32)
    ab = sb("ab", [128, 2 * 512], F32)
    qa_bf = sb("qa_bf", [128, 12, 512], BF16)
    qsq_bf = sqbf[:, 0:12, :]
    qar_bf = sb("qar_bf", [128, 2, 512], BF16)
    rq_b = sb("rq_b", [128, 512], F32)
    ckv_bf = sb("ckv_bf", [128, 4, 512], BF16)
    ckvsq_bf = sqbf[:, 12:16, :]
    ckvr_bf = sb("ckvr_bf", [128, 2, 512], BF16)
    rkv_b = sb("rkv_b", [128, 512], F32)
    cosk = sb("cosk", [64, 512], F32)
    sink = sb("sink", [64, 512], F32)
    kraw_bf = sb("kraw_bf", [64, 512], BF16)
    krope_t = sb("krope_t", [64, 2 * 512], F32)
    kpe_bf = sb("kpe_bf", [64, 512], BF16)

    ps_q = [psb("ps_q0", [128, 512], F32), psb("ps_q1", [128, 512], F32)]
    ps_row0 = psb("ps_row0", [128, 512], F32)
    ps_row1 = psb("ps_row1", [128, 512], F32)
    ps_bc0 = psb("ps_bc0", [128, 512], F32)
    ps_bc1 = psb("ps_bc1", [128, 512], F32)
    ps_rot = psb("ps_rot", [128, 512], F32)

    for d in range(16):
        for rch in range(3):
            rec.add("sync", lambda e, d=d, r=rch: e.dma_start_transpose(
                qa_wT[:, d, r * 512:(r + 1) * 512],
                W["QAB"][r * 512:(r + 1) * 512, d * 128:(d + 1) * 128]), s_ld, 16)
        rec.add("sync", lambda e, d=d: e.dma_start_transpose(
            kva_wT[:, d, 0:512], W["KVAB"][0:512, d * 128:(d + 1) * 128]), s_ld, 16)
        rec.add("sync", lambda e, d=d: e.dma_start_transpose(
            kva_wT[:, d, 448:576], W["KVAB"][448:576, d * 128:(d + 1) * 128]), s_ld, 16)
    wqln = ctx.wins["w_qa_ln"].rearrange("(j p b) -> j p b", p=128, b=1)
    for j in range(12):
        rec.add("sync", lambda e, j=j: e.dma_start(wq_cols[:, j:j + 1], wqln[j]), s_ld, 16)
    wkln = ctx.wins["w_kva_ln"].rearrange("(j p b) -> j p b", p=128, b=1)
    for j in range(4):
        rec.add("sync", lambda e, j=j: e.dma_start(wkv_cols[:, j:j + 1], wkln[j]), s_ld, 16)
    w_ready = s_ld.n

    for src, dst in (("w_W1", "W1B"), ("w_W3", "W3B"), ("w1_gate", "GATB"),
                     ("w1_up", "UPB"), ("w1_shg", "SHGB"), ("w1_shu", "SHUB"),
                     ("w1_shd", "SHDB"), ("w1_down", "DWNB"), ("w1_router", "RTRB")):
        rec.add("gpsimd", lambda e, a=ctx.wins[src], b=W[dst]: e.dma_start(b, a), sems["ldg"], 16)

    prev = {}
    for c in range(NT):
        c0 = c * 512
        if prev:
            for nm, val in prev["xt_free"]:
                rec.wait("sync", sems[nm], val)
        ld_tiles = []
        for d in range(16):
            rec.add("sync", lambda e, d=d, c0=c0: e.dma_start(
                xt[:, d, :], scr["XT"][d, :, c0:c0 + 512]), s_ld, 16)
            ld_tiles.append(s_ld.n)
        if prev:
            rec.wait("sync", s_v, prev["cos_free_v"])
        rec.add("sync", lambda e, c0=c0: e.dma_start(cosk[:, :], ctx.consts["c_cos2"][0:64, c0:c0 + 512]), s_ld, 16)
        rec.add("sync", lambda e, c0=c0: e.dma_start(sink[:, :], ctx.consts["c_sin2"][0:64, c0:c0 + 512]), s_ld, 16)
        ld_cos = s_ld.n

        for e_ in ("vector", "scalar"):
            rec.wait(e_, s_ld, ld_tiles[-1])
        if prev:
            rec.wait("vector", s_t, prev["xln_free_t"])
            rec.wait("scalar", s_t, prev["xln_free_t"])
            rec.wait("tensor", s_v, prev["psrow_free_v"])
            rec.wait("tensor", s_sc, prev["psbc_free_sc"])
            rec.wait("gpsimd", s_t, prev["xln_free_t"])
        lnc = rec_ln_chunk(rec, sems, P, {
            "xt": xt, "xbf": xbf, "sqbf": sqbf, "xln": xln, "tmpf": tmpf,
            "rows": rows, "ab": ab, "ab_guard": prev.get("ab_free"),
            "ps_row0": ps_row0, "ps_row1": ps_row1,
            "ps_bc0": ps_bc0, "ps_bc1": ps_bc1}, c, 16, 1.0 / 2048.0, 0)

        rec.wait("tensor", s_ld, w_ready)
        rec.wait("tensor", s_g, lnc["xln_done_g"])
        evac_sc = {}
        t_mm = {}
        for rc in range(12):
            bank = ps_q[rc % 2]
            if rc >= 2:
                rec.wait("tensor", s_sc, evac_sc[rc - 2])
            elif prev:
                rec.wait("tensor", s_sc, prev["psq_free_sc"][rc % 2])
            rec.mm_group(bank[:, :], [(qa_wT[:, d, rc * 128:(rc + 1) * 128], xln[:, d, :])
                                      for d in range(16)], s_t)
            t_mm[rc] = s_t.n
            rec.wait("scalar", s_t, t_mm[rc])
            if prev and rc == 0:
                rec.wait("scalar", s_v, prev["qabf_free_v"])
                rec.wait("scalar", s_t, prev["qsq_free_t"])
            rec.wait("scalar", s_t, lnc["sqbf_free_t"])
            rec.add("scalar", lambda e, rc=rc, b=bank: e.copy(qa_bf[:, rc, :], b[:, :]), s_sc, 1)
            rec.add("scalar", lambda e, rc=rc, b=bank: e.activation(qsq_bf[:, rc, :], b[:, :], AF.Square), s_sc, 1)
            evac_sc[rc] = s_sc.n
        rec.wait("tensor", s_sc, evac_sc[11])
        rec.wait("tensor", s_v, lnc["psrow_free_v"])
        rec.mm_group(ps_row1[0:1, :], [(P["ones_col_b"][:, :], qsq_bf[:, rc, :]) for rc in range(12)], s_t)
        t_qssq = s_t.n
        qsq_free_t = s_t.n
        mq = rows[:, 3072:3584]; sdq = rows[:, 3584:4096]; rq = rows[:, 4096:4608]
        rec.wait("vector", s_t, t_qssq)
        rec.add("vector", lambda e: e.tensor_scalar(mq, ps_row1[0:1, :], 1.0 / 1536.0, None, OP.mult), s_v, 1)
        rec.wait("scalar", s_v)
        rec.add("scalar", lambda e: e.activation(sdq, mq, AF.Sqrt, bias=P["eps"][0:1, 1:2]), s_sc, 1)
        rec.wait("vector", s_sc)
        rec.add("vector", lambda e: e.reciprocal(rq, sdq), s_v, 1)
        v_rq = s_v.n
        rec.wait("tensor", s_v, v_rq)
        rec.wait("tensor", s_sc, lnc["psbc_free_sc"])
        rec.add("tensor", lambda e: e.matmul(ps_bc0[:, :], P["ones_row_f"][:, :], rq, start=True, stop=True), s_t, 1)
        t_bcq = s_t.n
        rec.wait("scalar", s_t, t_bcq)
        if prev:
            rec.wait("scalar", s_v, prev["rqb_free_v"])
        rec.add("scalar", lambda e: e.copy(rq_b[:, :], ps_bc0[:, :]), s_sc, 1)
        sc_rqb = s_sc.n
        v_qar = {}
        st_qar = {}
        for rc in range(12):
            rec.wait("vector", s_sc, sc_rqb)
            rec.wait("vector", s_sc, evac_sc[rc])
            rec.wait("vector", s_ld, w_ready)
            if rc >= 2:
                rec.wait("vector", s_stg, st_qar[rc - 2])
            elif prev:
                rec.wait("vector", s_stg, prev["qar_store"])
            rec.add("vector", lambda e, rc=rc: e.scalar_tensor_tensor(
                qar_bf[:, rc % 2, :], qa_bf[:, rc, :], wq_cols[:, rc:rc + 1], rq_b[:, :],
                OP.mult, OP.mult), s_v, 1)
            v_qar[rc] = s_v.n
            rec.wait("gpsimd", s_v, v_qar[rc])
            rec.add("gpsimd", lambda e, rc=rc, c0=c0: e.dma_start(
                scr["QAR"][rc, :, c0:c0 + 512], qar_bf[:, rc % 2, :]), s_stg, 16)
            st_qar[rc] = s_stg.n
        qar_store = s_stg.n

        kv_mm = {}
        kv_evac = {}
        for rc in range(4):
            bank = ps_q[rc % 2]
            rec.wait("tensor", s_sc, evac_sc[10 + rc] if rc < 2 else kv_evac[rc - 2])
            rec.mm_group(bank[:, :], [(kva_wT[:, d, rc * 128:(rc + 1) * 128], xln[:, d, :])
                                      for d in range(16)], s_t)
            kv_mm[rc] = s_t.n
            rec.wait("scalar", s_t, kv_mm[rc])
            if prev and rc == 0:
                rec.wait("scalar", s_v, prev["ckv_free_v"])
                rec.wait("scalar", s_t, prev["ckvsq_free_t"])
            rec.add("scalar", lambda e, rc=rc, b=bank: e.copy(ckv_bf[:, rc, :], b[:, :]), s_sc, 1)
            rec.add("scalar", lambda e, rc=rc, b=bank: e.activation(ckvsq_bf[:, rc, :], b[:, :], AF.Square), s_sc, 1)
            kv_evac[rc] = s_sc.n
        rec.wait("tensor", s_sc, kv_evac[3])
        rec.wait("vector", s_t, t_qssq)
        rec.wait("tensor", s_v, v_rq)
        rec.mm_group(ps_row1[0:1, :], [(P["ones_col_b"][:, :], ckvsq_bf[:, rc, :]) for rc in range(4)], s_t)
        t_kssq = s_t.n
        ckvsq_free_t = s_t.n
        mkv = rows[:, 4608:5120]; sdkv = rows[:, 5120:5632]; rkv = rows[:, 5632:6144]
        rec.wait("vector", s_t, t_kssq)
        rec.add("vector", lambda e: e.tensor_scalar(mkv, ps_row1[0:1, :], 1.0 / 512.0, None, OP.mult), s_v, 1)
        rec.wait("scalar", s_v)
        rec.add("scalar", lambda e: e.activation(sdkv, mkv, AF.Sqrt, bias=P["eps"][0:1, 1:2]), s_sc, 1)
        rec.wait("vector", s_sc)
        rec.add("vector", lambda e: e.reciprocal(rkv, sdkv), s_v, 1)
        v_rkv = s_v.n
        rec.wait("tensor", s_v, v_rkv)
        rec.wait("tensor", s_sc, sc_rqb)
        rec.add("tensor", lambda e: e.matmul(ps_bc0[:, :], P["ones_row_f"][:, :], rkv, start=True, stop=True), s_t, 1)
        t_bckv = s_t.n
        rec.wait("scalar", s_t, t_bckv)
        if prev:
            rec.wait("scalar", s_v, prev["rkvb_free_v"])
        rec.add("scalar", lambda e: e.copy(rkv_b[:, :], ps_bc0[:, :]), s_sc, 1)
        sc_rkvb = s_sc.n
        v_ckvr = {}
        st_ckvr = {}
        for rc in range(4):
            rec.wait("vector", s_sc, sc_rkvb)
            rec.wait("vector", s_sc, kv_evac[rc])
            if rc >= 2:
                rec.wait("vector", s_stg, st_ckvr[rc - 2])
            elif prev:
                rec.wait("vector", s_stg, prev["ckvr_store"])
            rec.add("vector", lambda e, rc=rc: e.scalar_tensor_tensor(
                ckvr_bf[:, rc % 2, :], ckv_bf[:, rc, :], wkv_cols[:, rc:rc + 1], rkv_b[:, :],
                OP.mult, OP.mult), s_v, 1)
            v_ckvr[rc] = s_v.n
            rec.wait("gpsimd", s_v, v_ckvr[rc])
            rec.add("gpsimd", lambda e, rc=rc, c0=c0: e.dma_start(
                scr["CKVR"][rc, :, c0:c0 + 512], ckvr_bf[:, rc % 2, :]), s_stg, 16)
            st_ckvr[rc] = s_stg.n
        ckvr_store = s_stg.n

        bank = ps_q[0]
        rec.wait("tensor", s_sc, kv_evac[2])
        rec.mm_group(bank[0:64, :], [(kva_wT[:, d, 512:576], xln[:, d, :]) for d in range(16)], s_t)
        t_kpe = s_t.n
        rec.wait("scalar", s_t, t_kpe)
        if prev:
            rec.wait("scalar", s_t, prev["kraw_free_t"])
            rec.wait("scalar", s_v, prev["kraw_free_v"])
        rec.add("scalar", lambda e, b=bank: e.copy(kraw_bf[:, :], b[0:64, :]), s_sc, 1)
        sc_kraw = s_sc.n
        rec.wait("tensor", s_sc, sc_kraw)
        if prev:
            rec.wait("tensor", s_v, prev["psrot_free_v"])
        rec.add("tensor", lambda e, b=bank: e.matmul(ps_rot[0:64, :], P["rrot_b"][0:64, 0:64],
                                                     kraw_bf[:, :], start=True, stop=True), s_t, 1)
        t_rot = s_t.n
        kraw_free_t = s_t.n
        tmpc = krope_t[:, 0:512]; tmps = krope_t[:, 512:1024]
        rec.wait("vector", s_sc, sc_kraw)
        rec.wait("vector", s_ld, ld_cos)
        rec.add("vector", lambda e: e.tensor_tensor(tmpc, kraw_bf[:, :], cosk[:, :], OP.mult), s_v, 1)
        kraw_free_v = s_v.n
        rec.wait("vector", s_t, t_rot)
        rec.add("vector", lambda e: e.tensor_tensor(tmps, ps_rot[0:64, :], sink[:, :], OP.mult), s_v, 1)
        psrot_free_v = s_v.n
        cos_free_v = s_v.n
        rec.wait("vector", s_v)
        if prev:
            rec.wait("vector", s_stg, prev["kpe_store"])
        rec.add("vector", lambda e: e.tensor_tensor(kpe_bf[:, :], tmpc, tmps, OP.add), s_v, 1)
        v_kpe = s_v.n
        rec.wait("gpsimd", s_v, v_kpe)
        rec.add("gpsimd", lambda e, c0=c0: e.dma_start(scr["KPEA"][0:64, c0:c0 + 512], kpe_bf[:, :]), s_stg, 16)
        kpe_store = s_stg.n

        prev = {
            "xt_free": lnc["xt_free"],
            "ab_free": lnc["ab_free"],
            "xbf_sq_free_t": max(lnc["xbf_free_t"], lnc["sqbf_free_t"]),
            "psrow_free_v": s_v.n,
            "psbc_free_sc": sc_rkvb,
            "xln_free_g": lnc["xln_done_g"],
            "xln_free_t": t_kpe,
            "psq_free_sc": {0: sc_kraw, 1: kv_evac[3]},
            "qabf_free_v": v_qar[11],
            "qsq_free_t": qsq_free_t,
            "rqb_free_v": v_qar[11],
            "qar_store": qar_store,
            "ckv_free_v": v_ckvr[3],
            "ckvsq_free_t": ckvsq_free_t,
            "rkvb_free_v": v_ckvr[3],
            "ckvr_store": ckvr_store,
            "kraw_free_t": kraw_free_t,
            "kraw_free_v": kraw_free_v,
            "psrot_free_v": psrot_free_v,
            "cos_free_v": cos_free_v,
            "kpe_store": kpe_store,
        }
    return rec



def rec_s1b(ctx, sems, P):
    rec = Rec(ctx)
    s_ld, s_t, s_v, s_sc, s_g, s_st, s_stg = (sems[k] for k in ("ld", "t", "v", "sc", "g", "st", "stg"))
    sb, psb = ctx.sb, ctx.psb
    scr, W = ctx.scr, ctx.wdram
    SCALE = float(192.0 ** -0.5)

    qb_wT = sb("qb_wT", [128, 12, 3072], BF16)
    kvb_wT = sb("kvb_wT", [128, 4, 4096], BF16)
    qar = sb("qar", [128, 12, 512], BF16)
    ckvr = sb("ckvr", [128, 4, 512], BF16)
    cosq = sb("cosq", [128, 512], F32)
    sinq = sb("sinq", [128, 512], F32)
    ob = sb("ob", [128, 2, 512], BF16)
    qpraw = sb("qpraw", [128, 512], BF16)
    ropet = sb("ropet", [128, 2 * 512], F32)
    qpe_bf = sb("qpe_bf", [128, 512], BF16)
    vb = sb("vb", [128, 2, 512], BF16)

    banks = [psb("s1b_ps0", [128, 512], F32), psb("s1b_ps1", [128, 512], F32)]
    ps_rot = psb("s1b_rot", [128, 512], F32)

    for rc in range(12):
        for og in range(6):
            rec.add("sync", lambda e, rc=rc, og=og: e.dma_start_transpose(
                qb_wT[:, rc, og * 512:(og + 1) * 512],
                W["QBB"][og * 512:(og + 1) * 512, rc * 128:(rc + 1) * 128]), s_ld, 16)
    for rr in range(4):
        for og in range(8):
            rec.add("sync", lambda e, rr=rr, og=og: e.dma_start_transpose(
                kvb_wT[:, rr, og * 512:(og + 1) * 512],
                W["KVBB"][og * 512:(og + 1) * 512, rr * 128:(rr + 1) * 128]), s_ld, 16)
    w_ready = s_ld.n
    for src, dst in (("w1_q", "Q1B"), ("w1_k", "K1B"), ("w1_v", "V1B"),
                     ("w1_g", "G1B"), ("w1_s", "S1B"), ("w1_o", "O1B")):
        rec.add("gpsimd", lambda e, a=ctx.wins[src], b=W[dst]: e.dma_start(b, a), sems["ldg"], 16)

    prev = {}
    for c in range(NT):
        c0 = c * 512
        if prev:
            rec.wait("sync", s_t, prev["qar_free_t"])
        for rc in range(12):
            rec.add("sync", lambda e, rc=rc, c0=c0: e.dma_start(
                qar[:, rc, :], scr["QAR"][rc, :, c0:c0 + 512]), s_ld, 16)
        for rr in range(4):
            rec.add("sync", lambda e, rr=rr, c0=c0: e.dma_start(
                ckvr[:, rr, :], scr["CKVR"][rr, :, c0:c0 + 512]), s_ld, 16)
        ld_act = s_ld.n
        if prev:
            rec.wait("sync", s_v, prev["cos_free_v"])
        rec.add("sync", lambda e, c0=c0: e.dma_start(cosq[:, :], ctx.consts["c_cos2"][:, c0:c0 + 512]), s_ld, 16)
        rec.add("sync", lambda e, c0=c0: e.dma_start(sinq[:, :], ctx.consts["c_sin2"][:, c0:c0 + 512]), s_ld, 16)
        ld_cos = s_ld.n

        rec.wait("tensor", s_ld, w_ready)
        rec.wait("tensor", s_ld, ld_act)

        evac = {}
        store = {}
        slot_writer = {}

        def qk_block(kind, oc, pairs, scale):
            i = len(evac)
            bank = banks[i % 2]
            if i >= 2:
                rec.wait("tensor", s_sc, evac[i - 2])
            elif prev:
                nm, cnt = prev["bank_free"][i % 2]
                rec.wait("tensor", sems[nm], cnt)
            rec.mm_group(bank[:, :], pairs, s_t)
            tmm = s_t.n
            rec.wait("scalar", s_t, tmm)
            if i >= 2:
                rec.wait("scalar", s_stg, store[i - 2])
            elif prev:
                rec.wait("scalar", s_stg, prev["ob_store"])
            sl = i % 2
            if scale is None:
                rec.add("scalar", lambda e, sl=sl, b=bank: e.copy(ob[:, sl, :], b[:, :]), s_sc, 1)
            else:
                rec.add("scalar", lambda e, sl=sl, b=bank: e.mul(ob[:, sl, :], b[:, :], scale), s_sc, 1)
            evac[i] = s_sc.n
            rec.wait("gpsimd", s_sc, evac[i])
            dst = kind
            rec.add("gpsimd", lambda e, dst=dst, oc=oc, sl=sl, c0=c0: e.dma_start(
                scr[dst][oc, :, c0:c0 + 512], ob[:, sl, :]), s_stg, 16)
            store[i] = s_stg.n
            return tmm

        for oc in range(16):
            qk_block("QNT", oc, [(qb_wT[:, rc, oc * 128:(oc + 1) * 128], qar[:, rc, :])
                                 for rc in range(12)], SCALE)
        t_qar_last = None
        for oc in range(16):
            qk_block("KNT", oc, [(kvb_wT[:, rr, oc * 128:(oc + 1) * 128], ckvr[:, rr, :])
                                 for rr in range(4)], None)
        t_qar_free = 0
        rope_store = prev.get("qpe_store", 0)
        for oc in range(8):
            i = len(evac)
            bank = banks[i % 2]
            rec.wait("tensor", s_sc, evac[i - 2])
            rec.mm_group(bank[:, :], [(qb_wT[:, rc, 2048 + oc * 128: 2048 + (oc + 1) * 128], qar[:, rc, :])
                                      for rc in range(12)], s_t)
            t_qar_free = s_t.n
            tmm = s_t.n
            evac[i] = evac[i - 1]
            rec.wait("scalar", s_t, tmm)
            if oc >= 1:
                rec.wait("scalar", s_t, rot_done)
                rec.wait("scalar", s_v, v_mixc)
            elif prev:
                rec.wait("scalar", s_t, prev["qpraw_free_t"])
                rec.wait("scalar", s_v, prev["qpraw_free_v"])
            rec.add("scalar", lambda e, b=bank: e.mul(qpraw[:, :], b[:, :], SCALE), s_sc, 1)
            sc_qp = s_sc.n
            evac[i] = sc_qp
            rec.wait("tensor", s_sc, sc_qp)
            if oc >= 1:
                rec.wait("tensor", s_v, v_mixs)
            elif prev:
                rec.wait("tensor", s_v, prev["psrot_free_v"])
            rec.add("tensor", lambda e: e.matmul(ps_rot[:, :], P["rrot_b"][:, :], qpraw[:, :],
                                                 start=True, stop=True), s_t, 1)
            rot_done = s_t.n
            tmpc = ropet[:, 0:512]; tmps = ropet[:, 512:1024]
            rec.wait("vector", s_sc, sc_qp)
            rec.wait("vector", s_ld, ld_cos)
            if oc >= 1:
                rec.wait("vector", s_v, v_qpe)
            rec.add("vector", lambda e: e.tensor_tensor(tmpc, qpraw[:, :], cosq[:, :], OP.mult), s_v, 1)
            v_mixc = s_v.n
            rec.wait("vector", s_t, rot_done)
            rec.add("vector", lambda e: e.tensor_tensor(tmps, ps_rot[:, :], sinq[:, :], OP.mult), s_v, 1)
            v_mixs = s_v.n
            rec.wait("vector", s_v)
            if oc >= 1:
                rec.wait("vector", s_stg, qpe_st)
            elif prev:
                rec.wait("vector", s_stg, prev["qpe_store"])
            rec.add("vector", lambda e: e.tensor_tensor(qpe_bf[:, :], tmpc, tmps, OP.add), s_v, 1)
            v_qpe = s_v.n
            rec.wait("gpsimd", s_v, v_qpe)
            rec.add("gpsimd", lambda e, oc=oc, c0=c0: e.dma_start(
                scr["QPE"][oc, :, c0:c0 + 512], qpe_bf[:, :]), s_stg, 16)
            qpe_st = s_stg.n

        n_v = 0
        v_evac = {}
        v_store = {}
        for sub in range(4):
            for vc in range(4):
                i = n_v; n_v += 1
                bank = banks[i % 2]
                if i >= 2:
                    rec.wait("tensor", s_v, v_evac[i - 2])
                else:
                    rec.wait("tensor", s_sc, evac[len(evac) - 2 + i])
                rec.mm_group(bank[:, :], [(ckvr[:, rr, sub * 128:(sub + 1) * 128],
                                           kvb_wT[:, rr, 2048 + vc * 512: 2048 + (vc + 1) * 512])
                                          for rr in range(4)], s_t)
                tmm = s_t.n
                t_v_last = s_t.n
                rec.wait("vector", s_t, tmm)
                if i >= 2:
                    rec.wait("vector", s_stg, v_store[i - 2])
                elif prev:
                    rec.wait("vector", s_stg, prev["v_store"])
                sl = i % 2
                rec.add("vector", lambda e, sl=sl, b=bank: e.tensor_copy(vb[:, sl, :], b[:, :]), s_v, 1)
                v_evac[i] = s_v.n
                rec.wait("gpsimd", s_v, v_evac[i])
                rec.add("gpsimd", lambda e, sub=sub, vc=vc, sl=sl, c0=c0: e.dma_start(
                    scr["VNT"][c0 + sub * 128: c0 + (sub + 1) * 128, vc * 512:(vc + 1) * 512],
                    vb[:, sl, :]), s_stg, 16)
                v_store[i] = s_stg.n

        prev = {
            "qar_free_t": t_v_last,
            "cos_free_v": v_mixs,
            "bank_free": {0: ("v", v_evac[14]), 1: ("v", v_evac[15])},
            "ob_store": store[len(store) - 1] if store else 0,
            "qpraw_free_t": rot_done,
            "qpraw_free_v": v_mixc,
            "psrot_free_v": v_mixs,
            "qpe_store": qpe_st,
            "v_store": v_store[15],
        }
    return rec



def rec_s2(ctx, sems, P):
    rec = Rec(ctx)
    s_ld, s_t, s_v, s_sc, s_g, s_st, s_stg = (sems[k] for k in ("ld", "t", "v", "sc", "g", "st", "stg"))
    sb, psb = ctx.sb, ctx.psb
    scr = ctx.scr

    qn = sb("s2_qn", [128, T], BF16)
    qpa = sb("s2_qpa", [65, T], BF16)
    kn = sb("s2_kn", [128, T], BF16)
    kpa = sb("s2_kpa", [65, T], BF16)
    vn = sb("s2_vn", [128, T], BF16)
    sq = sb("s2_sq", [128, T], BF16)
    sqp = sb("s2_sqp", [64, T], BF16)
    pt = sb("s2_pt", [128, 3, 512], BF16)
    rb = sb("s2_rb", [128, 512], F32)
    ao = sb("s2_ao", [128, 512], BF16)
    kmx = sb("s2_kmx", [1, 4], F32)
    rr_f = sb("s2_rr", [1, 512], F32)

    ps_st = [psb("s2_st0", [128, 512], F32), psb("s2_st1", [128, 512], F32)]
    ps_attn = psb("s2_attn", [128, 512], F32)
    ps_r = psb("s2_r", [128, 512], F32)
    ps_rb = psb("s2_psrb", [128, 512], F32)

    free = {}

    def guard(engname, res):
        if res in free:
            nm, cnt = free[res]
            rec.wait(engname, sems[nm], cnt)

    def setfree(res, nm, cnt):
        free[res] = (nm, cnt)

    rec.add("sync", lambda e: e.dma_start(kpa[:, :], scr["KPEA"][0:65, :]), s_ld, 16)

    for h in range(16):
        guard("sync", "qk")
        rec.add("sync", lambda e, h=h: e.dma_start(qn[:, :], scr["QNT"][h]), s_ld, 16)
        rec.add("sync", lambda e, h=h: e.dma_start(kn[:, :], scr["KNT"][h]), s_ld, 16)
        rec.add("sync", lambda e, h=h: e.dma_start(
            qpa[0:64, :], scr["QPE"][h // 2, 64 * (h % 2):64 * (h % 2) + 64, :]), s_ld, 16)
        vsrc = scr["VNT"].rearrange("(b p) (g v) -> p b g v", p=128, v=128)
        rec.add("sync", lambda e, h=h, vs=vsrc: e.dma_start(
            vn.rearrange("p (b v) -> p b v", v=128), vs[:, :, h, :]), s_ld, 16)
        ld_head = s_ld.n

        rec.wait("scalar", s_ld, ld_head)
        guard("scalar", "sq")
        rec.add("scalar", lambda e: e.activation(sq[:, :], kn[:, :], AF.Square), s_sc, 1)
        rec.add("scalar", lambda e: e.activation(sqp[:, :], kpa[0:64, :], AF.Square), s_sc, 1)
        sc_sqk = s_sc.n
        rec.wait("tensor", s_sc, sc_sqk)
        for j in range(NT):
            bank = ps_st[j % 2]
            guard("tensor", f"st{j % 2}")
            rec.mm_group(bank[0:1, :],
                         [(P["ones_col_b"][:, :], sq[:, j * 512:(j + 1) * 512]),
                          (P["ones_col_b"][0:64, :], sqp[:, j * 512:(j + 1) * 512])], s_t)
            t_kn = s_t.n
            rec.wait("vector", s_t, t_kn)
            if j == 0:
                guard("vector", "kmx")
                rec.add("vector", lambda e, b=bank: e.reduce_max(
                    kmx[0:1, 0:1], b[0:1, :], axis=AX.X), s_v, 1)
            else:
                rec.add("vector", lambda e, b=bank: e.reduce_max(
                    kmx[0:1, 1:2], b[0:1, :], axis=AX.X), s_v, 1)
                rec.wait("vector", s_v)
                rec.add("vector", lambda e: e.tensor_tensor(
                    kmx[0:1, 0:1], kmx[0:1, 0:1], kmx[0:1, 1:2], OP.max), s_v, 1)
            setfree(f"st{j % 2}", "v", s_v.n)
        v_kmx = s_v.n
        t_knorm = s_t.n

        rec.wait("scalar", s_t, t_knorm)
        rec.add("scalar", lambda e: e.activation(sq[:, :], qn[:, :], AF.Square), s_sc, 1)
        rec.add("scalar", lambda e: e.activation(sqp[:, :], qpa[0:64, :], AF.Square), s_sc, 1)
        sc_sqq = s_sc.n
        rec.wait("tensor", s_sc, sc_sqq)
        for j in range(NT):
            bank = ps_st[j % 2]
            guard("tensor", f"st{j % 2}")
            rec.mm_group(bank[0:1, :],
                         [(P["ones_col_b"][:, :], sq[:, j * 512:(j + 1) * 512]),
                          (P["ones_col_b"][0:64, :], sqp[:, j * 512:(j + 1) * 512])], s_t)
            t_qn = s_t.n
            rec.wait("scalar", s_t, t_qn)
            if j == 0:
                rec.wait("scalar", s_v, v_kmx)
                guard("scalar", "qpa_row")
            rec.add("scalar", lambda e, b=bank, j=j: e.activation(
                qpa[64:65, j * 512:(j + 1) * 512], b[0:1, :], AF.Sqrt,
                scale=kmx[0:1, 0:1]), s_sc, 1)
            setfree(f"st{j % 2}", "sc", s_sc.n)
        sc_qrow = s_sc.n
        setfree("sq", "t", s_t.n)
        setfree("kmx", "sc", sc_qrow)

        rec.wait("tensor", s_ld, ld_head)
        rec.wait("tensor", s_sc, sc_qrow)
        for i in range(NT):
            i0 = i * 512
            njt = 4 * i + 4
            pend = None

            def emit_rpv(slot, sc_exp, j_sl, st, sp):
                rec.wait("tensor", s_sc, sc_exp)
                if st:
                    guard("tensor", "r")
                    guard("tensor", "attn")
                rec.add("tensor", lambda e, slot=slot, st=st, sp=sp: e.matmul(
                    ps_r[0:1, :], P["ones_col_b"][:, :], pt[:, slot, :],
                    start=st, stop=sp), s_t, 1)
                rec.add("tensor", lambda e, slot=slot, j_sl=j_sl, st=st, sp=sp: e.matmul(
                    ps_attn[:, :], vn[:, j_sl], pt[:, slot, :],
                    start=st, stop=sp), s_t, 1)
                setfree(f"pt{slot}", "t", s_t.n)

            for j128 in range(njt):
                bank = ps_st[j128 % 2]
                guard("tensor", f"st{j128 % 2}")
                j_sl = slice(j128 * 128, (j128 + 1) * 128)
                rec.mm_group(bank[:, :],
                             [(kn[:, j_sl], qn[:, i0:i0 + 512]),
                              (kpa[0:65, j_sl], qpa[0:65, i0:i0 + 512])], s_t)
                t_st = s_t.n
                rel = j128 - 4 * i
                slot = j128 % 3
                guard("vector", f"pt{slot}")
                guard("scalar", f"pt{slot}")
                if rel >= 0:
                    rec.wait("vector", s_t, t_st)
                    rec.add("vector", lambda e, b=bank, rel=rel: e.tensor_tensor(
                        b[:, rel * 128:(rel + 1) * 128], b[:, rel * 128:(rel + 1) * 128],
                        P["maskneg"][:, :], OP.add), s_v, 1)
                    if rel > 0:
                        rec.add("vector", lambda e, slot=slot, rel=rel: e.memset(
                            pt[:, slot, 0:rel * 128], 0.0), s_v, 1)
                    v_zero = s_v.n
                    rec.wait("scalar", s_v, v_zero)
                    rec.wait("scalar", s_t, t_st)
                    rec.add("scalar", lambda e, b=bank, slot=slot, rel=rel: e.activation(
                        pt[:, slot, rel * 128:512], b[:, rel * 128:512], AF.Exp), s_sc, 1)
                else:
                    rec.wait("scalar", s_t, t_st)
                    rec.add("scalar", lambda e, b=bank, slot=slot: e.activation(
                        pt[:, slot, :], b[:, :], AF.Exp), s_sc, 1)
                sc_exp = s_sc.n
                setfree(f"st{j128 % 2}", "sc", sc_exp)
                if pend is not None:
                    emit_rpv(*pend)
                pend = (slot, sc_exp, j_sl, j128 == 0, j128 == njt - 1)
            emit_rpv(*pend)
            t_pv = s_t.n
            rec.wait("vector", s_t, t_pv)
            guard("vector", "rr")
            rec.add("vector", lambda e: e.reciprocal(rr_f[:, :], ps_r[0:1, :]), s_v, 1)
            v_rr = s_v.n
            setfree("r", "v", v_rr)
            rec.wait("tensor", s_v, v_rr)
            guard("tensor", "rb")
            rec.add("tensor", lambda e: e.matmul(ps_rb[:, :], P["ones_row_f"][:, :], rr_f[:, :],
                                                 start=True, stop=True), s_t, 1)
            t_rb = s_t.n
            setfree("rr", "t", t_rb)
            rec.wait("scalar", s_t, t_rb)
            guard("scalar", "rb_sb")
            rec.add("scalar", lambda e: e.copy(rb[:, :], ps_rb[:, :]), s_sc, 1)
            sc_rb = s_sc.n
            setfree("rb", "sc", sc_rb)
            rec.wait("vector", s_sc, sc_rb)
            rec.wait("vector", s_t, t_pv)
            guard("vector", "ao")
            rec.add("vector", lambda e: e.tensor_tensor(ao[:, :], ps_attn[:, :], rb[:, :],
                                                        OP.mult), s_v, 1)
            v_ao = s_v.n
            setfree("attn", "v", v_ao)
            setfree("rb_sb", "v", v_ao)
            rec.wait("gpsimd", s_v, v_ao)
            rec.add("gpsimd", lambda e, h=h, i0=i0: e.dma_start(
                scr["AOT"][h, :, i0:i0 + 512], ao[:, :]), s_stg, 16)
            setfree("ao", "stg", s_stg.n)
        setfree("qk", "t", t_pv)
        setfree("qpa_row", "t", t_pv)
    return rec



def rec_s3(ctx, sems, P):
    rec = Rec(ctx)
    s_ld, s_t, s_v, s_sc, s_g, s_st, s_stg = (sems[k] for k in ("ld", "t", "v", "sc", "g", "st", "stg"))
    sb, psb = ctx.sb, ctx.psb
    scr, W = ctx.scr, ctx.wdram

    owT = sb("s3_owT", [128, 16, 2048], BF16)
    at = sb("s3_at", [128, 16, 512], BF16)
    xt = sb("s3_xt", [128, 16, 512], F32)
    xn = sb("s3_xn", [128, 2, 512], F32)
    banks = [psb("s3_ps0", [128, 512], F32), psb("s3_ps1", [128, 512], F32)]

    for hvt in range(16):
        for dg in range(4):
            rec.add("sync", lambda e, hvt=hvt, dg=dg: e.dma_start_transpose(
                owT[:, hvt, dg * 512:(dg + 1) * 512],
                W["OWB"][dg * 512:(dg + 1) * 512, hvt * 128:(hvt + 1) * 128]), s_ld, 16)
    w_ready = s_ld.n

    free = {}
    def guard(engname, res):
        if res in free:
            nm, cnt = free[res]
            rec.wait(engname, sems[nm], cnt)
    def setfree(res, nm, cnt):
        free[res] = (nm, cnt)

    for c in range(NT):
        c0 = c * 512
        guard("sync", "at")
        for hvt in range(16):
            rec.add("sync", lambda e, hvt=hvt, c0=c0: e.dma_start(
                at[:, hvt, :], scr["AOT"][hvt, :, c0:c0 + 512]), s_ld, 16)
        guard("sync", "xt")
        for d in range(16):
            rec.add("sync", lambda e, d=d, c0=c0: e.dma_start(
                xt[:, d, :], scr["XT"][d, :, c0:c0 + 512]), s_ld, 16)
        ld_c = s_ld.n
        rec.wait("tensor", s_ld, w_ready)
        rec.wait("tensor", s_ld, ld_c)
        last_t = 0
        v_done = {}
        st_done = {}
        for oc in range(16):
            bank = banks[oc % 2]
            guard("tensor", f"b{oc % 2}")
            rec.mm_group(bank[:, :], [(owT[:, hvt, oc * 128:(oc + 1) * 128], at[:, hvt, :])
                                      for hvt in range(16)], s_t)
            last_t = s_t.n
            sl = oc % 2
            rec.wait("vector", s_t, last_t)
            rec.wait("vector", s_ld, ld_c)
            guard("vector", f"xn{sl}")
            rec.add("vector", lambda e, sl=sl, b=bank, oc=oc: e.tensor_tensor(
                xn[:, sl, :], b[:, :], xt[:, oc, :], OP.add), s_v, 1)
            setfree(f"b{sl}", "v", s_v.n)
            v_done[oc] = s_v.n
            rec.wait("gpsimd", s_v, v_done[oc])
            rec.add("gpsimd", lambda e, oc=oc, sl=sl, c0=c0: e.dma_start(
                scr["XT"][oc, :, c0:c0 + 512], xn[:, sl, :]), s_stg, 16)
            st_done[oc] = s_stg.n
            setfree(f"xn{sl}", "stg", st_done[oc])
        setfree("at", "t", last_t)
        setfree("xt", "v", v_done[15])
    return rec



def rec_s4ln(ctx, sems, P, XL2T):
    rec = Rec(ctx)
    s_ld, s_t, s_v, s_sc, s_g, s_st, s_stg = (sems[k] for k in ("ld", "t", "v", "sc", "g", "st", "stg"))
    sb, psb = ctx.sb, ctx.psb
    scr = ctx.scr
    xt = sb("s4l_xt", [128, 16, 512], F32)
    xbf = None
    sqbf = sb("s4l_sqbf", [128, 16, 512], BF16)
    tmpf = sb("s4l_tmpf", [128, 2, 512], F32)
    rows = sb("s4l_rows", [1, 4 * 512], F32)
    ab = sb("s4l_ab", [128, 2 * 512], F32)
    ps_row0 = psb("s4l_r0", [128, 512], F32)
    ps_row1 = psb("s4l_r1", [128, 512], F32)
    ps_bc0 = psb("s4l_b0", [128, 512], F32)
    ps_bc1 = psb("s4l_b1", [128, 512], F32)
    prev = {}
    for c in range(NT):
        c0 = c * 512
        if prev:
            for nm, val in prev["xt_free"]:
                rec.wait("sync", sems[nm], val)
        for d in range(16):
            rec.add("sync", lambda e, d=d, c0=c0: e.dma_start(
                xt[:, d, :], scr["XT"][d, :, c0:c0 + 512]), s_ld, 16)
        ld_tiles = s_ld.n
        for e_ in ("vector", "scalar"):
            rec.wait(e_, s_ld, ld_tiles)
        if prev:
            rec.wait("scalar", s_t, prev["xbf_free_t"])
            rec.wait("tensor", s_v, prev["psrow_free_v"])
            rec.wait("tensor", s_sc, prev["psbc_free_sc"])
        lnc = rec_ln_chunk(rec, sems, P, {
            "xt": xt, "xbf": xbf, "sqbf": sqbf,
            "xln_at": (lambda d, c=c: XL2T[:, d, c, :]),
            "xt_ready": ("ld", ld_tiles),
            "tmpf": tmpf, "rows": rows, "ab": ab,
            "ab_guard": prev.get("ab_free"),
            "ps_row0": ps_row0, "ps_row1": ps_row1,
            "ps_bc0": ps_bc0, "ps_bc1": ps_bc1}, c, 16, 1.0 / 2048.0, 0)
        prev = {
            "xt_free": lnc["xt_free"],
            "ab_free": lnc["ab_free"],
            "xbf_free_t": max(lnc["xbf_free_t"], lnc["sqbf_free_t"]),
            "psrow_free_v": lnc["psrow_free_v"],
            "psbc_free_sc": lnc["psbc_free_sc"],
            "ab_free": lnc["ab_free"],
        }
    return rec


def rec_s4w2prep(ctx, sems, P, rec=None):
    if rec is None:
        rec = Rec(ctx)
    s_ld, s_t, s_v, s_sc, s_g, s_st, s_stg = (sems[k] for k in ("ld", "t", "v", "sc", "g", "st", "stg"))
    sb = ctx.sb
    W = ctx.wdram
    wA = sb("s4p_wA", [128, 8192], F32)
    wB = sb("s4p_wB", [128, 8192], F32)
    wbf = sb("s4p_wbf", [128, 8192], BF16)
    tb = sb("s4p_tb", [128, 2, 2048], BF16)
    prev = {}
    for rc in range(16):
        if prev:
            rec.wait("sync", s_v, prev["wA_free_v"])
            rec.wait("sync", s_g, prev["wA_free_g"])
        rec.add("sync", lambda e, rc=rc: e.dma_start(
            wA[:, :], ctx.wins["w_W2"][rc * 128:(rc + 1) * 128, :]), s_ld, 16)
        rec.wait("vector", s_ld, s_ld.n)
        rec.wait("gpsimd", s_ld, s_ld.n)
        src, dst = wA, wB
        for lvl in range(6):
            sseg = 1 << lvl
            nb = 8192 // (2 * sseg * 128)
            wl = sseg * 128
            def mk(out, a, b, op):
                def f(e, out=out, a=a, b=b, op=op):
                    return e.tensor_tensor(out, a, b, op)
                return f
            sv = src.rearrange("p (g two w) -> p g two w", two=2, w=wl)
            dv = dst.rearrange("p (g two w) -> p g two w", two=2, w=wl)
            if lvl > 0:
                rec.wait("vector", s_g)
                rec.wait("gpsimd", s_v)
            rec.add("vector", mk(dv[:, :, 0, :], sv[:, :, 0, :], sv[:, :, 1, :], OP.add), s_v, 1)
            rec.add("gpsimd", mk(dv[:, :, 1, :], sv[:, :, 0, :], sv[:, :, 1, :], OP.subtract), s_g, 1)
            src, dst = dst, src
        rec.wait("vector", s_g)
        rec.wait("vector", s_v)
        if prev:
            rec.wait("vector", s_stg, prev["wbf_store"])
        rec.add("vector", lambda e: e.tensor_copy(wbf[:, :], wA[:, :]), s_v, 1)
        wA_free_v = s_v.n
        rec.wait("gpsimd", s_v, wA_free_v)
        rec.add("gpsimd", lambda e, rc=rc: e.dma_start(
            W["W2BF"][rc * 128:(rc + 1) * 128, :], wbf[:, :]), s_stg, 16)
        prev = {"wA_free_v": wA_free_v, "wA_free_g": s_g.n, "wbf_store": s_stg.n}
    all_stored = s_stg.n
    rec.wait("scalar", s_stg, all_stored)
    pp = {}
    for ht in range(64):
        sl = ht % 2
        if ht >= 2:
            rec.wait("scalar", s_stg, pp[ht - 2])
        rec.add("scalar", lambda e, ht=ht, sl=sl: e.dma_start_transpose(
            tb[:, sl, :], W["W2BF"][:, ht * 128:(ht + 1) * 128]), s_ld, 16)
        rec.wait("gpsimd", s_ld, s_ld.n)
        rec.add("gpsimd", lambda e, ht=ht, sl=sl: e.dma_start(
            W["W2T"][ht], tb[:, sl, :]), s_stg, 16)
        pp[ht] = s_stg.n
    return rec


def rec_s4uv(ctx, sems, P, XL2T):
    rec = Rec(ctx)
    s_ld, s_ld2, s_t, s_v, s_sc, s_g, s_st, s_stg = (sems[k] for k in ("ld", "ld2", "t", "v", "sc", "g", "st", "stg"))
    sb, psb = ctx.sb, ctx.psb
    scr, W = ctx.scr, ctx.wdram
    w1t = sb("s4u_w1t", [128, 2, 16, 512], BF16)
    w3t = sb("s4u_w3t", [128, 2, 16, 512], BF16)
    sU = sb("s4u_sU", [128, 2, 512], BF16)
    hda = sb("s4u_hda", [128, 2, 512], BF16)
    hdo = sb("s4u_hdo", [128, 2, 512], BF16)
    psU = [psb("s4u_pU0", [128, 512], F32), psb("s4u_pU1", [128, 512], F32)]
    psV = [psb("s4u_pV0", [128, 512], F32), psb("s4u_pV1", [128, 512], F32)]
    psH = [psb("s4u_pH0", [128, 512], F32), psb("s4u_pH1", [128, 512], F32)]

    free = {}
    def guard(engname, res):
        if res in free:
            nm, cnt = free[res]
            rec.wait(engname, sems[nm], cnt)
    def setfree(res, nm, cnt):
        free[res] = (nm, cnt)

    def load_og(og):
        slw = og % 2
        guard("sync", f"w1t{slw}")
        for d in range(16):
            rec.add("sync", lambda e, og=og, d=d, slw=slw: e.dma_start_transpose(
                w1t[:, slw, d, :], W["W1B"][og * 512:(og + 1) * 512, d * 128:(d + 1) * 128]),
                s_ld, 16)
            rec.add("sync", lambda e, og=og, d=d, slw=slw: e.dma_start_transpose(
                w3t[:, slw, d, :], W["W3B"][og * 512:(og + 1) * 512, d * 128:(d + 1) * 128]),
                s_ld, 16)
        return s_ld.n

    wready = {0: load_og(0)}
    for og in range(16):
        if og + 1 < 16:
            wready[og + 1] = load_og(og + 1)
        slw = og % 2
        last_t = 0
        for oi in range(4):
            oc = og * 4 + oi
            for c in range(NT):
                ii = (oc * NT + c) % 2
                rec.wait("tensor", s_ld, wready[og])
                guard("tensor", f"pU{ii}")
                rec.mm_group(psU[ii][:, :],
                             [(w1t[:, slw, d, oi * 128:(oi + 1) * 128], XL2T[:, d, c, :])
                              for d in range(16)], s_t)
                tU = s_t.n
                guard("tensor", f"pV{ii}")
                rec.mm_group(psV[ii][:, :],
                             [(w3t[:, slw, d, oi * 128:(oi + 1) * 128], XL2T[:, d, c, :])
                              for d in range(16)], s_t)
                tV = s_t.n
                last_t = tV
                rec.wait("scalar", s_t, tU)
                guard("scalar", f"sU{ii}")
                rec.add("scalar", lambda e, ii=ii: e.activation(
                    sU[:, ii, :], psU[ii][:, :], AF.Silu), s_sc, 1)
                sc_silu = s_sc.n
                setfree(f"pU{ii}", "sc", sc_silu)
                rec.wait("vector", s_sc, sc_silu)
                rec.wait("vector", s_t, tV)
                guard("vector", f"hda{ii}")
                rec.add("vector", lambda e, ii=ii: e.tensor_tensor(
                    hda[:, ii, :], psV[ii][:, :], sU[:, ii, :], OP.mult), s_v, 1)
                v_hda = s_v.n
                setfree(f"pV{ii}", "v", v_hda)
                setfree(f"sU{ii}", "v", v_hda)
                rec.wait("tensor", s_v, v_hda)
                guard("tensor", f"pH{ii}")
                rec.add("tensor", lambda e, ii=ii: e.matmul(
                    psH[ii][:, :], P["h128s_b"][:, :], hda[:, ii, :],
                    start=True, stop=True), s_t, 1)
                tH = s_t.n
                setfree(f"hda{ii}", "t", tH)
                rec.wait("scalar", s_t, tH)
                guard("scalar", f"hdo{ii}")
                rec.add("scalar", lambda e, ii=ii: e.copy(hdo[:, ii, :], psH[ii][:, :]), s_sc, 1)
                sc_hdo = s_sc.n
                setfree(f"pH{ii}", "sc", sc_hdo)
                rec.wait("gpsimd", s_sc, sc_hdo)
                rec.add("gpsimd", lambda e, oc=oc, c=c, ii=ii: e.dma_start(
                    scr["HD"][oc, :, c * 512:(c + 1) * 512], hdo[:, ii, :]), s_stg, 16)
                setfree(f"hdo{ii}", "stg", s_stg.n)
        setfree(f"w1t{slw}", "t", last_t)
    return rec


def rec_s4w2(ctx, sems, P):
    rec = Rec(ctx)
    s_ld, s_ld2, s_t, s_v, s_sc, s_g, s_st, s_stg = (sems[k] for k in ("ld", "ld2", "t", "v", "sc", "g", "st", "stg"))
    sb, psb = ctx.sb, ctx.psb
    scr, W = ctx.scr, ctx.wdram
    hd = sb("s4w_hd", [128, 64, 512], BF16)
    xt = sb("s4w_xt", [128, 16, 512], F32)
    w2b = sb("s4w_w2b", [128, 2, 16, 1024], BF16)
    xn = sb("s4w_xn", [128, 2, 512], F32)
    banks = [psb(f"s4w_p{i}", [128, 512], F32) for i in range(8)]

    free = {}
    def guard(engname, res):
        if res in free:
            nm, cnt = free[res]
            rec.wait(engname, sems[nm], cnt)
    def setfree(res, nm, cnt):
        free[res] = (nm, cnt)

    for c in range(NT):
        c0 = c * 512
        guard("sync", "hd")
        for ht in range(64):
            rec.add("sync", lambda e, ht=ht, c0=c0: e.dma_start(
                hd[:, ht, :], scr["HD"][ht, :, c0:c0 + 512]), s_ld, 16)
        guard("sync", "xt")
        for d in range(16):
            rec.add("sync", lambda e, d=d, c0=c0: e.dma_start(
                xt[:, d, :], scr["XT"][d, :, c0:c0 + 512]), s_ld, 16)
        ld_c = s_ld.n
        rec.wait("tensor", s_ld, ld_c)
        last_t = 0
        v_res = 0
        for dh in range(2):
            for q in range(4):
                slq = q % 2
                guard("scalar", f"w2b{slq}")
                for ht16 in range(16):
                    rec.add("scalar", lambda e, q=q, ht16=ht16, dh=dh, slq=slq: e.dma_start(
                        w2b[:, slq, ht16, :],
                        W["W2T"][q * 16 + ht16, :, dh * 1024:(dh + 1) * 1024]), s_ld2, 16)
                ldw = s_ld2.n
                rec.wait("tensor", s_ld2, ldw)
                for d8 in range(8):
                    bank = banks[d8]
                    if q == 0:
                        guard("tensor", f"bk{d8}")
                    def grp(e, q=q, d8=d8, slq=slq, bank=bank, first=(q == 0), last=(q == 3)):
                        for ht16 in range(15):
                            e.matmul(bank[:, :], w2b[:, slq, ht16, d8 * 128:(d8 + 1) * 128],
                                     hd[:, q * 16 + ht16, :], start=(first and ht16 == 0), stop=False)
                        return e.matmul(bank[:, :], w2b[:, slq, 15, d8 * 128:(d8 + 1) * 128],
                                        hd[:, q * 16 + 15, :], start=False, stop=last)
                    rec.add("tensor", grp, s_t, 1)
                    last_t = s_t.n
                setfree(f"w2b{slq}", "t", last_t)
            for d8 in range(8):
                doc = dh * 8 + d8
                sl = doc % 2
                rec.wait("vector", s_t, last_t)
                guard("vector", f"xn{sl}")
                rec.add("vector", lambda e, d8=d8, doc=doc, sl=sl: e.tensor_tensor(
                    xn[:, sl, :], banks[d8][:, :], xt[:, doc, :], OP.add), s_v, 1)
                setfree(f"bk{d8}", "v", s_v.n)
                v_res = s_v.n
                rec.wait("gpsimd", s_v, v_res)
                rec.add("gpsimd", lambda e, doc=doc, sl=sl, c0=c0: e.dma_start(
                    scr["XT"][doc, :, c0:c0 + 512], xn[:, sl, :]), s_stg, 16)
                setfree(f"xn{sl}", "stg", s_stg.n)
        setfree("hd", "t", last_t)
        setfree("xt", "v", v_res)
    return rec



def rec_dumpxl(ctx, sems, XL2T):
    rec = Rec(ctx)
    s_stg = sems["stg"]
    for d in range(16):
        for c in range(NT):
            rec.add("gpsimd", lambda e, d=d, c=c: e.dma_start(
                ctx.scr["XLD"][d, :, c * 512:(c + 1) * 512], XL2T[:, d, c, :]), s_stg, 16)
    return rec



def rec_s5(ctx, sems, P):
    rec = Rec(ctx)
    s_ld, s_ldg, s_t, s_v, s_sc, s_g, s_st, s_stg = (
        sems[k] for k in ("ld", "ldg", "t", "v", "sc", "g", "st", "stg"))
    sb, psb = ctx.sb, ctx.psb
    scr = ctx.scr
    wins = ctx.wins
    SCALE = 0.125

    xt = sb("s5_xt", [128, 16, 512], F32)
    sqbf = sb("s5_sqbf", [128, 16, 512], BF16)
    xlt = sb("s5_xlt", [128, 16, 512], BF16)
    tmpf = sb("s5_tmpf", [128, 2, 512], F32)
    rows = sb("s5_rows", [1, 4 * 512], F32)
    ab = sb("s5_ab", [128, 2 * 512], F32)
    wmv = sb("s5_wmv", [128, 16, 512], BF16)
    qt = sb("s5_qt", [128, 8, 512], BF16)
    kt = sb("s5_kt", [128, 8, 512], BF16)
    vnb = sb("s5_vnb", [128, 4, 4, 512], BF16)
    gnb = sb("s5_gnb", [128, 4, 4, 512], BF16)
    knb = sb("s5_knb", [128, 4, 2, 512], BF16)
    enb = sb("s5_enb", [128, 4, 2, 512], BF16)
    rnb = sb("s5_rnb", [128, 4, 2, 512], BF16)
    scl = sb("s5_scl", [128, 2, 512], F32)
    carry = sb("s5_carry", [1, 1024], BF16)
    HkS = sb("s5_HkS", [64, 1024], F32)
    HkB = sb("s5_HkB", [128, 1024], BF16)
    HvS = sb("s5_HvS", [64, 2048], F32)
    HvB = sb("s5_HvB", [64, 2048], BF16)
    db = sb("s5_db", [128, 2, 128], BF16)
    gb = sb("s5_gb", [128, 2, 128], BF16)
    okn = sb("s5_okn", [128, 2, 64], F32)
    mxc = sb("s5_mxc", [128, 4], F32)
    pvb = sb("s5_pvb", [128, 2, 64], BF16)
    ptt = sb("s5_ptt", [64, 2, 128], BF16)
    ett = sb("s5_ett", [64, 2, 128], BF16)
    sqd = sb("s5_sqd", [128, 128], BF16)
    colf = sb("s5_colf", [128, 8], F32)
    sgb = sb("s5_sgb", [128, 128], BF16)
    ggf = sb("s5_ggf", [128, 2, 128], F32)
    o1f = sb("s5_o1f", [128, 2, 128], F32)
    awt = sqbf
    aob_at = lambda h: xlt[:, h, 0:128]
    gnw_row = sb("s5_gnwrow", [1, 128], F32)
    gnwB = sb("s5_gnwB", [128, 128], F32)
    xn = tmpf

    B = [psb(f"s5_b{i}", [128, 512], F32) for i in range(7)]
    b7 = psb("s5_b7", [128, 1024], BF16)

    free = {}
    def guard(engname, res):
        if res in free:
            nm, cnt = free[res]
            rec.wait(engname, sems[nm], cnt)
    def setfree(res, nm, cnt):
        free[res] = (nm, cnt)

    rec.add("gpsimd", lambda e: e.memset(HkS[:, :], 0.0), s_g, 1)
    rec.add("gpsimd", lambda e: e.memset(HkB[:, :], 0.0), s_g, 1)
    rec.add("gpsimd", lambda e: e.memset(HvS[:, :], 0.0), s_g, 1)
    rec.add("gpsimd", lambda e: e.memset(HvB[:, :], 0.0), s_g, 1)
    rec.add("gpsimd", lambda e: e.memset(carry[:, :], 0.0), s_g, 1)
    g_init = s_g.n
    for h in range(16):
        setfree(f"hkb{h}", "g", g_init)
        setfree(f"hvb{h}", "g", g_init)
    setfree("carry_w", "g", g_init)
    gn_row = wins["w1_gn"].rearrange("(a b) -> a b", a=1)
    rec.add("sync", lambda e: e.dma_start(gnw_row[:, :], gn_row), s_ld, 16)
    rec.wait("tensor", s_ld, s_ld.n)
    rec.add("tensor", lambda e: e.matmul(B[0][:, 0:128], P["ones_row_f"][:, :], gnw_row[:, :],
                                         start=True, stop=True), s_t, 1)
    rec.wait("scalar", s_t, s_t.n)
    rec.add("scalar", lambda e: e.copy(gnwB[:, :], B[0][:, 0:128]), s_sc, 1)
    setfree("B0", "sc", s_sc.n)

    WB = ctx.wdram
    L1MAP = {"w1_q": "Q1B", "w1_k": "K1B", "w1_v": "V1B", "w1_g": "G1B",
             "w1_s": "S1B", "w1_o": "O1B"}

    QKO = {"q": "Q1B", "k": "K1B", "o": "O1B"}

    def load_qk(which, oc, slot):
        src = WB[QKO[which]].rearrange("(dt p) o -> p dt o", p=128)[:, :, oc * 128:(oc + 1) * 128]
        guard("sync", f"mvq{slot}")
        guard("sync", "mvfull")
        rec.add("sync", lambda e, src=src, slot=slot: e.dma_start(
            wmv[:, :, slot * 128:(slot + 1) * 128], src), s_ld, 16)
        return s_ld.n

    def load_mv(name, ocols):
        src = WB[L1MAP[name]].rearrange("(dt p) o -> p dt o", p=128)[:, :, ocols]
        guard("sync", "mvq0")
        guard("sync", "mvq1")
        guard("sync", "mvfull")
        rec.add("sync", lambda e, src=src: e.dma_start(wmv[:, :, :], src), s_ld, 16)
        return s_ld.n

    def load_wo(oc, slot):
        return load_qk("o", oc, slot)

    prev_ab = None
    hk_gdone = {}
    hv_gdone = {}
    for c in range(NT):
        c0 = c * 512
        guard("sync", "xt")
        for d in range(16):
            rec.add("sync", lambda e, d=d, c0=c0: e.dma_start(
                xt[:, d, :], scr["XT"][d, :, c0:c0 + 512]), s_ld, 16)
        ld_tiles = s_ld.n
        for e_ in ("vector", "scalar"):
            rec.wait(e_, s_ld, ld_tiles)
        guard("scalar", "sqbf")
        guard("vector", "xn0")
        guard("vector", "xn1")
        guard("tensor", "B2")
        guard("tensor", "B3")
        guard("tensor", "B4")
        guard("tensor", "B5")
        guard("gpsimd", "xlt_w")
        lnc = rec_ln_chunk(rec, sems, P, {
            "xt": xt, "xbf": None, "sqbf": sqbf,
            "xln_at": (lambda d: xlt[:, d, :]),
            "xt_ready": ("ld", ld_tiles),
            "tmpf": tmpf, "rows": rows, "ab": ab, "ab_guard": prev_ab,
            "ps_row0": B[2], "ps_row1": B[3],
            "ps_bc0": B[4], "ps_bc1": B[5]}, c, 16, 1.0 / 2048.0, 0)
        prev_ab = lnc["ab_free"]
        sq_free_t = lnc["sqbf_free_t"]
        setfree("B2", "v", lnc["psrow_free_v"])
        setfree("B3", "v", lnc["psrow_free_v"])
        setfree("B4", "sc", lnc["psbc_free_sc"])
        setfree("B5", "sc", lnc["psbc_free_sc"])
        xln_g = lnc["xln_done_g"]
        rec.wait("tensor", s_g, xln_g)

        nmm = 0
        pe = {}
        for which, buf in (("q", qt), ("k", kt)):
            for oc in range(8):
                slot = oc % 2
                lw = load_qk(which, oc, slot)
                par = nmm % 2
                bank = B[par]
                guard("tensor", f"B{par}")
                rec.wait("tensor", s_ld, lw)
                rec.mm_group(bank[:, :], [(wmv[:, d, slot * 128:(slot + 1) * 128], xlt[:, d, :])
                                          for d in range(16)], s_t)
                tmm = s_t.n
                setfree(f"mvq{slot}", "t", tmm)
                rec.wait("scalar", s_t, tmm)
                guard("scalar", "qt" if which == "q" else "kt")
                if which == "q":
                    rec.add("scalar", lambda e, oc=oc, b=bank: e.mul(qt[:, oc, :], b[:, :], SCALE), s_sc, 1)
                else:
                    rec.add("scalar", lambda e, oc=oc, b=bank: e.copy(kt[:, oc, :], b[:, :]), s_sc, 1)
                setfree(f"B{par}", "sc", s_sc.n)
                nmm += 1
        if "QKD" in ctx.dbg:
            rec.wait("gpsimd", s_sc, s_sc.n)
            for oc in range(8):
                rec.add("gpsimd", lambda e, c0=c0, oc=oc: e.dma_start(
                    ctx.scr["QKD"][0, oc * 128:(oc + 1) * 128, c0:c0 + 512], qt[:, oc, :]), s_stg, 16)
                rec.add("gpsimd", lambda e, c0=c0, oc=oc: e.dma_start(
                    ctx.scr["QKD"][1, oc * 128:(oc + 1) * 128, c0:c0 + 512], kt[:, oc, :]), s_stg, 16)
        ldi = 0
        for name, buf, nch, kind in (("w1_v", vnb, 4, "v"), ("w1_g", gnb, 4, "g"),
                                     ("w1_k", knb, 2, "kn"), ("w1_s", None, 2, "s")):
            for j in range(nch):
                ldi += 1
                lw = load_mv(name, slice(j * 512, (j + 1) * 512))
                for sub in range(4):
                    t_sl = slice(sub * 128, (sub + 1) * 128)
                    par = nmm % 2
                    bank = B[par]
                    guard("tensor", f"B{par}")
                    rec.wait("tensor", s_ld, lw)
                    rec.mm_group(bank[:, :], [(xlt[:, d, t_sl], wmv[:, d, :])
                                              for d in range(16)], s_t)
                    tmm = s_t.n
                    if sub == 3:
                        setfree("mvfull", "t", tmm)
                    if kind == "s":
                        cpar = nmm % 2
                        rec.wait("vector", s_t, tmm)
                        guard("vector", f"scl{cpar}")
                        rec.add("vector", lambda e, b=bank, cpar=cpar: e.tensor_scalar(
                            scl[:, cpar, :], b[:, :], -32.0, 32.0, OP.max, OP.min), s_v, 1)
                        v_cl = s_v.n
                        setfree(f"B{par}", "v", v_cl)
                        rec.wait("scalar", s_v, v_cl)
                        guard("scalar", "mvbufs_t")
                        rec.add("scalar", lambda e, sub=sub, j=j, cpar=cpar: e.activation(
                            enb[:, sub, j, :], scl[:, cpar, :], AF.Exp), s_sc, 1)
                        setfree(f"scl{cpar}", "sc", s_sc.n)
                        pe[(sub, j)] = s_sc.n
                    else:
                        rec.wait("vector", s_t, tmm)
                        guard("vector", "mvbufs_t")
                        if kind == "g":
                            guard("vector", "gnb_sc")
                        rec.add("vector", lambda e, b=bank, buf=buf, sub=sub, j=j: e.tensor_copy(
                            buf[:, sub, j, :], b[:, :]), s_v, 1)
                        setfree(f"B{par}", "v", s_v.n)
                    nmm += 1
        xlt_free_t = s_t.n
        rec.wait("vector", s_t, xlt_free_t)
        for sub in range(4):
            for j in range(2):
                guard("tensor", "B2")
                rec.wait("tensor", s_sc, pe[(sub, j)])
                guard("tensor", "carry_w")
                def cmm(e, sub=sub, j=j):
                    e.matmul(B[2][:, :], P["uincl_b"][:, :], enb[:, sub, j, :], start=True, stop=False)
                    return e.matmul(B[2][:, :], P["ones_row_b"][:, :], carry[:, j * 512:(j + 1) * 512],
                                    start=False, stop=True)
                rec.add("tensor", cmm, s_t, 1)
                t_c = s_t.n
                guard("tensor", "B3")
                def csum(e, sub=sub, j=j):
                    e.matmul(B[3][0:1, :], P["ones_col_b"][:, :], enb[:, sub, j, :], start=True, stop=False)
                    return e.matmul(B[3][0:1, :], P["ones_1b"][:, :], carry[:, j * 512:(j + 1) * 512],
                                    start=False, stop=True)
                rec.add("tensor", csum, s_t, 1)
                t_cs = s_t.n
                rec.wait("vector", s_t, t_c)
                def reciprnb(e, sub=sub, j=j):
                    with ctx.nc.allow_low_precision(reason="R=1/C used as bf16 matmul operand"):
                        return e.reciprocal(rnb[:, sub, j, :], B[2][:, :])
                rec.add("vector", reciprnb, s_v, 1)
                setfree("B2", "v", s_v.n)
                rec.wait("scalar", s_t, t_cs)
                rec.wait("scalar", s_t, t_c)
                rec.add("scalar", lambda e, j=j: e.copy(carry[:, j * 512:(j + 1) * 512], B[3][0:1, :]), s_sc, 1)
                setfree("B3", "sc", s_sc.n)
                setfree("carry_w", "sc", s_sc.n)

        for sub in range(4):
            for h in range(16):
                par = h % 2
                qsl = qt[64 * par:64 * par + 64, h // 2, sub * 128:(sub + 1) * 128]
                ksl = kt[64 * par:64 * par + 64, h // 2, sub * 128:(sub + 1) * 128]
                esl = enb[:, sub, h // 8, (h % 8) * 64:(h % 8) * 64 + 64]
                rsl = rnb[:, sub, h // 8, (h % 8) * 64:(h % 8) * 64 + 64]
                knsl = knb[:, sub, h // 8, (h % 8) * 64:(h % 8) * 64 + 64]
                vsl = vnb[:, sub, h // 4, (h % 4) * 128:(h % 4) * 128 + 128]
                gsl = gnb[:, sub, h // 4, (h % 4) * 128:(h % 4) * 128 + 128]
                cellb = B[4 + par]
                ovb = B[par]
                guard("tensor", f"B{4 + par}")
                rec.wait("tensor", s_sc, pe[(sub, h // 8)])
                rec.add("tensor", lambda e, b=cellb, ksl=ksl, qsl=qsl: e.matmul(
                    b[:, 0:128], ksl, qsl, start=True, stop=True), s_t, 1)
                t_a = s_t.n
                rec.wait("vector", s_t, t_a)
                guard("vector", f"db{par}")
                rec.add("vector", lambda e, b=cellb, par=par: e.tensor_tensor(
                    db[:, par, :], b[:, 0:128], P["uincl_f"][:, :], OP.mult), s_v, 1)
                v_d = s_v.n
                guard("tensor", "B6ok")
                rec.wait("tensor", s_v, v_d)
                guard("tensor", f"hkb{h}")
                def okmm(e, par=par, esl=esl, qsl=qsl, h=h):
                    e.matmul(B[6][:, 0:64], db[:, par, :], esl, start=True, stop=False)
                    return e.matmul(B[6][:, 0:64], qsl,
                                    HkB[64 * par:64 * par + 64, h * 64:(h + 1) * 64],
                                    start=False, stop=True)
                rec.add("tensor", okmm, s_t, 1)
                t_ok = s_t.n
                setfree(f"db{par}", "t", t_ok)
                rec.wait("vector", s_t, t_ok)
                guard("vector", f"okn{par}")
                rec.add("vector", lambda e, par=par, rsl=rsl: e.tensor_tensor(
                    okn[:, par, :], B[6][:, 0:64], rsl, OP.mult), s_v, 1)
                v_okn = s_v.n
                setfree("B6ok", "v", v_okn)
                rec.wait("vector", s_v)
                rec.add("vector", lambda e, par=par: e.reduce_max(
                    mxc[:, par:par + 1], okn[:, par, :], axis=AX.X), s_v, 1)
                rec.wait("vector", s_v)
                rec.add("vector", lambda e, par=par: e.tensor_scalar(
                    mxc[:, 2 + par:3 + par], mxc[:, par:par + 1], -1.0, None, OP.mult), s_v, 1)
                v_nm = s_v.n
                rec.wait("scalar", s_v, v_nm)
                guard("scalar", f"pvb{par}")
                guard("scalar", "colf01")
                rec.add("scalar", lambda e, par=par: e.activation(
                    pvb[:, par, :], okn[:, par, :], AF.Exp,
                    bias=mxc[:, 2 + par:3 + par], accum_out=colf[:, par:par + 1]), s_sc, 1)
                sc_pv = s_sc.n
                setfree(f"okn{par}", "sc", sc_pv)
                rec.wait("vector", s_sc, sc_pv)
                rec.add("vector", lambda e, par=par: e.reciprocal(
                    colf[:, 2 + par:3 + par], colf[:, par:par + 1]), s_v, 1)
                setfree("colf01", "v", s_v.n)
                rec.wait("vector", s_v)
                rec.add("vector", lambda e, par=par, rsl=rsl: e.scalar_tensor_tensor(
                    pvb[:, par, :], pvb[:, par, :], colf[:, 2 + par:3 + par], rsl,
                    OP.mult, OP.mult), s_v, 1)
                v_pt = s_v.n
                if "PVD" in ctx.dbg:
                    rec.wait("gpsimd", s_v, v_pt)
                    rec.add("gpsimd", lambda e, c0=c0, sub=sub, h=h, par=par: e.dma_start(
                        ctx.scr["PVD"][c0 + sub * 128:c0 + (sub + 1) * 128, h * 64:(h + 1) * 64],
                        pvb[:, par, :]), s_stg, 16)
                guard("tensor", "B7a")
                rec.wait("tensor", s_v, v_pt)
                rec.add("tensor", lambda e, par=par: e.transpose(
                    b7[0:64, 0:128], pvb[:, par, :], P["ident_b"][:, :]), s_t, 1)
                setfree(f"pvb{par}", "t", s_t.n)
                rec.add("tensor", lambda e, esl=esl: e.transpose(
                    b7[0:64, 128:256], esl, P["ident_b"][:, :]), s_t, 1)
                t_tr = s_t.n
                rec.wait("scalar", s_t, t_tr)
                guard("scalar", f"ptt{par}")
                rec.add("scalar", lambda e, par=par: e.copy(ptt[:, par, :], b7[0:64, 0:128]), s_sc, 1)
                rec.add("scalar", lambda e, par=par: e.copy(ett[:, par, :], b7[0:64, 128:256]), s_sc, 1)
                sc_tr = s_sc.n
                setfree("B7a", "sc", sc_tr)
                rec.wait("tensor", s_sc, sc_tr)
                rec.add("tensor", lambda e, b=cellb, par=par: e.matmul(
                    b[:, 128:256], ett[:, par, :], ptt[:, par, :], start=True, stop=True), s_t, 1)
                t_g = s_t.n
                setfree(f"B{4 + par}", "t", t_g)
                rec.wait("vector", s_t, t_g)
                guard("vector", f"gb{par}")
                rec.add("vector", lambda e, b=cellb, par=par: e.tensor_tensor(
                    gb[:, par, :], b[:, 128:256], P["uincl_f"][:, :], OP.mult), s_v, 1)
                v_gb = s_v.n
                if "GBD" in ctx.dbg and c == 0 and sub == 0 and h == 0:
                    rec.wait("gpsimd", s_v, v_gb)
                    rec.add("gpsimd", lambda e, par=par: e.dma_start(
                        ctx.scr["GBD"][:, :], gb[:, par, :]), s_stg, 16)
                    rec.wait("gpsimd", s_sc, sc_tr)
                    rec.add("gpsimd", lambda e, par=par: e.dma_start(
                        ctx.scr["TTD"][:, 0:128], ptt[:, par, :]), s_stg, 16)
                    rec.add("gpsimd", lambda e, par=par: e.dma_start(
                        ctx.scr["TTD"][:, 128:256], ett[:, par, :]), s_stg, 16)
                guard("tensor", f"B{par}")
                rec.wait("tensor", s_v, v_gb)
                guard("tensor", f"hvb{h}")
                def ovmm(e, par=par, vsl=vsl, h=h):
                    e.matmul(B[par][:, 0:128], ptt[:, par, :], HvB[:, h * 128:(h + 1) * 128],
                             start=True, stop=False)
                    return e.matmul(B[par][:, 0:128], gb[:, par, :], vsl,
                                    start=False, stop=True)
                rec.add("tensor", ovmm, s_t, 1)
                t_ov = s_t.n
                setfree(f"gb{par}", "t", t_ov)
                setfree(f"ptt{par}", "t", t_ov)
                if "OVD" in ctx.dbg and c == 0 and sub == 0 and h == 0:
                    rec.wait("scalar", s_t, t_ov)
                    rec.add("scalar", lambda e, par=par: e.copy(
                        ctx.P["ovd_sb"][:, 0:128], B[par][:, 0:128]), s_sc, 1)
                    rec.wait("gpsimd", s_sc, s_sc.n)
                    rec.add("gpsimd", lambda e: e.dma_start(
                        ctx.scr["OVD"][:, 0:128], ctx.P["ovd_sb"][:, 0:128]), s_stg, 16)
                rec.wait("tensor", s_v, v_okn)
                def hkmm(e, knsl=knsl, esl=esl):
                    return e.matmul(B[6][0:64, 64:128], knsl, esl, start=True, stop=True)
                rec.add("tensor", hkmm, s_t, 1)
                t_hk = s_t.n
                def hvmm(e, esl=esl, vsl=vsl):
                    return e.matmul(B[6][0:64, 128:256], esl, vsl, start=True, stop=True)
                rec.add("tensor", hvmm, s_t, 1)
                t_hv = s_t.n
                rec.wait("vector", s_t, t_hk)
                if h in hk_gdone:
                    rec.wait("vector", s_g, hk_gdone[h])
                rec.add("vector", lambda e, h=h: e.tensor_tensor(
                    HkS[:, h * 64:(h + 1) * 64], HkS[:, h * 64:(h + 1) * 64],
                    B[6][0:64, 64:128], OP.add), s_v, 1)
                v_hk_add = s_v.n
                rec.wait("gpsimd", s_v, v_hk_add)
                rec.add("gpsimd", lambda e, h=h: e.tensor_copy(
                    HkB[0:64, h * 64:(h + 1) * 64], HkS[:, h * 64:(h + 1) * 64]), s_g, 1)
                rec.add("gpsimd", lambda e, h=h: e.tensor_copy(
                    HkB[64:128, h * 64:(h + 1) * 64], HkS[:, h * 64:(h + 1) * 64]), s_g, 1)
                hk_gdone[h] = s_g.n
                setfree(f"hkb{h}", "g", s_g.n)
                rec.wait("vector", s_t, t_hv)
                if h in hv_gdone:
                    rec.wait("vector", s_g, hv_gdone[h])
                rec.add("vector", lambda e, h=h: e.tensor_tensor(
                    HvS[:, h * 128:(h + 1) * 128], HvS[:, h * 128:(h + 1) * 128],
                    B[6][0:64, 128:256], OP.add), s_v, 1)
                v_hv = s_v.n
                rec.wait("gpsimd", s_v, v_hv)
                rec.add("gpsimd", lambda e, h=h: e.tensor_copy(
                    HvB[:, h * 128:(h + 1) * 128], HvS[:, h * 128:(h + 1) * 128]), s_g, 1)
                hv_gdone[h] = s_g.n
                setfree(f"hvb{h}", "g", s_g.n)
                rec.wait("scalar", s_t, t_ov)
                guard("scalar", "sqd")
                guard("scalar", "colf45")
                rec.add("scalar", lambda e, par=par: e.activation(
                    sqd[:, :], B[par][:, 0:128], AF.Square,
                    accum_out=colf[:, 4:5]), s_sc, 1)
                sc_sq = s_sc.n
                setfree("sqd", "sc", sc_sq)
                guard("scalar", "colf6")
                rec.add("scalar", lambda e: e.activation(
                    colf[:, 6:7], colf[:, 4:5], AF.Sqrt,
                    bias=P["eps"][:, 0:1], scale=1.0 / 128.0), s_sc, 1)
                sc_sd = s_sc.n
                setfree("colf45", "sc", sc_sd)
                guard("scalar", f"ggf{par}")
                rec.add("scalar", lambda e, gsl=gsl, par=par: e.activation(
                    ggf[:, par, :], gsl, AF.Silu), s_sc, 1)
                sc_sg = s_sc.n
                rec.wait("vector", s_sc, sc_sd)
                guard("vector", "colf7")
                rec.add("vector", lambda e: e.reciprocal(colf[:, 7:8], colf[:, 6:7]), s_v, 1)
                setfree("colf6", "v", s_v.n)
                guard("vector", f"o1f{par}")
                rec.add("vector", lambda e, par=par: e.scalar_tensor_tensor(
                    o1f[:, par, :], B[par][:, 0:128], colf[:, 7:8], gnwB[:, :],
                    OP.mult, OP.mult), s_v, 1)
                v_o1 = s_v.n
                setfree(f"B{par}", "v", v_o1)
                setfree("colf7", "v", v_o1)
                rec.wait("vector", s_sc, sc_sg)
                guard("vector", f"aob{h}")
                rec.add("vector", lambda e, par=par, h=h: e.tensor_tensor(
                    aob_at(h), o1f[:, par, :], ggf[:, par, :], OP.mult), s_v, 1)
                if "OVD" in ctx.dbg and c == 0 and sub == 0 and h == 0:
                    rec.wait("gpsimd", s_v, s_v.n)
                    rec.add("gpsimd", lambda e, par=par: e.dma_start(
                        ctx.scr["OVD"][:, 128:256], o1f[:, par, :]), s_stg, 16)
                    rec.add("gpsimd", lambda e, par=par: e.dma_start(
                        ctx.scr["OVD"][:, 256:384], ggf[:, par, :]), s_stg, 16)
                v_ao = s_v.n
                setfree(f"ggf{par}", "v", v_ao)
                setfree(f"o1f{par}", "v", v_ao)
                setfree(f"aob_w{h}", "v", v_ao)
            for ht in range(16):
                guard("tensor", "B7c")
                nm_, cnt_ = free[f"aob_w{ht}"]
                rec.wait("tensor", sems[nm_], cnt_)
                rec.add("tensor", lambda e, ht=ht: e.transpose(
                    b7[:, 256:384], aob_at(ht), P["ident_b"][:, :]), s_t, 1)
                t_tr2 = s_t.n
                setfree(f"aob{ht}", "t", t_tr2)
                rec.wait("scalar", s_t, t_tr2)
                rec.wait("scalar", s_t, sq_free_t)
                guard("scalar", f"awt{ht}")
                rec.add("scalar", lambda e, ht=ht, sub=sub: e.copy(
                    awt[:, ht, sub * 128:(sub + 1) * 128], b7[:, 256:384]), s_sc, 1)
                setfree("B7c", "sc", s_sc.n)
        awt_sc = s_sc.n
        setfree("mvbufs_t", "t", s_t.n)
        setfree("gnb_sc", "sc", s_sc.n)
        last_t = 0
        end_v = 0
        for oc in range(16):
            slot = oc % 2
            lw = load_wo(oc, slot)
            bank = B[2 + slot]
            guard("tensor", f"B{2 + slot}")
            rec.wait("tensor", s_ld, lw)
            rec.wait("tensor", s_sc, awt_sc)
            rec.mm_group(bank[:, :], [(wmv[:, hv, slot * 128:(slot + 1) * 128], awt[:, hv, :])
                                      for hv in range(16)], s_t)
            last_t = s_t.n
            setfree(f"mvq{slot}", "t", last_t)
            rec.wait("vector", s_t, last_t)
            rec.wait("vector", s_ld, ld_tiles)
            rec.wait("vector", s_g, xln_g)
            guard("vector", f"xn{slot}")
            rec.add("vector", lambda e, oc=oc, b=bank, slot=slot: e.tensor_tensor(
                xn[:, slot, :], b[:, :], xt[:, oc, :], OP.add), s_v, 1)
            end_v = s_v.n
            setfree(f"B{2 + slot}", "v", end_v)
            rec.wait("gpsimd", s_v, end_v)
            rec.add("gpsimd", lambda e, oc=oc, slot=slot, c0=c0: e.dma_start(
                scr["XT"][oc, :, c0:c0 + 512], xn[:, slot, :]), s_stg, 16)
            setfree(f"xn{slot}", "stg", s_stg.n)
        setfree("xt", "v", end_v)
        setfree("xlt_w", "t", last_t)
        setfree("qt", "t", last_t)
        setfree("kt", "t", last_t)
        setfree("sqbf", "t", last_t)
        for ht in range(16):
            setfree(f"awt{ht}", "t", last_t)
    return rec

def rec_s6(ctx, sems, P):
    rec = Rec(ctx)
    s_ld, s_ldg, s_t, s_v, s_sc, s_g, s_st, s_stg = (
        sems[k] for k in ("ld", "ldg", "t", "v", "sc", "g", "st", "stg"))
    sb, psb = ctx.sb, ctx.psb
    scr = ctx.scr
    wins = ctx.wins

    xt = sb("s6_xt", [128, 16, 512], F32)
    sqbf = sb("s6_sqbf", [128, 16, 512], BF16)
    xlt = sb("s6_xlt", [128, 16, 512], BF16)
    tmpf = sb("s6_tmpf", [128, 2, 512], F32)
    rows = sb("s6_rows", [1, 4 * 512], F32)
    ab = sb("s6_ab", [128, 2 * 512], F32)
    wrt = sb("s6_wrt", [128, 16, 128], BF16)
    amp = sb("s6_amp", [128, 16], F32)
    esc = sb("s6_esc", [128, 1], F32)
    mxk = sb("s6_mxk", [128, 16], F32)
    prb = sb("s6_prb", [128, 3, 128], F32)
    mk = sb("s6_mk", [128, 128], F32)
    mskd = sb("s6_mskd", [128, 128], BF16)
    mskdT = sb("s6_mskdT", [128, 128], BF16)
    Mx = sb("s6_Mx", [128, 8], BF16)
    cbuf = sb("s6_cbuf", [128, 4, 8], F32)
    wgt = sb("s6_wgt", [128, 16, 512], BF16)
    wut = sb("s6_wut", [128, 16, 512], BF16)
    sg = sb("s6_sg", [128, 512], BF16)
    wpart = sb("s6_wpart", [128, 512], F32)
    wsum = sb("s6_wsum", [128, 4, 1024], F32)
    shb = sb("s6_shb", [128, 4, 4, 512], BF16)
    wsT = sb("s6_wsT", [128, 8, 4, 128], BF16)
    shT = sb("s6_shT", [128, 16, 4, 128], BF16)
    wdnt = sb("s6_wdnt", [128, 8, 128], BF16)
    shdt = sb("s6_shdt", [128, 16, 128], BF16)
    xn = sb("s6_xn", [128, 2, 512], F32)

    B = [psb(f"s6_b{i}", [128, 512], F32) for i in range(7)]
    b7 = psb("s6_b7", [128, 1024], BF16)

    free = {}
    def guard(engname, res):
        if res in free:
            nm, cnt = free[res]
            rec.wait(engname, sems[nm], cnt)
    def setfree(res, nm, cnt):
        free[res] = (nm, cnt)

    wr_src = ctx.wdram["RTRB"].rearrange("(dt p) o -> p dt o", p=128)
    rec.add("sync", lambda e: e.dma_start(wrt[:, :, :], wr_src), s_ld, 16)
    rec.add("gpsimd", lambda e: e.dma_start(
        amp[:, :], wins["w1_amps"].rearrange("e k two -> e (k two)")), s_ldg, 16)
    rec.add("gpsimd", lambda e: e.dma_start(
        esc[:, :], wins["w1_escale"].rearrange("(e b) -> e b", b=1)), s_ldg, 16)
    ld_const = s_ldg.n
    rec.wait("vector", s_ldg, ld_const)
    amp3 = amp.rearrange("p (k two) -> p k two", two=2)
    rec.add("vector", lambda e: e.tensor_tensor(mxk[:, 0:8], amp3[:, :, 0], amp3[:, :, 0], OP.mult), s_v, 1)
    rec.wait("vector", s_v)
    rec.add("vector", lambda e: e.tensor_tensor(prb[:, 0, 0:8], amp3[:, :, 1], amp3[:, :, 1], OP.mult), s_v, 1)
    rec.wait("vector", s_v)
    rec.add("vector", lambda e: e.tensor_tensor(mxk[:, 0:8], mxk[:, 0:8], prb[:, 0, 0:8], OP.add), s_v, 1)
    rec.wait("vector", s_v)
    rec.add("vector", lambda e: e.reduce_sum(mxk[:, 8:9], mxk[:, 0:8], axis=AX.X), s_v, 1)
    rec.wait("vector", s_v)
    rec.add("vector", lambda e: e.tensor_scalar(mxk[:, 9:10], mxk[:, 8:9], 1.0, 1e-8, OP.mult, OP.add), s_v, 1)
    rec.wait("vector", s_v)
    rec.add("vector", lambda e: e.reciprocal(mxk[:, 10:11], mxk[:, 9:10]), s_v, 1)
    rec.wait("vector", s_v)
    rec.add("vector", lambda e: e.tensor_scalar(mxk[:, 0:8], mxk[:, 0:8], mxk[:, 10:11], None, OP.mult), s_v, 1)
    rec.wait("vector", s_v)
    def mkmx(e):
        with ctx.nc.allow_low_precision(reason="expert mix as bf16 matmul operand"):
            return e.tensor_scalar(Mx[:, :], mxk[:, 0:8], esc[:, :], None, OP.mult)
    rec.add("vector", mkmx, s_v, 1)
    v_mx = s_v.n

    prev_ab = None
    for c in range(NT):
        c0 = c * 512
        guard("sync", "xt")
        for d in range(16):
            rec.add("sync", lambda e, d=d, c0=c0: e.dma_start(
                xt[:, d, :], scr["XT"][d, :, c0:c0 + 512]), s_ld, 16)
        ld_tiles = s_ld.n
        for e_ in ("vector", "scalar"):
            rec.wait(e_, s_ld, ld_tiles)
        guard("scalar", "sqbf")
        guard("tensor", "B2")
        guard("tensor", "B3")
        guard("tensor", "B4")
        guard("tensor", "B5")
        guard("gpsimd", "xlt_w")
        lnc = rec_ln_chunk(rec, sems, P, {
            "xt": xt, "xbf": None, "sqbf": sqbf,
            "xln_at": (lambda d: xlt[:, d, :]),
            "xt_ready": ("ld", ld_tiles),
            "tmpf": tmpf, "rows": rows, "ab": ab, "ab_guard": prev_ab,
            "ps_row0": B[2], "ps_row1": B[3],
            "ps_bc0": B[4], "ps_bc1": B[5]}, c, 16, 1.0 / 2048.0, 0)
        prev_ab = lnc["ab_free"]
        setfree("sqbf", "t", lnc["sqbf_free_t"])
        setfree("B2", "v", lnc["psrow_free_v"])
        setfree("B3", "v", lnc["psrow_free_v"])
        setfree("B4", "sc", lnc["psbc_free_sc"])
        setfree("B5", "sc", lnc["psbc_free_sc"])
        rec.wait("tensor", s_g, lnc["xln_done_g"])

        for sub in range(4):
            t_sl = slice(sub * 128, (sub + 1) * 128)
            guard("tensor", "B2")
            rec.wait("tensor", s_ld, s_ld.n)
            rec.wait("tensor", s_ldg, ld_const)
            rec.mm_group(B[2][:, 0:128], [(xlt[:, d, t_sl], wrt[:, d, :]) for d in range(16)], s_t)
            t_rt = s_t.n
            rec.wait("vector", s_t, t_rt)
            guard("vector", "rtcols")
            rec.add("vector", lambda e: e.reduce_max(mxk[:, 11:12], B[2][:, 0:128], axis=AX.X), s_v, 1)
            rec.wait("vector", s_v)
            rec.add("vector", lambda e: e.tensor_scalar(mxk[:, 12:13], mxk[:, 11:12], -1.0, None, OP.mult), s_v, 1)
            v_neg = s_v.n
            rec.wait("scalar", s_v, v_neg)
            guard("scalar", "expb")
            rec.add("scalar", lambda e: e.activation(
                prb[:, 0, :], B[2][:, 0:128], AF.Exp, bias=mxk[:, 12:13],
                accum_out=mxk[:, 13:14]), s_sc, 1)
            sc_exp = s_sc.n
            setfree("B2", "sc", sc_exp)
            rec.wait("vector", s_sc, sc_exp)
            rec.add("vector", lambda e: e.reciprocal(mxk[:, 14:15], mxk[:, 13:14]), s_v, 1)
            rec.wait("vector", s_v)
            rec.add("vector", lambda e: e.tensor_scalar(
                prb[:, 0, :], prb[:, 0, :], mxk[:, 14:15], None, OP.mult), s_v, 1)
            v_probs = s_v.n
            cur = 0
            for it in range(3):
                rec.wait("vector", s_v)
                rec.add("vector", lambda e, cur=cur: e.reduce_max(
                    mxk[:, 11:12], prb[:, cur, :], axis=AX.X), s_v, 1)
                rec.wait("vector", s_v)
                rec.add("vector", lambda e, cur=cur: e.tensor_scalar(
                    mk[:, :], prb[:, cur, :], mxk[:, 11:12], None, OP.is_ge), s_v, 1)
                rec.wait("vector", s_v)
                nxt = 1 + (it % 2)
                rec.add("vector", lambda e, cur=cur, nxt=nxt: e.scalar_tensor_tensor(
                    prb[:, nxt, :], mk[:, :], -2.0, prb[:, cur, :], OP.mult, OP.add), s_v, 1)
                cur = nxt
            rec.wait("vector", s_v)
            rec.add("vector", lambda e, cur=cur: e.reduce_max(
                mxk[:, 11:12], prb[:, cur, :], axis=AX.X), s_v, 1)
            rec.wait("vector", s_v)
            guard("vector", "mskd")
            rec.add("vector", lambda e: e.scalar_tensor_tensor(
                mskd[:, :], prb[:, 0, :], mxk[:, 11:12], prb[:, 0, :],
                OP.is_ge, OP.mult, accum_out=mxk[:, 15:16]), s_v, 1)
            v_mskd = s_v.n
            rec.wait("vector", s_v)
            rec.add("vector", lambda e: e.tensor_scalar(
                mxk[:, 13:14], mxk[:, 15:16], 1.0, 1e-8, OP.mult, OP.add), s_v, 1)
            rec.wait("vector", s_v)
            rec.add("vector", lambda e: e.reciprocal(mxk[:, 14:15], mxk[:, 13:14]), s_v, 1)
            v_rcp = s_v.n
            guard("tensor", "B7")
            rec.wait("tensor", s_v, v_mskd)
            rec.add("tensor", lambda e: e.transpose(
                b7[:, 0:128], mskd[:, :], P["ident_b"][:, :]), s_t, 1)
            t_tm = s_t.n
            setfree("mskd", "t", t_tm)
            rec.wait("scalar", s_t, t_tm)
            guard("scalar", "mskdT")
            rec.add("scalar", lambda e: e.copy(mskdT[:, :], b7[:, 0:128]), s_sc, 1)
            sc_tm = s_sc.n
            setfree("B7", "sc", sc_tm)
            guard("tensor", "B3")
            rec.wait("tensor", s_sc, sc_tm)
            rec.wait("tensor", s_v, v_mx)
            rec.add("tensor", lambda e: e.matmul(B[3][:, 0:8], mskdT[:, :], Mx[:, :],
                                                 start=True, stop=True), s_t, 1)
            t_c = s_t.n
            setfree("mskdT", "t", t_c)
            rec.wait("vector", s_t, t_c)
            rec.wait("vector", s_v, v_rcp)
            guard("vector", f"cbuf{sub}")
            rec.add("vector", lambda e, sub=sub: e.tensor_scalar(
                cbuf[:, sub, :], B[3][:, 0:8], mxk[:, 14:15], None, OP.mult), s_v, 1)
            v_c = s_v.n
            setfree("B3", "v", v_c)
            setfree("rtcols", "v", v_c)
            setfree("expb", "v", v_c)

        if "CBD" in ctx.dbg:
            for sub in range(4):
                rec.wait("gpsimd", s_v, s_v.n)
                rec.add("gpsimd", lambda e, c0=c0, sub=sub: e.dma_start(
                    ctx.scr["CBD"][c0 + sub * 128:c0 + (sub + 1) * 128, :], cbuf[:, sub, :]), s_stg, 16)
        for oc in range(16):
            kk = oc // 2
            half = oc % 2
            guard("sync", "wgt")
            wg_src = ctx.wdram["GATB"].rearrange("(dt p) o -> p dt o", p=128)[:, :, oc * 512:(oc + 1) * 512]
            wu_src = ctx.wdram["UPB"].rearrange("(dt p) o -> p dt o", p=128)[:, :, oc * 512:(oc + 1) * 512]
            rec.add("sync", lambda e, src=wg_src: e.dma_start(wgt[:, :, :], src), s_ld, 16)
            rec.add("sync", lambda e, src=wu_src: e.dma_start(wut[:, :, :], src), s_ld, 16)
            ldw = s_ld.n
            for sub in range(4):
                t_sl = slice(sub * 128, (sub + 1) * 128)
                guard("tensor", "B0")
                rec.wait("tensor", s_ld, ldw)
                rec.mm_group(B[0][:, :], [(xlt[:, d, t_sl], wgt[:, d, :]) for d in range(16)], s_t)
                t_gt = s_t.n
                guard("tensor", "B1")
                rec.mm_group(B[1][:, :], [(xlt[:, d, t_sl], wut[:, d, :]) for d in range(16)], s_t)
                t_up = s_t.n
                rec.wait("scalar", s_t, t_gt)
                guard("scalar", "sg")
                rec.add("scalar", lambda e: e.activation(sg[:, :], B[0][:, :], AF.Silu), s_sc, 1)
                sc_sg = s_sc.n
                setfree("B0", "sc", sc_sg)
                rec.wait("vector", s_sc, sc_sg)
                rec.wait("vector", s_t, t_up)
                rec.wait("vector", s_v, v_c if sub == 3 else s_v.n)
                if kk == 0:
                    guard("vector", f"wsumw{sub}{half}")
                    rec.add("vector", lambda e, sub=sub, half=half, kk=kk: e.scalar_tensor_tensor(
                        wsum[:, sub, half * 512:(half + 1) * 512], B[1][:, :],
                        cbuf[:, sub, kk:kk + 1], sg[:, :], OP.mult, OP.mult), s_v, 1)
                else:
                    rec.add("vector", lambda e, sub=sub, kk=kk: e.scalar_tensor_tensor(
                        wpart[:, :], B[1][:, :], cbuf[:, sub, kk:kk + 1], sg[:, :],
                        OP.mult, OP.mult), s_v, 1)
                    rec.wait("vector", s_v)
                    rec.add("vector", lambda e, sub=sub, half=half: e.tensor_tensor(
                        wsum[:, sub, half * 512:(half + 1) * 512],
                        wsum[:, sub, half * 512:(half + 1) * 512], wpart[:, :], OP.add), s_v, 1)
                v_ws = s_v.n
                setfree("B1", "v", v_ws)
                setfree("sg", "v", v_ws)
            setfree("wgt", "t", s_t.n)
        v_wsum_done = s_v.n

        for sc_ch in range(4):
            shg_src = ctx.wdram["SHGB"].rearrange("(dt p) o -> p dt o", p=128)[:, :, sc_ch * 512:(sc_ch + 1) * 512]
            shu_src = ctx.wdram["SHUB"].rearrange("(dt p) o -> p dt o", p=128)[:, :, sc_ch * 512:(sc_ch + 1) * 512]
            guard("sync", "wgt")
            rec.add("sync", lambda e, src=shg_src: e.dma_start(wgt[:, :, :], src), s_ld, 16)
            rec.add("sync", lambda e, src=shu_src: e.dma_start(wut[:, :, :], src), s_ld, 16)
            ldw = s_ld.n
            for sub in range(4):
                t_sl = slice(sub * 128, (sub + 1) * 128)
                guard("tensor", "B4")
                rec.wait("tensor", s_ld, ldw)
                rec.mm_group(B[4][:, :], [(xlt[:, d, t_sl], wgt[:, d, :]) for d in range(16)], s_t)
                t_gt = s_t.n
                guard("tensor", "B5")
                rec.mm_group(B[5][:, :], [(xlt[:, d, t_sl], wut[:, d, :]) for d in range(16)], s_t)
                t_up = s_t.n
                rec.wait("scalar", s_t, t_gt)
                guard("scalar", "sg")
                rec.add("scalar", lambda e: e.activation(sg[:, :], B[4][:, :], AF.Silu), s_sc, 1)
                sc_sg2 = s_sc.n
                setfree("B4", "sc", sc_sg2)
                rec.wait("vector", s_sc, sc_sg2)
                rec.wait("vector", s_t, t_up)
                guard("vector", f"shb{sub}{sc_ch}")
                rec.add("vector", lambda e, sub=sub, sc_ch=sc_ch: e.tensor_tensor(
                    shb[:, sub, sc_ch, :], B[5][:, :], sg[:, :], OP.mult), s_v, 1)
                v_sh = s_v.n
                setfree("B5", "v", v_sh)
                setfree("sg", "v", v_sh)
            setfree("wgt", "t", s_t.n)
        v_sh_done = s_v.n

        if "WSD" in ctx.dbg:
            for sub in range(4):
                rec.wait("gpsimd", s_v, v_wsum_done)
                rec.add("gpsimd", lambda e, c0=c0, sub=sub: e.dma_start(
                    ctx.scr["WSD"][c0 + sub * 128:c0 + (sub + 1) * 128, :], wsum[:, sub, :]), s_stg, 16)
        if "SHBD" in ctx.dbg:
            for sub in range(4):
                rec.wait("gpsimd", s_v, v_sh_done)
                for j in range(4):
                    rec.add("gpsimd", lambda e, c0=c0, sub=sub, j=j: e.dma_start(
                        ctx.scr["SHBD"][c0 + sub * 128:c0 + (sub + 1) * 128, j * 512:(j + 1) * 512],
                        shb[:, sub, j, :]), s_stg, 16)
        for sub in range(4):
            for i8 in range(8):
                guard("tensor", "B6")
                rec.wait("vector", s_t, 0)
                rec.wait("tensor", s_v, v_wsum_done)
                rec.add("tensor", lambda e, sub=sub, i8=i8: e.transpose(
                    B[6][:, 0:128], wsum[:, sub, i8 * 128:(i8 + 1) * 128],
                    P["ident_f"][:, :]), s_t, 1)
                t_tr = s_t.n
                rec.wait("scalar", s_t, t_tr)
                guard("scalar", f"wsT{sub}{i8}")
                rec.add("scalar", lambda e, sub=sub, i8=i8: e.copy(
                    wsT[:, i8, sub, :], B[6][:, 0:128]), s_sc, 1)
                setfree("B6", "sc", s_sc.n)
            for i16 in range(16):
                guard("tensor", "B7")
                rec.wait("tensor", s_v, v_sh_done)
                rec.add("tensor", lambda e, sub=sub, i16=i16: e.transpose(
                    b7[:, 0:128], shb[:, sub, i16 // 4, (i16 % 4) * 128:(i16 % 4) * 128 + 128],
                    P["ident_b"][:, :]), s_t, 1)
                t_tr = s_t.n
                rec.wait("scalar", s_t, t_tr)
                guard("scalar", f"shT{sub}{i16}")
                rec.add("scalar", lambda e, sub=sub, i16=i16: e.copy(
                    shT[:, i16, sub, :], b7[:, 0:128]), s_sc, 1)
                setfree("B7", "sc", s_sc.n)
        sc_tr_done = s_sc.n
        setfree("wsum_w", "t", s_t.n)

        for doc in range(16):
            wd_src = ctx.wdram["DWNB"].rearrange("(dt p) o -> p dt o", p=128)[:, :, doc * 128:(doc + 1) * 128]
            sd_src = ctx.wdram["SHDB"].rearrange("(dt p) o -> p dt o", p=128)[:, :, doc * 128:(doc + 1) * 128]
            guard("sync", "wdnt")
            rec.add("sync", lambda e, src=wd_src: e.dma_start(wdnt[:, :, :], src), s_ld, 16)
            rec.add("sync", lambda e, src=sd_src: e.dma_start(shdt[:, :, :], src), s_ld, 16)
            ldw = s_ld.n
            par = doc % 2
            bank = B[par]
            guard("tensor", f"B{par}")
            rec.wait("tensor", s_ld, ldw)
            rec.wait("tensor", s_sc, sc_tr_done)
            def outmm2(e, doc=doc, bank=bank):
                for i8 in range(8):
                    e.matmul(bank[:, :], wdnt[:, i8, :],
                             wsT[:, i8, :, :].rearrange("p a b -> p (a b)"),
                             start=(i8 == 0), stop=False)
                last = None
                for i16 in range(16):
                    last = e.matmul(bank[:, :], shdt[:, i16, :],
                                    shT[:, i16, :, :].rearrange("p a b -> p (a b)"),
                                    start=False, stop=(i16 == 15))
                return last
            rec.add("tensor", outmm2, s_t, 1)
            t_out = s_t.n
            setfree("wdnt", "t", t_out)
            rec.wait("vector", s_t, t_out)
            rec.wait("vector", s_ld, ld_tiles)
            guard("vector", f"xn{par}")
            rec.add("vector", lambda e, doc=doc, bank=bank, par=par: e.tensor_tensor(
                xn[:, par, :], bank[:, :], xt[:, doc, :], OP.add), s_v, 1)
            v_out = s_v.n
            setfree(f"B{par}", "v", v_out)
            rec.wait("gpsimd", s_v, v_out)
            rec.add("gpsimd", lambda e, doc=doc, par=par, c0=c0: e.dma_start(
                scr["XT"][doc, :, c0:c0 + 512], xn[:, par, :]), s_stg, 16)
            setfree(f"xn{par}", "stg", s_stg.n)
        end_t = s_t.n
        end_v = s_v.n
        setfree("xt", "v", end_v)
        setfree("xlt_w", "t", end_t)
        for sub in range(4):
            setfree(f"cbuf{sub}", "v", end_v)
            for i8 in range(8):
                setfree(f"wsT{sub}{i8}", "t", end_t)
            for i16 in range(16):
                setfree(f"shT{sub}{i16}", "t", end_t)
            for h_ in range(2):
                setfree(f"wsumw{sub}{h_}", "t", end_t)
            for scch in range(4):
                setfree(f"shb{sub}{scch}", "t", end_t)
    return rec


SCRATCH = {
    "XT":   ([16, 128, T], F32),
    "QAR":  ([12, 128, T], BF16),
    "CKVR": ([4, 128, T], BF16),
    "KPEA": ([65, T], BF16),
    "QNT":  ([16, 128, T], BF16),
    "QPE":  ([8, 128, T], BF16),
    "KNT":  ([16, 128, T], BF16),
    "VNT":  ([T, 2048], BF16),
    "AOT":  ([16, 128, T], BF16),
    "HD":   ([64, 128, T], BF16),
    "XLD":  ([16, 128, T], BF16),
    "AOD":  ([T, 2048], BF16),
    "ED":   ([T, 1024], BF16),
    "RD":   ([T, 1024], BF16),
    "QKD":  ([2, 1024, T], BF16),
    "PVD":  ([T, 1024], BF16),
    "GBD":  ([128, 128], BF16),
    "VGD":  ([2, 128, 2048], BF16),
    "OVD":  ([128, 384], F32),
    "CBD":  ([T, 8], F32),
    "WSD":  ([T, 1024], F32),
    "SHBD": ([T, 2048], BF16),
    "TTD":  ([64, 256], BF16),
}
WSCRATCH = {
    "QAB": [1536, 2048], "QBB": [3072, 1536], "KVAB": [576, 2048],
    "KVBB": [4096, 512], "OWB": [2048, 2048],
    "W1B": [8192, 2048], "W3B": [8192, 2048],
    "W2BF": [2048, 8192], "W2T": [64, 128, 2048],
    "Q1B": [2048, 1024], "K1B": [2048, 1024], "V1B": [2048, 2048],
    "G1B": [2048, 2048], "S1B": [2048, 1024], "O1B": [2048, 2048],
    "RTRB": [2048, 128], "GATB": [2048, 8192], "UPB": [2048, 8192],
    "DWNB": [1024, 2048], "SHGB": [2048, 2048], "SHUB": [2048, 2048],
    "SHDB": [2048, 2048],
}
WEIGHT_INPUTS = {
    "w_qa": [1536, 2048], "w_qa_ln": [1536], "w_qb": [3072, 1536],
    "w_kva": [576, 2048], "w_kva_ln": [512], "w_kvb": [4096, 512],
    "w_ow": [2048, 2048], "w_W1": [8192, 2048], "w_W2": [2048, 8192],
    "w_W3": [8192, 2048],
    "w1_q": [2048, 1024], "w1_k": [2048, 1024], "w1_v": [2048, 2048],
    "w1_g": [2048, 2048], "w1_s": [2048, 1024], "w1_o": [2048, 2048],
    "w1_gn": [128], "w1_router": [2048, 128], "w1_gate": [2048, 8192],
    "w1_up": [2048, 8192], "w1_down": [1024, 2048], "w1_amps": [128, 8, 2],
    "w1_escale": [128], "w1_shg": [2048, 2048], "w1_shu": [2048, 2048],
    "w1_shd": [2048, 2048],
}


def build_program(upto=99, dbg=(), skip=(), **bass_kw):
    nc = bass.Bass("TRN2", target_bir_lowering=False, **bass_kw)
    es = ExitStack()
    nc._es_keepalive = es

    x_in = nc.dram_tensor("x_in", [T, D], F32, kind="ExternalInput").ap()
    y_out = nc.dram_tensor("y_out", [T, D], F32, kind="ExternalOutput").ap()
    consts = {}
    for name, arr in host_constants().items():
        consts[name] = nc.dram_tensor(name, list(arr.shape), F32, kind="ExternalInput").ap()
    wins = {n: nc.dram_tensor(n, shp, F32, kind="ExternalInput").ap()
            for n, shp in WEIGHT_INPUTS.items()}
    scr = {}
    for n, (shp, dt) in SCRATCH.items():
        kind = "ExternalOutput" if n in dbg else "Internal"
        scr[n] = nc.dram_tensor(n, shp, dt, kind=kind).ap()
    wdram = {n: nc.dram_tensor("ws_" + n, shp, BF16,
                               kind=("ExternalOutput" if n in dbg else "Internal")).ap()
             for n, shp in WSCRATCH.items()}

    ctx = Ctx(nc, es)
    ctx.cident = consts["c_ident"]
    ctx.consts = consts
    ctx.wins = wins
    ctx.scr = scr
    ctx.wdram = wdram
    ctx.dbg = dbg

    sems = {k: ctx.sem("s_" + k) for k in ("ld", "ld2", "ldg", "t", "v", "sc", "g", "st", "stg")}
    track = list(sems.values()) + [ctx.s_bar]

    P = {}
    P["ident_f"] = es.enter_context(nc.sbuf_tensor("P_ident_f", [128, 128], F32))
    P["ident_b"] = es.enter_context(nc.sbuf_tensor("P_ident_b", [128, 128], BF16))
    P["uincl_f"] = es.enter_context(nc.sbuf_tensor("P_uincl_f", [128, 128], F32))
    P["uincl_b"] = es.enter_context(nc.sbuf_tensor("P_uincl_b", [128, 128], BF16))
    P["maskneg"] = es.enter_context(nc.sbuf_tensor("P_maskneg", [128, 128], F32))
    P["h128s_b"] = es.enter_context(nc.sbuf_tensor("P_h128s_b", [128, 128], BF16))
    P["rrot_b"] = es.enter_context(nc.sbuf_tensor("P_rrot_b", [128, 128], BF16))
    P["ones_col_b"] = es.enter_context(nc.sbuf_tensor("P_ones_col_b", [128, 1], BF16))
    P["ones_row_f"] = es.enter_context(nc.sbuf_tensor("P_ones_row_f", [1, 128], F32))
    P["rowT_b"] = es.enter_context(nc.sbuf_tensor("P_rowT_b", [1, 512], BF16))
    P["eps"] = es.enter_context(nc.sbuf_tensor("P_eps", [128, 4], F32))
    P["ovd_sb"] = es.enter_context(nc.sbuf_tensor("P_ovd_sb", [128, 128], F32))
    P["ones_col_f"] = es.enter_context(nc.sbuf_tensor("P_ones_col_f", [128, 1], F32))
    P["ones_row_b"] = es.enter_context(nc.sbuf_tensor("P_ones_row_b", [1, 128], BF16))
    P["ones_1b"] = es.enter_context(nc.sbuf_tensor("P_ones_1b", [1, 1], BF16))
    ctx.P = P

    stage_es = [None]

    def open_stage():
        stage_es[0] = ExitStack()

    def close_stage():
        stage_es[0].close()

    ctx.sb = lambda n, shp, dt: stage_es[0].enter_context(nc.sbuf_tensor(n, shp, dt))
    ctx.psb = lambda n, shp, dt=F32: stage_es[0].enter_context(nc.psum_tensor(n, shp, dt))

    open_stage()
    run_rec_stage(ctx, "prep", rec_prep(ctx, sems, P), track)
    close_stage()

    open_stage()
    st_sb = lambda n, shp, dt: stage_es[0].enter_context(nc.sbuf_tensor(n, shp, dt))
    st_ps = lambda n, shp, dt: stage_es[0].enter_context(nc.psum_tensor(n, shp, dt))
    run_stage(ctx, "import", emit_import(ctx, sems, x_in, scr["XT"], st_sb, st_ps), track)
    close_stage()

    if upto >= 1:
        if "s1a" not in skip:
            open_stage()
            run_rec_stage(ctx, "s1a", rec_s1a(ctx, sems, P), track)
            close_stage()

    if upto >= 2:
        if "s1b" not in skip:
            open_stage()
            run_rec_stage(ctx, "s1b", rec_s1b(ctx, sems, P), track)
            close_stage()

    if upto >= 3:
        if "s2" not in skip:
            open_stage()
            rec2 = rec_s2(ctx, sems, P)
            if upto >= 5 and "s4w2prep" not in skip:
                rec_s4w2prep(ctx, sems, P, rec=rec2)
            run_rec_stage(ctx, "s2", rec2, track)
            close_stage()

    if upto >= 4:
        if "s3" not in skip:
            open_stage()
            run_rec_stage(ctx, "s3", rec_s3(ctx, sems, P), track)
            close_stage()

    if upto >= 5:
        if "s4w2prep" not in skip and ("s2" in skip or upto < 3):
            open_stage()
            run_rec_stage(ctx, "s4w2prep", rec_s4w2prep(ctx, sems, P), track)
            close_stage()
        grp_es = ExitStack()
        XL2T = grp_es.enter_context(nc.sbuf_tensor("XL2T", [128, 16, NT, 512], BF16))
        if "s4ln" not in skip:
            open_stage()
            run_rec_stage(ctx, "s4ln", rec_s4ln(ctx, sems, P, XL2T), track)
            close_stage()
        if "XLD" in dbg:
            open_stage()
            run_rec_stage(ctx, "dumpxl", rec_dumpxl(ctx, sems, XL2T), track)
            close_stage()
        if "s4uv" not in skip:
            open_stage()
            run_rec_stage(ctx, "s4uv", rec_s4uv(ctx, sems, P, XL2T), track)
            close_stage()
        grp_es.close()
        if "s4w2" not in skip:
            open_stage()
            run_rec_stage(ctx, "s4w2", rec_s4w2(ctx, sems, P), track)
            close_stage()

    if upto >= 6 and "s5" not in skip:
        open_stage()
        run_rec_stage(ctx, "s5", rec_s5(ctx, sems, P), track)
        close_stage()

    if upto >= 7 and "s6" not in skip:
        open_stage()
        run_rec_stage(ctx, "s6", rec_s6(ctx, sems, P), track)
        close_stage()


    open_stage()
    st_sb = lambda n, shp, dt: stage_es[0].enter_context(nc.sbuf_tensor(n, shp, dt))
    st_ps = lambda n, shp, dt: stage_es[0].enter_context(nc.psum_tensor(n, shp, dt))
    run_stage(ctx, "export", emit_export(ctx, sems, scr["XT"], y_out, st_sb, st_ps), track)
    close_stage()

    with nc.Block() as block:
        @block.sync
        def _(s):
            s.wait_ge(ctx.s_bar.h, ctx.s_bar.n)
            s.wait_ge(sems["st"].h, sems["st"].n)
            s.wait_ge(sems["stg"].h, sems["stg"].n)
    return nc


def host_weight_maps(inputs):
    H, NOPE, ROPE, VD = 16, 128, 64, 128
    qb = np.ascontiguousarray(inputs["L0_attn_q_b_w"])
    idx_nope = np.concatenate([np.arange(h * 192, h * 192 + 128) for h in range(H)])
    idx_pe = np.concatenate([np.arange(h * 192 + 128, h * 192 + 192) for h in range(H)])
    qb_r = np.ascontiguousarray(qb[np.concatenate([idx_nope, idx_pe])])
    kvb = np.ascontiguousarray(inputs["L0_attn_kv_b_w"])
    idx_k = np.concatenate([np.arange(h * 256, h * 256 + 128) for h in range(H)])
    idx_v = np.concatenate([np.arange(h * 256 + 128, h * 256 + 256) for h in range(H)])
    kvb_r = np.ascontiguousarray(kvb[np.concatenate([idx_k, idx_v])])
    return {
        "w_qa": inputs["L0_attn_q_a_w"],
        "w_qa_ln": inputs["L0_attn_q_a_ln_w"],
        "w_qb": qb_r,
        "w_kva": inputs["L0_attn_kv_a_w"],
        "w_kva_ln": inputs["L0_attn_kv_a_ln_w"],
        "w_kvb": kvb_r,
        "w_ow": inputs["L0_attn_o_w"],
        "w_W1": inputs["L0_ffn_W1"],
        "w_W2": inputs["L0_ffn_W2"],
        "w_W3": inputs["L0_ffn_W3"],
        "w1_q": inputs["L1_attn_Wq"],
        "w1_k": inputs["L1_attn_Wk"],
        "w1_v": inputs["L1_attn_Wv"],
        "w1_g": inputs["L1_attn_Wg"],
        "w1_s": inputs["L1_attn_Ws"],
        "w1_o": inputs["L1_attn_Wo"],
        "w1_gn": inputs["L1_attn_g_norm_weight"],
        "w1_router": inputs["L1_ffn_w_router"],
        "w1_gate": inputs["L1_ffn_w_gate"],
        "w1_up": inputs["L1_ffn_w_up"],
        "w1_down": inputs["L1_ffn_w_down"],
        "w1_amps": inputs["L1_ffn_amplitudes"],
        "w1_escale": inputs["L1_ffn_expert_scale"],
        "w1_shg": inputs["L1_ffn_shared_w_gate"],
        "w1_shu": inputs["L1_ffn_shared_w_up"],
        "w1_shd": inputs["L1_ffn_shared_w_down"],
    }


def kernel(**inputs):
    x = np.ascontiguousarray(inputs["L0_attn_hidden_states"])
    nc = build_program()
    consts = host_constants()
    wmap = {k: np.ascontiguousarray(v, dtype=np.float32) for k, v in host_weight_maps(inputs).items()}
    in_maps = []
    for c in range(8):
        m = {"x_in": np.ascontiguousarray(x[c])}
        m.update(consts)
        m.update(wmap)
        in_maps.append(m)
    res = run_bass_kernel_spmd(nc, in_maps, core_ids=list(range(8)))
    out = np.stack([res.results[c]["y_out"] for c in range(8)], axis=0)
    return out
